# version 05
# baseline (speedup 1.0000x reference)
import math
from contextlib import ExitStack
import numpy as np
import ml_dtypes
import concourse.bass as bass
import concourse.mybir as mybir
from concourse.bass_utils import run_bass_kernel_spmd

F32 = mybir.dt.float32
BF16 = mybir.dt.bfloat16
AF = mybir.ActivationFunctionType
ALU = mybir.AluOpType
AX = mybir.AxisListType
ds = bass.ds
bf16 = ml_dtypes.bfloat16

NCORE = 8
SEQ = 8192
TPC = SEQ // NCORE
DM = 2048
NDC = DM // 128
CH = 256
NCHUNK = TPC // CH

INTER = 4096
NHEAD = 64
PD = 64
ND = 128
KCONV = 4
CONV_DIM = INTER + 2 * ND
PROJ_DIM = INTER + CONV_DIM + NHEAD
DT_MIN, DT_MAX = 1e-3, 100.0
MEPS = 1e-5
LEPS = 1e-5

DFF0 = 5632
W1_ROWS = DFF0 // NCORE
W3_ROWS = DM // NCORE
NBLK = 22528
NGRP = NBLK // 256
GPP = NBLK // 128
TOT_BLOCKS = 180224
_NF4_REF = np.array([-1.0, -0.6962, -0.5251, -0.3949, -0.2844, -0.1848,
                     -0.0911, 0.0, 0.0796, 0.1609, 0.2461, 0.3379, 0.4407,
                     0.5626, 0.723, 1.0], dtype=bf16).astype(np.float64)
NF4L = [float(v) for v in _NF4_REF]
NF4_MID = [(NF4L[i] + NF4L[i + 1]) / 2.0 for i in range(15)]
RNE_MAGIC = 12582912.0

AHEADS = 16
HD = 128
ROPE_BASE = 10000.0

DFF1 = 8192

ENGINES = ("sync", "gpsimd", "vector", "scalar", "tensor")

LOGICAL = ("sync", "gpdma", "gpop", "vector", "scalar", "tensor")
NL = len(LOGICAL)
LIDX = {e: i for i, e in enumerate(LOGICAL)}
PHYS_OF = {"sync": "sync", "gpdma": "gpsimd", "gpop": "gpsimd",
           "vector": "vector", "scalar": "scalar", "tensor": "tensor"}
TENS_LI = LIDX["tensor"]

_DT_SIZE = {}


def _ap_ranges(ap):
    pat = ap.ap
    dt = ap.dtype
    esz = _DT_SIZE.get(dt)
    if esz is None:
        esz = dt.size()
        _DT_SIZE[dt] = esz
    off = ap.offset
    if str(ap.space).endswith("DRAM"):
        span = sum(st * (sz - 1) for st, sz in pat) + 1
        return (ap.name, 0, 128, off * esz, (off + span) * esz)
    p0 = ap.base_partition()
    pcnt = pat[0][1]
    pstride = pat[0][0]
    prow = off // pstride if pstride > 0 else 0
    f0 = off - prow * pstride
    span = (sum(st * (sz - 1) for st, sz in pat[1:]) + 1) if len(pat) > 1 else 1
    return (ap.name, p0, p0 + pcnt, f0 * esz, (f0 + span) * esz)


class Seq:

    def __init__(self):
        self.steps = []

    def emit(self, engine, fn, inc=1, reads=None, writes=None):
        assert engine in ENGINES, engine
        if engine == "gpsimd":
            engine = "gpdma" if inc == 16 else "gpop"
        elif engine == "sync":
            assert inc == 16, "sync engine hosts only DMA"
        self.steps.append((engine, fn, inc, reads, writes))

    def dma(self, engine, out, in_, **kw):
        self.emit(engine, lambda e: e.dma_start(out=out, in_=in_, **kw), 16,
                  reads=(in_,), writes=(out,))

    def install(self, nc, sems, block):
        steps = self.steps
        nst = len(steps)
        cum = [0] * NL
        issue_guar = {p: [0] * NL for p in ENGINES}
        cells = {}
        fbsz = {}
        waits = [None] * nst
        barrier_vec = [0] * NL

        def cells_of(ap):
            try:
                name, p0, p1, b0, b1 = _ap_ranges(ap)
            except Exception:
                return None
            fb = fbsz.get(name)
            if fb is None:
                try:
                    t = ap.tensor
                    totb = 1
                    for s in t.shape:
                        totb *= s
                    totb *= _DT_SIZE.get(ap.dtype) or ap.dtype.size()
                except Exception:
                    totb = 1 << 20
                fb = max(2048, totb >> 6)
                fbsz[name] = fb
            out = []
            for pb in range(p0 >> 5, ((p1 - 1) >> 5) + 1):
                for fbk in range(b0 // fb, (b1 - 1) // fb + 1):
                    out.append((name, pb, fbk))
            return out

        for k, (leng, fn, inc, reads, writes) in enumerate(steps):
            li = LIDX[leng]
            phys = PHYS_OF[leng]
            R = list(barrier_vec)
            rcells = []
            wcells = []
            bad = reads is None and writes is None
            if not bad:
                for ap in (reads or ()):
                    if not hasattr(ap, "offset"):
                        continue
                    cl = cells_of(ap)
                    if cl is None:
                        bad = True
                        break
                    rcells.extend(cl)
                for ap in (writes or ()):
                    if bad or not hasattr(ap, "offset"):
                        continue
                    cl = cells_of(ap)
                    if cl is None:
                        bad = True
                        break
                    wcells.extend(cl)
            if bad:
                for i in range(NL):
                    if cum[i] > R[i]:
                        R[i] = cum[i]
            else:
                for key in rcells:
                    c = cells.get(key)
                    if c is not None and c[0] is not None:
                        w = c[0]
                        for i in range(NL):
                            if w[i] > R[i]:
                                R[i] = w[i]
                for key in wcells:
                    c = cells.get(key)
                    if c is not None:
                        for v in c:
                            if v is not None:
                                for i in range(NL):
                                    if v[i] > R[i]:
                                        R[i] = v[i]
            iq = issue_guar[phys]
            wts = []
            for j in range(NL):
                if R[j] > iq[j]:
                    if phys == "tensor" and j == TENS_LI:
                        iq[j] = R[j]
                        continue
                    wts.append((j, R[j]))
                    iq[j] = R[j]
            waits[k] = wts
            cum[li] += inc
            g = [max(R[i], iq[i]) for i in range(NL)]
            g[li] = cum[li]
            if bad:
                barrier_vec = g
                cells.clear()
            else:
                for key in wcells:
                    cells[key] = [g, None]
                for key in rcells:
                    c = cells.get(key)
                    if c is None:
                        cells[key] = [None, list(g)]
                    elif c[1] is None:
                        c[1] = list(g)
                    else:
                        r = c[1]
                        for i in range(NL):
                            if g[i] > r[i]:
                                r[i] = g[i]

        totals = list(cum)

        def make(phys_name):
            def body(eng):
                for k, (leng, fn, inc, _r, _w) in enumerate(steps):
                    if PHYS_OF[leng] != phys_name:
                        continue
                    for (j, v) in waits[k]:
                        eng.wait_ge(sems[LOGICAL[j]], v)
                    inst = fn(eng)
                    assert inst is not None, f"{leng} step returned None"
                    inst.then_inc(sems[leng], inc)
                if phys_name == "sync":
                    for j, nm in enumerate(LOGICAL):
                        if totals[j]:
                            eng.wait_ge(sems[nm], totals[j])
            return body

        for phys_name in ENGINES:
            getattr(block, phys_name)(make(phys_name))
        return totals


def mk_sems(b):
    return {nm: b.sem(f"SQ_{nm}") for nm in LOGICAL}


def new_nc():
    return bass.Bass("TRN2", target_bir_lowering=False)


class Bufs:

    def __init__(self, nc, ctx):
        self.nc, self.ctx = nc, ctx

    def sb(self, name, shape, dtype=F32):
        return self.ctx.enter_context(self.nc.sbuf_tensor(name, shape, dtype))

    def ps(self, name, shape, dtype=F32):
        return self.ctx.enter_context(self.nc.psum_tensor(name, shape, dtype))

    def sem(self, name="S"):
        return self.ctx.enter_context(self.nc.semaphore(name))

    def block(self):
        return self.ctx.enter_context(self.nc.Block())



def e_memset(seq, ap, val, engine="vector"):
    seq.emit(engine, lambda e: e.memset(ap, float(val)), reads=(), writes=(ap,))


def e_identity(seq, ident):
    seq.emit("gpsimd", lambda e: e.memset(ident[:, :], 0.0),
             reads=(), writes=(ident[:, :],))
    seq.emit("gpsimd", lambda e: e.affine_select(
        out=ident[:, :], in_=ident[:, :], compare_op=ALU.not_equal,
        fill=1.0, base=0, pattern=[[-1, ident.shape[1]]], channel_multiplier=1),
        reads=(ident[:, :],), writes=(ident[:, :],))


def e_copy(seq, out, in_, engine="vector"):
    seq.emit(engine, lambda e: e.tensor_copy(out, in_),
             reads=(in_,), writes=(out,))


def e_act(seq, out, in_, func, bias=0.0, scale=1.0, accum_out=None):
    wr = (out,) if accum_out is None else (out, accum_out)
    seq.emit("scalar", lambda e: e.activation(out, in_, func, bias=bias,
                                              scale=scale, accum_out=accum_out),
             reads=(in_, bias), writes=wr)


def e_ts(seq, out, in0, s1, s2=None, op0=ALU.mult, op1=ALU.bypass, engine="vector"):
    if s2 is None:
        seq.emit(engine, lambda e: e.tensor_scalar(out, in0, s1, None, op0),
                 reads=(in0, s1), writes=(out,))
    else:
        seq.emit(engine, lambda e: e.tensor_scalar(out, in0, s1, s2, op0, op1),
                 reads=(in0, s1, s2), writes=(out,))


def e_tt(seq, out, in0, in1, op, engine="vector"):
    seq.emit(engine, lambda e: e.tensor_tensor(out, in0, in1, op),
             reads=(in0, in1), writes=(out,))


def e_stt(seq, out, in0, scalar, in1, op0, op1, engine="vector"):
    seq.emit(engine, lambda e: e.scalar_tensor_tensor(out, in0, scalar, in1,
                                                      op0, op1),
             reads=(in0, scalar, in1), writes=(out,))


def e_red(seq, out, in_, op, absval=False, negate=False):
    seq.emit("vector", lambda e: e.tensor_reduce(out, in_, AX.X, op,
                                                 apply_absolute_value=absval,
                                                 negate=negate),
             reads=(in_,), writes=(out,))


def e_mm(seq, out, lhsT, rhs, start=True, stop=True):
    rd = (lhsT, rhs) if start else (lhsT, rhs, out)
    seq.emit("tensor", lambda e: e.matmul(out, lhsT, rhs, start=start,
                                          stop=stop),
             reads=rd, writes=(out,))


def e_tr(seq, out, in_, ident):
    seq.emit("tensor", lambda e: e.transpose(out, in_, ident),
             reads=(in_, ident), writes=(out,))


def e_recip(seq, out, in_):
    seq.emit("vector", lambda e: e.reciprocal(out, in_),
             reads=(in_,), writes=(out,))


def e_bcast_col(seq, nc, ones_row, src_row, psum, dst_col=None):
    e_mm(seq, psum, ones_row[:, :], src_row)
    if dst_col is not None:
        e_copy(seq, dst_col, psum)



def build_launch_A(do_mod=True, do_nf4=True):
    nc = new_nc()
    cT = {}
    wmod = {}
    for L in range(2):
        cT[L] = nc.dram_tensor(f"c{L}T", [128, 16], F32, kind="ExternalInput").ap()
        wmod[L] = nc.dram_tensor(f"wmod{L}s", [DM, 12288 // NCORE], F32,
                                 kind="ExternalInput").ap()
    wsh = {}
    for i, (r, c) in enumerate([(W1_ROWS, DM), (W1_ROWS, DM), (W3_ROWS, DFF0)]):
        wsh[i] = nc.dram_tensor(f"w{i+1}s", [r, c], BF16, kind="ExternalInput").ap()
    modrow = nc.dram_tensor("modrow", [2, 12288 // NCORE], F32,
                            kind="ExternalOutput").ap()
    ssum = nc.dram_tensor("ssum", [1, 4], F32, kind="ExternalOutput").ap()

    FSH = 12288 // NCORE
    with ExitStack() as ctx:
        b = Bufs(nc, ctx)
        sc = b.sb("sc", [128, 16])
        silu_c = b.sb("silu_c", [128, 16])
        wbuf = b.sb("wbuf", [128, 16 * 512])
        mrow = [b.sb(f"mrow{L}", [1, FSH]) for L in range(2)]
        wg = b.sb("wg", [NGRP, 256 * 64], BF16)
        s1g = b.sb("s1g", [NGRP, 256])
        rowsum = b.sb("rowsum", [NGRP, 1])
        ones_col = b.sb("ones_col", [128, 1])
        ssum_sb = b.sb("ssum_sb", [1, 4])
        p_row = b.ps("p_row", [1, 512])
        p_sum = b.ps("p_sum", [1, 1])
        sems = mk_sems(b)
        block = b.block()
        seq = Seq()
        e_memset(seq, ones_col[:, :], 1.0)
        e_memset(seq, ssum_sb[:, :], 0.0)
        for L in range(2 if do_mod else 0):
            seq.dma("sync", sc[:, :], cT[L])
            e_act(seq, silu_c[:, :], sc[:, :], AF.Silu)
            for fs in range(FSH // 512):
                for c in range(NDC):
                    seq.dma("sync", wbuf[:, c * 512:(c + 1) * 512],
                            wmod[L][c * 128:(c + 1) * 128, fs * 512:(fs + 1) * 512])
                for c in range(NDC):
                    e_mm(seq, p_row[:, :], silu_c[:, c:c + 1], wbuf[:, c * 512:(c + 1) * 512],
                         start=(c == 0), stop=(c == NDC - 1))
                e_copy(seq, mrow[L][:, fs * 512:(fs + 1) * 512], p_row[:, :])
        for L in range(2):
            seq.dma("sync", modrow[L:L + 1, :], mrow[L][:, :])
        for i in range(3 if do_nf4 else 0):
            flat = wsh[i].rearrange("a b -> (a b)")
            src = flat.rearrange("(g j) -> g j", g=NGRP)
            seq.dma("sync", wg[:, :], src)
            e_red(seq, s1g[:, :], wg[:, :].rearrange("g (b k) -> g b k", k=64),
                  ALU.max, absval=True)
            e_red(seq, rowsum[:, :], s1g[:, :], ALU.add)
            e_mm(seq, p_sum[:, :], ones_col[:NGRP, :], rowsum[:, :])
            e_copy(seq, ssum_sb[:, i:i + 1], p_sum[:, :])
        seq.dma("sync", ssum, ssum_sb[:, :])
        seq.install(nc, sems, block)
    return nc



def build_launch_B():
    nc = new_nc()
    wsh = {}
    shapes = [(W1_ROWS, DM), (W1_ROWS, DM), (W3_ROWS, DFF0)]
    for i, (r, c) in enumerate(shapes):
        wsh[i] = nc.dram_tensor(f"w{i+1}s", [r, c], BF16, kind="ExternalInput").ap()
    allsum = nc.dram_tensor("allsum", [NCORE, 4], F32, kind="ExternalInput").ap()
    wq = {}
    for i, (r, c) in enumerate(shapes):
        wq[i] = nc.dram_tensor(f"wq{i+1}", [r, c], BF16, kind="ExternalOutput").ap()

    GW = 256 * 64
    with ExitStack() as ctx:
        b = Bufs(nc, ctx)
        wg = b.sb("wg", [NGRP, GW], BF16)
        s1g = b.sb("s1g", [NGRP, 256])
        r1g = b.sb("r1g", [NGRP, 256])
        cent = b.sb("cent", [NGRP, 256], BF16)
        centf = b.sb("centf", [NGRP, 256])
        gmax = b.sb("gmax", [NGRP, 1])
        qf = b.sb("qf", [NGRP, 1])
        rqf = b.sb("rqf", [NGRP, 1])
        qs = b.sb("qs", [NGRP, 256])
        deqb = b.sb("deqb", [NGRP, 256], BF16)
        sums = b.sb("sums", [1, 4 * NCORE])
        tot = b.sb("tot", [1, 4])
        meanb = b.sb("meanb", [1, 1], BF16)
        meanf = b.sb("meanf", [1, 1])
        ones_row = b.sb("ones_row", [1, 128])
        mean_col = b.sb("mean_col", [128, 1])
        scl = b.sb("scl", [NGRP, GW], BF16)
        tmp = b.sb("tmp", [NGRP, GW], BF16)
        acc = b.sb("acc", [NGRP, GW], BF16)
        p_mean = b.ps("p_mean", [128, 1])
        sems = mk_sems(b)
        block = b.block()
        seq = Seq()
        e_memset(seq, ones_row[:, :], 1.0)
        seq.dma("sync", sums[:, :], allsum.rearrange("a b -> (a b)")[None, :])

        for i in range(3):
            flat = wsh[i].rearrange("a b -> (a b)")
            gview = flat.rearrange("(g j) -> g j", g=NGRP)
            seq.dma("sync", wg[:, :], gview)
            e_red(seq, s1g[:, :], wg[:, :].rearrange("g (b k) -> g b k", k=64),
                  ALU.max, absval=True)
            e_red(seq, tot[:, i:i + 1],
                  sums[:, :].rearrange("o (c w) -> o w c", w=4)[:, i:i + 1, :],
                  ALU.add)
            seq.emit("vector", lambda e, i=i: e.tensor_single_scalar(
                meanb[:, :], tot[:, i:i + 1], float(1.0 / TOT_BLOCKS), ALU.mult),
                reads=(tot[:, i:i + 1],), writes=(meanb[:, :],))
            e_copy(seq, meanf[:, :], meanb[:, :])
            e_mm(seq, p_mean[:, :], ones_row[:, :], meanf[:, :])
            e_copy(seq, mean_col[:, :], p_mean[:, :])
            e_ts(seq, cent[:, :], s1g[:, :], mean_col[:NGRP, :], None, ALU.subtract)
            e_red(seq, gmax[:, :], cent[:, :], ALU.max, absval=True)
            e_recip(seq, qf[:, :], gmax[:, :])
            e_ts(seq, qf[:, :], qf[:, :], 128.0, None, ALU.mult)
            e_copy(seq, centf[:, :], cent[:, :])
            e_ts(seq, qs[:, :], centf[:, :], qf[:, :], RNE_MAGIC, ALU.mult, ALU.add)
            e_ts(seq, qs[:, :], qs[:, :], RNE_MAGIC, None, ALU.subtract)
            e_ts(seq, qs[:, :], qs[:, :], -128.0, 127.0, ALU.max, ALU.min)
            e_recip(seq, rqf[:, :], qf[:, :])
            e_ts(seq, deqb[:, :], qs[:, :], rqf[:, :], None, ALU.mult)
            e_copy(seq, centf[:, :], deqb[:, :])
            e_ts(seq, deqb[:, :], centf[:, :], mean_col[:NGRP, :], None, ALU.add)
            e_recip(seq, r1g[:, :], s1g[:, :])
            e_tt(seq, scl[:, :].rearrange("g (b k) -> g b k", k=64),
                 wg[:, :].rearrange("g (b k) -> g b k", k=64),
                 r1g[:, :].rearrange("g (b o) -> g b o", o=1).broadcast_to([NGRP, 256, 64]),
                 ALU.mult)
            e_memset(seq, acc[:, :], 0.0)
            for lv in range(7):
                e_ts(seq, tmp[:, :], scl[:, :], float(NF4_MID[lv]), float(NF4L[lv]),
                     ALU.is_le, ALU.mult)
                e_tt(seq, acc[:, :], acc[:, :], tmp[:, :], ALU.min)
            e_memset(seq, tmp[:, :], 0.0)
            for lv in range(7, 15):
                e_ts(seq, wg[:, :], scl[:, :], float(NF4_MID[lv]), float(NF4L[lv + 1]),
                     ALU.is_gt, ALU.mult)
                e_tt(seq, tmp[:, :], tmp[:, :], wg[:, :], ALU.max)
            e_tt(seq, acc[:, :], acc[:, :], tmp[:, :], ALU.add)
            e_tt(seq, scl[:, :].rearrange("g (b k) -> g b k", k=64),
                 acc[:, :].rearrange("g (b k) -> g b k", k=64),
                 deqb[:, :].rearrange("g (b o) -> g b o", o=1).broadcast_to([NGRP, 256, 64]),
                 ALU.mult)
            out_g = wq[i].rearrange("a b -> (a b)").rearrange("(g j) -> g j", g=NGRP)
            seq.dma("sync", out_g, scl[:, :])
        seq.install(nc, sems, block)
    return nc



def e_clamp_gate(seq, out_cols, in_cols, sgn_scratch):
    e_ts(seq, sgn_scratch, in_cols, 0.0, 2.0, ALU.is_ge, ALU.mult)
    e_ts(seq, sgn_scratch, sgn_scratch, -1.0, None, ALU.add)
    e_tt(seq, out_cols, in_cols, sgn_scratch, ALU.mult)
    e_ts(seq, out_cols, out_cols, 0.25, None, ALU.max)
    e_tt(seq, out_cols, out_cols, sgn_scratch, ALU.mult)


class LNBufs:
    def __init__(self, b, tag, p_b=None, p_sum=None, p_sq=None,
                 mean_b=None, rstd_b=None):
        self.sq = b.sb(f"ln_sq_{tag}", [128, 512])
        rt = b.sb(f"ln_rows_{tag}", [1, 1024])
        rt2 = b.sb(f"ln_rows2_{tag}", [1, 512])
        self.mean_row = rt[0:1, 0:512]
        self.rstd_row = rt[0:1, 512:1024]
        self.msq_row = rt2[0:1, :]
        self.mean_b = mean_b if mean_b is not None else b.sb(
            f"ln_mb_{tag}", [128, 512])
        self.rstd_b = rstd_b if rstd_b is not None else b.sb(
            f"ln_rb_{tag}", [128, 512])
        self.t1 = b.sb(f"ln_t1_{tag}", [128, 512])
        self.p_sum = p_sum if p_sum is not None else b.ps(
            f"ln_ps_{tag}", [1, 512])
        self.p_sq = p_sq if p_sq is not None else b.ps(f"ln_pq_{tag}", [1, 512])
        self.p_b = p_b if p_b is not None else b.ps(f"ln_pb_{tag}", [128, 512])


def emit_ln_mod(seq, lb, x_sb, h_out, ones_col, ones_row, ca1_cols, sa_cols,
                toks=TPC, nch=NDC):
    for t0 in range(0, toks, 512):
        tw = min(512, toks - t0)
        for c in range(nch):
            xc = x_sb[:, c * toks + t0: c * toks + t0 + tw]
            e_act(seq, lb.sq[:, :tw], xc, AF.Square)
            e_mm(seq, lb.p_sum[:, :tw], ones_col[:, :], xc,
                 start=(c == 0), stop=(c == nch - 1))
            e_mm(seq, lb.p_sq[:, :tw], ones_col[:, :], lb.sq[:, :tw],
                 start=(c == 0), stop=(c == nch - 1))
        dim = 128 * nch
        e_ts(seq, lb.mean_row[:, :tw], lb.p_sum[:, :tw], 1.0 / dim, None, ALU.mult)
        e_ts(seq, lb.msq_row[:, :tw], lb.p_sq[:, :tw], 1.0 / dim, None, ALU.mult)
        e_stt(seq, lb.rstd_row[:, :tw], lb.mean_row[:, :tw], -1.0,
              lb.mean_row[:, :tw], ALU.mult, ALU.mult)
        e_tt(seq, lb.msq_row[:, :tw], lb.msq_row[:, :tw], lb.rstd_row[:, :tw],
             ALU.add)
        e_ts(seq, lb.msq_row[:, :tw], lb.msq_row[:, :tw], LEPS, None, ALU.add)
        e_act(seq, lb.rstd_row[:, :tw], lb.msq_row[:, :tw], AF.Sqrt)
        e_recip(seq, lb.rstd_row[:, :tw], lb.rstd_row[:, :tw])
        e_mm(seq, lb.p_b[:, :tw], ones_row[:, :], lb.mean_row[:, :tw])
        e_copy(seq, lb.mean_b[:, :tw], lb.p_b[:, :tw])
        e_mm(seq, lb.p_b[:, :tw], ones_row[:, :], lb.rstd_row[:, :tw])
        e_copy(seq, lb.rstd_b[:, :tw], lb.p_b[:, :tw])
        for c in range(nch):
            xc = x_sb[:, c * toks + t0: c * toks + t0 + tw]
            hc = h_out[:, c * toks + t0: c * toks + t0 + tw]
            e_stt(seq, lb.t1[:, :tw], lb.mean_b[:, :tw], -1.0, xc, ALU.mult, ALU.add)
            e_stt(seq, lb.t1[:, :tw], lb.t1[:, :tw], ca1_cols[:, c:c + 1],
                  lb.rstd_b[:, :tw], ALU.mult, ALU.mult)
            e_ts(seq, hc, lb.t1[:, :tw], sa_cols[:, c:c + 1], None, ALU.add)


def dma_wslice_fm(seq, wbuf, wT_dram, n_in, f0, fw, engine="gpsimd"):
    nch = n_in // 128
    src = wT_dram.rearrange("(c p) f -> p c f", p=128)[:, :, f0:f0 + fw]
    dst = wbuf[:, :nch * fw].rearrange("p (c f) -> p c f", f=fw)
    seq.dma(engine, dst, src)


def emit_gemm_fm(seq, wT_dram, act_bf, n_in, f_lo, f_hi, toks, wbuf, psum,
                 consume, tok_tile=512):
    nch = n_in // 128
    for f0 in range(f_lo, f_hi, 128):
        dma_wslice_fm(seq, wbuf, wT_dram, n_in, f0, 128)
        for t0 in range(0, toks, tok_tile):
            tw = min(tok_tile, toks - t0)
            for c in range(nch):
                e_mm(seq, psum[:, :tw], wbuf[:, c * 128:(c + 1) * 128],
                     act_bf[:, c * toks + t0: c * toks + t0 + tw],
                     start=(c == 0), stop=(c == nch - 1))
            consume(f0 - f_lo, t0, tw, psum[:, :tw])



def build_launch_E():
    nc = new_nc()
    x1d = nc.dram_tensor("x1", [DM, TPC], F32, kind="ExternalInput").ap()
    w1T = nc.dram_tensor("w1pT", [DM, DFF0], BF16, kind="ExternalInput").ap()
    w2T = nc.dram_tensor("w2pT", [DM, DFF0], BF16, kind="ExternalInput").ap()
    w3T = nc.dram_tensor("w3pT", [DFF0, DM], BF16, kind="ExternalInput").ap()
    wqkvT = nc.dram_tensor("wqkvT", [DM, 3 * DM], F32, kind="ExternalInput").ap()
    modc = nc.dram_tensor("modc", [128, 2 * 6 * NDC], F32, kind="ExternalInput").ap()
    rrot = nc.dram_tensor("rrot", [128, 128], BF16, kind="ExternalInput").ap()
    posr = nc.dram_tensor("posr", [1, TPC], F32, kind="ExternalInput").ap()
    x2d = nc.dram_tensor("x2", [DM, TPC], F32, kind="ExternalOutput").ap()
    qTd = nc.dram_tensor("qT", [AHEADS, HD, TPC], BF16, kind="ExternalOutput").ap()
    kTd = nc.dram_tensor("kT", [AHEADS, HD, TPC], BF16, kind="ExternalOutput").ap()
    vtd = nc.dram_tensor("vt", [TPC, DM], BF16, kind="ExternalOutput").ap()
    qkmaxd = nc.dram_tensor("qkmax", [1, 2 * AHEADS], F32, kind="ExternalOutput").ap()

    NF0 = DFF0 // 128
    with ExitStack() as ctx:
        b = Bufs(nc, ctx)
        x_sb = b.sb("x_sb", [128, NDC * TPC])
        h_sb = b.sb("h_sb", [128, NDC * TPC], BF16)
        hid = b.sb("hid", [128, NF0 * 512], BF16)
        mcol = b.sb("mcol", [128, 2 * 6 * NDC])
        ca1 = b.sb("ca1", [128, NDC])
        gbc = b.sb("gbc", [128, NDC])
        sgn = b.sb("sgn", [128, NDC])
        ones_col = b.sb("ones_col", [128, 1])
        ones_row = b.sb("ones_row", [1, 128])
        wbuf1 = [b.sb(f"wbuf1_{i}", [128, NDC * 128], BF16) for i in range(2)]
        wbuf2 = [b.sb(f"wbuf2_{i}", [128, NDC * 128], BF16) for i in range(2)]
        wbufq = b.sb("wbufq", [128, NDC * 512], BF16)
        silu_s = b.sb("silu_s", [128, 512])
        rope_cos = b.sb("rope_cos", [128, TPC])
        rope_sin = b.sb("rope_sin", [128, TPC])
        ifr = b.sb("ifr", [128, 1])
        qk_bf = b.sb("qk_bf", [128, TPC], BF16)
        rot_sb = b.sb("rot_sb", [128, TPC], BF16)
        rconst = b.sb("rconst", [128, 128], BF16)
        posrow = b.sb("posrow", [1, TPC])
        lb = LNBufs(b, "e")
        p_a = b.ps("p_a", [128, 512])
        p_b2 = b.ps("p_b2", [128, 512])
        p_o = b.ps("p_o", [128, 512])
        p_r = b.ps("p_r", [128, 512])
        p_ah = [p_a, p_r]
        p_bh = [p_b2, p_o]
        sems = mk_sems(b)
        block = b.block()
        seq = Seq()

        e_memset(seq, ones_col[:, :], 1.0)
        e_memset(seq, ones_row[:, :], 1.0)
        seq.dma("sync", mcol[:, :], modc)
        for c in range(NDC):
            seq.dma("sync", x_sb[:, c * TPC:(c + 1) * TPC],
                    x1d[c * 128:(c + 1) * 128, :])
        e_ts(seq, ca1[:, :], mcol[:, 4 * NDC:5 * NDC], 1.0, None, ALU.add)
        e_clamp_gate(seq, gbc[:, :], mcol[:, 5 * NDC:6 * NDC], sgn[:, :])
        emit_ln_mod(seq, lb, x_sb, h_sb, ones_col, ones_row,
                    ca1, mcol[:, 3 * NDC:4 * NDC])

        for t0 in range(0, TPC, 512):
            for fi in range(NF0):
                f0 = fi * 128
                pr = fi % 2
                w1c = wbuf1[pr]
                w2c = wbuf2[pr]
                dma_wslice_fm(seq, w1c, w1T, DM, f0, 128)
                dma_wslice_fm(seq, w2c, w2T, DM, f0, 128)
                for c in range(NDC):
                    e_mm(seq, p_ah[pr][:, :], w1c[:, c * 128:(c + 1) * 128],
                         h_sb[:, c * TPC + t0: c * TPC + t0 + 512],
                         start=(c == 0), stop=(c == NDC - 1))
                for c in range(NDC):
                    e_mm(seq, p_bh[pr][:, :], w2c[:, c * 128:(c + 1) * 128],
                         h_sb[:, c * TPC + t0: c * TPC + t0 + 512],
                         start=(c == 0), stop=(c == NDC - 1))
                e_act(seq, silu_s[:, :], p_ah[pr][:, :], AF.Silu)
                e_tt(seq, hid[:, fi * 512:(fi + 1) * 512], silu_s[:, :],
                     p_bh[pr][:, :], ALU.mult)
            for dc in range(NDC):
                dma_wslice_fm(seq, wbufq, w3T, DFF0, dc * 128, 128)
                for fc in range(NF0):
                    e_mm(seq, p_o[:, :], wbufq[:, fc * 128:(fc + 1) * 128],
                         hid[:, fc * 512:(fc + 1) * 512],
                         start=(fc == 0), stop=(fc == NF0 - 1))
                xc = x_sb[:, dc * TPC + t0: dc * TPC + t0 + 512]
                e_stt(seq, xc, p_o[:, :], gbc[:, dc:dc + 1], xc, ALU.mult, ALU.add)
        for c in range(NDC):
            seq.dma("sync", x2d[c * 128:(c + 1) * 128, :],
                    x_sb[:, c * TPC:(c + 1) * TPC])

        e_ts(seq, ca1[:, :], mcol[:, 6 * NDC + 1 * NDC: 6 * NDC + 2 * NDC],
             1.0, None, ALU.add)
        emit_ln_mod(seq, lb, x_sb, h_sb, ones_col, ones_row,
                    ca1, mcol[:, 6 * NDC: 6 * NDC + NDC])

        seq.dma("sync", posrow[:, :], posr)
        seq.emit("gpsimd", lambda e: e.iota(ifr[:, :], [[0, 1]], base=0,
                                            channel_multiplier=1,
                                            allow_small_or_imprecise_dtypes=True),
                 reads=(), writes=(ifr[:, :],))
        e_ts(seq, silu_s[:, 0:1], ifr[:, :], 64.0, 64.0, ALU.is_ge, ALU.mult)
        e_tt(seq, ifr[:, :], ifr[:, :], silu_s[:, 0:1], ALU.subtract)
        e_act(seq, ifr[:, :], ifr[:, :], AF.Exp, scale=-math.log(10000.0) / 64.0)
        TWO_PI = 2.0 * math.pi
        c1 = float(np.float32(6.28125))
        c2 = float(np.float32(np.float64(TWO_PI) - 6.28125))
        c3 = float(np.float64(TWO_PI) - np.float64(c1) - np.float64(c2))
        for th in range(0, TPC, 512):
            ang = silu_s[:, :]
            kk = lb.t1[:, :]
            e_mm(seq, p_r[:, :], ones_row[:, :], posrow[:, th:th + 512])
            e_tt(seq, ang, p_r[:, :], ifr[:, :].broadcast_to([128, 512]), ALU.mult)
            e_ts(seq, kk, ang, 1.0 / TWO_PI, RNE_MAGIC, ALU.mult, ALU.add)
            e_ts(seq, kk, kk, RNE_MAGIC, None, ALU.subtract)
            for cw in (c1, c2, c3):
                e_stt(seq, ang, kk, -cw, ang, ALU.mult, ALU.add)
            e_act(seq, rope_sin[:, th:th + 512], ang, AF.Sin)
            e_ts(seq, kk, ang, float(math.pi / 2), TWO_PI, ALU.is_gt, ALU.mult)
            e_tt(seq, ang, ang, kk, ALU.subtract)
            e_ts(seq, ang, ang, float(math.pi / 2), None, ALU.add)
            e_act(seq, rope_cos[:, th:th + 512], ang, AF.Sin)

        seq.dma("sync", rconst[:, :], rrot)
        qkmax = b.sb("qkmax_sb", [1, 2 * AHEADS])
        nmtmp = b.sb("nmtmp", [1, 1])
        e_memset(seq, qkmax[:, :], 0.0)

        def qk_consume(dst, head, mx_off):
            def consume(f0, t0, tw, psum):
                e_copy(seq, qk_bf[:, t0:t0 + tw], psum)
                if t0 + tw == TPC:
                    for tt in range(0, TPC, 512):
                        e_mm(seq, p_r[:, :], rconst[:, :], qk_bf[:, tt:tt + 512])
                        e_copy(seq, rot_sb[:, tt:tt + 512], p_r[:, :])
                    for tt in range(0, TPC, 512):
                        e_tt(seq, silu_s[:, :], qk_bf[:, tt:tt + 512],
                             rope_cos[:, tt:tt + 512], ALU.mult)
                        e_tt(seq, lb.t1[:, :], rot_sb[:, tt:tt + 512],
                             rope_sin[:, tt:tt + 512], ALU.mult)
                        e_tt(seq, qk_bf[:, tt:tt + 512], silu_s[:, :],
                             lb.t1[:, :], ALU.add)
                    seq.dma("sync", dst[head], qk_bf[:, :])
                    mcell = qkmax[:, mx_off + head: mx_off + head + 1]
                    for tt in range(0, TPC, 512):
                        e_act(seq, silu_s[:, :], qk_bf[:, tt:tt + 512], AF.Square)
                        e_mm(seq, lb.p_sum[:, :], ones_col[:, :], silu_s[:, :])
                        e_red(seq, nmtmp[:, :], lb.p_sum[:, :], ALU.max)
                        e_tt(seq, mcell, mcell, nmtmp[:, :], ALU.max)
            return consume

        for head in range(AHEADS):
            emit_gemm_fm(seq, wqkvT, h_sb, DM, head * HD, (head + 1) * HD, TPC,
                         wbufq, p_a, qk_consume(qTd, head, 0))
        for head in range(AHEADS):
            emit_gemm_fm(seq, wqkvT, h_sb, DM, DM + head * HD, DM + (head + 1) * HD,
                         TPC, wbufq, p_a, qk_consume(kTd, head, AHEADS))
        seq.dma("sync", qkmaxd, qkmax[:, :])
        for f0 in range(0, DM, 512):
            src = wqkvT.rearrange("(c p) f -> p c f", p=128)[
                :, :, 2 * DM + f0: 2 * DM + f0 + 512]
            dst = wbufq[:, :NDC * 512].rearrange("p (c f) -> p c f", f=512)
            seq.dma("gpsimd", dst, src)
            for t0 in range(0, TPC, 128):
                for c in range(NDC):
                    e_mm(seq, p_o[:, :], h_sb[:, c * TPC + t0: c * TPC + t0 + 128],
                         wbufq[:, c * 512:(c + 1) * 512],
                         start=(c == 0), stop=(c == NDC - 1))
                e_copy(seq, qk_bf[:, 0:512], p_o[:, :])
                seq.dma("sync", vtd[t0:t0 + 128, f0:f0 + 512], qk_bf[:, 0:512])
        seq.install(nc, sems, block)
    return nc



def build_launch_F1(debug=False):
    nc = new_nc()
    x2d = nc.dram_tensor("x2", [DM, TPC], F32, kind="ExternalInput").ap()
    qTd = nc.dram_tensor("qT", [AHEADS, HD, TPC], BF16, kind="ExternalInput").ap()
    kTd = nc.dram_tensor("kTall", [AHEADS, HD, SEQ], BF16, kind="ExternalInput").ap()
    vd = nc.dram_tensor("vall", [AHEADS, SEQ, HD], BF16, kind="ExternalInput").ap()
    woT = nc.dram_tensor("woT", [DM, DM], F32, kind="ExternalInput").ap()
    modc = nc.dram_tensor("modc", [128, 2 * 6 * NDC], F32, kind="ExternalInput").ap()
    qkm = nc.dram_tensor("qkmaxall", [NCORE, 2 * AHEADS], F32,
                         kind="ExternalInput").ap()
    x3d = nc.dram_tensor("x3", [DM, TPC], F32, kind="ExternalOutput").ap()
    mx_bounce = nc.dram_tensor("mx_bounce", [4, 128], F32)
    if debug:
        dbg_e = nc.dram_tensor("dbg_e", [128, 512], BF16,
                               kind="ExternalOutput").ap()
        dbg_s = nc.dram_tensor("dbg_s", [1, 512], F32, kind="ExternalOutput").ap()
        dbg_a = nc.dram_tensor("dbg_a", [128, TPC], BF16,
                               kind="ExternalOutput").ap()

    NKC = SEQ // 128
    with ExitStack() as ctx:
        b = Bufs(nc, ctx)
        expS = b.sb("expS", [128, NKC * 512], BF16)
        kTb = [b.sb(f"kT{i}", [128, SEQ], BF16) for i in range(2)]
        v_sbb = [b.sb(f"v_sb{i}", [128, NKC * HD], BF16) for i in range(2)]
        qTb = [b.sb(f"qT_sb{i}", [128, TPC], BF16) for i in range(2)]
        attn = b.sb("attn", [128, AHEADS * TPC], BF16)
        x2c = b.sb("x2c", [128, TPC])
        stage = b.sb("stage", [128, 512])
        wbuf = b.sb("wbufo", [128, NDC * 128], BF16)
        mcol = b.sb("mcolf", [128, 2 * 6 * NDC])
        gac = b.sb("gac", [128, NDC])
        sgn = b.sb("sgn_f", [128, NDC])
        rsum = b.sb("rsum", [1, 512])
        rbc = b.sb("rbc", [128, 512])
        sacc = b.sb("sacc", [128, 4 * 512])
        ones_col = b.sb("ones_colf", [128, 1])
        ones_bf1 = b.sb("ones_bf1", [1, 128], BF16)
        ones_row = b.sb("ones_rowf", [1, 128])
        identf = b.sb("identf", [128, 128])
        pm4 = b.sb("pm4", [128, 4])
        mx1 = b.sb("mx1", [128, 1])
        nmrow = b.sb("nmrow", [1, 512], BF16)
        p_s = b.ps("p_s", [128, 512])
        p_sum = b.ps("p_sumf", [1, 512])
        p_b = b.ps("p_bf", [128, 512])
        p_m = b.ps("p_m", [128, 512])
        sems = mk_sems(b)
        block = b.block()
        seq = Seq()

        e_memset(seq, ones_col[:, :], 1.0)
        e_memset(seq, ones_bf1[:, :], 1.0)
        e_memset(seq, ones_row[:, :], 1.0)
        e_identity(seq, identf)
        seq.dma("sync", mcol[:, :], modc)

        for head in range(AHEADS):
            kT = kTb[head % 2]
            v_sb = v_sbb[head % 2]
            qT = qTb[head % 2]
            seq.dma("sync", kT[:, :], kTd[head])
            for part in range(8):
                seq.dma("sync",
                        v_sb[:, :].rearrange("p (kc d) -> p kc d", d=HD)[
                            :, part * 8:(part + 1) * 8, :],
                        vd[head].rearrange("(kc p) d -> p kc d", p=128)[
                            :, part * 8:(part + 1) * 8, :])
            seq.dma("sync", qT[:, :], qTd[head])
            for t0 in range(0, TPC, 512):
                for qc in range(4):
                    for kt in range(SEQ // 512):
                        e_mm(seq, p_m[:, 0:512],
                             qT[:, t0 + qc * 128: t0 + (qc + 1) * 128],
                             kT[:, kt * 512:(kt + 1) * 512])
                        if kt == 0:
                            e_red(seq, pm4[:, qc:qc + 1], p_m[:, 0:512], ALU.max)
                        else:
                            e_red(seq, mx1[:, :], p_m[:, 0:512], ALU.max)
                            e_tt(seq, pm4[:, qc:qc + 1], pm4[:, qc:qc + 1],
                                 mx1[:, :], ALU.max)
                e_tr(seq, p_m[0:4, 0:128], pm4[:, :], identf[:, :])
                e_ts(seq, rbc[0:4, 0:128], p_m[0:4, 0:128], -1.0, None, ALU.mult)
                seq.dma("sync", mx_bounce.ap(), rbc[0:4, 0:128])
                seq.dma("gpsimd", nmrow[:, :],
                        mx_bounce.ap().rearrange("a b -> (a b)")[None, :])
                for kc in range(NKC):
                    ps = p_s if kc % 2 == 0 else p_m
                    ch = expS[:, kc * 512:(kc + 1) * 512]
                    e_mm(seq, ps[:, :], ones_bf1[:, :], nmrow[:, :], stop=False)
                    e_mm(seq, ps[:, :], kT[:, kc * 128:(kc + 1) * 128],
                         qT[:, t0:t0 + 512], start=False)
                    e_act(seq, ch, ps[:, :], AF.Exp, scale=1.0 / math.sqrt(HD))
                    sa = sacc[:, (kc % 4) * 512:(kc % 4 + 1) * 512]
                    if kc < 4:
                        e_copy(seq, sa, ch, engine="vector")
                    else:
                        e_tt(seq, sa, sa, ch, ALU.add, engine="vector")
                e_tt(seq, sacc[:, 0:512], sacc[:, 0:512], sacc[:, 512:1024],
                     ALU.add, engine="vector")
                e_tt(seq, sacc[:, 1024:1536], sacc[:, 1024:1536],
                     sacc[:, 1536:2048], ALU.add, engine="vector")
                e_tt(seq, sacc[:, 0:512], sacc[:, 0:512], sacc[:, 1024:1536],
                     ALU.add, engine="vector")
                e_mm(seq, p_sum[:, :], ones_col[:, :], sacc[:, 0:512])
                e_copy(seq, rsum[:, :], p_sum[:, :])
                e_recip(seq, rsum[:, :], rsum[:, :])
                e_mm(seq, p_b[:, :], ones_row[:, :], rsum[:, :])
                e_copy(seq, rbc[:, :], p_b[:, :])
                for kc in range(NKC):
                    e_mm(seq, p_s[:, :], v_sb[:, kc * HD:(kc + 1) * HD],
                         expS[:, kc * 512:(kc + 1) * 512],
                         start=(kc == 0), stop=(kc == NKC - 1))
                e_tt(seq, attn[:, head * TPC + t0: head * TPC + t0 + 512],
                     p_s[:, :], rbc[:, :], ALU.mult)
        if debug:
            seq.dma("sync", dbg_a, attn[:, 0:TPC])

        e_clamp_gate(seq, gac[:, :], mcol[:, 6 * NDC + 2 * NDC: 6 * NDC + 3 * NDC],
                     sgn[:, :])

        def oproj_consume(dc2, t0, tw, psum):
            dc = dc2 // 128
            xc = x2c[:, t0:t0 + tw]
            if t0 == 0:
                for tt in range(0, TPC, 512):
                    seq.dma("sync", x2c[:, tt:tt + 512],
                            x2d[dc * 128:(dc + 1) * 128, tt:tt + 512])
            e_stt(seq, stage[:, :tw], psum, gac[:, dc:dc + 1], xc,
                  ALU.mult, ALU.add)
            seq.dma("sync", x3d[dc * 128:(dc + 1) * 128, t0:t0 + tw],
                    stage[:, :tw])

        emit_gemm_fm(seq, woT, attn, DM, 0, DM, TPC, wbuf, p_s, oproj_consume)
        seq.install(nc, sems, block)
    return nc



def build_launch_F2():
    nc = new_nc()
    x3d = nc.dram_tensor("x3", [DM, TPC], F32, kind="ExternalInput").ap()
    W1T = nc.dram_tensor("W1T", [DM, DFF1], F32, kind="ExternalInput").ap()
    W3T = nc.dram_tensor("W3T", [DM, DFF1], F32, kind="ExternalInput").ap()
    W2T = nc.dram_tensor("W2T", [DFF1, DM], F32, kind="ExternalInput").ap()
    modc = nc.dram_tensor("modc", [128, 2 * 6 * NDC], F32, kind="ExternalInput").ap()
    h128d = nc.dram_tensor("h128", [128, 128], BF16, kind="ExternalInput").ap()
    hi2d = nc.dram_tensor("hi2", [128, 128], BF16, kind="ExternalInput").ap()
    x4d = nc.dram_tensor("x4", [DM, TPC], F32, kind="ExternalOutput").ap()

    NF1 = DFF1 // 128
    with ExitStack() as ctx:
        b = Bufs(nc, ctx)
        x_sb = b.sb("x_sb", [128, NDC * TPC])
        h_sb = b.sb("h_sb", [128, NDC * TPC], BF16)
        hid = b.sb("hid", [128, NF1 * 512], BF16)
        mcol = b.sb("mcol", [128, 2 * 6 * NDC])
        ca1 = b.sb("ca1", [128, NDC])
        gbc = b.sb("gbc", [128, NDC])
        sgn = b.sb("sgn", [128, NDC])
        ones_col = b.sb("ones_col", [128, 1])
        ones_row = b.sb("ones_row", [1, 128])
        wbuf1 = [b.sb(f"wbuf1_{i}", [128, NDC * 128], BF16) for i in range(2)]
        wbuf2 = [b.sb(f"wbuf2_{i}", [128, NDC * 128], BF16) for i in range(2)]
        wbuf3 = [b.sb(f"wbuf3_{i}", [128, NF1 * 64], BF16) for i in range(2)]
        silu_s = b.sb("silu_s", [128, 512])
        h128c = b.sb("h128c", [128, 128], BF16)
        hi2c = b.sb("hi2c", [128, 128], BF16)
        identb = b.sb("identb", [128, 128], BF16)
        tr1 = [b.sb(f"tr1_{i}", [128, 128], BF16) for i in range(2)]
        tr2 = [b.sb(f"tr2_{i}", [128, 128], BF16) for i in range(2)]
        p_a = [b.ps(f"p_a{i}", [128, 512]) for i in range(2)]
        p_b2 = [b.ps(f"p_b2{i}", [128, 512]) for i in range(2)]
        p_o = b.ps("p_o", [128, 512])
        p_t = [b.ps(f"p_t{i}", [128, 128], BF16) for i in range(2)]
        p_tm = b.ps("p_tm", [128, 128])
        lb = LNBufs(b, "f2", p_b=p_o[:, :], p_sum=p_a[0][0:1, 0:512],
                    p_sq=p_b2[0][0:1, 0:512],
                    mean_b=wbuf3[0][:, 0:1024].bitcast(F32),
                    rstd_b=wbuf3[0][:, 1024:2048].bitcast(F32))
        sems = mk_sems(b)
        block = b.block()
        seq = Seq()

        e_memset(seq, ones_col[:, :], 1.0)
        e_memset(seq, ones_row[:, :], 1.0)
        e_identity(seq, identb)
        seq.dma("sync", mcol[:, :], modc)
        seq.dma("sync", h128c[:, :], h128d)
        seq.dma("sync", hi2c[:, :], hi2d)
        for c in range(NDC):
            seq.dma("sync", x_sb[:, c * TPC:(c + 1) * TPC],
                    x3d[c * 128:(c + 1) * 128, :])
        e_ts(seq, ca1[:, :], mcol[:, 6 * NDC + 4 * NDC: 6 * NDC + 5 * NDC],
             1.0, None, ALU.add)
        e_clamp_gate(seq, gbc[:, :], mcol[:, 6 * NDC + 5 * NDC: 6 * NDC + 6 * NDC],
                     sgn[:, :])
        emit_ln_mod(seq, lb, x_sb, h_sb, ones_col, ones_row,
                    ca1, mcol[:, 6 * NDC + 3 * NDC: 6 * NDC + 4 * NDC])

        for t0 in range(0, TPC, 512):
            for fi in range(NF1):
                f0 = fi * 128
                pr = fi % 2
                w1c = wbuf1[pr]
                w2c = wbuf2[pr]
                dma_wslice_fm(seq, w1c, W1T, DM, f0, 128)
                dma_wslice_fm(seq, w2c, W3T, DM, f0, 128)
                for c in range(NDC):
                    e_mm(seq, p_a[pr][:, :], w1c[:, c * 128:(c + 1) * 128],
                         h_sb[:, c * TPC + t0: c * TPC + t0 + 512],
                         start=(c == 0), stop=(c == NDC - 1))
                for c in range(NDC):
                    e_mm(seq, p_b2[pr][:, :], w2c[:, c * 128:(c + 1) * 128],
                         h_sb[:, c * TPC + t0: c * TPC + t0 + 512],
                         start=(c == 0), stop=(c == NDC - 1))
                e_act(seq, silu_s[:, :], p_a[pr][:, :], AF.Silu)
                e_tt(seq, hid[:, fi * 512:(fi + 1) * 512], silu_s[:, :],
                     p_b2[pr][:, :], ALU.mult)
            for fc in range(NF1):
                pr = fc % 2
                e_mm(seq, p_a[pr][:, :], h128c[:, :],
                     hid[:, fc * 512:(fc + 1) * 512])
                e_copy(seq, hid[:, fc * 512:(fc + 1) * 512], p_a[pr][:, :])
            for tp in range(256):
                pr = tp % 2
                tview = hid[:, :].rearrange("p (c t) -> p c t", t=512)[
                    :, :, 2 * tp: 2 * tp + 2]
                e_copy(seq, tr1[pr][:, :].rearrange("p (c t) -> p c t", t=2),
                       tview)
                e_tr(seq, p_t[pr][:, :], tr1[pr][:, :], identb[:, :])
                e_copy(seq, tr1[pr][:, :], p_t[pr][:, :])
                e_mm(seq, p_tm[:, :], hi2c[:, :], tr1[pr][:, :])
                e_ts(seq, tr2[pr][:, :], p_tm[:, :],
                     float(1.0 / math.sqrt(DFF1)), None, ALU.mult)
                e_tr(seq, p_t[pr][:, :], tr2[pr][:, :], identb[:, :])
                e_copy(seq, tview, p_t[pr][:, :])
            for dc in range(NDC):
                for hh in range(2):
                    w3c = wbuf3[(dc * 2 + hh) % 2]
                    src3 = W2T.rearrange("(c p) f -> p c f", p=128)[
                        :, hh * 32:(hh + 1) * 32, dc * 128:(dc + 1) * 128]
                    dst3 = w3c[:, :].rearrange("p (c f) -> p c f", f=128)
                    seq.dma("gpsimd", dst3, src3)
                    for fl in range(32):
                        fc = hh * 32 + fl
                        e_mm(seq, p_o[:, :], w3c[:, fl * 128:(fl + 1) * 128],
                             hid[:, fc * 512:(fc + 1) * 512],
                             start=(fc == 0), stop=(fc == NF1 - 1))
                xc = x_sb[:, dc * TPC + t0: dc * TPC + t0 + 512]
                e_stt(seq, xc, p_o[:, :], gbc[:, dc:dc + 1], xc, ALU.mult, ALU.add)
        for c in range(NDC):
            seq.dma("sync", x4d[c * 128:(c + 1) * 128, :],
                    x_sb[:, c * TPC:(c + 1) * TPC])
        seq.install(nc, sems, block)
    return nc



def build_launch_C():
    nc = new_nc()
    x0d = nc.dram_tensor("x0", [DM, TPC], F32, kind="ExternalInput").ap()
    xhd = nc.dram_tensor("xh", [DM, 4], F32, kind="ExternalInput").ap()
    wipT = nc.dram_tensor("wipT", [DM, PROJ_DIM], F32, kind="ExternalInput").ap()
    cwd = nc.dram_tensor("convw", [128, 34 * KCONV], F32, kind="ExternalInput").ap()
    cbd = nc.dram_tensor("convb", [128, 34], F32, kind="ExternalInput").ap()
    dtbd = nc.dram_tensor("dtb", [NHEAD, 1], F32, kind="ExternalInput").ap()
    alogd = nc.dram_tensor("alog", [NHEAD, 1], F32, kind="ExternalInput").ap()
    drowd = nc.dram_tensor("drow", [1, NHEAD], F32, kind="ExternalInput").ap()
    modc = nc.dram_tensor("modc", [128, 2 * 6 * NDC], F32, kind="ExternalInput").ap()
    hmaskd = nc.dram_tensor("hmask", [128, 1], F32, kind="ExternalInput").ap()
    gate_d = nc.dram_tensor("gate", [TPC, INTER], BF16, kind="ExternalOutput").ap()
    yp_d = nc.dram_tensor("yp", [TPC, INTER], F32, kind="ExternalOutput").ap()
    st_d = nc.dram_tensor("cstates", [NCHUNK, NHEAD, ND, PD], BF16,
                          kind="ExternalOutput").ap()
    sum_d = nc.dram_tensor("csum", [NHEAD, ND, PD], BF16, kind="ExternalOutput").ap()
    lt_d = nc.dram_tensor("clt", [NHEAD, 1], F32, kind="ExternalOutput").ap()
    acum_d = nc.dram_tensor("acum", [NHEAD, TPC], F32, kind="ExternalOutput").ap()
    cfm_d = nc.dram_tensor("cfm", [ND, TPC], BF16, kind="ExternalOutput").ap()

    NHB = CONV_DIM // 128
    with ExitStack() as ctx:
        b = Bufs(nc, ctx)
        big0 = b.sb("big0", [128, NDC * TPC])
        h_sb = b.sb("h_sb", [128, NDC * TPC], BF16)
        arena = b.sb("arena", [128, 32 * TPC], BF16)
        hh_sb = b.sb("hh_sb", [128, 4 * NDC], BF16)
        xh_sb = b.sb("xh_sb", [128, 4 * NDC])
        B_fm = b.sb("B_fm", [128, TPC], BF16)
        C_fm = b.sb("C_fm", [128, TPC], BF16)
        B_tm = b.sb("B_tm", [128, 8 * ND], BF16)
        dtr = b.sb("dtr", [NHEAD, TPC])
        acum = b.sb("acum_sb", [NHEAD, TPC])
        hbc = b.sb("hbc", [128, TPC + 4])
        dtT = b.sb("dtT", [128, 8 * NHEAD])
        decT = b.sb("decT", [128, 8 * NHEAD])
        nacT = b.sb("nacT", [128, 8 * NHEAD])
        ltc = b.sb("ltc", [NHEAD, 1])
        cvw = b.sb("cvw", [128, 34 * KCONV])
        cvb = b.sb("cvb", [128, 34])
        hmask = b.sb("hmaskc", [128, 1])
        dcol = b.sb("dcolc", [NHEAD, 1])
        acol = b.sb("acolc", [NHEAD, 1])
        mcol = b.sb("mcolc", [128, 2 * 6 * NDC])
        ca1 = b.sb("ca1c", [128, NDC])
        wbuf = b.sb("wbufc", [128, NDC * 128], BF16)

        ones_col = b.sb("ones_colc", [128, 1])
        ones_row = b.sb("ones_rowc", [1, 128])
        identf = b.sb("identfc", [128, 128])
        identb2 = b.sb("identbc", [128, 128], BF16)
        lb = LNBufs(b, "c")
        p_g = b.ps("p_gc", [128, 512])
        p_misc = b.ps("p_miscc", [128, 512])
        p_y = b.ps("p_yc", [128, 128])
        p_st = b.ps("p_stc", [128, PD])
        p_tb = b.ps("p_tbc", [128, 128], BF16)
        sems = mk_sems(b)
        block = b.block()
        seq = Seq()

        zrow = lb.msq_row[0:1, 0:CH]
        stg = lb.t1[:, :].bitcast(BF16)[:, 0:512]
        ar_f = arena[:, :].bitcast(F32)
        ytile = [ar_f[:, 0:2048], ar_f[:, 2048:4096]]
        ar_b = arena[:, :]
        run = ar_b[:, 16384:16384 + NHEAD * PD]
        ststg = ar_b[:, 20480:20480 + 16 * PD]
        bdec = ar_b[:, 21504:21504 + ND]
        mt_sb = ar_b[:, 21632:21632 + 2 * CH]
        lt_sb = ar_b[:, 22144:22144 + CH]
        gt_sb = ar_f[:, 11200:11200 + 2 * CH]
        dchb = ar_f[:, 11712:11712 + NHEAD]
        Db = ar_f[:, 11776:11776 + NHEAD]
        adt = hbc[0:NHEAD, 0:TPC]
        t64 = b.sb("t64c", [NHEAD, CH])
        zer = b.sb("zerc", [NHEAD, CH])
        hs_tm = big0[:, :].bitcast(BF16)

        e_memset(seq, ones_col[:, :], 1.0)
        e_memset(seq, ones_row[:, :], 1.0)
        e_identity(seq, identf)
        e_identity(seq, identb2)
        seq.dma("sync", mcol[:, :], modc)
        seq.dma("sync", cvw[:, :], cwd)
        seq.dma("sync", cvb[:, :], cbd)
        seq.dma("sync", hmask[:, :], hmaskd)
        seq.dma("sync", dcol[:, :], dtbd)
        seq.dma("sync", acol[:, :], alogd)
        seq.dma("sync", xh_sb[:, :].rearrange("p (c t) -> p c t", t=4),
                xhd.rearrange("(c p) t -> p c t", p=128))
        for c in range(NDC):
            seq.dma("sync", big0[:, c * TPC:(c + 1) * TPC],
                    x0d[c * 128:(c + 1) * 128, :])
        e_ts(seq, ca1[:, :], mcol[:, NDC:2 * NDC], 1.0, None, ALU.add)
        emit_ln_mod(seq, lb, big0, h_sb, ones_col, ones_row, ca1, mcol[:, 0:NDC])
        emit_ln_mod(seq, lb, xh_sb, hh_sb, ones_col, ones_row, ca1,
                    mcol[:, 0:NDC], toks=4)

        wbufg = ar_b[:, 0:NDC * 512]
        for f0 in range(0, INTER, 512):
            src = wipT.rearrange("(c p) f -> p c f", p=128)[:, :, f0:f0 + 512]
            seq.dma("gpsimd", wbufg.rearrange("p (c f) -> p c f", f=512), src)
            for t0 in range(0, TPC, 128):
                for c in range(NDC):
                    e_mm(seq, p_g[:, :], h_sb[:, c * TPC + t0: c * TPC + t0 + 128],
                         wbufg[:, c * 512:(c + 1) * 512],
                         start=(c == 0), stop=(c == NDC - 1))
                e_copy(seq, stg[:, :], p_g[:, :])
                seq.dma("sync", gate_d[t0:t0 + 128, f0:f0 + 512], stg[:, :])

        for fi in range(NHB):
            f0 = INTER + fi * 128
            dma_wslice_fm(seq, wbuf, wipT, DM, f0, 128)
            for c in range(NDC):
                e_mm(seq, p_misc[:, 0:4], wbuf[:, c * 128:(c + 1) * 128],
                     hh_sb[:, :].rearrange("p (c t) -> p c t", t=4)[:, c, :],
                     start=(c == 0), stop=(c == NDC - 1))
            e_ts(seq, hbc[:, 0:4], p_misc[:, 0:4], hmask[:, 0:1], None, ALU.mult)
            for t0 in range(0, TPC, 512):
                for c in range(NDC):
                    e_mm(seq, p_g[:, :], wbuf[:, c * 128:(c + 1) * 128],
                         h_sb[:, c * TPC + t0: c * TPC + t0 + 512],
                         start=(c == 0), stop=(c == NDC - 1))
                e_copy(seq, hbc[:, 3 + t0: 3 + t0 + 512], p_g[:, :])
            acc = lb.mean_b
            for t0 in range(0, TPC, 512):
                e_ts(seq, acc[:, :], hbc[:, t0:t0 + 512],
                     cvw[:, fi * KCONV: fi * KCONV + 1], None, ALU.mult)
                for k in range(1, KCONV):
                    e_stt(seq, acc[:, :], hbc[:, t0 + k: t0 + k + 512],
                          cvw[:, fi * KCONV + k: fi * KCONV + k + 1],
                          acc[:, :], ALU.mult, ALU.add)
                e_ts(seq, acc[:, :], acc[:, :], cvb[:, fi:fi + 1], None, ALU.add)
                dst = (arena[:, fi * TPC + t0: fi * TPC + t0 + 512] if fi < 32
                       else (B_fm if fi == 32 else C_fm)[:, t0:t0 + 512])
                e_act(seq, dst, acc[:, :], AF.Silu)
        seq.dma("sync", cfm_d, C_fm[:, :])

        dma_wslice_fm(seq, wbuf, wipT, DM, INTER + CONV_DIM, 64)
        for t0 in range(0, TPC, 512):
            for c in range(NDC):
                e_mm(seq, p_g[0:NHEAD, :], wbuf[:, c * 64:(c + 1) * 64],
                     h_sb[:, c * TPC + t0: c * TPC + t0 + 512],
                     start=(c == 0), stop=(c == NDC - 1))
            e_copy(seq, dtr[:, t0:t0 + 512], p_g[0:NHEAD, :])
        e_ts(seq, dtr[:, :], dtr[:, :], dcol[:, :], None, ALU.add)
        e_act(seq, adt, dtr[:, :], AF.Abs)
        e_act(seq, adt, adt, AF.Exp, scale=-1.0)
        e_ts(seq, adt, adt, 1.0, None, ALU.add)
        e_act(seq, adt, adt, AF.Ln)
        e_ts(seq, dtr[:, :], dtr[:, :], 0.0, None, ALU.max)
        e_tt(seq, dtr[:, :], dtr[:, :], adt, ALU.add)
        e_ts(seq, dtr[:, :], dtr[:, :], DT_MIN, DT_MAX, ALU.max, ALU.min)
        e_act(seq, acol[:, :], acol[:, :], AF.Exp)
        e_ts(seq, acol[:, :], acol[:, :], -1.0, None, ALU.mult)
        e_ts(seq, adt, dtr[:, :], acol[:, :], None, ALU.mult)
        e_memset(seq, zer[:, :], 0.0)
        for c in range(NCHUNK):
            seq.emit("vector", lambda e, c=c: e.tensor_tensor_scan(
                acum[:, c * CH:(c + 1) * CH], adt[:, c * CH:(c + 1) * CH],
                zer[:, :], 0.0, ALU.add, ALU.add),
                reads=(adt[:, c * CH:(c + 1) * CH], zer[:, :]),
                writes=(acum[:, c * CH:(c + 1) * CH],))
        seq.dma("sync", acum_d, acum[:, :])
        e_copy(seq, ltc[:, :], acum[:, CH - 1:CH])
        for c in range(1, NCHUNK):
            e_tt(seq, ltc[:, :], ltc[:, :],
                 acum[:, (c + 1) * CH - 1:(c + 1) * CH], ALU.add)
        seq.dma("sync", lt_d, ltc[:, :])

        for c in range(NCHUNK):
            e_act(seq, t64[:, :], acum[:, c * CH:(c + 1) * CH], AF.Exp,
                  bias=acum[:, (c + 1) * CH - 1:(c + 1) * CH], scale=-1.0)
            for hf in range(2):
                col = (2 * c + hf) * NHEAD
                e_tr(seq, p_misc[:, 0:NHEAD],
                     dtr[:, c * CH + hf * 128: c * CH + (hf + 1) * 128],
                     identf[0:NHEAD, 0:NHEAD])
                e_copy(seq, dtT[:, col:col + NHEAD], p_misc[:, 0:NHEAD])
                e_tr(seq, p_misc[:, 0:NHEAD], t64[:, hf * 128:(hf + 1) * 128],
                     identf[0:NHEAD, 0:NHEAD])
                e_copy(seq, decT[:, col:col + NHEAD], p_misc[:, 0:NHEAD])
                e_tr(seq, p_misc[:, 0:NHEAD],
                     acum[:, c * CH + hf * 128: c * CH + (hf + 1) * 128],
                     identf[0:NHEAD, 0:NHEAD])
                e_ts(seq, nacT[:, col:col + NHEAD], p_misc[:, 0:NHEAD],
                     -1.0, None, ALU.mult)

        identb = identb2[:, :]
        pt_b = p_tb[:, :]
        for tc in range(8):
            for fc in range(32):
                e_tr(seq, pt_b,
                     arena[:, fc * TPC + tc * 128: fc * TPC + (tc + 1) * 128],
                     identb)
                e_copy(seq, hs_tm[:, tc * 4096 + fc * 128:
                                  tc * 4096 + (fc + 1) * 128], pt_b)
            e_tr(seq, pt_b, B_fm[:, tc * 128:(tc + 1) * 128], identb)
            e_copy(seq, B_tm[:, tc * ND:(tc + 1) * ND], pt_b)

        seq.dma("sync", zrow[:, 0:NHEAD], drowd)
        e_mm(seq, p_misc[:, 0:NHEAD], ones_row[:, :], zrow[:, 0:NHEAD])
        e_copy(seq, Db, p_misc[:, 0:NHEAD])

        e_memset(seq, run, 0.0)
        for c in range(NCHUNK):
            for lh in range(2):
                e_mm(seq, p_misc[:, 0:CH],
                     B_fm[:, c * CH + lh * 128: c * CH + (lh + 1) * 128],
                     C_fm[:, c * CH:(c + 1) * CH])
                e_copy(seq, gt_sb[:, lh * CH:(lh + 1) * CH], p_misc[:, 0:CH])
            e_act(seq, ltc[:, :], acum[:, (c + 1) * CH - 1:(c + 1) * CH], AF.Exp)
            e_tr(seq, p_misc[0:1, 0:NHEAD], ltc[:, :],
                 identf[0:NHEAD, 0:NHEAD])
            e_copy(seq, zrow[:, 0:NHEAD], p_misc[0:1, 0:NHEAD])
            e_mm(seq, p_misc[:, 0:NHEAD], ones_row[:, :], zrow[:, 0:NHEAD])
            e_copy(seq, dchb, p_misc[:, 0:NHEAD])
            for h in range(NHEAD):
                for zf in range(2):
                    col = (2 * c + zf) * NHEAD + h
                    e_tr(seq, p_g[0:1, 0:128], nacT[:, col:col + 1],
                         identf[:, :])
                    e_ts(seq, zrow[:, zf * 128:(zf + 1) * 128], p_g[0:1, 0:128],
                         -1.0, None, ALU.mult)
                e_mm(seq, p_misc[:, 0:CH], ones_row[:, :], zrow[:, 0:CH])
                for lh in range(2):
                    col = (2 * c + lh) * NHEAD + h
                    e_act(seq, lt_sb, p_misc[:, 0:CH], AF.Exp,
                          bias=nacT[:, col:col + 1])
                    seq.emit("gpsimd", lambda e, lh=lh: e.affine_select(
                        out=lt_sb, in_=lt_sb, compare_op=ALU.is_ge,
                        fill=0.0, base=-(lh * 128), pattern=[[1, CH]],
                        channel_multiplier=-1),
                        reads=(lt_sb,), writes=(lt_sb,))
                    e_stt(seq, mt_sb[:, lh * CH:(lh + 1) * CH],
                          gt_sb[:, lh * CH:(lh + 1) * CH], dtT[:, col:col + 1],
                          lt_sb, ALU.mult, ALU.mult)
                hs0 = hs_tm[:, (2 * c) * 4096 + h * PD: (2 * c) * 4096 + h * PD + PD]
                hs1 = hs_tm[:, (2 * c + 1) * 4096 + h * PD:
                            (2 * c + 1) * 4096 + h * PD + PD]
                e_mm(seq, p_y[:, 0:PD], mt_sb[:, 0:128], hs0)
                e_mm(seq, p_y[:, PD:2 * PD], mt_sb[:, 128:CH], hs0, stop=False)
                e_mm(seq, p_y[:, PD:2 * PD], mt_sb[:, CH + 128:2 * CH], hs1,
                     start=False)
                for lh in range(2):
                    col = (2 * c + lh) * NHEAD + h
                    e_ts(seq, bdec, B_tm[:, (2 * c + lh) * ND:(2 * c + lh + 1) * ND],
                         decT[:, col:col + 1], dtT[:, col:col + 1],
                         ALU.mult, ALU.mult)
                    e_mm(seq, p_st[:, :], bdec, (hs0 if lh == 0 else hs1),
                         start=(lh == 0), stop=(lh == 1))
                hq = h % 16
                e_copy(seq, ststg[:, hq * PD:(hq + 1) * PD], p_st[:, :])
                e_stt(seq, run[:, h * PD:(h + 1) * PD], run[:, h * PD:(h + 1) * PD],
                      dchb[:, h:h + 1], ststg[:, hq * PD:(hq + 1) * PD],
                      ALU.mult, ALU.add)
                if hq == 15:
                    seq.dma("sync",
                            st_d[c, h - 15:h + 1].transpose([1, 0, 2]),
                            ststg[:, :].rearrange("n (h p) -> n h p", p=PD))
                hw = h % 32
                for zh in range(2):
                    e_stt(seq, ytile[zh][:, hw * PD:(hw + 1) * PD],
                          (hs0 if zh == 0 else hs1), Db[:, h:h + 1],
                          p_y[:, zh * PD:(zh + 1) * PD], ALU.mult, ALU.add)
                if hw == 31:
                    g0 = h - 31
                    for zh in range(2):
                        seq.dma("sync",
                                yp_d[c * CH + zh * 128: c * CH + (zh + 1) * 128,
                                     g0 * PD:(g0 + 32) * PD], ytile[zh])
        for g in range(4):
            seq.dma("sync",
                    sum_d[g * 16:(g + 1) * 16].transpose([1, 0, 2]),
                    run[:, g * 1024:(g + 1) * 1024].rearrange(
                        "n (h p) -> n h p", p=PD))
        seq.install(nc, sems, block)
    return nc



def build_launch_D():
    nc = new_nc()
    x0d = nc.dram_tensor("x0", [DM, TPC], F32, kind="ExternalInput").ap()
    ypd = nc.dram_tensor("yp", [TPC, INTER], F32, kind="ExternalInput").ap()
    gated = nc.dram_tensor("gate", [TPC, INTER], BF16, kind="ExternalInput").ap()
    cfmd = nc.dram_tensor("cfm", [ND, TPC], BF16, kind="ExternalInput").ap()
    acumd = nc.dram_tensor("acum", [NHEAD, TPC], F32, kind="ExternalInput").ap()
    cstd = nc.dram_tensor("cstates", [NCHUNK, NHEAD, ND, PD], BF16,
                          kind="ExternalInput").ap()
    csall = nc.dram_tensor("csumall", [NCORE, NHEAD, ND, PD], BF16,
                           kind="ExternalInput").ap()
    cltall = nc.dram_tensor("cltall", [NCORE, NHEAD], F32,
                            kind="ExternalInput").ap()
    miTd = nc.dram_tensor("miT", [NCORE, NCORE], F32, kind="ExternalInput").ap()
    incd = nc.dram_tensor("incm", [NCORE, 1], F32, kind="ExternalInput").ap()
    wopT = nc.dram_tensor("wopT", [INTER, DM], F32, kind="ExternalInput").ap()
    nwT = nc.dram_tensor("nwT", [128, 32], F32, kind="ExternalInput").ap()
    modc = nc.dram_tensor("modc", [128, 2 * 6 * NDC], F32, kind="ExternalInput").ap()
    x1d = nc.dram_tensor("x1", [DM, TPC], F32, kind="ExternalOutput").ap()
    coef_bounce = nc.dram_tensor("coef_bounce", [NCORE, NHEAD], F32)

    with ExitStack() as ctx:
        b = Bufs(nc, ctx)
        run = b.sb("run_d", [128, NHEAD * PD], BF16)
        sgbuf = b.sb("sgbuf", [128, NHEAD * PD], BF16)
        ybuf = b.sb("ybuf", [128, INTER])
        gbuf = b.sb("gbuf", [128, INTER], BF16)
        yn_tm = b.sb("yn_tm", [128, INTER], BF16)
        yn_fm = b.sb("yn_fm", [128, 32 * TPC], BF16)
        cfm = b.sb("cfm_sb", [128, TPC], BF16)
        acum = b.sb("acum_sbd", [NHEAD, TPC])
        sdoT = b.sb("sdoT", [128, 8 * NHEAD])
        x0c = b.sb("x0c", [128, TPC])
        wbuf = b.sb("wbufd", [128, 32 * 128], BF16)
        ltall = b.sb("ltall", [NCORE, NHEAD])
        miT = b.sb("miT_sb", [NCORE, NCORE])
        incm = b.sb("incm_sb", [NCORE, 1])
        coef = b.sb("coef_sb", [NCORE, NHEAD])
        coefrow = b.sb("coefrow", [1, NCORE * NHEAD])
        cbj = b.sb("cbj", [128, NHEAD], BF16)
        dchb = b.sb("dchb_d", [128, NHEAD], BF16)
        ltc = b.sb("ltc_d", [NHEAD, 1])
        zrow = b.sb("zrow_d", [1, 128])
        nwc = b.sb("nwc", [128, 32])
        mcol = b.sb("mcold", [128, 2 * 6 * NDC])
        gac = b.sb("gacd", [128, NDC])
        sgn = b.sb("sgnd", [128, NDC])
        sq = b.sb("sq_d", [128, 512])
        ss = b.sb("ss_d", [128, 1])
        sst = b.sb("sst_d", [128, 1])
        ones_row = b.sb("ones_rowd", [1, 128])
        identf = b.sb("identfd", [128, 128])
        identb2 = b.sb("identbd", [128, 128], BF16)
        stg = b.sb("stg_d", [128, 512])
        p_yo = b.ps("p_yod", [128, 128])
        p_b = b.ps("p_bd", [128, 512])
        p_o = b.ps("p_od", [128, 512])
        p_tb = b.ps("p_tbd", [128, 128], BF16)
        sems = mk_sems(b)
        block = b.block()
        seq = Seq()

        e_memset(seq, ones_row[:, :], 1.0)
        e_identity(seq, identf)
        e_identity(seq, identb2)
        seq.dma("sync", mcol[:, :], modc)
        seq.dma("sync", nwc[:, :], nwT)
        seq.dma("sync", ltall[:, :], cltall)
        seq.dma("sync", miT[:, :], miTd)
        seq.dma("sync", incm[:, :], incd)
        seq.dma("sync", cfm[:, :], cfmd)
        seq.dma("sync", acum[:, :], acumd)
        e_clamp_gate(seq, gac[:, :], mcol[:, 2 * NDC:3 * NDC], sgn[:, :])

        e_mm(seq, p_b[0:NCORE, 0:NHEAD], miT[:, :], ltall[:, :])
        e_act(seq, coef[:, :], p_b[0:NCORE, 0:NHEAD], AF.Exp)
        e_ts(seq, coef[:, :], coef[:, :], incm[:, 0:1], None, ALU.mult)
        seq.dma("sync", coef_bounce.ap(), coef[:, :])
        seq.dma("sync", coefrow[:, :],
                coef_bounce.ap().rearrange("a b -> (a b)")[None, :])

        e_memset(seq, run[:, :], 0.0)
        for j in range(NCORE):
            seq.dma("sync", sgbuf[:, :].rearrange("n (h p) -> n h p", p=PD),
                    csall[j].transpose([1, 0, 2]))
            e_mm(seq, p_b[:, 0:NHEAD], ones_row[:, :],
                 coefrow[:, j * NHEAD:(j + 1) * NHEAD])
            e_copy(seq, cbj[:, :], p_b[:, 0:NHEAD])
            for h in range(NHEAD):
                e_stt(seq, run[:, h * PD:(h + 1) * PD],
                      sgbuf[:, h * PD:(h + 1) * PD], cbj[:, h:h + 1],
                      run[:, h * PD:(h + 1) * PD], ALU.mult, ALU.add)

        for c in range(NCHUNK):
            for hf in range(2):
                col = (2 * c + hf) * NHEAD
                e_tr(seq, p_yo[:, 0:NHEAD],
                     acum[:, c * CH + hf * 128: c * CH + (hf + 1) * 128],
                     identf[0:NHEAD, 0:NHEAD])
                e_act(seq, sdoT[:, col:col + NHEAD], p_yo[:, 0:NHEAD], AF.Exp)

        for c in range(NCHUNK):
            for lh in range(2):
                tok0 = c * CH + lh * 128
                seq.dma("sync", ybuf[:, :], ypd[tok0:tok0 + 128, :])
                seq.dma("sync", gbuf[:, :], gated[tok0:tok0 + 128, :])
                for hb in range(NHEAD // 8):
                    e_mm(seq, p_b[:, :], cfm[:, tok0:tok0 + 128],
                         run[:, hb * 512:(hb + 1) * 512])
                    for hl in range(8):
                        h = hb * 8 + hl
                        e_stt(seq, ybuf[:, h * PD:(h + 1) * PD],
                              p_b[:, hl * PD:(hl + 1) * PD],
                              sdoT[:, (2 * c + lh) * NHEAD + h:
                                   (2 * c + lh) * NHEAD + h + 1],
                              ybuf[:, h * PD:(h + 1) * PD], ALU.mult, ALU.add)
                for s0 in range(0, INTER, 512):
                    e_act(seq, sq[:, :], gbuf[:, s0:s0 + 512], AF.Silu)
                    e_tt(seq, ybuf[:, s0:s0 + 512], ybuf[:, s0:s0 + 512],
                         sq[:, :], ALU.mult)
                    e_act(seq, sq[:, :], ybuf[:, s0:s0 + 512], AF.Square,
                          accum_out=sst[:, :])
                    if s0 == 0:
                        e_copy(seq, ss[:, :], sst[:, :])
                    else:
                        e_tt(seq, ss[:, :], ss[:, :], sst[:, :], ALU.add)
                e_ts(seq, ss[:, :], ss[:, :], 1.0 / INTER, MEPS, ALU.mult, ALU.add)
                e_act(seq, ss[:, :], ss[:, :], AF.Sqrt)
                e_recip(seq, ss[:, :], ss[:, :])
                for s0 in range(0, INTER, 512):
                    e_ts(seq, yn_tm[:, s0:s0 + 512], ybuf[:, s0:s0 + 512],
                         ss[:, 0:1], None, ALU.mult)
                for fc in range(32):
                    e_tr(seq, p_tb[:, :], yn_tm[:, fc * 128:(fc + 1) * 128],
                         identb2[:, :])
                    e_ts(seq, yn_fm[:, fc * TPC + tok0: fc * TPC + tok0 + 128],
                         p_tb[:, :], nwc[:, fc:fc + 1], None, ALU.mult)
            e_act(seq, ltc[:, :], acum[:, (c + 1) * CH - 1:(c + 1) * CH], AF.Exp)
            e_tr(seq, p_yo[0:1, 0:NHEAD], ltc[:, :],
                 identf[0:NHEAD, 0:NHEAD])
            e_copy(seq, zrow[:, 0:NHEAD], p_yo[0:1, 0:NHEAD])
            e_mm(seq, p_b[:, 0:NHEAD], ones_row[:, :], zrow[:, 0:NHEAD])
            e_copy(seq, dchb[:, :], p_b[:, 0:NHEAD])
            seq.dma("sync", sgbuf[:, :].rearrange("n (h p) -> n h p", p=PD),
                    cstd[c].transpose([1, 0, 2]))
            for h in range(NHEAD):
                e_stt(seq, run[:, h * PD:(h + 1) * PD],
                      run[:, h * PD:(h + 1) * PD], dchb[:, h:h + 1],
                      sgbuf[:, h * PD:(h + 1) * PD], ALU.mult, ALU.add)

        def op_consume(d0, t0, tw, psum):
            dc = d0 // 128
            if t0 == 0:
                for tt in range(0, TPC, 512):
                    seq.dma("sync", x0c[:, tt:tt + 512],
                            x0d[dc * 128:(dc + 1) * 128, tt:tt + 512])
            e_stt(seq, stg[:, :tw], psum, gac[:, dc:dc + 1], x0c[:, t0:t0 + tw],
                  ALU.mult, ALU.add)
            seq.dma("sync", x1d[dc * 128:(dc + 1) * 128, t0:t0 + tw], stg[:, :tw])

        NIC = INTER // 128
        for d0 in range(0, DM, 128):
            dma_wslice_fm(seq, wbuf, wopT, INTER, d0, 128)
            for t0 in range(0, TPC, 512):
                for c in range(NIC):
                    e_mm(seq, p_o[:, :], wbuf[:, c * 128:(c + 1) * 128],
                         yn_fm[:, c * TPC + t0: c * TPC + t0 + 512],
                         start=(c == 0), stop=(c == NIC - 1))
                op_consume(d0, t0, 512, p_o[:, :])
        seq.install(nc, sems, block)
    return nc



_PROGS = {}


def _prog(name, builder):
    if name not in _PROGS:
        _PROGS[name] = builder()
    return _PROGS[name]


def _hadamard(n):
    H = np.array([[1.0]], np.float32)
    while H.shape[0] < n:
        H = np.block([[H, H], [H, -H]])
    return H


def kernel(**inputs):
    f32 = np.float32
    CORES = list(range(NCORE))
    ca = np.ascontiguousarray

    x_all = ca(np.asarray(inputs["L0_attn_input_states"], f32)[0])
    w1 = np.asarray(inputs["L0_ffn_weight1"])
    w2 = np.asarray(inputs["L0_ffn_weight2"])
    w3 = np.asarray(inputs["L0_ffn_weight3"])

    inA = []
    for core in CORES:
        m = {}
        for L in range(2):
            cvec = np.asarray(inputs[f"L{L}_c"], f32).reshape(16, 128)
            m[f"c{L}T"] = ca(cvec.T)
            wm = np.asarray(inputs[f"L{L}_wmod"], f32)
            m[f"wmod{L}s"] = ca(wm[:, core * 1536:(core + 1) * 1536])
        m["w1s"] = ca(w1[core * W1_ROWS:(core + 1) * W1_ROWS])
        m["w2s"] = ca(w2[core * W1_ROWS:(core + 1) * W1_ROWS])
        m["w3s"] = ca(w3[core * W3_ROWS:(core + 1) * W3_ROWS])
        inA.append(m)
    resA = run_bass_kernel_spmd(_prog("A", build_launch_A), inA, core_ids=CORES)
    mod_full = np.zeros((2, 12288), f32)
    for c in CORES:
        mod_full[:, c * 1536:(c + 1) * 1536] = resA.results[c]["modrow"]
    mc = np.zeros((128, 192), f32)
    for L in range(2):
        mc[:, L * 96:(L + 1) * 96] = (
            mod_full[L].reshape(6, 16, 128).transpose(2, 0, 1).reshape(128, 96))
    ssums = ca(np.stack([resA.results[c]["ssum"][0] for c in CORES]))

    inB = [dict(w1s=inA[c]["w1s"], w2s=inA[c]["w2s"], w3s=inA[c]["w3s"],
                allsum=ssums) for c in CORES]
    resB = run_bass_kernel_spmd(_prog("B", build_launch_B), inB, core_ids=CORES)
    w1pT = ca(np.concatenate([resB.results[c]["wq1"] for c in CORES], axis=0).T)
    w2pT = ca(np.concatenate([resB.results[c]["wq2"] for c in CORES], axis=0).T)
    w3pT = ca(np.concatenate([resB.results[c]["wq3"] for c in CORES], axis=0).T)

    cw = np.asarray(inputs["L0_attn_conv_w"], f32).reshape(CONV_DIM, KCONV)
    cwc = ca(cw.reshape(34, 128, KCONV).transpose(1, 0, 2).reshape(128, 34 * KCONV))
    cbc = ca(np.asarray(inputs["L0_attn_conv_b"], f32).reshape(34, 128).T)
    wipT = ca(np.asarray(inputs["L0_attn_in_proj_w"], f32).T)
    dtb = ca(np.asarray(inputs["L0_attn_dt_bias"], f32).reshape(NHEAD, 1))
    alog = ca(np.asarray(inputs["L0_attn_A_log"], f32).reshape(NHEAD, 1))
    drow = ca(np.asarray(inputs["L0_attn_D"], f32).reshape(1, NHEAD))
    x0s = [ca(x_all[c * TPC:(c + 1) * TPC].T) for c in CORES]
    inC = []
    for core in CORES:
        xh = np.zeros((DM, KCONV), f32)
        if core > 0:
            xh[:, :] = x_all[core * TPC - 3: core * TPC + 1].T
        hm = np.full((128, 1), 0.0 if core == 0 else 1.0, f32)
        inC.append(dict(x0=x0s[core], xh=ca(xh), wipT=wipT, convw=cwc,
                        convb=cbc, dtb=dtb, alog=alog, drow=drow, modc=mc,
                        hmask=hm))
    resC = run_bass_kernel_spmd(_prog("C", build_launch_C), inC, core_ids=CORES)

    csumall = ca(np.stack([resC.results[c]["csum"] for c in CORES]))
    cltall = ca(np.stack([resC.results[c]["clt"][:, 0] for c in CORES]))
    wopT = ca(np.asarray(inputs["L0_attn_out_proj_w"], f32).T)
    nwT = ca(np.asarray(inputs["L0_attn_norm_w"], f32).reshape(32, 128).T)
    inD = []
    for core in CORES:
        miT = np.zeros((NCORE, NCORE), f32)
        incm = np.zeros((NCORE, 1), f32)
        for j in range(NCORE):
            if j < core:
                incm[j, 0] = 1.0
            for k in range(NCORE):
                if j < k < core:
                    miT[k, j] = 1.0
        r = resC.results[core]
        inD.append(dict(x0=x0s[core], yp=r["yp"], gate=r["gate"], cfm=r["cfm"],
                        acum=r["acum"], cstates=r["cstates"], csumall=csumall,
                        cltall=cltall, miT=miT, incm=incm, wopT=wopT, nwT=nwT,
                        modc=mc))
    resD = run_bass_kernel_spmd(_prog("D", build_launch_D), inD, core_ids=CORES)

    wqkvT = ca(np.asarray(inputs["L1_attn_w_qkv"], f32).T)
    RL = np.zeros((128, 128), f32)
    for i in range(64):
        RL[i + 64, i] = -1.0
        RL[i, i + 64] = 1.0
    RLb = RL.astype(bf16)
    inE = []
    for core in CORES:
        pos = ca((np.arange(TPC, dtype=f32) + core * TPC)[None, :])
        inE.append(dict(x1=resD.results[core]["x1"], w1pT=w1pT, w2pT=w2pT,
                        w3pT=w3pT, wqkvT=wqkvT, modc=mc, rrot=RLb, posr=pos))
    resE = run_bass_kernel_spmd(_prog("E", build_launch_E), inE, core_ids=CORES)

    kTall = ca(np.concatenate([resE.results[c]["kT"] for c in CORES], axis=2))
    vt_full = np.concatenate([resE.results[c]["vt"] for c in CORES], axis=0)
    vall = ca(vt_full.reshape(SEQ, AHEADS, HD).transpose(1, 0, 2))
    qkmaxall = ca(np.concatenate([resE.results[c]["qkmax"] for c in CORES],
                                 axis=0))
    woT = ca(np.asarray(inputs["L1_attn_w_out"], f32).T)
    inF1 = [dict(x2=resE.results[c]["x2"], qT=resE.results[c]["qT"],
                 kTall=kTall, vall=vall, woT=woT, modc=mc, qkmaxall=qkmaxall)
            for c in CORES]
    resF1 = run_bass_kernel_spmd(_prog("F1", build_launch_F1), inF1,
                                 core_ids=CORES)

    W1T = ca(np.asarray(inputs["L1_ffn_W1"], f32).T)
    W3T = ca(np.asarray(inputs["L1_ffn_W3"], f32).T)
    W2T = ca(np.asarray(inputs["L1_ffn_W2"], f32).T)
    H128b = _hadamard(128).astype(bf16)
    H64 = _hadamard(64)
    HI2 = np.zeros((128, 128), f32)
    for cp in range(64):
        for t2 in range(2):
            HI2[cp * 2 + t2, t2::2] = H64[cp]
    HI2b = HI2.astype(bf16)
    inF2 = [dict(x3=resF1.results[c]["x3"], W1T=W1T, W3T=W3T, W2T=W2T,
                 modc=mc, h128=H128b, hi2=HI2b) for c in CORES]
    resF2 = run_bass_kernel_spmd(_prog("F2", build_launch_F2), inF2,
                                 core_ids=CORES)

    out = np.concatenate([resF2.results[c]["x4"].T for c in CORES], axis=0)
    return ca(out.astype(np.float32))[None]


# version 06
# speedup vs baseline: 1.1400x; 1.1400x over previous
import math
from contextlib import ExitStack
import numpy as np
import ml_dtypes
import concourse.bass as bass
import concourse.mybir as mybir
from concourse.bass_utils import run_bass_kernel_spmd

F32 = mybir.dt.float32
BF16 = mybir.dt.bfloat16
AF = mybir.ActivationFunctionType
ALU = mybir.AluOpType
AX = mybir.AxisListType
ds = bass.ds
bf16 = ml_dtypes.bfloat16

NCORE = 8
SEQ = 8192
TPC = SEQ // NCORE
DM = 2048
NDC = DM // 128
CH = 256
NCHUNK = TPC // CH

INTER = 4096
NHEAD = 64
PD = 64
ND = 128
KCONV = 4
CONV_DIM = INTER + 2 * ND
PROJ_DIM = INTER + CONV_DIM + NHEAD
DT_MIN, DT_MAX = 1e-3, 100.0
MEPS = 1e-5
LEPS = 1e-5

DFF0 = 5632
W1_ROWS = DFF0 // NCORE
W3_ROWS = DM // NCORE
NBLK = 22528
NGRP = NBLK // 256
GPP = NBLK // 128
TOT_BLOCKS = 180224
_NF4_REF = np.array([-1.0, -0.6962, -0.5251, -0.3949, -0.2844, -0.1848,
                     -0.0911, 0.0, 0.0796, 0.1609, 0.2461, 0.3379, 0.4407,
                     0.5626, 0.723, 1.0], dtype=bf16).astype(np.float64)
NF4L = [float(v) for v in _NF4_REF]
NF4_MID = [(NF4L[i] + NF4L[i + 1]) / 2.0 for i in range(15)]
RNE_MAGIC = 12582912.0

AHEADS = 16
HD = 128
ROPE_BASE = 10000.0

DFF1 = 8192

ENGINES = ("sync", "gpsimd", "vector", "scalar", "tensor")

LOGICAL = ("sync", "gpdma", "gpop", "vector", "scalar", "tensor")
NL = len(LOGICAL)
LIDX = {e: i for i, e in enumerate(LOGICAL)}
PHYS_OF = {"sync": "sync", "gpdma": "gpsimd", "gpop": "gpsimd",
           "vector": "vector", "scalar": "scalar", "tensor": "tensor"}
TENS_LI = LIDX["tensor"]

_DT_SIZE = {}


def _ap_ranges(ap):
    pat = ap.ap
    dt = ap.dtype
    esz = _DT_SIZE.get(dt)
    if esz is None:
        esz = dt.size()
        _DT_SIZE[dt] = esz
    off = ap.offset
    if str(ap.space).endswith("DRAM"):
        span = sum(st * (sz - 1) for st, sz in pat) + 1
        return (ap.name, 0, 128, off * esz, (off + span) * esz, True)
    p0 = ap.base_partition()
    pcnt = pat[0][1]
    pstride = pat[0][0]
    prow = off // pstride if pstride > 0 else 0
    f0 = off - prow * pstride
    span = (sum(st * (sz - 1) for st, sz in pat[1:]) + 1) if len(pat) > 1 else 1
    return (ap.name, p0, p0 + pcnt, f0 * esz, (f0 + span) * esz, False)


class Seq:

    def __init__(self):
        self.steps = []

    def emit(self, engine, fn, inc=1, reads=None, writes=None):
        assert engine in ENGINES, engine
        if engine == "gpsimd":
            engine = "gpdma" if inc == 16 else "gpop"
        elif engine == "sync":
            assert inc == 16, "sync engine hosts only DMA"
        self.steps.append((engine, fn, inc, reads, writes))

    def dma(self, engine, out, in_, **kw):
        self.emit(engine, lambda e: e.dma_start(out=out, in_=in_, **kw), 16,
                  reads=(in_,), writes=(out,))

    def install(self, nc, sems, block):
        steps = self.steps
        nst = len(steps)
        cum = [0] * NL
        issue_guar = {p: [0] * NL for p in ENGINES}
        cells = {}
        fbsz = {}
        waits = [None] * nst
        barrier_vec = [0] * NL

        def cells_of(ap):
            try:
                name, p0, p1, b0, b1, is_dram = _ap_ranges(ap)
            except Exception:
                return None
            fb = fbsz.get(name)
            if fb is None:
                if is_dram:
                    try:
                        t = ap.tensor
                        totb = 1
                        for s in t.shape:
                            totb *= s
                        totb *= _DT_SIZE.get(ap.dtype) or ap.dtype.size()
                    except Exception:
                        totb = 1 << 20
                    fb = max(4096, totb >> 6)
                else:
                    fb = 512
                fbsz[name] = fb
            out = []
            for pb in range(p0 >> 5, ((p1 - 1) >> 5) + 1):
                for fbk in range(b0 // fb, (b1 - 1) // fb + 1):
                    out.append((name, pb, fbk))
            return out

        for k, (leng, fn, inc, reads, writes) in enumerate(steps):
            li = LIDX[leng]
            phys = PHYS_OF[leng]
            R = list(barrier_vec)
            rcells = []
            wcells = []
            bad = reads is None and writes is None
            if not bad:
                for ap in (reads or ()):
                    if not hasattr(ap, "offset"):
                        continue
                    cl = cells_of(ap)
                    if cl is None:
                        bad = True
                        break
                    rcells.extend(cl)
                for ap in (writes or ()):
                    if bad or not hasattr(ap, "offset"):
                        continue
                    cl = cells_of(ap)
                    if cl is None:
                        bad = True
                        break
                    wcells.extend(cl)
            if bad:
                for i in range(NL):
                    if cum[i] > R[i]:
                        R[i] = cum[i]
            else:
                for key in rcells:
                    c = cells.get(key)
                    if c is not None and c[0] is not None:
                        w = c[0]
                        for i in range(NL):
                            if w[i] > R[i]:
                                R[i] = w[i]
                for key in wcells:
                    c = cells.get(key)
                    if c is not None:
                        for v in c:
                            if v is not None:
                                for i in range(NL):
                                    if v[i] > R[i]:
                                        R[i] = v[i]
            iq = issue_guar[phys]
            wts = []
            for j in range(NL):
                if R[j] > iq[j]:
                    if phys == "tensor" and j == TENS_LI:
                        iq[j] = R[j]
                        continue
                    wts.append((j, R[j]))
                    iq[j] = R[j]
            waits[k] = wts
            cum[li] += inc
            g = [max(R[i], iq[i]) for i in range(NL)]
            g[li] = cum[li]
            if bad:
                barrier_vec = g
                cells.clear()
            else:
                for key in wcells:
                    cells[key] = [g, None]
                for key in rcells:
                    c = cells.get(key)
                    if c is None:
                        cells[key] = [None, list(g)]
                    elif c[1] is None:
                        c[1] = list(g)
                    else:
                        r = c[1]
                        for i in range(NL):
                            if g[i] > r[i]:
                                r[i] = g[i]

        totals = list(cum)

        def make(phys_name):
            def body(eng):
                for k, (leng, fn, inc, _r, _w) in enumerate(steps):
                    if PHYS_OF[leng] != phys_name:
                        continue
                    for (j, v) in waits[k]:
                        eng.wait_ge(sems[LOGICAL[j]], v)
                    inst = fn(eng)
                    assert inst is not None, f"{leng} step returned None"
                    inst.then_inc(sems[leng], inc)
                if phys_name == "sync":
                    for j, nm in enumerate(LOGICAL):
                        if totals[j]:
                            eng.wait_ge(sems[nm], totals[j])
            return body

        for phys_name in ENGINES:
            getattr(block, phys_name)(make(phys_name))
        return totals


def mk_sems(b):
    return {nm: b.sem(f"SQ_{nm}") for nm in LOGICAL}


def new_nc():
    return bass.Bass("TRN2", target_bir_lowering=False)


class Bufs:

    def __init__(self, nc, ctx):
        self.nc, self.ctx = nc, ctx

    def sb(self, name, shape, dtype=F32):
        return self.ctx.enter_context(self.nc.sbuf_tensor(name, shape, dtype))

    def ps(self, name, shape, dtype=F32):
        return self.ctx.enter_context(self.nc.psum_tensor(name, shape, dtype))

    def sem(self, name="S"):
        return self.ctx.enter_context(self.nc.semaphore(name))

    def block(self):
        return self.ctx.enter_context(self.nc.Block())



def e_memset(seq, ap, val, engine="vector"):
    seq.emit(engine, lambda e: e.memset(ap, float(val)), reads=(), writes=(ap,))


def e_identity(seq, ident):
    seq.emit("gpsimd", lambda e: e.memset(ident[:, :], 0.0),
             reads=(), writes=(ident[:, :],))
    seq.emit("gpsimd", lambda e: e.affine_select(
        out=ident[:, :], in_=ident[:, :], compare_op=ALU.not_equal,
        fill=1.0, base=0, pattern=[[-1, ident.shape[1]]], channel_multiplier=1),
        reads=(ident[:, :],), writes=(ident[:, :],))


def e_copy(seq, out, in_, engine="vector"):
    seq.emit(engine, lambda e: e.tensor_copy(out, in_),
             reads=(in_,), writes=(out,))


def e_act(seq, out, in_, func, bias=0.0, scale=1.0, accum_out=None):
    wr = (out,) if accum_out is None else (out, accum_out)
    seq.emit("scalar", lambda e: e.activation(out, in_, func, bias=bias,
                                              scale=scale, accum_out=accum_out),
             reads=(in_, bias), writes=wr)


def e_ts(seq, out, in0, s1, s2=None, op0=ALU.mult, op1=ALU.bypass, engine="vector"):
    if s2 is None:
        seq.emit(engine, lambda e: e.tensor_scalar(out, in0, s1, None, op0),
                 reads=(in0, s1), writes=(out,))
    else:
        seq.emit(engine, lambda e: e.tensor_scalar(out, in0, s1, s2, op0, op1),
                 reads=(in0, s1, s2), writes=(out,))


def e_tt(seq, out, in0, in1, op, engine="vector"):
    seq.emit(engine, lambda e: e.tensor_tensor(out, in0, in1, op),
             reads=(in0, in1), writes=(out,))


def e_stt(seq, out, in0, scalar, in1, op0, op1, engine="vector"):
    seq.emit(engine, lambda e: e.scalar_tensor_tensor(out, in0, scalar, in1,
                                                      op0, op1),
             reads=(in0, scalar, in1), writes=(out,))


def e_red(seq, out, in_, op, absval=False, negate=False):
    seq.emit("vector", lambda e: e.tensor_reduce(out, in_, AX.X, op,
                                                 apply_absolute_value=absval,
                                                 negate=negate),
             reads=(in_,), writes=(out,))


def e_mm(seq, out, lhsT, rhs, start=True, stop=True):
    rd = (lhsT, rhs) if start else (lhsT, rhs, out)
    seq.emit("tensor", lambda e: e.matmul(out, lhsT, rhs, start=start,
                                          stop=stop),
             reads=rd, writes=(out,))


def e_tr(seq, out, in_, ident):
    seq.emit("tensor", lambda e: e.transpose(out, in_, ident),
             reads=(in_, ident), writes=(out,))


def e_recip(seq, out, in_):
    seq.emit("vector", lambda e: e.reciprocal(out, in_),
             reads=(in_,), writes=(out,))


def e_bcast_col(seq, nc, ones_row, src_row, psum, dst_col=None):
    e_mm(seq, psum, ones_row[:, :], src_row)
    if dst_col is not None:
        e_copy(seq, dst_col, psum)



def build_launch_A(do_mod=True, do_nf4=True):
    nc = new_nc()
    cT = {}
    wmod = {}
    for L in range(2):
        cT[L] = nc.dram_tensor(f"c{L}T", [128, 16], F32, kind="ExternalInput").ap()
        wmod[L] = nc.dram_tensor(f"wmod{L}s", [DM, 12288 // NCORE], F32,
                                 kind="ExternalInput").ap()
    wsh = {}
    for i, (r, c) in enumerate([(W1_ROWS, DM), (W1_ROWS, DM), (W3_ROWS, DFF0)]):
        wsh[i] = nc.dram_tensor(f"w{i+1}s", [r, c], BF16, kind="ExternalInput").ap()
    modrow = nc.dram_tensor("modrow", [2, 12288 // NCORE], F32,
                            kind="ExternalOutput").ap()
    ssum = nc.dram_tensor("ssum", [1, 4], F32, kind="ExternalOutput").ap()

    FSH = 12288 // NCORE
    with ExitStack() as ctx:
        b = Bufs(nc, ctx)
        sc = b.sb("sc", [128, 16])
        silu_c = b.sb("silu_c", [128, 16])
        wbuf = b.sb("wbuf", [128, 16 * 512])
        mrow = [b.sb(f"mrow{L}", [1, FSH]) for L in range(2)]
        wg = b.sb("wg", [NGRP, 256 * 64], BF16)
        s1g = b.sb("s1g", [NGRP, 256])
        rowsum = b.sb("rowsum", [NGRP, 1])
        ones_col = b.sb("ones_col", [128, 1])
        ssum_sb = b.sb("ssum_sb", [1, 4])
        p_row = b.ps("p_row", [1, 512])
        p_sum = b.ps("p_sum", [1, 1])
        sems = mk_sems(b)
        block = b.block()
        seq = Seq()
        e_memset(seq, ones_col[:, :], 1.0)
        e_memset(seq, ssum_sb[:, :], 0.0)
        for L in range(2 if do_mod else 0):
            seq.dma("sync", sc[:, :], cT[L])
            e_act(seq, silu_c[:, :], sc[:, :], AF.Silu)
            for fs in range(FSH // 512):
                for c in range(NDC):
                    seq.dma("sync", wbuf[:, c * 512:(c + 1) * 512],
                            wmod[L][c * 128:(c + 1) * 128, fs * 512:(fs + 1) * 512])
                for c in range(NDC):
                    e_mm(seq, p_row[:, :], silu_c[:, c:c + 1], wbuf[:, c * 512:(c + 1) * 512],
                         start=(c == 0), stop=(c == NDC - 1))
                e_copy(seq, mrow[L][:, fs * 512:(fs + 1) * 512], p_row[:, :])
        for L in range(2):
            seq.dma("sync", modrow[L:L + 1, :], mrow[L][:, :])
        for i in range(3 if do_nf4 else 0):
            flat = wsh[i].rearrange("a b -> (a b)")
            src = flat.rearrange("(g j) -> g j", g=NGRP)
            seq.dma("sync", wg[:, :], src)
            e_red(seq, s1g[:, :], wg[:, :].rearrange("g (b k) -> g b k", k=64),
                  ALU.max, absval=True)
            e_red(seq, rowsum[:, :], s1g[:, :], ALU.add)
            e_mm(seq, p_sum[:, :], ones_col[:NGRP, :], rowsum[:, :])
            e_copy(seq, ssum_sb[:, i:i + 1], p_sum[:, :])
        seq.dma("sync", ssum, ssum_sb[:, :])
        seq.install(nc, sems, block)
    return nc



def build_launch_B():
    nc = new_nc()
    wsh = {}
    shapes = [(W1_ROWS, DM), (W1_ROWS, DM), (W3_ROWS, DFF0)]
    for i, (r, c) in enumerate(shapes):
        wsh[i] = nc.dram_tensor(f"w{i+1}s", [r, c], BF16, kind="ExternalInput").ap()
    allsum = nc.dram_tensor("allsum", [NCORE, 4], F32, kind="ExternalInput").ap()
    wq = {}
    for i, (r, c) in enumerate(shapes):
        wq[i] = nc.dram_tensor(f"wq{i+1}", [r, c], BF16, kind="ExternalOutput").ap()

    GW = 256 * 64
    with ExitStack() as ctx:
        b = Bufs(nc, ctx)
        wg = b.sb("wg", [NGRP, GW], BF16)
        s1g = b.sb("s1g", [NGRP, 256])
        r1g = b.sb("r1g", [NGRP, 256])
        cent = b.sb("cent", [NGRP, 256], BF16)
        centf = b.sb("centf", [NGRP, 256])
        gmax = b.sb("gmax", [NGRP, 1])
        qf = b.sb("qf", [NGRP, 1])
        rqf = b.sb("rqf", [NGRP, 1])
        qs = b.sb("qs", [NGRP, 256])
        deqb = b.sb("deqb", [NGRP, 256], BF16)
        sums = b.sb("sums", [1, 4 * NCORE])
        tot = b.sb("tot", [1, 4])
        meanb = b.sb("meanb", [1, 1], BF16)
        meanf = b.sb("meanf", [1, 1])
        ones_row = b.sb("ones_row", [1, 128])
        mean_col = b.sb("mean_col", [128, 1])
        scl = b.sb("scl", [NGRP, GW], BF16)
        tmp = b.sb("tmp", [NGRP, GW], BF16)
        acc = b.sb("acc", [NGRP, GW], BF16)
        p_mean = b.ps("p_mean", [128, 1])
        sems = mk_sems(b)
        block = b.block()
        seq = Seq()
        e_memset(seq, ones_row[:, :], 1.0)
        seq.dma("sync", sums[:, :], allsum.rearrange("a b -> (a b)")[None, :])

        for i in range(3):
            flat = wsh[i].rearrange("a b -> (a b)")
            gview = flat.rearrange("(g j) -> g j", g=NGRP)
            seq.dma("sync", wg[:, :], gview)
            e_red(seq, s1g[:, :], wg[:, :].rearrange("g (b k) -> g b k", k=64),
                  ALU.max, absval=True)
            e_red(seq, tot[:, i:i + 1],
                  sums[:, :].rearrange("o (c w) -> o w c", w=4)[:, i:i + 1, :],
                  ALU.add)
            seq.emit("vector", lambda e, i=i: e.tensor_single_scalar(
                meanb[:, :], tot[:, i:i + 1], float(1.0 / TOT_BLOCKS), ALU.mult),
                reads=(tot[:, i:i + 1],), writes=(meanb[:, :],))
            e_copy(seq, meanf[:, :], meanb[:, :])
            e_mm(seq, p_mean[:, :], ones_row[:, :], meanf[:, :])
            e_copy(seq, mean_col[:, :], p_mean[:, :])
            e_ts(seq, cent[:, :], s1g[:, :], mean_col[:NGRP, :], None, ALU.subtract)
            e_red(seq, gmax[:, :], cent[:, :], ALU.max, absval=True)
            e_recip(seq, qf[:, :], gmax[:, :])
            e_ts(seq, qf[:, :], qf[:, :], 128.0, None, ALU.mult)
            e_copy(seq, centf[:, :], cent[:, :])
            e_ts(seq, qs[:, :], centf[:, :], qf[:, :], RNE_MAGIC, ALU.mult, ALU.add)
            e_ts(seq, qs[:, :], qs[:, :], RNE_MAGIC, None, ALU.subtract)
            e_ts(seq, qs[:, :], qs[:, :], -128.0, 127.0, ALU.max, ALU.min)
            e_recip(seq, rqf[:, :], qf[:, :])
            e_ts(seq, deqb[:, :], qs[:, :], rqf[:, :], None, ALU.mult)
            e_copy(seq, centf[:, :], deqb[:, :])
            e_ts(seq, deqb[:, :], centf[:, :], mean_col[:NGRP, :], None, ALU.add)
            e_recip(seq, r1g[:, :], s1g[:, :])
            e_tt(seq, scl[:, :].rearrange("g (b k) -> g b k", k=64),
                 wg[:, :].rearrange("g (b k) -> g b k", k=64),
                 r1g[:, :].rearrange("g (b o) -> g b o", o=1).broadcast_to([NGRP, 256, 64]),
                 ALU.mult)
            e_memset(seq, acc[:, :], 0.0)
            for lv in range(7):
                e_ts(seq, tmp[:, :], scl[:, :], float(NF4_MID[lv]), float(NF4L[lv]),
                     ALU.is_le, ALU.mult)
                e_tt(seq, acc[:, :], acc[:, :], tmp[:, :], ALU.min)
            e_memset(seq, tmp[:, :], 0.0)
            for lv in range(7, 15):
                e_ts(seq, wg[:, :], scl[:, :], float(NF4_MID[lv]), float(NF4L[lv + 1]),
                     ALU.is_gt, ALU.mult)
                e_tt(seq, tmp[:, :], tmp[:, :], wg[:, :], ALU.max)
            e_tt(seq, acc[:, :], acc[:, :], tmp[:, :], ALU.add)
            e_tt(seq, scl[:, :].rearrange("g (b k) -> g b k", k=64),
                 acc[:, :].rearrange("g (b k) -> g b k", k=64),
                 deqb[:, :].rearrange("g (b o) -> g b o", o=1).broadcast_to([NGRP, 256, 64]),
                 ALU.mult)
            out_g = wq[i].rearrange("a b -> (a b)").rearrange("(g j) -> g j", g=NGRP)
            seq.dma("sync", out_g, scl[:, :])
        seq.install(nc, sems, block)
    return nc



def e_clamp_gate(seq, out_cols, in_cols, sgn_scratch):
    e_ts(seq, sgn_scratch, in_cols, 0.0, 2.0, ALU.is_ge, ALU.mult)
    e_ts(seq, sgn_scratch, sgn_scratch, -1.0, None, ALU.add)
    e_tt(seq, out_cols, in_cols, sgn_scratch, ALU.mult)
    e_ts(seq, out_cols, out_cols, 0.25, None, ALU.max)
    e_tt(seq, out_cols, out_cols, sgn_scratch, ALU.mult)


class LNBufs:
    def __init__(self, b, tag, p_b=None, p_sum=None, p_sq=None,
                 mean_b=None, rstd_b=None, t1=None):
        self.sq = b.sb(f"ln_sq_{tag}", [128, 512])
        rt = b.sb(f"ln_rows_{tag}", [1, 1024])
        rt2 = b.sb(f"ln_rows2_{tag}", [1, 512])
        self.mean_row = rt[0:1, 0:512]
        self.rstd_row = rt[0:1, 512:1024]
        self.msq_row = rt2[0:1, :]
        self.mean_b = mean_b if mean_b is not None else b.sb(
            f"ln_mb_{tag}", [128, 512])
        self.rstd_b = rstd_b if rstd_b is not None else b.sb(
            f"ln_rb_{tag}", [128, 512])
        self.t1 = t1 if t1 is not None else b.sb(f"ln_t1_{tag}", [128, 512])
        self.p_sum = p_sum if p_sum is not None else b.ps(
            f"ln_ps_{tag}", [1, 512])
        self.p_sq = p_sq if p_sq is not None else b.ps(f"ln_pq_{tag}", [1, 512])
        self.p_b = p_b if p_b is not None else b.ps(f"ln_pb_{tag}", [128, 512])


def emit_ln_mod(seq, lb, x_sb, h_out, ones_col, ones_row, ca1_cols, sa_cols,
                toks=TPC, nch=NDC):
    for t0 in range(0, toks, 512):
        tw = min(512, toks - t0)
        for c in range(nch):
            xc = x_sb[:, c * toks + t0: c * toks + t0 + tw]
            e_act(seq, lb.sq[:, :tw], xc, AF.Square)
            e_mm(seq, lb.p_sum[:, :tw], ones_col[:, :], xc,
                 start=(c == 0), stop=(c == nch - 1))
            e_mm(seq, lb.p_sq[:, :tw], ones_col[:, :], lb.sq[:, :tw],
                 start=(c == 0), stop=(c == nch - 1))
        dim = 128 * nch
        e_ts(seq, lb.mean_row[:, :tw], lb.p_sum[:, :tw], 1.0 / dim, None, ALU.mult)
        e_ts(seq, lb.msq_row[:, :tw], lb.p_sq[:, :tw], 1.0 / dim, None, ALU.mult)
        e_stt(seq, lb.rstd_row[:, :tw], lb.mean_row[:, :tw], -1.0,
              lb.mean_row[:, :tw], ALU.mult, ALU.mult)
        e_tt(seq, lb.msq_row[:, :tw], lb.msq_row[:, :tw], lb.rstd_row[:, :tw],
             ALU.add)
        e_ts(seq, lb.msq_row[:, :tw], lb.msq_row[:, :tw], LEPS, None, ALU.add)
        e_act(seq, lb.rstd_row[:, :tw], lb.msq_row[:, :tw], AF.Sqrt)
        e_recip(seq, lb.rstd_row[:, :tw], lb.rstd_row[:, :tw])
        e_mm(seq, lb.p_b[:, :tw], ones_row[:, :], lb.mean_row[:, :tw])
        e_copy(seq, lb.mean_b[:, :tw], lb.p_b[:, :tw])
        e_mm(seq, lb.p_b[:, :tw], ones_row[:, :], lb.rstd_row[:, :tw])
        e_copy(seq, lb.rstd_b[:, :tw], lb.p_b[:, :tw])
        for c in range(nch):
            xc = x_sb[:, c * toks + t0: c * toks + t0 + tw]
            hc = h_out[:, c * toks + t0: c * toks + t0 + tw]
            e_stt(seq, lb.t1[:, :tw], lb.mean_b[:, :tw], -1.0, xc, ALU.mult, ALU.add)
            e_stt(seq, lb.t1[:, :tw], lb.t1[:, :tw], ca1_cols[:, c:c + 1],
                  lb.rstd_b[:, :tw], ALU.mult, ALU.mult)
            e_ts(seq, hc, lb.t1[:, :tw], sa_cols[:, c:c + 1], None, ALU.add)


def dma_wslice_fm(seq, wbuf, wT_dram, n_in, f0, fw, engine="gpsimd"):
    nch = n_in // 128
    src = wT_dram.rearrange("(c p) f -> p c f", p=128)[:, :, f0:f0 + fw]
    dst = wbuf[:, :nch * fw].rearrange("p (c f) -> p c f", f=fw)
    seq.dma(engine, dst, src)


def emit_gemm_fm(seq, wT_dram, act_bf, n_in, f_lo, f_hi, toks, wbuf, psum,
                 consume, tok_tile=512):
    nch = n_in // 128
    for f0 in range(f_lo, f_hi, 128):
        dma_wslice_fm(seq, wbuf, wT_dram, n_in, f0, 128)
        for t0 in range(0, toks, tok_tile):
            tw = min(tok_tile, toks - t0)
            for c in range(nch):
                e_mm(seq, psum[:, :tw], wbuf[:, c * 128:(c + 1) * 128],
                     act_bf[:, c * toks + t0: c * toks + t0 + tw],
                     start=(c == 0), stop=(c == nch - 1))
            consume(f0 - f_lo, t0, tw, psum[:, :tw])



def build_launch_E():
    nc = new_nc()
    x1d = nc.dram_tensor("x1", [DM, TPC], F32, kind="ExternalInput").ap()
    w1T = nc.dram_tensor("w1pT", [DM, DFF0], BF16, kind="ExternalInput").ap()
    w2T = nc.dram_tensor("w2pT", [DM, DFF0], BF16, kind="ExternalInput").ap()
    w3T = nc.dram_tensor("w3pT", [DFF0, DM], BF16, kind="ExternalInput").ap()
    wqkvT = nc.dram_tensor("wqkvT", [DM, 3 * DM], F32, kind="ExternalInput").ap()
    modc = nc.dram_tensor("modc", [128, 2 * 6 * NDC], F32, kind="ExternalInput").ap()
    rrot = nc.dram_tensor("rrot", [128, 128], BF16, kind="ExternalInput").ap()
    posr = nc.dram_tensor("posr", [1, TPC], F32, kind="ExternalInput").ap()
    x2d = nc.dram_tensor("x2", [DM, TPC], F32, kind="ExternalOutput").ap()
    qTd = nc.dram_tensor("qT", [AHEADS, HD, TPC], BF16, kind="ExternalOutput").ap()
    kTd = nc.dram_tensor("kT", [AHEADS, HD, TPC], BF16, kind="ExternalOutput").ap()
    vtd = nc.dram_tensor("vt", [TPC, DM], BF16, kind="ExternalOutput").ap()
    qkmaxd = nc.dram_tensor("qkmax", [1, 2 * AHEADS], F32, kind="ExternalOutput").ap()

    NF0 = DFF0 // 128
    with ExitStack() as ctx:
        b = Bufs(nc, ctx)
        x_sb = b.sb("x_sb", [128, NDC * TPC])
        h_sb = b.sb("h_sb", [128, NDC * TPC], BF16)
        hid = b.sb("hid", [128, NF0 * 512], BF16)
        mcol = b.sb("mcol", [128, 2 * 6 * NDC])
        ca1 = b.sb("ca1", [128, NDC])
        gbc = b.sb("gbc", [128, NDC])
        sgn = b.sb("sgn", [128, NDC])
        ones_col = b.sb("ones_col", [128, 1])
        ones_row = b.sb("ones_row", [1, 128])
        wbuf1 = [b.sb(f"wbuf1_{i}", [128, NDC * 128], BF16) for i in range(2)]
        wbuf2 = [b.sb(f"wbuf2_{i}", [128, NDC * 128], BF16) for i in range(2)]
        wbufq = b.sb("wbufq", [128, NDC * 512], BF16)
        silu_s = b.sb("silu_s", [128, 512])
        rope_cos = b.sb("rope_cos", [128, TPC])
        rope_sin = b.sb("rope_sin", [128, TPC])
        ifr = b.sb("ifr", [128, 1])
        qk_bf = b.sb("qk_bf", [128, TPC], BF16)
        rot_sb = b.sb("rot_sb", [128, TPC], BF16)
        rconst = b.sb("rconst", [128, 128], BF16)
        posrow = b.sb("posrow", [1, TPC])
        lb = LNBufs(b, "e")
        p_a = b.ps("p_a", [128, 512])
        p_b2 = b.ps("p_b2", [128, 512])
        p_o = b.ps("p_o", [128, 512])
        p_r = b.ps("p_r", [128, 512])
        p_ah = [p_a, p_r]
        p_bh = [p_b2, p_o]
        sems = mk_sems(b)
        block = b.block()
        seq = Seq()

        e_memset(seq, ones_col[:, :], 1.0)
        e_memset(seq, ones_row[:, :], 1.0)
        seq.dma("sync", mcol[:, :], modc)
        for c in range(NDC):
            seq.dma("sync", x_sb[:, c * TPC:(c + 1) * TPC],
                    x1d[c * 128:(c + 1) * 128, :])
        e_ts(seq, ca1[:, :], mcol[:, 4 * NDC:5 * NDC], 1.0, None, ALU.add)
        e_clamp_gate(seq, gbc[:, :], mcol[:, 5 * NDC:6 * NDC], sgn[:, :])
        emit_ln_mod(seq, lb, x_sb, h_sb, ones_col, ones_row,
                    ca1, mcol[:, 3 * NDC:4 * NDC])

        for t0 in range(0, TPC, 512):
            for fi in range(NF0):
                f0 = fi * 128
                pr = fi % 2
                w1c = wbuf1[pr]
                w2c = wbuf2[pr]
                dma_wslice_fm(seq, w1c, w1T, DM, f0, 128)
                dma_wslice_fm(seq, w2c, w2T, DM, f0, 128)
                for c in range(NDC):
                    e_mm(seq, p_ah[pr][:, :], w1c[:, c * 128:(c + 1) * 128],
                         h_sb[:, c * TPC + t0: c * TPC + t0 + 512],
                         start=(c == 0), stop=(c == NDC - 1))
                for c in range(NDC):
                    e_mm(seq, p_bh[pr][:, :], w2c[:, c * 128:(c + 1) * 128],
                         h_sb[:, c * TPC + t0: c * TPC + t0 + 512],
                         start=(c == 0), stop=(c == NDC - 1))
                e_act(seq, silu_s[:, :], p_ah[pr][:, :], AF.Silu)
                e_tt(seq, hid[:, fi * 512:(fi + 1) * 512], silu_s[:, :],
                     p_bh[pr][:, :], ALU.mult)
            for dc in range(NDC):
                dma_wslice_fm(seq, wbufq, w3T, DFF0, dc * 128, 128)
                for fc in range(NF0):
                    e_mm(seq, p_o[:, :], wbufq[:, fc * 128:(fc + 1) * 128],
                         hid[:, fc * 512:(fc + 1) * 512],
                         start=(fc == 0), stop=(fc == NF0 - 1))
                xc = x_sb[:, dc * TPC + t0: dc * TPC + t0 + 512]
                e_stt(seq, xc, p_o[:, :], gbc[:, dc:dc + 1], xc, ALU.mult, ALU.add)
        for c in range(NDC):
            seq.dma("sync", x2d[c * 128:(c + 1) * 128, :],
                    x_sb[:, c * TPC:(c + 1) * TPC])

        e_ts(seq, ca1[:, :], mcol[:, 6 * NDC + 1 * NDC: 6 * NDC + 2 * NDC],
             1.0, None, ALU.add)
        emit_ln_mod(seq, lb, x_sb, h_sb, ones_col, ones_row,
                    ca1, mcol[:, 6 * NDC: 6 * NDC + NDC])

        seq.dma("sync", posrow[:, :], posr)
        seq.emit("gpsimd", lambda e: e.iota(ifr[:, :], [[0, 1]], base=0,
                                            channel_multiplier=1,
                                            allow_small_or_imprecise_dtypes=True),
                 reads=(), writes=(ifr[:, :],))
        e_ts(seq, silu_s[:, 0:1], ifr[:, :], 64.0, 64.0, ALU.is_ge, ALU.mult)
        e_tt(seq, ifr[:, :], ifr[:, :], silu_s[:, 0:1], ALU.subtract)
        e_act(seq, ifr[:, :], ifr[:, :], AF.Exp, scale=-math.log(10000.0) / 64.0)
        TWO_PI = 2.0 * math.pi
        c1 = float(np.float32(6.28125))
        c2 = float(np.float32(np.float64(TWO_PI) - 6.28125))
        c3 = float(np.float64(TWO_PI) - np.float64(c1) - np.float64(c2))
        for th in range(0, TPC, 512):
            ang = silu_s[:, :]
            kk = lb.t1[:, :]
            e_mm(seq, p_r[:, :], ones_row[:, :], posrow[:, th:th + 512])
            e_tt(seq, ang, p_r[:, :], ifr[:, :].broadcast_to([128, 512]), ALU.mult)
            e_ts(seq, kk, ang, 1.0 / TWO_PI, RNE_MAGIC, ALU.mult, ALU.add)
            e_ts(seq, kk, kk, RNE_MAGIC, None, ALU.subtract)
            for cw in (c1, c2, c3):
                e_stt(seq, ang, kk, -cw, ang, ALU.mult, ALU.add)
            e_act(seq, rope_sin[:, th:th + 512], ang, AF.Sin)
            e_ts(seq, kk, ang, float(math.pi / 2), TWO_PI, ALU.is_gt, ALU.mult)
            e_tt(seq, ang, ang, kk, ALU.subtract)
            e_ts(seq, ang, ang, float(math.pi / 2), None, ALU.add)
            e_act(seq, rope_cos[:, th:th + 512], ang, AF.Sin)

        seq.dma("sync", rconst[:, :], rrot)
        qkmax = b.sb("qkmax_sb", [1, 2 * AHEADS])
        nmtmp = b.sb("nmtmp", [1, 1])
        e_memset(seq, qkmax[:, :], 0.0)

        def qk_consume(dst, head, mx_off):
            def consume(f0, t0, tw, psum):
                e_copy(seq, qk_bf[:, t0:t0 + tw], psum)
                if t0 + tw == TPC:
                    for tt in range(0, TPC, 512):
                        e_mm(seq, p_r[:, :], rconst[:, :], qk_bf[:, tt:tt + 512])
                        e_copy(seq, rot_sb[:, tt:tt + 512], p_r[:, :])
                    for tt in range(0, TPC, 512):
                        e_tt(seq, silu_s[:, :], qk_bf[:, tt:tt + 512],
                             rope_cos[:, tt:tt + 512], ALU.mult)
                        e_tt(seq, lb.t1[:, :], rot_sb[:, tt:tt + 512],
                             rope_sin[:, tt:tt + 512], ALU.mult)
                        e_tt(seq, qk_bf[:, tt:tt + 512], silu_s[:, :],
                             lb.t1[:, :], ALU.add)
                    seq.dma("sync", dst[head], qk_bf[:, :])
                    mcell = qkmax[:, mx_off + head: mx_off + head + 1]
                    for tt in range(0, TPC, 512):
                        e_act(seq, silu_s[:, :], qk_bf[:, tt:tt + 512], AF.Square)
                        e_mm(seq, lb.p_sum[:, :], ones_col[:, :], silu_s[:, :])
                        e_red(seq, nmtmp[:, :], lb.p_sum[:, :], ALU.max)
                        e_tt(seq, mcell, mcell, nmtmp[:, :], ALU.max)
            return consume

        for head in range(AHEADS):
            emit_gemm_fm(seq, wqkvT, h_sb, DM, head * HD, (head + 1) * HD, TPC,
                         wbufq, p_a, qk_consume(qTd, head, 0))
        for head in range(AHEADS):
            emit_gemm_fm(seq, wqkvT, h_sb, DM, DM + head * HD, DM + (head + 1) * HD,
                         TPC, wbufq, p_a, qk_consume(kTd, head, AHEADS))
        seq.dma("sync", qkmaxd, qkmax[:, :])
        for f0 in range(0, DM, 512):
            src = wqkvT.rearrange("(c p) f -> p c f", p=128)[
                :, :, 2 * DM + f0: 2 * DM + f0 + 512]
            dst = wbufq[:, :NDC * 512].rearrange("p (c f) -> p c f", f=512)
            seq.dma("gpsimd", dst, src)
            for t0 in range(0, TPC, 128):
                for c in range(NDC):
                    e_mm(seq, p_o[:, :], h_sb[:, c * TPC + t0: c * TPC + t0 + 128],
                         wbufq[:, c * 512:(c + 1) * 512],
                         start=(c == 0), stop=(c == NDC - 1))
                e_copy(seq, qk_bf[:, 0:512], p_o[:, :])
                seq.dma("sync", vtd[t0:t0 + 128, f0:f0 + 512], qk_bf[:, 0:512])
        seq.install(nc, sems, block)
    return nc



def build_launch_F1(debug=False):
    nc = new_nc()
    x2d = nc.dram_tensor("x2", [DM, TPC], F32, kind="ExternalInput").ap()
    qTd = nc.dram_tensor("qT", [AHEADS, HD, TPC], BF16, kind="ExternalInput").ap()
    kTd = nc.dram_tensor("kTall", [AHEADS, HD, SEQ], BF16, kind="ExternalInput").ap()
    vd = nc.dram_tensor("vall", [AHEADS, SEQ, HD], BF16, kind="ExternalInput").ap()
    woT = nc.dram_tensor("woT", [DM, DM], F32, kind="ExternalInput").ap()
    modc = nc.dram_tensor("modc", [128, 2 * 6 * NDC], F32, kind="ExternalInput").ap()
    qkm = nc.dram_tensor("qkmaxall", [NCORE, 2 * AHEADS], F32,
                         kind="ExternalInput").ap()
    x3d = nc.dram_tensor("x3", [DM, TPC], F32, kind="ExternalOutput").ap()
    mx_bounce = nc.dram_tensor("mx_bounce", [4, 128], F32)
    if debug:
        dbg_e = nc.dram_tensor("dbg_e", [128, 512], BF16,
                               kind="ExternalOutput").ap()
        dbg_s = nc.dram_tensor("dbg_s", [1, 512], F32, kind="ExternalOutput").ap()
        dbg_a = nc.dram_tensor("dbg_a", [128, TPC], BF16,
                               kind="ExternalOutput").ap()

    NKC = SEQ // 128
    with ExitStack() as ctx:
        b = Bufs(nc, ctx)
        expS = b.sb("expS", [128, NKC * 512], BF16)
        kTb = [b.sb(f"kT{i}", [128, SEQ], BF16) for i in range(2)]
        v_sbb = [b.sb(f"v_sb{i}", [128, NKC * HD], BF16) for i in range(2)]
        qTb = [b.sb(f"qT_sb{i}", [128, TPC], BF16) for i in range(2)]
        attn = b.sb("attn", [128, AHEADS * TPC], BF16)
        x2c = b.sb("x2c", [128, TPC])
        stage = b.sb("stage", [128, 512])
        wbuf = b.sb("wbufo", [128, NDC * 128], BF16)
        mcol = b.sb("mcolf", [128, 2 * 6 * NDC])
        gac = b.sb("gac", [128, NDC])
        sgn = b.sb("sgn_f", [128, NDC])
        rsum = b.sb("rsum", [1, 512])
        rbc = b.sb("rbc", [128, 512])
        sacc = b.sb("sacc", [128, 4 * 512])
        ones_col = b.sb("ones_colf", [128, 1])
        ones_bf1 = b.sb("ones_bf1", [1, 128], BF16)
        ones_row = b.sb("ones_rowf", [1, 128])
        identf = b.sb("identf", [128, 128])
        pm4 = b.sb("pm4", [128, 4])
        mx1 = b.sb("mx1", [128, 16])
        nmrow = b.sb("nmrow", [1, 512], BF16)
        p_s = b.ps("p_s", [128, 512])
        p_sum = b.ps("p_sumf", [1, 512])
        p_b = b.ps("p_bf", [128, 512])
        p_m = b.ps("p_m", [128, 512])
        p_s2 = b.ps("p_s2", [128, 512])
        p_s3 = b.ps("p_s3", [128, 512])
        prot = [p_s, p_m, p_s2, p_s3]
        sems = mk_sems(b)
        block = b.block()
        seq = Seq()

        e_memset(seq, ones_col[:, :], 1.0)
        e_memset(seq, ones_bf1[:, :], 1.0)
        e_memset(seq, ones_row[:, :], 1.0)
        e_identity(seq, identf)
        seq.dma("sync", mcol[:, :], modc)

        for head in range(AHEADS):
            kT = kTb[head % 2]
            v_sb = v_sbb[head % 2]
            qT = qTb[head % 2]
            seq.dma("sync", kT[:, :], kTd[head])
            for part in range(8):
                seq.dma("sync",
                        v_sb[:, :].rearrange("p (kc d) -> p kc d", d=HD)[
                            :, part * 8:(part + 1) * 8, :],
                        vd[head].rearrange("(kc p) d -> p kc d", p=128)[
                            :, part * 8:(part + 1) * 8, :])
            seq.dma("sync", qT[:, :], qTd[head])
            for t0 in range(0, TPC, 512):
                mxs = mx1
                for qc in range(4):
                    for kt in range(SEQ // 512):
                        pr = prot[kt % 4]
                        e_mm(seq, pr[:, 0:512],
                             qT[:, t0 + qc * 128: t0 + (qc + 1) * 128],
                             kT[:, kt * 512:(kt + 1) * 512])
                        e_red(seq, mxs[:, kt:kt + 1], pr[:, 0:512], ALU.max)
                    e_red(seq, pm4[:, qc:qc + 1],
                          mxs[:, :].rearrange("p (a k) -> p a k", a=1), ALU.max)
                e_tr(seq, p_m[0:4, 0:128], pm4[:, :], identf[:, :])
                e_ts(seq, rbc[0:4, 0:128], p_m[0:4, 0:128], -1.0, None, ALU.mult)
                seq.dma("sync", mx_bounce.ap(), rbc[0:4, 0:128])
                seq.dma("gpsimd", nmrow[:, :],
                        mx_bounce.ap().rearrange("a b -> (a b)")[None, :])
                for kc in range(NKC):
                    ps = prot[kc % 4]
                    ch = expS[:, kc * 512:(kc + 1) * 512]
                    e_mm(seq, ps[:, :], ones_bf1[:, :], nmrow[:, :], stop=False)
                    e_mm(seq, ps[:, :], kT[:, kc * 128:(kc + 1) * 128],
                         qT[:, t0:t0 + 512], start=False)
                    e_act(seq, ch, ps[:, :], AF.Exp, scale=1.0 / math.sqrt(HD))
                    sa = sacc[:, (kc % 4) * 512:(kc % 4 + 1) * 512]
                    if kc < 4:
                        e_copy(seq, sa, ch, engine="gpsimd")
                    else:
                        e_tt(seq, sa, sa, ch, ALU.add, engine="gpsimd")
                e_tt(seq, sacc[:, 0:512], sacc[:, 0:512], sacc[:, 512:1024],
                     ALU.add, engine="gpsimd")
                e_tt(seq, sacc[:, 1024:1536], sacc[:, 1024:1536],
                     sacc[:, 1536:2048], ALU.add, engine="gpsimd")
                e_tt(seq, sacc[:, 0:512], sacc[:, 0:512], sacc[:, 1024:1536],
                     ALU.add, engine="gpsimd")
                e_mm(seq, p_sum[:, :], ones_col[:, :], sacc[:, 0:512])
                e_copy(seq, rsum[:, :], p_sum[:, :])
                e_recip(seq, rsum[:, :], rsum[:, :])
                e_mm(seq, p_b[:, :], ones_row[:, :], rsum[:, :])
                e_copy(seq, rbc[:, :], p_b[:, :])
                for kc in range(NKC):
                    e_mm(seq, p_s[:, :], v_sb[:, kc * HD:(kc + 1) * HD],
                         expS[:, kc * 512:(kc + 1) * 512],
                         start=(kc == 0), stop=(kc == NKC - 1))
                e_tt(seq, attn[:, head * TPC + t0: head * TPC + t0 + 512],
                     p_s[:, :], rbc[:, :], ALU.mult)
        if debug:
            seq.dma("sync", dbg_a, attn[:, 0:TPC])

        e_clamp_gate(seq, gac[:, :], mcol[:, 6 * NDC + 2 * NDC: 6 * NDC + 3 * NDC],
                     sgn[:, :])

        def oproj_consume(dc2, t0, tw, psum):
            dc = dc2 // 128
            xc = x2c[:, t0:t0 + tw]
            if t0 == 0:
                for tt in range(0, TPC, 512):
                    seq.dma("sync", x2c[:, tt:tt + 512],
                            x2d[dc * 128:(dc + 1) * 128, tt:tt + 512])
            e_stt(seq, stage[:, :tw], psum, gac[:, dc:dc + 1], xc,
                  ALU.mult, ALU.add)
            seq.dma("sync", x3d[dc * 128:(dc + 1) * 128, t0:t0 + tw],
                    stage[:, :tw])

        emit_gemm_fm(seq, woT, attn, DM, 0, DM, TPC, wbuf, p_s, oproj_consume)
        seq.install(nc, sems, block)
    return nc



def build_launch_F2():
    nc = new_nc()
    x3d = nc.dram_tensor("x3", [DM, TPC], F32, kind="ExternalInput").ap()
    W1T = nc.dram_tensor("W1T", [DM, DFF1], F32, kind="ExternalInput").ap()
    W3T = nc.dram_tensor("W3T", [DM, DFF1], F32, kind="ExternalInput").ap()
    W2T = nc.dram_tensor("W2T", [DFF1, DM], F32, kind="ExternalInput").ap()
    modc = nc.dram_tensor("modc", [128, 2 * 6 * NDC], F32, kind="ExternalInput").ap()
    h128d = nc.dram_tensor("h128", [128, 128], BF16, kind="ExternalInput").ap()
    hi2d = nc.dram_tensor("hi2", [128, 128], BF16, kind="ExternalInput").ap()
    x4d = nc.dram_tensor("x4", [DM, TPC], F32, kind="ExternalOutput").ap()

    NF1 = DFF1 // 128
    with ExitStack() as ctx:
        b = Bufs(nc, ctx)
        x_sb = b.sb("x_sb", [128, NDC * TPC])
        h_sb = b.sb("h_sb", [128, NDC * TPC], BF16)
        hid = b.sb("hid", [128, NF1 * 512], BF16)
        mcol = b.sb("mcol", [128, 2 * 6 * NDC])
        ca1 = b.sb("ca1", [128, NDC])
        gbc = b.sb("gbc", [128, NDC])
        sgn = b.sb("sgn", [128, NDC])
        ones_col = b.sb("ones_col", [128, 1])
        ones_row = b.sb("ones_row", [1, 128])
        wbuf1 = [b.sb(f"wbuf1_{i}", [128, NDC * 128], BF16) for i in range(2)]
        wbuf2 = [b.sb(f"wbuf2_{i}", [128, NDC * 128], BF16) for i in range(2)]
        wbuf3 = [b.sb(f"wbuf3_{i}", [128, NF1 * 64], BF16) for i in range(2)]
        silu_s = b.sb("silu_s", [128, 512])
        h128c = b.sb("h128c", [128, 128], BF16)
        hi2c = b.sb("hi2c", [128, 128], BF16)
        identb = b.sb("identb", [128, 128], BF16)
        tr1 = [b.sb(f"tr1_{i}", [128, 128], BF16) for i in range(4)]
        tr2 = [b.sb(f"tr2_{i}", [128, 128], BF16) for i in range(4)]
        p_a = [b.ps(f"p_a{i}", [128, 512]) for i in range(2)]
        p_b2 = [b.ps(f"p_b2{i}", [128, 512]) for i in range(2)]
        p_o = b.ps("p_o", [128, 512])
        p_tb = b.ps("p_tbank", [128, 1024], BF16)
        p_t = [p_tb[:, i * 128:(i + 1) * 128] for i in range(8)]
        p_tmb = b.ps("p_tmbank", [128, 512])
        p_tm = [p_tmb[:, i * 128:(i + 1) * 128] for i in range(4)]
        lb = LNBufs(b, "f2", p_b=p_o[:, :], p_sum=p_a[0][0:1, 0:512],
                    p_sq=p_b2[0][0:1, 0:512],
                    mean_b=wbuf3[0][:, 0:1024].bitcast(F32),
                    rstd_b=wbuf3[0][:, 1024:2048].bitcast(F32),
                    t1=wbuf3[1][:, 0:1024].bitcast(F32))
        sems = mk_sems(b)
        block = b.block()
        seq = Seq()

        e_memset(seq, ones_col[:, :], 1.0)
        e_memset(seq, ones_row[:, :], 1.0)
        e_identity(seq, identb)
        seq.dma("sync", mcol[:, :], modc)
        seq.dma("sync", h128c[:, :], h128d)
        seq.dma("sync", hi2c[:, :], hi2d)
        for c in range(NDC):
            seq.dma("sync", x_sb[:, c * TPC:(c + 1) * TPC],
                    x3d[c * 128:(c + 1) * 128, :])
        e_ts(seq, ca1[:, :], mcol[:, 6 * NDC + 4 * NDC: 6 * NDC + 5 * NDC],
             1.0, None, ALU.add)
        e_clamp_gate(seq, gbc[:, :], mcol[:, 6 * NDC + 5 * NDC: 6 * NDC + 6 * NDC],
                     sgn[:, :])
        emit_ln_mod(seq, lb, x_sb, h_sb, ones_col, ones_row,
                    ca1, mcol[:, 6 * NDC + 3 * NDC: 6 * NDC + 4 * NDC])

        for t0 in range(0, TPC, 512):
            for fi in range(NF1):
                f0 = fi * 128
                pr = fi % 2
                w1c = wbuf1[pr]
                w2c = wbuf2[pr]
                dma_wslice_fm(seq, w1c, W1T, DM, f0, 128)
                dma_wslice_fm(seq, w2c, W3T, DM, f0, 128)
                for c in range(NDC):
                    e_mm(seq, p_a[pr][:, :], w1c[:, c * 128:(c + 1) * 128],
                         h_sb[:, c * TPC + t0: c * TPC + t0 + 512],
                         start=(c == 0), stop=(c == NDC - 1))
                for c in range(NDC):
                    e_mm(seq, p_b2[pr][:, :], w2c[:, c * 128:(c + 1) * 128],
                         h_sb[:, c * TPC + t0: c * TPC + t0 + 512],
                         start=(c == 0), stop=(c == NDC - 1))
                e_act(seq, silu_s[:, :], p_a[pr][:, :], AF.Silu)
                e_tt(seq, hid[:, fi * 512:(fi + 1) * 512], silu_s[:, :],
                     p_b2[pr][:, :], ALU.mult)
            for fc in range(NF1):
                pr = fc % 2
                e_mm(seq, p_a[pr][:, :], h128c[:, :],
                     hid[:, fc * 512:(fc + 1) * 512])
                e_copy(seq, hid[:, fc * 512:(fc + 1) * 512], p_a[pr][:, :])
            for tp in range(256):
                pr = tp % 4
                tview = hid[:, :].rearrange("p (c t) -> p c t", t=512)[
                    :, :, 2 * tp: 2 * tp + 2]
                e_copy(seq, tr1[pr][:, :].rearrange("p (c t) -> p c t", t=2),
                       tview)
                e_tr(seq, p_t[(2 * tp) % 8], tr1[pr][:, :], identb[:, :])
                e_copy(seq, tr1[pr][:, :], p_t[(2 * tp) % 8])
                e_mm(seq, p_tm[pr], hi2c[:, :], tr1[pr][:, :])
                e_ts(seq, tr2[pr][:, :], p_tm[pr],
                     float(1.0 / math.sqrt(DFF1)), None, ALU.mult)
                e_tr(seq, p_t[(2 * tp + 1) % 8], tr2[pr][:, :], identb[:, :])
                e_copy(seq, tview, p_t[(2 * tp + 1) % 8])
            for dc in range(NDC):
                for hh in range(2):
                    w3c = wbuf3[(dc * 2 + hh) % 2]
                    src3 = W2T.rearrange("(c p) f -> p c f", p=128)[
                        :, hh * 32:(hh + 1) * 32, dc * 128:(dc + 1) * 128]
                    dst3 = w3c[:, :].rearrange("p (c f) -> p c f", f=128)
                    seq.dma("gpsimd", dst3, src3)
                    for fl in range(32):
                        fc = hh * 32 + fl
                        e_mm(seq, p_o[:, :], w3c[:, fl * 128:(fl + 1) * 128],
                             hid[:, fc * 512:(fc + 1) * 512],
                             start=(fc == 0), stop=(fc == NF1 - 1))
                xc = x_sb[:, dc * TPC + t0: dc * TPC + t0 + 512]
                e_stt(seq, xc, p_o[:, :], gbc[:, dc:dc + 1], xc, ALU.mult, ALU.add)
        for c in range(NDC):
            seq.dma("sync", x4d[c * 128:(c + 1) * 128, :],
                    x_sb[:, c * TPC:(c + 1) * TPC])
        seq.install(nc, sems, block)
    return nc



def build_launch_C():
    nc = new_nc()
    x0d = nc.dram_tensor("x0", [DM, TPC], F32, kind="ExternalInput").ap()
    xhd = nc.dram_tensor("xh", [DM, 4], F32, kind="ExternalInput").ap()
    wipT = nc.dram_tensor("wipT", [DM, PROJ_DIM], F32, kind="ExternalInput").ap()
    cwd = nc.dram_tensor("convw", [128, 34 * KCONV], F32, kind="ExternalInput").ap()
    cbd = nc.dram_tensor("convb", [128, 34], F32, kind="ExternalInput").ap()
    dtbd = nc.dram_tensor("dtb", [NHEAD, 1], F32, kind="ExternalInput").ap()
    alogd = nc.dram_tensor("alog", [NHEAD, 1], F32, kind="ExternalInput").ap()
    drowd = nc.dram_tensor("drow", [1, NHEAD], F32, kind="ExternalInput").ap()
    modc = nc.dram_tensor("modc", [128, 2 * 6 * NDC], F32, kind="ExternalInput").ap()
    hmaskd = nc.dram_tensor("hmask", [128, 1], F32, kind="ExternalInput").ap()
    gate_d = nc.dram_tensor("gate", [TPC, INTER], BF16, kind="ExternalOutput").ap()
    yp_d = nc.dram_tensor("yp", [TPC, INTER], F32, kind="ExternalOutput").ap()
    st_d = nc.dram_tensor("cstates", [NCHUNK, NHEAD, ND, PD], BF16,
                          kind="ExternalOutput").ap()
    sum_d = nc.dram_tensor("csum", [NHEAD, ND, PD], BF16, kind="ExternalOutput").ap()
    lt_d = nc.dram_tensor("clt", [NHEAD, 1], F32, kind="ExternalOutput").ap()
    acum_d = nc.dram_tensor("acum", [NHEAD, TPC], F32, kind="ExternalOutput").ap()
    cfm_d = nc.dram_tensor("cfm", [ND, TPC], BF16, kind="ExternalOutput").ap()

    NHB = CONV_DIM // 128
    with ExitStack() as ctx:
        b = Bufs(nc, ctx)
        big0 = b.sb("big0", [128, NDC * TPC])
        h_sb = b.sb("h_sb", [128, NDC * TPC], BF16)
        arena = b.sb("arena", [128, 32 * TPC], BF16)
        hh_sb = b.sb("hh_sb", [128, 4 * NDC], BF16)
        xh_sb = b.sb("xh_sb", [128, 4 * NDC])
        B_fm = b.sb("B_fm", [128, TPC], BF16)
        C_fm = b.sb("C_fm", [128, TPC], BF16)
        B_tm = b.sb("B_tm", [128, 8 * ND], BF16)
        dtr = b.sb("dtr", [NHEAD, TPC])
        acum = b.sb("acum_sb", [NHEAD, TPC])
        hbc = b.sb("hbc", [128, TPC + 4])
        dtT = b.sb("dtT", [128, 8 * NHEAD])
        decT = b.sb("decT", [128, 8 * NHEAD])
        nacT = b.sb("nacT", [128, 8 * NHEAD])
        ltc = b.sb("ltc", [NHEAD, 1])
        cvw = b.sb("cvw", [128, 34 * KCONV])
        cvb = b.sb("cvb", [128, 34])
        hmask = b.sb("hmaskc", [128, 1])
        dcol = b.sb("dcolc", [NHEAD, 1])
        acol = b.sb("acolc", [NHEAD, 1])
        mcol = b.sb("mcolc", [128, 2 * 6 * NDC])
        ca1 = b.sb("ca1c", [128, NDC])
        wbuf = b.sb("wbufc", [128, NDC * 128], BF16)

        ones_col = b.sb("ones_colc", [128, 1])
        ones_row = b.sb("ones_rowc", [1, 128])
        identf = b.sb("identfc", [128, 128])
        identb2 = b.sb("identbc", [128, 128], BF16)
        lb = LNBufs(b, "c")
        p_g = b.ps("p_gc", [128, 512])
        p_misc = b.ps("p_miscc", [128, 512])
        p_y = b.ps("p_yc", [128, 128])
        p_st = b.ps("p_stc", [128, PD])
        p_tb = b.ps("p_tbc", [128, 128], BF16)
        sems = mk_sems(b)
        block = b.block()
        seq = Seq()

        zrow = lb.msq_row[0:1, 0:CH]
        stg = lb.t1[:, :].bitcast(BF16)[:, 0:512]
        ar_f = arena[:, :].bitcast(F32)
        ytile = [ar_f[:, 0:2048], ar_f[:, 2048:4096]]
        ar_b = arena[:, :]
        run = ar_b[:, 16384:16384 + NHEAD * PD]
        ststg = ar_b[:, 20480:20480 + 16 * PD]
        bdec = ar_b[:, 21504:21504 + ND]
        mt_sb = ar_b[:, 21632:21632 + 2 * CH]
        lt_sb = ar_b[:, 22144:22144 + CH]
        gt_sb = ar_f[:, 11200:11200 + 2 * CH]
        dchb = ar_f[:, 11712:11712 + NHEAD]
        Db = ar_f[:, 11776:11776 + NHEAD]
        adt = hbc[0:NHEAD, 0:TPC]
        t64 = b.sb("t64c", [NHEAD, CH])
        zer = b.sb("zerc", [NHEAD, CH])
        hs_tm = big0[:, :].bitcast(BF16)

        e_memset(seq, ones_col[:, :], 1.0)
        e_memset(seq, ones_row[:, :], 1.0)
        e_identity(seq, identf)
        e_identity(seq, identb2)
        seq.dma("sync", mcol[:, :], modc)
        seq.dma("sync", cvw[:, :], cwd)
        seq.dma("sync", cvb[:, :], cbd)
        seq.dma("sync", hmask[:, :], hmaskd)
        seq.dma("sync", dcol[:, :], dtbd)
        seq.dma("sync", acol[:, :], alogd)
        seq.dma("sync", xh_sb[:, :].rearrange("p (c t) -> p c t", t=4),
                xhd.rearrange("(c p) t -> p c t", p=128))
        for c in range(NDC):
            seq.dma("sync", big0[:, c * TPC:(c + 1) * TPC],
                    x0d[c * 128:(c + 1) * 128, :])
        e_ts(seq, ca1[:, :], mcol[:, NDC:2 * NDC], 1.0, None, ALU.add)
        emit_ln_mod(seq, lb, big0, h_sb, ones_col, ones_row, ca1, mcol[:, 0:NDC])
        emit_ln_mod(seq, lb, xh_sb, hh_sb, ones_col, ones_row, ca1,
                    mcol[:, 0:NDC], toks=4)

        wbufg = ar_b[:, 0:NDC * 512]
        for f0 in range(0, INTER, 512):
            src = wipT.rearrange("(c p) f -> p c f", p=128)[:, :, f0:f0 + 512]
            seq.dma("gpsimd", wbufg.rearrange("p (c f) -> p c f", f=512), src)
            for t0 in range(0, TPC, 128):
                for c in range(NDC):
                    e_mm(seq, p_g[:, :], h_sb[:, c * TPC + t0: c * TPC + t0 + 128],
                         wbufg[:, c * 512:(c + 1) * 512],
                         start=(c == 0), stop=(c == NDC - 1))
                e_copy(seq, stg[:, :], p_g[:, :])
                seq.dma("sync", gate_d[t0:t0 + 128, f0:f0 + 512], stg[:, :])

        for fi in range(NHB):
            f0 = INTER + fi * 128
            dma_wslice_fm(seq, wbuf, wipT, DM, f0, 128)
            for c in range(NDC):
                e_mm(seq, p_misc[:, 0:4], wbuf[:, c * 128:(c + 1) * 128],
                     hh_sb[:, :].rearrange("p (c t) -> p c t", t=4)[:, c, :],
                     start=(c == 0), stop=(c == NDC - 1))
            e_ts(seq, hbc[:, 0:4], p_misc[:, 0:4], hmask[:, 0:1], None, ALU.mult)
            for t0 in range(0, TPC, 512):
                for c in range(NDC):
                    e_mm(seq, p_g[:, :], wbuf[:, c * 128:(c + 1) * 128],
                         h_sb[:, c * TPC + t0: c * TPC + t0 + 512],
                         start=(c == 0), stop=(c == NDC - 1))
                e_copy(seq, hbc[:, 3 + t0: 3 + t0 + 512], p_g[:, :])
            acc = lb.mean_b
            for t0 in range(0, TPC, 512):
                e_ts(seq, acc[:, :], hbc[:, t0:t0 + 512],
                     cvw[:, fi * KCONV: fi * KCONV + 1], None, ALU.mult)
                for k in range(1, KCONV):
                    e_stt(seq, acc[:, :], hbc[:, t0 + k: t0 + k + 512],
                          cvw[:, fi * KCONV + k: fi * KCONV + k + 1],
                          acc[:, :], ALU.mult, ALU.add)
                e_ts(seq, acc[:, :], acc[:, :], cvb[:, fi:fi + 1], None, ALU.add)
                dst = (arena[:, fi * TPC + t0: fi * TPC + t0 + 512] if fi < 32
                       else (B_fm if fi == 32 else C_fm)[:, t0:t0 + 512])
                e_act(seq, dst, acc[:, :], AF.Silu)
        seq.dma("sync", cfm_d, C_fm[:, :])

        dma_wslice_fm(seq, wbuf, wipT, DM, INTER + CONV_DIM, 64)
        for t0 in range(0, TPC, 512):
            for c in range(NDC):
                e_mm(seq, p_g[0:NHEAD, :], wbuf[:, c * 64:(c + 1) * 64],
                     h_sb[:, c * TPC + t0: c * TPC + t0 + 512],
                     start=(c == 0), stop=(c == NDC - 1))
            e_copy(seq, dtr[:, t0:t0 + 512], p_g[0:NHEAD, :])
        e_ts(seq, dtr[:, :], dtr[:, :], dcol[:, :], None, ALU.add)
        e_act(seq, adt, dtr[:, :], AF.Abs)
        e_act(seq, adt, adt, AF.Exp, scale=-1.0)
        e_ts(seq, adt, adt, 1.0, None, ALU.add)
        e_act(seq, adt, adt, AF.Ln)
        e_ts(seq, dtr[:, :], dtr[:, :], 0.0, None, ALU.max)
        e_tt(seq, dtr[:, :], dtr[:, :], adt, ALU.add)
        e_ts(seq, dtr[:, :], dtr[:, :], DT_MIN, DT_MAX, ALU.max, ALU.min)
        e_act(seq, acol[:, :], acol[:, :], AF.Exp)
        e_ts(seq, acol[:, :], acol[:, :], -1.0, None, ALU.mult)
        e_ts(seq, adt, dtr[:, :], acol[:, :], None, ALU.mult)
        e_memset(seq, zer[:, :], 0.0)
        for c in range(NCHUNK):
            seq.emit("vector", lambda e, c=c: e.tensor_tensor_scan(
                acum[:, c * CH:(c + 1) * CH], adt[:, c * CH:(c + 1) * CH],
                zer[:, :], 0.0, ALU.add, ALU.add),
                reads=(adt[:, c * CH:(c + 1) * CH], zer[:, :]),
                writes=(acum[:, c * CH:(c + 1) * CH],))
        seq.dma("sync", acum_d, acum[:, :])
        e_copy(seq, ltc[:, :], acum[:, CH - 1:CH])
        for c in range(1, NCHUNK):
            e_tt(seq, ltc[:, :], ltc[:, :],
                 acum[:, (c + 1) * CH - 1:(c + 1) * CH], ALU.add)
        seq.dma("sync", lt_d, ltc[:, :])

        for c in range(NCHUNK):
            e_act(seq, t64[:, :], acum[:, c * CH:(c + 1) * CH], AF.Exp,
                  bias=acum[:, (c + 1) * CH - 1:(c + 1) * CH], scale=-1.0)
            for hf in range(2):
                col = (2 * c + hf) * NHEAD
                e_tr(seq, p_misc[:, 0:NHEAD],
                     dtr[:, c * CH + hf * 128: c * CH + (hf + 1) * 128],
                     identf[0:NHEAD, 0:NHEAD])
                e_copy(seq, dtT[:, col:col + NHEAD], p_misc[:, 0:NHEAD])
                e_tr(seq, p_misc[:, 0:NHEAD], t64[:, hf * 128:(hf + 1) * 128],
                     identf[0:NHEAD, 0:NHEAD])
                e_copy(seq, decT[:, col:col + NHEAD], p_misc[:, 0:NHEAD])
                e_tr(seq, p_misc[:, 0:NHEAD],
                     acum[:, c * CH + hf * 128: c * CH + (hf + 1) * 128],
                     identf[0:NHEAD, 0:NHEAD])
                e_ts(seq, nacT[:, col:col + NHEAD], p_misc[:, 0:NHEAD],
                     -1.0, None, ALU.mult)

        identb = identb2[:, :]
        pt_b = p_tb[:, :]
        for tc in range(8):
            for fc in range(32):
                e_tr(seq, pt_b,
                     arena[:, fc * TPC + tc * 128: fc * TPC + (tc + 1) * 128],
                     identb)
                e_copy(seq, hs_tm[:, tc * 4096 + fc * 128:
                                  tc * 4096 + (fc + 1) * 128], pt_b)
            e_tr(seq, pt_b, B_fm[:, tc * 128:(tc + 1) * 128], identb)
            e_copy(seq, B_tm[:, tc * ND:(tc + 1) * ND], pt_b)

        seq.dma("sync", zrow[:, 0:NHEAD], drowd)
        e_mm(seq, p_misc[:, 0:NHEAD], ones_row[:, :], zrow[:, 0:NHEAD])
        e_copy(seq, Db, p_misc[:, 0:NHEAD])

        e_memset(seq, run, 0.0)
        for c in range(NCHUNK):
            for lh in range(2):
                e_mm(seq, p_misc[:, 0:CH],
                     B_fm[:, c * CH + lh * 128: c * CH + (lh + 1) * 128],
                     C_fm[:, c * CH:(c + 1) * CH])
                e_copy(seq, gt_sb[:, lh * CH:(lh + 1) * CH], p_misc[:, 0:CH])
            e_act(seq, ltc[:, :], acum[:, (c + 1) * CH - 1:(c + 1) * CH], AF.Exp)
            e_tr(seq, p_misc[0:1, 0:NHEAD], ltc[:, :],
                 identf[0:NHEAD, 0:NHEAD])
            e_copy(seq, zrow[:, 0:NHEAD], p_misc[0:1, 0:NHEAD])
            e_mm(seq, p_misc[:, 0:NHEAD], ones_row[:, :], zrow[:, 0:NHEAD])
            e_copy(seq, dchb, p_misc[:, 0:NHEAD])
            for h in range(NHEAD):
                for zf in range(2):
                    col = (2 * c + zf) * NHEAD + h
                    e_tr(seq, p_g[0:1, 0:128], nacT[:, col:col + 1],
                         identf[:, :])
                    e_ts(seq, zrow[:, zf * 128:(zf + 1) * 128], p_g[0:1, 0:128],
                         -1.0, None, ALU.mult)
                e_mm(seq, p_misc[:, 0:CH], ones_row[:, :], zrow[:, 0:CH])
                for lh in range(2):
                    col = (2 * c + lh) * NHEAD + h
                    e_act(seq, lt_sb, p_misc[:, 0:CH], AF.Exp,
                          bias=nacT[:, col:col + 1])
                    seq.emit("gpsimd", lambda e, lh=lh: e.affine_select(
                        out=lt_sb, in_=lt_sb, compare_op=ALU.is_ge,
                        fill=0.0, base=-(lh * 128), pattern=[[1, CH]],
                        channel_multiplier=-1),
                        reads=(lt_sb,), writes=(lt_sb,))
                    e_stt(seq, mt_sb[:, lh * CH:(lh + 1) * CH],
                          gt_sb[:, lh * CH:(lh + 1) * CH], dtT[:, col:col + 1],
                          lt_sb, ALU.mult, ALU.mult)
                hs0 = hs_tm[:, (2 * c) * 4096 + h * PD: (2 * c) * 4096 + h * PD + PD]
                hs1 = hs_tm[:, (2 * c + 1) * 4096 + h * PD:
                            (2 * c + 1) * 4096 + h * PD + PD]
                e_mm(seq, p_y[:, 0:PD], mt_sb[:, 0:128], hs0)
                e_mm(seq, p_y[:, PD:2 * PD], mt_sb[:, 128:CH], hs0, stop=False)
                e_mm(seq, p_y[:, PD:2 * PD], mt_sb[:, CH + 128:2 * CH], hs1,
                     start=False)
                for lh in range(2):
                    col = (2 * c + lh) * NHEAD + h
                    e_ts(seq, bdec, B_tm[:, (2 * c + lh) * ND:(2 * c + lh + 1) * ND],
                         decT[:, col:col + 1], dtT[:, col:col + 1],
                         ALU.mult, ALU.mult)
                    e_mm(seq, p_st[:, :], bdec, (hs0 if lh == 0 else hs1),
                         start=(lh == 0), stop=(lh == 1))
                hq = h % 16
                e_copy(seq, ststg[:, hq * PD:(hq + 1) * PD], p_st[:, :])
                e_stt(seq, run[:, h * PD:(h + 1) * PD], run[:, h * PD:(h + 1) * PD],
                      dchb[:, h:h + 1], ststg[:, hq * PD:(hq + 1) * PD],
                      ALU.mult, ALU.add)
                if hq == 15:
                    seq.dma("sync",
                            st_d[c, h - 15:h + 1].transpose([1, 0, 2]),
                            ststg[:, :].rearrange("n (h p) -> n h p", p=PD))
                hw = h % 32
                for zh in range(2):
                    e_stt(seq, ytile[zh][:, hw * PD:(hw + 1) * PD],
                          (hs0 if zh == 0 else hs1), Db[:, h:h + 1],
                          p_y[:, zh * PD:(zh + 1) * PD], ALU.mult, ALU.add)
                if hw == 31:
                    g0 = h - 31
                    for zh in range(2):
                        seq.dma("sync",
                                yp_d[c * CH + zh * 128: c * CH + (zh + 1) * 128,
                                     g0 * PD:(g0 + 32) * PD], ytile[zh])
        for g in range(4):
            seq.dma("sync",
                    sum_d[g * 16:(g + 1) * 16].transpose([1, 0, 2]),
                    run[:, g * 1024:(g + 1) * 1024].rearrange(
                        "n (h p) -> n h p", p=PD))
        seq.install(nc, sems, block)
    return nc



def build_launch_D():
    nc = new_nc()
    x0d = nc.dram_tensor("x0", [DM, TPC], F32, kind="ExternalInput").ap()
    ypd = nc.dram_tensor("yp", [TPC, INTER], F32, kind="ExternalInput").ap()
    gated = nc.dram_tensor("gate", [TPC, INTER], BF16, kind="ExternalInput").ap()
    cfmd = nc.dram_tensor("cfm", [ND, TPC], BF16, kind="ExternalInput").ap()
    acumd = nc.dram_tensor("acum", [NHEAD, TPC], F32, kind="ExternalInput").ap()
    cstd = nc.dram_tensor("cstates", [NCHUNK, NHEAD, ND, PD], BF16,
                          kind="ExternalInput").ap()
    csall = nc.dram_tensor("csumall", [NCORE, NHEAD, ND, PD], BF16,
                           kind="ExternalInput").ap()
    cltall = nc.dram_tensor("cltall", [NCORE, NHEAD], F32,
                            kind="ExternalInput").ap()
    miTd = nc.dram_tensor("miT", [NCORE, NCORE], F32, kind="ExternalInput").ap()
    incd = nc.dram_tensor("incm", [NCORE, 1], F32, kind="ExternalInput").ap()
    wopT = nc.dram_tensor("wopT", [INTER, DM], F32, kind="ExternalInput").ap()
    nwT = nc.dram_tensor("nwT", [128, 32], F32, kind="ExternalInput").ap()
    modc = nc.dram_tensor("modc", [128, 2 * 6 * NDC], F32, kind="ExternalInput").ap()
    x1d = nc.dram_tensor("x1", [DM, TPC], F32, kind="ExternalOutput").ap()
    coef_bounce = nc.dram_tensor("coef_bounce", [NCORE, NHEAD], F32)

    with ExitStack() as ctx:
        b = Bufs(nc, ctx)
        run = b.sb("run_d", [128, NHEAD * PD], BF16)
        sgbuf = b.sb("sgbuf", [128, NHEAD * PD], BF16)
        ybuf = b.sb("ybuf", [128, INTER])
        gbuf = b.sb("gbuf", [128, INTER], BF16)
        yn_tm = b.sb("yn_tm", [128, INTER], BF16)
        yn_fm = b.sb("yn_fm", [128, 32 * TPC], BF16)
        cfm = b.sb("cfm_sb", [128, TPC], BF16)
        acum = b.sb("acum_sbd", [NHEAD, TPC])
        sdoT = b.sb("sdoT", [128, 8 * NHEAD])
        x0c = b.sb("x0c", [128, TPC])
        wbuf = b.sb("wbufd", [128, 32 * 128], BF16)
        ltall = b.sb("ltall", [NCORE, NHEAD])
        miT = b.sb("miT_sb", [NCORE, NCORE])
        incm = b.sb("incm_sb", [NCORE, 1])
        coef = b.sb("coef_sb", [NCORE, NHEAD])
        coefrow = b.sb("coefrow", [1, NCORE * NHEAD])
        cbj = b.sb("cbj", [128, NHEAD], BF16)
        dchb = b.sb("dchb_d", [128, NHEAD], BF16)
        ltc = b.sb("ltc_d", [NHEAD, 1])
        zrow = b.sb("zrow_d", [1, 128])
        nwc = b.sb("nwc", [128, 32])
        mcol = b.sb("mcold", [128, 2 * 6 * NDC])
        gac = b.sb("gacd", [128, NDC])
        sgn = b.sb("sgnd", [128, NDC])
        sq = b.sb("sq_d", [128, 512])
        ss = b.sb("ss_d", [128, 1])
        sst = b.sb("sst_d", [128, 1])
        ones_row = b.sb("ones_rowd", [1, 128])
        identf = b.sb("identfd", [128, 128])
        identb2 = b.sb("identbd", [128, 128], BF16)
        stg = b.sb("stg_d", [128, 512])
        p_yo = b.ps("p_yod", [128, 128])
        p_b = b.ps("p_bd", [128, 512])
        p_o = b.ps("p_od", [128, 512])
        p_tb = b.ps("p_tbd", [128, 128], BF16)
        sems = mk_sems(b)
        block = b.block()
        seq = Seq()

        e_memset(seq, ones_row[:, :], 1.0)
        e_identity(seq, identf)
        e_identity(seq, identb2)
        seq.dma("sync", mcol[:, :], modc)
        seq.dma("sync", nwc[:, :], nwT)
        seq.dma("sync", ltall[:, :], cltall)
        seq.dma("sync", miT[:, :], miTd)
        seq.dma("sync", incm[:, :], incd)
        seq.dma("sync", cfm[:, :], cfmd)
        seq.dma("sync", acum[:, :], acumd)
        e_clamp_gate(seq, gac[:, :], mcol[:, 2 * NDC:3 * NDC], sgn[:, :])

        e_mm(seq, p_b[0:NCORE, 0:NHEAD], miT[:, :], ltall[:, :])
        e_act(seq, coef[:, :], p_b[0:NCORE, 0:NHEAD], AF.Exp)
        e_ts(seq, coef[:, :], coef[:, :], incm[:, 0:1], None, ALU.mult)
        seq.dma("sync", coef_bounce.ap(), coef[:, :])
        seq.dma("sync", coefrow[:, :],
                coef_bounce.ap().rearrange("a b -> (a b)")[None, :])

        e_memset(seq, run[:, :], 0.0)
        for j in range(NCORE):
            seq.dma("sync", sgbuf[:, :].rearrange("n (h p) -> n h p", p=PD),
                    csall[j].transpose([1, 0, 2]))
            e_mm(seq, p_b[:, 0:NHEAD], ones_row[:, :],
                 coefrow[:, j * NHEAD:(j + 1) * NHEAD])
            e_copy(seq, cbj[:, :], p_b[:, 0:NHEAD])
            for h in range(NHEAD):
                e_stt(seq, run[:, h * PD:(h + 1) * PD],
                      sgbuf[:, h * PD:(h + 1) * PD], cbj[:, h:h + 1],
                      run[:, h * PD:(h + 1) * PD], ALU.mult, ALU.add)

        for c in range(NCHUNK):
            for hf in range(2):
                col = (2 * c + hf) * NHEAD
                e_tr(seq, p_yo[:, 0:NHEAD],
                     acum[:, c * CH + hf * 128: c * CH + (hf + 1) * 128],
                     identf[0:NHEAD, 0:NHEAD])
                e_act(seq, sdoT[:, col:col + NHEAD], p_yo[:, 0:NHEAD], AF.Exp)

        for c in range(NCHUNK):
            for lh in range(2):
                tok0 = c * CH + lh * 128
                seq.dma("sync", ybuf[:, :], ypd[tok0:tok0 + 128, :])
                seq.dma("sync", gbuf[:, :], gated[tok0:tok0 + 128, :])
                for hb in range(NHEAD // 8):
                    e_mm(seq, p_b[:, :], cfm[:, tok0:tok0 + 128],
                         run[:, hb * 512:(hb + 1) * 512])
                    for hl in range(8):
                        h = hb * 8 + hl
                        e_stt(seq, ybuf[:, h * PD:(h + 1) * PD],
                              p_b[:, hl * PD:(hl + 1) * PD],
                              sdoT[:, (2 * c + lh) * NHEAD + h:
                                   (2 * c + lh) * NHEAD + h + 1],
                              ybuf[:, h * PD:(h + 1) * PD], ALU.mult, ALU.add)
                for s0 in range(0, INTER, 512):
                    e_act(seq, sq[:, :], gbuf[:, s0:s0 + 512], AF.Silu)
                    e_tt(seq, ybuf[:, s0:s0 + 512], ybuf[:, s0:s0 + 512],
                         sq[:, :], ALU.mult)
                    e_act(seq, sq[:, :], ybuf[:, s0:s0 + 512], AF.Square,
                          accum_out=sst[:, :])
                    if s0 == 0:
                        e_copy(seq, ss[:, :], sst[:, :])
                    else:
                        e_tt(seq, ss[:, :], ss[:, :], sst[:, :], ALU.add)
                e_ts(seq, ss[:, :], ss[:, :], 1.0 / INTER, MEPS, ALU.mult, ALU.add)
                e_act(seq, ss[:, :], ss[:, :], AF.Sqrt)
                e_recip(seq, ss[:, :], ss[:, :])
                for s0 in range(0, INTER, 512):
                    e_ts(seq, yn_tm[:, s0:s0 + 512], ybuf[:, s0:s0 + 512],
                         ss[:, 0:1], None, ALU.mult)
                for fc in range(32):
                    e_tr(seq, p_tb[:, :], yn_tm[:, fc * 128:(fc + 1) * 128],
                         identb2[:, :])
                    e_ts(seq, yn_fm[:, fc * TPC + tok0: fc * TPC + tok0 + 128],
                         p_tb[:, :], nwc[:, fc:fc + 1], None, ALU.mult)
            e_act(seq, ltc[:, :], acum[:, (c + 1) * CH - 1:(c + 1) * CH], AF.Exp)
            e_tr(seq, p_yo[0:1, 0:NHEAD], ltc[:, :],
                 identf[0:NHEAD, 0:NHEAD])
            e_copy(seq, zrow[:, 0:NHEAD], p_yo[0:1, 0:NHEAD])
            e_mm(seq, p_b[:, 0:NHEAD], ones_row[:, :], zrow[:, 0:NHEAD])
            e_copy(seq, dchb[:, :], p_b[:, 0:NHEAD])
            seq.dma("sync", sgbuf[:, :].rearrange("n (h p) -> n h p", p=PD),
                    cstd[c].transpose([1, 0, 2]))
            for h in range(NHEAD):
                e_stt(seq, run[:, h * PD:(h + 1) * PD],
                      run[:, h * PD:(h + 1) * PD], dchb[:, h:h + 1],
                      sgbuf[:, h * PD:(h + 1) * PD], ALU.mult, ALU.add)

        def op_consume(d0, t0, tw, psum):
            dc = d0 // 128
            if t0 == 0:
                for tt in range(0, TPC, 512):
                    seq.dma("sync", x0c[:, tt:tt + 512],
                            x0d[dc * 128:(dc + 1) * 128, tt:tt + 512])
            e_stt(seq, stg[:, :tw], psum, gac[:, dc:dc + 1], x0c[:, t0:t0 + tw],
                  ALU.mult, ALU.add)
            seq.dma("sync", x1d[dc * 128:(dc + 1) * 128, t0:t0 + tw], stg[:, :tw])

        NIC = INTER // 128
        for d0 in range(0, DM, 128):
            dma_wslice_fm(seq, wbuf, wopT, INTER, d0, 128)
            for t0 in range(0, TPC, 512):
                for c in range(NIC):
                    e_mm(seq, p_o[:, :], wbuf[:, c * 128:(c + 1) * 128],
                         yn_fm[:, c * TPC + t0: c * TPC + t0 + 512],
                         start=(c == 0), stop=(c == NIC - 1))
                op_consume(d0, t0, 512, p_o[:, :])
        seq.install(nc, sems, block)
    return nc



_PROGS = {}


def _prog(name, builder):
    if name not in _PROGS:
        _PROGS[name] = builder()
    return _PROGS[name]


def _hadamard(n):
    H = np.array([[1.0]], np.float32)
    while H.shape[0] < n:
        H = np.block([[H, H], [H, -H]])
    return H


def kernel(**inputs):
    f32 = np.float32
    CORES = list(range(NCORE))
    ca = np.ascontiguousarray

    x_all = ca(np.asarray(inputs["L0_attn_input_states"], f32)[0])
    w1 = np.asarray(inputs["L0_ffn_weight1"])
    w2 = np.asarray(inputs["L0_ffn_weight2"])
    w3 = np.asarray(inputs["L0_ffn_weight3"])

    inA = []
    for core in CORES:
        m = {}
        for L in range(2):
            cvec = np.asarray(inputs[f"L{L}_c"], f32).reshape(16, 128)
            m[f"c{L}T"] = ca(cvec.T)
            wm = np.asarray(inputs[f"L{L}_wmod"], f32)
            m[f"wmod{L}s"] = ca(wm[:, core * 1536:(core + 1) * 1536])
        m["w1s"] = ca(w1[core * W1_ROWS:(core + 1) * W1_ROWS])
        m["w2s"] = ca(w2[core * W1_ROWS:(core + 1) * W1_ROWS])
        m["w3s"] = ca(w3[core * W3_ROWS:(core + 1) * W3_ROWS])
        inA.append(m)
    resA = run_bass_kernel_spmd(_prog("A", build_launch_A), inA, core_ids=CORES)
    mod_full = np.zeros((2, 12288), f32)
    for c in CORES:
        mod_full[:, c * 1536:(c + 1) * 1536] = resA.results[c]["modrow"]
    mc = np.zeros((128, 192), f32)
    for L in range(2):
        mc[:, L * 96:(L + 1) * 96] = (
            mod_full[L].reshape(6, 16, 128).transpose(2, 0, 1).reshape(128, 96))
    ssums = ca(np.stack([resA.results[c]["ssum"][0] for c in CORES]))

    inB = [dict(w1s=inA[c]["w1s"], w2s=inA[c]["w2s"], w3s=inA[c]["w3s"],
                allsum=ssums) for c in CORES]
    resB = run_bass_kernel_spmd(_prog("B", build_launch_B), inB, core_ids=CORES)
    w1pT = ca(np.concatenate([resB.results[c]["wq1"] for c in CORES], axis=0).T)
    w2pT = ca(np.concatenate([resB.results[c]["wq2"] for c in CORES], axis=0).T)
    w3pT = ca(np.concatenate([resB.results[c]["wq3"] for c in CORES], axis=0).T)

    cw = np.asarray(inputs["L0_attn_conv_w"], f32).reshape(CONV_DIM, KCONV)
    cwc = ca(cw.reshape(34, 128, KCONV).transpose(1, 0, 2).reshape(128, 34 * KCONV))
    cbc = ca(np.asarray(inputs["L0_attn_conv_b"], f32).reshape(34, 128).T)
    wipT = ca(np.asarray(inputs["L0_attn_in_proj_w"], f32).T)
    dtb = ca(np.asarray(inputs["L0_attn_dt_bias"], f32).reshape(NHEAD, 1))
    alog = ca(np.asarray(inputs["L0_attn_A_log"], f32).reshape(NHEAD, 1))
    drow = ca(np.asarray(inputs["L0_attn_D"], f32).reshape(1, NHEAD))
    x0s = [ca(x_all[c * TPC:(c + 1) * TPC].T) for c in CORES]
    inC = []
    for core in CORES:
        xh = np.zeros((DM, KCONV), f32)
        if core > 0:
            xh[:, :] = x_all[core * TPC - 3: core * TPC + 1].T
        hm = np.full((128, 1), 0.0 if core == 0 else 1.0, f32)
        inC.append(dict(x0=x0s[core], xh=ca(xh), wipT=wipT, convw=cwc,
                        convb=cbc, dtb=dtb, alog=alog, drow=drow, modc=mc,
                        hmask=hm))
    resC = run_bass_kernel_spmd(_prog("C", build_launch_C), inC, core_ids=CORES)

    csumall = ca(np.stack([resC.results[c]["csum"] for c in CORES]))
    cltall = ca(np.stack([resC.results[c]["clt"][:, 0] for c in CORES]))
    wopT = ca(np.asarray(inputs["L0_attn_out_proj_w"], f32).T)
    nwT = ca(np.asarray(inputs["L0_attn_norm_w"], f32).reshape(32, 128).T)
    inD = []
    for core in CORES:
        miT = np.zeros((NCORE, NCORE), f32)
        incm = np.zeros((NCORE, 1), f32)
        for j in range(NCORE):
            if j < core:
                incm[j, 0] = 1.0
            for k in range(NCORE):
                if j < k < core:
                    miT[k, j] = 1.0
        r = resC.results[core]
        inD.append(dict(x0=x0s[core], yp=r["yp"], gate=r["gate"], cfm=r["cfm"],
                        acum=r["acum"], cstates=r["cstates"], csumall=csumall,
                        cltall=cltall, miT=miT, incm=incm, wopT=wopT, nwT=nwT,
                        modc=mc))
    resD = run_bass_kernel_spmd(_prog("D", build_launch_D), inD, core_ids=CORES)

    wqkvT = ca(np.asarray(inputs["L1_attn_w_qkv"], f32).T)
    RL = np.zeros((128, 128), f32)
    for i in range(64):
        RL[i + 64, i] = -1.0
        RL[i, i + 64] = 1.0
    RLb = RL.astype(bf16)
    inE = []
    for core in CORES:
        pos = ca((np.arange(TPC, dtype=f32) + core * TPC)[None, :])
        inE.append(dict(x1=resD.results[core]["x1"], w1pT=w1pT, w2pT=w2pT,
                        w3pT=w3pT, wqkvT=wqkvT, modc=mc, rrot=RLb, posr=pos))
    resE = run_bass_kernel_spmd(_prog("E", build_launch_E), inE, core_ids=CORES)

    kTall = ca(np.concatenate([resE.results[c]["kT"] for c in CORES], axis=2))
    vt_full = np.concatenate([resE.results[c]["vt"] for c in CORES], axis=0)
    vall = ca(vt_full.reshape(SEQ, AHEADS, HD).transpose(1, 0, 2))
    qkmaxall = ca(np.concatenate([resE.results[c]["qkmax"] for c in CORES],
                                 axis=0))
    woT = ca(np.asarray(inputs["L1_attn_w_out"], f32).T)
    inF1 = [dict(x2=resE.results[c]["x2"], qT=resE.results[c]["qT"],
                 kTall=kTall, vall=vall, woT=woT, modc=mc, qkmaxall=qkmaxall)
            for c in CORES]
    resF1 = run_bass_kernel_spmd(_prog("F1", build_launch_F1), inF1,
                                 core_ids=CORES)

    W1T = ca(np.asarray(inputs["L1_ffn_W1"], f32).T)
    W3T = ca(np.asarray(inputs["L1_ffn_W3"], f32).T)
    W2T = ca(np.asarray(inputs["L1_ffn_W2"], f32).T)
    H128b = _hadamard(128).astype(bf16)
    H64 = _hadamard(64)
    HI2 = np.zeros((128, 128), f32)
    for cp in range(64):
        for t2 in range(2):
            HI2[cp * 2 + t2, t2::2] = H64[cp]
    HI2b = HI2.astype(bf16)
    inF2 = [dict(x3=resF1.results[c]["x3"], W1T=W1T, W3T=W3T, W2T=W2T,
                 modc=mc, h128=H128b, hi2=HI2b) for c in CORES]
    resF2 = run_bass_kernel_spmd(_prog("F2", build_launch_F2), inF2,
                                 core_ids=CORES)

    out = np.concatenate([resF2.results[c]["x4"].T for c in CORES], axis=0)
    return ca(out.astype(np.float32))[None]


# version 07
# speedup vs baseline: 1.1453x; 1.0047x over previous
import math
from contextlib import ExitStack
import numpy as np
import ml_dtypes
import concourse.bass as bass
import concourse.mybir as mybir
from concourse.bass_utils import run_bass_kernel_spmd

F32 = mybir.dt.float32
BF16 = mybir.dt.bfloat16
AF = mybir.ActivationFunctionType
ALU = mybir.AluOpType
AX = mybir.AxisListType
ds = bass.ds
bf16 = ml_dtypes.bfloat16

NCORE = 8
SEQ = 8192
TPC = SEQ // NCORE
DM = 2048
NDC = DM // 128
CH = 256
NCHUNK = TPC // CH

INTER = 4096
NHEAD = 64
PD = 64
ND = 128
KCONV = 4
CONV_DIM = INTER + 2 * ND
PROJ_DIM = INTER + CONV_DIM + NHEAD
DT_MIN, DT_MAX = 1e-3, 100.0
MEPS = 1e-5
LEPS = 1e-5

DFF0 = 5632
W1_ROWS = DFF0 // NCORE
W3_ROWS = DM // NCORE
NBLK = 22528
NGRP = NBLK // 256
GPP = NBLK // 128
TOT_BLOCKS = 180224
_NF4_REF = np.array([-1.0, -0.6962, -0.5251, -0.3949, -0.2844, -0.1848,
                     -0.0911, 0.0, 0.0796, 0.1609, 0.2461, 0.3379, 0.4407,
                     0.5626, 0.723, 1.0], dtype=bf16).astype(np.float64)
NF4L = [float(v) for v in _NF4_REF]
NF4_MID = [(NF4L[i] + NF4L[i + 1]) / 2.0 for i in range(15)]
RNE_MAGIC = 12582912.0

AHEADS = 16
HD = 128
ROPE_BASE = 10000.0

DFF1 = 8192

ENGINES = ("sync", "gpsimd", "vector", "scalar", "tensor")

LOGICAL = ("sync", "gpdma", "gpop", "vector", "scalar", "tensor")
NL = len(LOGICAL)
LIDX = {e: i for i, e in enumerate(LOGICAL)}
PHYS_OF = {"sync": "sync", "gpdma": "gpsimd", "gpop": "gpsimd",
           "vector": "vector", "scalar": "scalar", "tensor": "tensor"}
TENS_LI = LIDX["tensor"]

_DT_SIZE = {}


def _ap_ranges(ap):
    pat = ap.ap
    dt = ap.dtype
    esz = _DT_SIZE.get(dt)
    if esz is None:
        esz = dt.size()
        _DT_SIZE[dt] = esz
    off = ap.offset
    if str(ap.space).endswith("DRAM"):
        span = sum(st * (sz - 1) for st, sz in pat) + 1
        return (ap.name, 0, 128, off * esz, (off + span) * esz, True)
    p0 = ap.base_partition()
    pcnt = pat[0][1]
    pstride = pat[0][0]
    prow = off // pstride if pstride > 0 else 0
    f0 = off - prow * pstride
    span = (sum(st * (sz - 1) for st, sz in pat[1:]) + 1) if len(pat) > 1 else 1
    return (ap.name, p0, p0 + pcnt, f0 * esz, (f0 + span) * esz, False)


class Seq:

    def __init__(self):
        self.steps = []

    def emit(self, engine, fn, inc=1, reads=None, writes=None):
        assert engine in ENGINES, engine
        if engine == "gpsimd":
            engine = "gpdma" if inc == 16 else "gpop"
        elif engine == "sync":
            assert inc == 16, "sync engine hosts only DMA"
        self.steps.append((engine, fn, inc, reads, writes))

    def dma(self, engine, out, in_, **kw):
        self.emit(engine, lambda e: e.dma_start(out=out, in_=in_, **kw), 16,
                  reads=(in_,), writes=(out,))

    def install(self, nc, sems, block):
        steps = self.steps
        nst = len(steps)
        cum = [0] * NL
        issue_guar = {p: [0] * NL for p in ENGINES}
        cells = {}
        fbsz = {}
        waits = [None] * nst
        barrier_vec = [0] * NL

        def cells_of(ap):
            try:
                name, p0, p1, b0, b1, is_dram = _ap_ranges(ap)
            except Exception:
                return None
            fb = fbsz.get(name)
            if fb is None:
                if is_dram:
                    try:
                        t = ap.tensor
                        totb = 1
                        for s in t.shape:
                            totb *= s
                        totb *= _DT_SIZE.get(ap.dtype) or ap.dtype.size()
                    except Exception:
                        totb = 1 << 20
                    fb = max(4096, totb >> 6)
                else:
                    fb = 512
                fbsz[name] = fb
            out = []
            for pb in range(p0 >> 5, ((p1 - 1) >> 5) + 1):
                for fbk in range(b0 // fb, (b1 - 1) // fb + 1):
                    out.append((name, pb, fbk))
            return out

        for k, (leng, fn, inc, reads, writes) in enumerate(steps):
            li = LIDX[leng]
            phys = PHYS_OF[leng]
            R = list(barrier_vec)
            rcells = []
            wcells = []
            bad = reads is None and writes is None
            if not bad:
                for ap in (reads or ()):
                    if not hasattr(ap, "offset"):
                        continue
                    cl = cells_of(ap)
                    if cl is None:
                        bad = True
                        break
                    rcells.extend(cl)
                for ap in (writes or ()):
                    if bad or not hasattr(ap, "offset"):
                        continue
                    cl = cells_of(ap)
                    if cl is None:
                        bad = True
                        break
                    wcells.extend(cl)
            if bad:
                for i in range(NL):
                    if cum[i] > R[i]:
                        R[i] = cum[i]
            else:
                for key in rcells:
                    c = cells.get(key)
                    if c is not None and c[0] is not None:
                        w = c[0]
                        for i in range(NL):
                            if w[i] > R[i]:
                                R[i] = w[i]
                for key in wcells:
                    c = cells.get(key)
                    if c is not None:
                        for v in c:
                            if v is not None:
                                for i in range(NL):
                                    if v[i] > R[i]:
                                        R[i] = v[i]
            iq = issue_guar[phys]
            wts = []
            for j in range(NL):
                if R[j] > iq[j]:
                    if phys == "tensor" and j == TENS_LI:
                        iq[j] = R[j]
                        continue
                    wts.append((j, R[j]))
                    iq[j] = R[j]
            waits[k] = wts
            cum[li] += inc
            g = list(R)
            g[li] = cum[li]
            if bad:
                barrier_vec = g
                cells.clear()
            else:
                for key in wcells:
                    cells[key] = [g, None]
                for key in rcells:
                    c = cells.get(key)
                    if c is None:
                        cells[key] = [None, list(g)]
                    elif c[1] is None:
                        c[1] = list(g)
                    else:
                        r = c[1]
                        for i in range(NL):
                            if g[i] > r[i]:
                                r[i] = g[i]

        totals = list(cum)

        def make(phys_name):
            def body(eng):
                for k, (leng, fn, inc, _r, _w) in enumerate(steps):
                    if PHYS_OF[leng] != phys_name:
                        continue
                    for (j, v) in waits[k]:
                        eng.wait_ge(sems[LOGICAL[j]], v)
                    inst = fn(eng)
                    assert inst is not None, f"{leng} step returned None"
                    inst.then_inc(sems[leng], inc)
                if phys_name == "sync":
                    for j, nm in enumerate(LOGICAL):
                        if totals[j]:
                            eng.wait_ge(sems[nm], totals[j])
            return body

        for phys_name in ENGINES:
            getattr(block, phys_name)(make(phys_name))
        return totals


def mk_sems(b):
    return {nm: b.sem(f"SQ_{nm}") for nm in LOGICAL}


def new_nc():
    return bass.Bass("TRN2", target_bir_lowering=False)


class Bufs:

    def __init__(self, nc, ctx):
        self.nc, self.ctx = nc, ctx

    def sb(self, name, shape, dtype=F32):
        return self.ctx.enter_context(self.nc.sbuf_tensor(name, shape, dtype))

    def ps(self, name, shape, dtype=F32):
        return self.ctx.enter_context(self.nc.psum_tensor(name, shape, dtype))

    def sem(self, name="S"):
        return self.ctx.enter_context(self.nc.semaphore(name))

    def block(self):
        return self.ctx.enter_context(self.nc.Block())



def e_memset(seq, ap, val, engine="vector"):
    seq.emit(engine, lambda e: e.memset(ap, float(val)), reads=(), writes=(ap,))


def e_identity(seq, ident):
    seq.emit("gpsimd", lambda e: e.memset(ident[:, :], 0.0),
             reads=(), writes=(ident[:, :],))
    seq.emit("gpsimd", lambda e: e.affine_select(
        out=ident[:, :], in_=ident[:, :], compare_op=ALU.not_equal,
        fill=1.0, base=0, pattern=[[-1, ident.shape[1]]], channel_multiplier=1),
        reads=(ident[:, :],), writes=(ident[:, :],))


def e_copy(seq, out, in_, engine="vector"):
    seq.emit(engine, lambda e: e.tensor_copy(out, in_),
             reads=(in_,), writes=(out,))


def e_act(seq, out, in_, func, bias=0.0, scale=1.0, accum_out=None):
    wr = (out,) if accum_out is None else (out, accum_out)
    seq.emit("scalar", lambda e: e.activation(out, in_, func, bias=bias,
                                              scale=scale, accum_out=accum_out),
             reads=(in_, bias), writes=wr)


def e_ts(seq, out, in0, s1, s2=None, op0=ALU.mult, op1=ALU.bypass, engine="vector"):
    if s2 is None:
        seq.emit(engine, lambda e: e.tensor_scalar(out, in0, s1, None, op0),
                 reads=(in0, s1), writes=(out,))
    else:
        seq.emit(engine, lambda e: e.tensor_scalar(out, in0, s1, s2, op0, op1),
                 reads=(in0, s1, s2), writes=(out,))


def e_tt(seq, out, in0, in1, op, engine="vector"):
    seq.emit(engine, lambda e: e.tensor_tensor(out, in0, in1, op),
             reads=(in0, in1), writes=(out,))


def e_stt(seq, out, in0, scalar, in1, op0, op1, engine="vector"):
    seq.emit(engine, lambda e: e.scalar_tensor_tensor(out, in0, scalar, in1,
                                                      op0, op1),
             reads=(in0, scalar, in1), writes=(out,))


def e_red(seq, out, in_, op, absval=False, negate=False):
    seq.emit("vector", lambda e: e.tensor_reduce(out, in_, AX.X, op,
                                                 apply_absolute_value=absval,
                                                 negate=negate),
             reads=(in_,), writes=(out,))


def e_mm(seq, out, lhsT, rhs, start=True, stop=True):
    rd = (lhsT, rhs) if start else (lhsT, rhs, out)
    seq.emit("tensor", lambda e: e.matmul(out, lhsT, rhs, start=start,
                                          stop=stop),
             reads=rd, writes=(out,))


def e_tr(seq, out, in_, ident):
    seq.emit("tensor", lambda e: e.transpose(out, in_, ident),
             reads=(in_, ident), writes=(out,))


def e_recip(seq, out, in_):
    seq.emit("vector", lambda e: e.reciprocal(out, in_),
             reads=(in_,), writes=(out,))


def e_bcast_col(seq, nc, ones_row, src_row, psum, dst_col=None):
    e_mm(seq, psum, ones_row[:, :], src_row)
    if dst_col is not None:
        e_copy(seq, dst_col, psum)



def build_launch_A(do_mod=True, do_nf4=True):
    nc = new_nc()
    cT = {}
    wmod = {}
    for L in range(2):
        cT[L] = nc.dram_tensor(f"c{L}T", [128, 16], F32, kind="ExternalInput").ap()
        wmod[L] = nc.dram_tensor(f"wmod{L}s", [DM, 12288 // NCORE], F32,
                                 kind="ExternalInput").ap()
    wsh = {}
    for i, (r, c) in enumerate([(W1_ROWS, DM), (W1_ROWS, DM), (W3_ROWS, DFF0)]):
        wsh[i] = nc.dram_tensor(f"w{i+1}s", [r, c], BF16, kind="ExternalInput").ap()
    modrow = nc.dram_tensor("modrow", [2, 12288 // NCORE], F32,
                            kind="ExternalOutput").ap()
    ssum = nc.dram_tensor("ssum", [1, 4], F32, kind="ExternalOutput").ap()

    FSH = 12288 // NCORE
    with ExitStack() as ctx:
        b = Bufs(nc, ctx)
        sc = b.sb("sc", [128, 16])
        silu_c = b.sb("silu_c", [128, 16])
        wbuf = b.sb("wbuf", [128, 16 * 512])
        mrow = [b.sb(f"mrow{L}", [1, FSH]) for L in range(2)]
        wg = b.sb("wg", [NGRP, 256 * 64], BF16)
        s1g = b.sb("s1g", [NGRP, 256])
        rowsum = b.sb("rowsum", [NGRP, 1])
        ones_col = b.sb("ones_col", [128, 1])
        ssum_sb = b.sb("ssum_sb", [1, 4])
        p_row = b.ps("p_row", [1, 512])
        p_sum = b.ps("p_sum", [1, 1])
        sems = mk_sems(b)
        block = b.block()
        seq = Seq()
        e_memset(seq, ones_col[:, :], 1.0)
        e_memset(seq, ssum_sb[:, :], 0.0)
        for L in range(2 if do_mod else 0):
            seq.dma("sync", sc[:, :], cT[L])
            e_act(seq, silu_c[:, :], sc[:, :], AF.Silu)
            for fs in range(FSH // 512):
                for c in range(NDC):
                    seq.dma("sync", wbuf[:, c * 512:(c + 1) * 512],
                            wmod[L][c * 128:(c + 1) * 128, fs * 512:(fs + 1) * 512])
                for c in range(NDC):
                    e_mm(seq, p_row[:, :], silu_c[:, c:c + 1], wbuf[:, c * 512:(c + 1) * 512],
                         start=(c == 0), stop=(c == NDC - 1))
                e_copy(seq, mrow[L][:, fs * 512:(fs + 1) * 512], p_row[:, :])
        for L in range(2):
            seq.dma("sync", modrow[L:L + 1, :], mrow[L][:, :])
        for i in range(3 if do_nf4 else 0):
            flat = wsh[i].rearrange("a b -> (a b)")
            src = flat.rearrange("(g j) -> g j", g=NGRP)
            seq.dma("sync", wg[:, :], src)
            e_red(seq, s1g[:, :], wg[:, :].rearrange("g (b k) -> g b k", k=64),
                  ALU.max, absval=True)
            e_red(seq, rowsum[:, :], s1g[:, :], ALU.add)
            e_mm(seq, p_sum[:, :], ones_col[:NGRP, :], rowsum[:, :])
            e_copy(seq, ssum_sb[:, i:i + 1], p_sum[:, :])
        seq.dma("sync", ssum, ssum_sb[:, :])
        seq.install(nc, sems, block)
    return nc



def build_launch_B():
    nc = new_nc()
    wsh = {}
    shapes = [(W1_ROWS, DM), (W1_ROWS, DM), (W3_ROWS, DFF0)]
    for i, (r, c) in enumerate(shapes):
        wsh[i] = nc.dram_tensor(f"w{i+1}s", [r, c], BF16, kind="ExternalInput").ap()
    allsum = nc.dram_tensor("allsum", [NCORE, 4], F32, kind="ExternalInput").ap()
    wq = {}
    for i, (r, c) in enumerate(shapes):
        wq[i] = nc.dram_tensor(f"wq{i+1}", [r, c], BF16, kind="ExternalOutput").ap()

    GW = 256 * 64
    with ExitStack() as ctx:
        b = Bufs(nc, ctx)
        wg = b.sb("wg", [NGRP, GW], BF16)
        s1g = b.sb("s1g", [NGRP, 256])
        r1g = b.sb("r1g", [NGRP, 256])
        cent = b.sb("cent", [NGRP, 256], BF16)
        centf = b.sb("centf", [NGRP, 256])
        gmax = b.sb("gmax", [NGRP, 1])
        qf = b.sb("qf", [NGRP, 1])
        rqf = b.sb("rqf", [NGRP, 1])
        qs = b.sb("qs", [NGRP, 256])
        deqb = b.sb("deqb", [NGRP, 256], BF16)
        sums = b.sb("sums", [1, 4 * NCORE])
        tot = b.sb("tot", [1, 4])
        meanb = b.sb("meanb", [1, 1], BF16)
        meanf = b.sb("meanf", [1, 1])
        ones_row = b.sb("ones_row", [1, 128])
        mean_col = b.sb("mean_col", [128, 1])
        scl = b.sb("scl", [NGRP, GW], BF16)
        tmp = b.sb("tmp", [NGRP, GW], BF16)
        acc = b.sb("acc", [NGRP, GW], BF16)
        p_mean = b.ps("p_mean", [128, 1])
        sems = mk_sems(b)
        block = b.block()
        seq = Seq()
        e_memset(seq, ones_row[:, :], 1.0)
        seq.dma("sync", sums[:, :], allsum.rearrange("a b -> (a b)")[None, :])

        for i in range(3):
            flat = wsh[i].rearrange("a b -> (a b)")
            gview = flat.rearrange("(g j) -> g j", g=NGRP)
            seq.dma("sync", wg[:, :], gview)
            e_red(seq, s1g[:, :], wg[:, :].rearrange("g (b k) -> g b k", k=64),
                  ALU.max, absval=True)
            e_red(seq, tot[:, i:i + 1],
                  sums[:, :].rearrange("o (c w) -> o w c", w=4)[:, i:i + 1, :],
                  ALU.add)
            seq.emit("vector", lambda e, i=i: e.tensor_single_scalar(
                meanb[:, :], tot[:, i:i + 1], float(1.0 / TOT_BLOCKS), ALU.mult),
                reads=(tot[:, i:i + 1],), writes=(meanb[:, :],))
            e_copy(seq, meanf[:, :], meanb[:, :])
            e_mm(seq, p_mean[:, :], ones_row[:, :], meanf[:, :])
            e_copy(seq, mean_col[:, :], p_mean[:, :])
            e_ts(seq, cent[:, :], s1g[:, :], mean_col[:NGRP, :], None, ALU.subtract)
            e_red(seq, gmax[:, :], cent[:, :], ALU.max, absval=True)
            e_recip(seq, qf[:, :], gmax[:, :])
            e_ts(seq, qf[:, :], qf[:, :], 128.0, None, ALU.mult)
            e_copy(seq, centf[:, :], cent[:, :])
            e_ts(seq, qs[:, :], centf[:, :], qf[:, :], RNE_MAGIC, ALU.mult, ALU.add)
            e_ts(seq, qs[:, :], qs[:, :], RNE_MAGIC, None, ALU.subtract)
            e_ts(seq, qs[:, :], qs[:, :], -128.0, 127.0, ALU.max, ALU.min)
            e_recip(seq, rqf[:, :], qf[:, :])
            e_ts(seq, deqb[:, :], qs[:, :], rqf[:, :], None, ALU.mult)
            e_copy(seq, centf[:, :], deqb[:, :])
            e_ts(seq, deqb[:, :], centf[:, :], mean_col[:NGRP, :], None, ALU.add)
            e_recip(seq, r1g[:, :], s1g[:, :])
            e_tt(seq, scl[:, :].rearrange("g (b k) -> g b k", k=64),
                 wg[:, :].rearrange("g (b k) -> g b k", k=64),
                 r1g[:, :].rearrange("g (b o) -> g b o", o=1).broadcast_to([NGRP, 256, 64]),
                 ALU.mult)
            e_memset(seq, acc[:, :], 0.0)
            for lv in range(7):
                e_ts(seq, tmp[:, :], scl[:, :], float(NF4_MID[lv]), float(NF4L[lv]),
                     ALU.is_le, ALU.mult)
                e_tt(seq, acc[:, :], acc[:, :], tmp[:, :], ALU.min)
            e_memset(seq, tmp[:, :], 0.0)
            for lv in range(7, 15):
                e_ts(seq, wg[:, :], scl[:, :], float(NF4_MID[lv]), float(NF4L[lv + 1]),
                     ALU.is_gt, ALU.mult)
                e_tt(seq, tmp[:, :], tmp[:, :], wg[:, :], ALU.max)
            e_tt(seq, acc[:, :], acc[:, :], tmp[:, :], ALU.add)
            e_tt(seq, scl[:, :].rearrange("g (b k) -> g b k", k=64),
                 acc[:, :].rearrange("g (b k) -> g b k", k=64),
                 deqb[:, :].rearrange("g (b o) -> g b o", o=1).broadcast_to([NGRP, 256, 64]),
                 ALU.mult)
            out_g = wq[i].rearrange("a b -> (a b)").rearrange("(g j) -> g j", g=NGRP)
            seq.dma("sync", out_g, scl[:, :])
        seq.install(nc, sems, block)
    return nc



def e_clamp_gate(seq, out_cols, in_cols, sgn_scratch):
    e_ts(seq, sgn_scratch, in_cols, 0.0, 2.0, ALU.is_ge, ALU.mult)
    e_ts(seq, sgn_scratch, sgn_scratch, -1.0, None, ALU.add)
    e_tt(seq, out_cols, in_cols, sgn_scratch, ALU.mult)
    e_ts(seq, out_cols, out_cols, 0.25, None, ALU.max)
    e_tt(seq, out_cols, out_cols, sgn_scratch, ALU.mult)


class LNBufs:
    def __init__(self, b, tag, p_b=None, p_sum=None, p_sq=None,
                 mean_b=None, rstd_b=None, t1=None):
        self.sq = b.sb(f"ln_sq_{tag}", [128, 512])
        rt = b.sb(f"ln_rows_{tag}", [1, 1024])
        rt2 = b.sb(f"ln_rows2_{tag}", [1, 512])
        self.mean_row = rt[0:1, 0:512]
        self.rstd_row = rt[0:1, 512:1024]
        self.msq_row = rt2[0:1, :]
        self.mean_b = mean_b if mean_b is not None else b.sb(
            f"ln_mb_{tag}", [128, 512])
        self.rstd_b = rstd_b if rstd_b is not None else b.sb(
            f"ln_rb_{tag}", [128, 512])
        self.t1 = t1 if t1 is not None else b.sb(f"ln_t1_{tag}", [128, 512])
        self.p_sum = p_sum if p_sum is not None else b.ps(
            f"ln_ps_{tag}", [1, 512])
        self.p_sq = p_sq if p_sq is not None else b.ps(f"ln_pq_{tag}", [1, 512])
        self.p_b = p_b if p_b is not None else b.ps(f"ln_pb_{tag}", [128, 512])


def emit_ln_mod(seq, lb, x_sb, h_out, ones_col, ones_row, ca1_cols, sa_cols,
                toks=TPC, nch=NDC):
    for t0 in range(0, toks, 512):
        tw = min(512, toks - t0)
        for c in range(nch):
            xc = x_sb[:, c * toks + t0: c * toks + t0 + tw]
            e_act(seq, lb.sq[:, :tw], xc, AF.Square)
            e_mm(seq, lb.p_sum[:, :tw], ones_col[:, :], xc,
                 start=(c == 0), stop=(c == nch - 1))
            e_mm(seq, lb.p_sq[:, :tw], ones_col[:, :], lb.sq[:, :tw],
                 start=(c == 0), stop=(c == nch - 1))
        dim = 128 * nch
        e_ts(seq, lb.mean_row[:, :tw], lb.p_sum[:, :tw], 1.0 / dim, None, ALU.mult)
        e_ts(seq, lb.msq_row[:, :tw], lb.p_sq[:, :tw], 1.0 / dim, None, ALU.mult)
        e_stt(seq, lb.rstd_row[:, :tw], lb.mean_row[:, :tw], -1.0,
              lb.mean_row[:, :tw], ALU.mult, ALU.mult)
        e_tt(seq, lb.msq_row[:, :tw], lb.msq_row[:, :tw], lb.rstd_row[:, :tw],
             ALU.add)
        e_ts(seq, lb.msq_row[:, :tw], lb.msq_row[:, :tw], LEPS, None, ALU.add)
        e_act(seq, lb.rstd_row[:, :tw], lb.msq_row[:, :tw], AF.Sqrt)
        e_recip(seq, lb.rstd_row[:, :tw], lb.rstd_row[:, :tw])
        e_mm(seq, lb.p_b[:, :tw], ones_row[:, :], lb.mean_row[:, :tw])
        e_copy(seq, lb.mean_b[:, :tw], lb.p_b[:, :tw])
        e_mm(seq, lb.p_b[:, :tw], ones_row[:, :], lb.rstd_row[:, :tw])
        e_copy(seq, lb.rstd_b[:, :tw], lb.p_b[:, :tw])
        for c in range(nch):
            xc = x_sb[:, c * toks + t0: c * toks + t0 + tw]
            hc = h_out[:, c * toks + t0: c * toks + t0 + tw]
            e_stt(seq, lb.t1[:, :tw], lb.mean_b[:, :tw], -1.0, xc, ALU.mult, ALU.add)
            e_stt(seq, lb.t1[:, :tw], lb.t1[:, :tw], ca1_cols[:, c:c + 1],
                  lb.rstd_b[:, :tw], ALU.mult, ALU.mult)
            e_ts(seq, hc, lb.t1[:, :tw], sa_cols[:, c:c + 1], None, ALU.add)


def dma_wslice_fm(seq, wbuf, wT_dram, n_in, f0, fw, engine="gpsimd"):
    nch = n_in // 128
    src = wT_dram.rearrange("(c p) f -> p c f", p=128)[:, :, f0:f0 + fw]
    dst = wbuf[:, :nch * fw].rearrange("p (c f) -> p c f", f=fw)
    seq.dma(engine, dst, src)


def emit_gemm_fm(seq, wT_dram, act_bf, n_in, f_lo, f_hi, toks, wbuf, psum,
                 consume, tok_tile=512):
    nch = n_in // 128
    for f0 in range(f_lo, f_hi, 128):
        dma_wslice_fm(seq, wbuf, wT_dram, n_in, f0, 128)
        for t0 in range(0, toks, tok_tile):
            tw = min(tok_tile, toks - t0)
            for c in range(nch):
                e_mm(seq, psum[:, :tw], wbuf[:, c * 128:(c + 1) * 128],
                     act_bf[:, c * toks + t0: c * toks + t0 + tw],
                     start=(c == 0), stop=(c == nch - 1))
            consume(f0 - f_lo, t0, tw, psum[:, :tw])



def build_launch_E():
    nc = new_nc()
    x1d = nc.dram_tensor("x1", [DM, TPC], F32, kind="ExternalInput").ap()
    w1T = nc.dram_tensor("w1pT", [DM, DFF0], BF16, kind="ExternalInput").ap()
    w2T = nc.dram_tensor("w2pT", [DM, DFF0], BF16, kind="ExternalInput").ap()
    w3T = nc.dram_tensor("w3pT", [DFF0, DM], BF16, kind="ExternalInput").ap()
    wqkvT = nc.dram_tensor("wqkvT", [DM, 3 * DM], F32, kind="ExternalInput").ap()
    modc = nc.dram_tensor("modc", [128, 2 * 6 * NDC], F32, kind="ExternalInput").ap()
    rrot = nc.dram_tensor("rrot", [128, 128], BF16, kind="ExternalInput").ap()
    posr = nc.dram_tensor("posr", [1, TPC], F32, kind="ExternalInput").ap()
    x2d = nc.dram_tensor("x2", [DM, TPC], F32, kind="ExternalOutput").ap()
    qTd = nc.dram_tensor("qT", [AHEADS, HD, TPC], BF16, kind="ExternalOutput").ap()
    kTd = nc.dram_tensor("kT", [AHEADS, HD, TPC], BF16, kind="ExternalOutput").ap()
    vtd = nc.dram_tensor("vt", [TPC, DM], BF16, kind="ExternalOutput").ap()
    qkmaxd = nc.dram_tensor("qkmax", [1, 2 * AHEADS], F32, kind="ExternalOutput").ap()

    NF0 = DFF0 // 128
    with ExitStack() as ctx:
        b = Bufs(nc, ctx)
        x_sb = b.sb("x_sb", [128, NDC * TPC])
        h_sb = b.sb("h_sb", [128, NDC * TPC], BF16)
        hid = b.sb("hid", [128, NF0 * 512], BF16)
        mcol = b.sb("mcol", [128, 2 * 6 * NDC])
        ca1 = b.sb("ca1", [128, NDC])
        gbc = b.sb("gbc", [128, NDC])
        sgn = b.sb("sgn", [128, NDC])
        ones_col = b.sb("ones_col", [128, 1])
        ones_row = b.sb("ones_row", [1, 128])
        wbuf1 = [b.sb(f"wbuf1_{i}", [128, NDC * 128], BF16) for i in range(2)]
        wbuf2 = [b.sb(f"wbuf2_{i}", [128, NDC * 128], BF16) for i in range(2)]
        wbufq = b.sb("wbufq", [128, NDC * 512], BF16)
        silu_s = b.sb("silu_s", [128, 512])
        rope_cos = b.sb("rope_cos", [128, TPC])
        rope_sin = b.sb("rope_sin", [128, TPC])
        ifr = b.sb("ifr", [128, 1])
        qk_bf = b.sb("qk_bf", [128, TPC], BF16)
        rot_sb = b.sb("rot_sb", [128, TPC], BF16)
        rconst = b.sb("rconst", [128, 128], BF16)
        posrow = b.sb("posrow", [1, TPC])
        lb = LNBufs(b, "e")
        p_a = b.ps("p_a", [128, 512])
        p_b2 = b.ps("p_b2", [128, 512])
        p_o = b.ps("p_o", [128, 512])
        p_r = b.ps("p_r", [128, 512])
        p_ah = [p_a, p_r]
        p_bh = [p_b2, p_o]
        sems = mk_sems(b)
        block = b.block()
        seq = Seq()

        e_memset(seq, ones_col[:, :], 1.0)
        e_memset(seq, ones_row[:, :], 1.0)
        seq.dma("sync", mcol[:, :], modc)
        for c in range(NDC):
            seq.dma("sync", x_sb[:, c * TPC:(c + 1) * TPC],
                    x1d[c * 128:(c + 1) * 128, :])
        e_ts(seq, ca1[:, :], mcol[:, 4 * NDC:5 * NDC], 1.0, None, ALU.add)
        e_clamp_gate(seq, gbc[:, :], mcol[:, 5 * NDC:6 * NDC], sgn[:, :])
        emit_ln_mod(seq, lb, x_sb, h_sb, ones_col, ones_row,
                    ca1, mcol[:, 3 * NDC:4 * NDC])

        for t0 in range(0, TPC, 512):
            for fi in range(NF0):
                f0 = fi * 128
                pr = fi % 2
                w1c = wbuf1[pr]
                w2c = wbuf2[pr]
                dma_wslice_fm(seq, w1c, w1T, DM, f0, 128)
                dma_wslice_fm(seq, w2c, w2T, DM, f0, 128)
                for c in range(NDC):
                    e_mm(seq, p_ah[pr][:, :], w1c[:, c * 128:(c + 1) * 128],
                         h_sb[:, c * TPC + t0: c * TPC + t0 + 512],
                         start=(c == 0), stop=(c == NDC - 1))
                for c in range(NDC):
                    e_mm(seq, p_bh[pr][:, :], w2c[:, c * 128:(c + 1) * 128],
                         h_sb[:, c * TPC + t0: c * TPC + t0 + 512],
                         start=(c == 0), stop=(c == NDC - 1))
                e_act(seq, silu_s[:, :], p_ah[pr][:, :], AF.Silu)
                e_tt(seq, hid[:, fi * 512:(fi + 1) * 512], silu_s[:, :],
                     p_bh[pr][:, :], ALU.mult)
            for dc in range(NDC):
                dma_wslice_fm(seq, wbufq, w3T, DFF0, dc * 128, 128)
                for fc in range(NF0):
                    e_mm(seq, p_o[:, :], wbufq[:, fc * 128:(fc + 1) * 128],
                         hid[:, fc * 512:(fc + 1) * 512],
                         start=(fc == 0), stop=(fc == NF0 - 1))
                xc = x_sb[:, dc * TPC + t0: dc * TPC + t0 + 512]
                e_stt(seq, xc, p_o[:, :], gbc[:, dc:dc + 1], xc, ALU.mult, ALU.add)
        for c in range(NDC):
            seq.dma("sync", x2d[c * 128:(c + 1) * 128, :],
                    x_sb[:, c * TPC:(c + 1) * TPC])

        e_ts(seq, ca1[:, :], mcol[:, 6 * NDC + 1 * NDC: 6 * NDC + 2 * NDC],
             1.0, None, ALU.add)
        emit_ln_mod(seq, lb, x_sb, h_sb, ones_col, ones_row,
                    ca1, mcol[:, 6 * NDC: 6 * NDC + NDC])

        seq.dma("sync", posrow[:, :], posr)
        seq.emit("gpsimd", lambda e: e.iota(ifr[:, :], [[0, 1]], base=0,
                                            channel_multiplier=1,
                                            allow_small_or_imprecise_dtypes=True),
                 reads=(), writes=(ifr[:, :],))
        e_ts(seq, silu_s[:, 0:1], ifr[:, :], 64.0, 64.0, ALU.is_ge, ALU.mult)
        e_tt(seq, ifr[:, :], ifr[:, :], silu_s[:, 0:1], ALU.subtract)
        e_act(seq, ifr[:, :], ifr[:, :], AF.Exp, scale=-math.log(10000.0) / 64.0)
        TWO_PI = 2.0 * math.pi
        c1 = float(np.float32(6.28125))
        c2 = float(np.float32(np.float64(TWO_PI) - 6.28125))
        c3 = float(np.float64(TWO_PI) - np.float64(c1) - np.float64(c2))
        for th in range(0, TPC, 512):
            ang = silu_s[:, :]
            kk = lb.t1[:, :]
            e_mm(seq, p_r[:, :], ones_row[:, :], posrow[:, th:th + 512])
            e_tt(seq, ang, p_r[:, :], ifr[:, :].broadcast_to([128, 512]), ALU.mult)
            e_ts(seq, kk, ang, 1.0 / TWO_PI, RNE_MAGIC, ALU.mult, ALU.add)
            e_ts(seq, kk, kk, RNE_MAGIC, None, ALU.subtract)
            for cw in (c1, c2, c3):
                e_stt(seq, ang, kk, -cw, ang, ALU.mult, ALU.add)
            e_act(seq, rope_sin[:, th:th + 512], ang, AF.Sin)
            e_ts(seq, kk, ang, float(math.pi / 2), TWO_PI, ALU.is_gt, ALU.mult)
            e_tt(seq, ang, ang, kk, ALU.subtract)
            e_ts(seq, ang, ang, float(math.pi / 2), None, ALU.add)
            e_act(seq, rope_cos[:, th:th + 512], ang, AF.Sin)

        seq.dma("sync", rconst[:, :], rrot)
        qkmax = b.sb("qkmax_sb", [1, 2 * AHEADS])
        nmtmp = b.sb("nmtmp", [1, 1])
        e_memset(seq, qkmax[:, :], 0.0)

        def qk_consume(dst, head, mx_off):
            def consume(f0, t0, tw, psum):
                e_copy(seq, qk_bf[:, t0:t0 + tw], psum)
                if t0 + tw == TPC:
                    for tt in range(0, TPC, 512):
                        e_mm(seq, p_r[:, :], rconst[:, :], qk_bf[:, tt:tt + 512])
                        e_copy(seq, rot_sb[:, tt:tt + 512], p_r[:, :])
                    for tt in range(0, TPC, 512):
                        e_tt(seq, silu_s[:, :], qk_bf[:, tt:tt + 512],
                             rope_cos[:, tt:tt + 512], ALU.mult)
                        e_tt(seq, lb.t1[:, :], rot_sb[:, tt:tt + 512],
                             rope_sin[:, tt:tt + 512], ALU.mult)
                        e_tt(seq, qk_bf[:, tt:tt + 512], silu_s[:, :],
                             lb.t1[:, :], ALU.add)
                    seq.dma("sync", dst[head], qk_bf[:, :])
                    mcell = qkmax[:, mx_off + head: mx_off + head + 1]
                    for tt in range(0, TPC, 512):
                        e_act(seq, silu_s[:, :], qk_bf[:, tt:tt + 512], AF.Square)
                        e_mm(seq, lb.p_sum[:, :], ones_col[:, :], silu_s[:, :])
                        e_red(seq, nmtmp[:, :], lb.p_sum[:, :], ALU.max)
                        e_tt(seq, mcell, mcell, nmtmp[:, :], ALU.max)
            return consume

        for head in range(AHEADS):
            emit_gemm_fm(seq, wqkvT, h_sb, DM, head * HD, (head + 1) * HD, TPC,
                         wbufq, p_a, qk_consume(qTd, head, 0))
        for head in range(AHEADS):
            emit_gemm_fm(seq, wqkvT, h_sb, DM, DM + head * HD, DM + (head + 1) * HD,
                         TPC, wbufq, p_a, qk_consume(kTd, head, AHEADS))
        seq.dma("sync", qkmaxd, qkmax[:, :])
        for f0 in range(0, DM, 512):
            src = wqkvT.rearrange("(c p) f -> p c f", p=128)[
                :, :, 2 * DM + f0: 2 * DM + f0 + 512]
            dst = wbufq[:, :NDC * 512].rearrange("p (c f) -> p c f", f=512)
            seq.dma("gpsimd", dst, src)
            for t0 in range(0, TPC, 128):
                for c in range(NDC):
                    e_mm(seq, p_o[:, :], h_sb[:, c * TPC + t0: c * TPC + t0 + 128],
                         wbufq[:, c * 512:(c + 1) * 512],
                         start=(c == 0), stop=(c == NDC - 1))
                e_copy(seq, qk_bf[:, 0:512], p_o[:, :])
                seq.dma("sync", vtd[t0:t0 + 128, f0:f0 + 512], qk_bf[:, 0:512])
        seq.install(nc, sems, block)
    return nc



def build_launch_F1(debug=False):
    nc = new_nc()
    x2d = nc.dram_tensor("x2", [DM, TPC], F32, kind="ExternalInput").ap()
    qTd = nc.dram_tensor("qT", [AHEADS, HD, TPC], BF16, kind="ExternalInput").ap()
    kTd = nc.dram_tensor("kTall", [AHEADS, HD, SEQ], BF16, kind="ExternalInput").ap()
    vd = nc.dram_tensor("vall", [AHEADS, SEQ, HD], BF16, kind="ExternalInput").ap()
    woT = nc.dram_tensor("woT", [DM, DM], F32, kind="ExternalInput").ap()
    modc = nc.dram_tensor("modc", [128, 2 * 6 * NDC], F32, kind="ExternalInput").ap()
    qkm = nc.dram_tensor("qkmaxall", [NCORE, 2 * AHEADS], F32,
                         kind="ExternalInput").ap()
    x3d = nc.dram_tensor("x3", [DM, TPC], F32, kind="ExternalOutput").ap()
    mx_bounce = nc.dram_tensor("mx_bounce", [4, 128], F32)
    if debug:
        dbg_e = nc.dram_tensor("dbg_e", [128, 512], BF16,
                               kind="ExternalOutput").ap()
        dbg_s = nc.dram_tensor("dbg_s", [1, 512], F32, kind="ExternalOutput").ap()
        dbg_a = nc.dram_tensor("dbg_a", [128, TPC], BF16,
                               kind="ExternalOutput").ap()

    NKC = SEQ // 128
    with ExitStack() as ctx:
        b = Bufs(nc, ctx)
        expS = b.sb("expS", [128, NKC * 512], BF16)
        kTb = [b.sb(f"kT{i}", [128, SEQ], BF16) for i in range(2)]
        v_sbb = [b.sb(f"v_sb{i}", [128, NKC * HD], BF16) for i in range(2)]
        qTb = [b.sb(f"qT_sb{i}", [128, TPC], BF16) for i in range(2)]
        attn = b.sb("attn", [128, AHEADS * TPC], BF16)
        x2c = b.sb("x2c", [128, TPC])
        stage = b.sb("stage", [128, 512])
        wbuf = b.sb("wbufo", [128, NDC * 128], BF16)
        mcol = b.sb("mcolf", [128, 2 * 6 * NDC])
        gac = b.sb("gac", [128, NDC])
        sgn = b.sb("sgn_f", [128, NDC])
        rsum = b.sb("rsum", [1, 512])
        rbc = b.sb("rbc", [128, 512])
        sacc = b.sb("sacc", [128, 4 * 512])
        ones_col = b.sb("ones_colf", [128, 1])
        ones_bf1 = b.sb("ones_bf1", [1, 128], BF16)
        ones_row = b.sb("ones_rowf", [1, 128])
        identf = b.sb("identf", [128, 128])
        pm4 = b.sb("pm4", [128, 4])
        mx1 = b.sb("mx1", [128, 16])
        nmrow = b.sb("nmrow", [1, 512], BF16)
        p_s = b.ps("p_s", [128, 512])
        p_sum = b.ps("p_sumf", [1, 512])
        p_b = b.ps("p_bf", [128, 512])
        p_m = b.ps("p_m", [128, 512])
        p_s2 = b.ps("p_s2", [128, 512])
        p_s3 = b.ps("p_s3", [128, 512])
        prot = [p_s, p_m, p_s2, p_s3]
        sems = mk_sems(b)
        block = b.block()
        seq = Seq()

        e_memset(seq, ones_col[:, :], 1.0)
        e_memset(seq, ones_bf1[:, :], 1.0)
        e_memset(seq, ones_row[:, :], 1.0)
        e_identity(seq, identf)
        seq.dma("sync", mcol[:, :], modc)

        for head in range(AHEADS):
            kT = kTb[head % 2]
            v_sb = v_sbb[head % 2]
            qT = qTb[head % 2]
            seq.dma("sync", kT[:, :], kTd[head])
            for part in range(8):
                seq.dma("sync",
                        v_sb[:, :].rearrange("p (kc d) -> p kc d", d=HD)[
                            :, part * 8:(part + 1) * 8, :],
                        vd[head].rearrange("(kc p) d -> p kc d", p=128)[
                            :, part * 8:(part + 1) * 8, :])
            seq.dma("sync", qT[:, :], qTd[head])
            for t0 in range(0, TPC, 512):
                mxs = mx1
                for qc in range(4):
                    for kt in range(SEQ // 512):
                        pr = prot[kt % 4]
                        e_mm(seq, pr[:, 0:512],
                             qT[:, t0 + qc * 128: t0 + (qc + 1) * 128],
                             kT[:, kt * 512:(kt + 1) * 512])
                        e_red(seq, mxs[:, kt:kt + 1], pr[:, 0:512], ALU.max)
                    e_red(seq, pm4[:, qc:qc + 1],
                          mxs[:, :].rearrange("p (a k) -> p a k", a=1), ALU.max)
                e_tr(seq, p_m[0:4, 0:128], pm4[:, :], identf[:, :])
                e_ts(seq, rbc[0:4, 0:128], p_m[0:4, 0:128], -1.0, None, ALU.mult)
                seq.dma("sync", mx_bounce.ap(), rbc[0:4, 0:128])
                seq.dma("gpsimd", nmrow[:, :],
                        mx_bounce.ap().rearrange("a b -> (a b)")[None, :])
                for kc in range(NKC):
                    ps = prot[kc % 4]
                    ch = expS[:, kc * 512:(kc + 1) * 512]
                    e_mm(seq, ps[:, :], ones_bf1[:, :], nmrow[:, :], stop=False)
                    e_mm(seq, ps[:, :], kT[:, kc * 128:(kc + 1) * 128],
                         qT[:, t0:t0 + 512], start=False)
                    e_act(seq, ch, ps[:, :], AF.Exp, scale=1.0 / math.sqrt(HD))
                    sa = sacc[:, (kc % 4) * 512:(kc % 4 + 1) * 512]
                    if kc < 4:
                        e_copy(seq, sa, ch, engine="gpsimd")
                    else:
                        e_tt(seq, sa, sa, ch, ALU.add, engine="gpsimd")
                e_tt(seq, sacc[:, 0:512], sacc[:, 0:512], sacc[:, 512:1024],
                     ALU.add, engine="gpsimd")
                e_tt(seq, sacc[:, 1024:1536], sacc[:, 1024:1536],
                     sacc[:, 1536:2048], ALU.add, engine="gpsimd")
                e_tt(seq, sacc[:, 0:512], sacc[:, 0:512], sacc[:, 1024:1536],
                     ALU.add, engine="gpsimd")
                e_mm(seq, p_sum[:, :], ones_col[:, :], sacc[:, 0:512])
                e_copy(seq, rsum[:, :], p_sum[:, :])
                e_recip(seq, rsum[:, :], rsum[:, :])
                e_mm(seq, p_b[:, :], ones_row[:, :], rsum[:, :])
                e_copy(seq, rbc[:, :], p_b[:, :])
                for kc in range(NKC):
                    e_mm(seq, p_s[:, :], v_sb[:, kc * HD:(kc + 1) * HD],
                         expS[:, kc * 512:(kc + 1) * 512],
                         start=(kc == 0), stop=(kc == NKC - 1))
                e_tt(seq, attn[:, head * TPC + t0: head * TPC + t0 + 512],
                     p_s[:, :], rbc[:, :], ALU.mult)
        if debug:
            seq.dma("sync", dbg_a, attn[:, 0:TPC])

        e_clamp_gate(seq, gac[:, :], mcol[:, 6 * NDC + 2 * NDC: 6 * NDC + 3 * NDC],
                     sgn[:, :])

        def oproj_consume(dc2, t0, tw, psum):
            dc = dc2 // 128
            xc = x2c[:, t0:t0 + tw]
            if t0 == 0:
                for tt in range(0, TPC, 512):
                    seq.dma("sync", x2c[:, tt:tt + 512],
                            x2d[dc * 128:(dc + 1) * 128, tt:tt + 512])
            e_stt(seq, stage[:, :tw], psum, gac[:, dc:dc + 1], xc,
                  ALU.mult, ALU.add)
            seq.dma("sync", x3d[dc * 128:(dc + 1) * 128, t0:t0 + tw],
                    stage[:, :tw])

        emit_gemm_fm(seq, woT, attn, DM, 0, DM, TPC, wbuf, p_s, oproj_consume)
        seq.install(nc, sems, block)
    return nc



def build_launch_F2():
    nc = new_nc()
    x3d = nc.dram_tensor("x3", [DM, TPC], F32, kind="ExternalInput").ap()
    W1T = nc.dram_tensor("W1T", [DM, DFF1], F32, kind="ExternalInput").ap()
    W3T = nc.dram_tensor("W3T", [DM, DFF1], F32, kind="ExternalInput").ap()
    W2T = nc.dram_tensor("W2T", [DFF1, DM], F32, kind="ExternalInput").ap()
    modc = nc.dram_tensor("modc", [128, 2 * 6 * NDC], F32, kind="ExternalInput").ap()
    h128d = nc.dram_tensor("h128", [128, 128], BF16, kind="ExternalInput").ap()
    hi2d = nc.dram_tensor("hi2", [128, 128], BF16, kind="ExternalInput").ap()
    x4d = nc.dram_tensor("x4", [DM, TPC], F32, kind="ExternalOutput").ap()

    NF1 = DFF1 // 128
    with ExitStack() as ctx:
        b = Bufs(nc, ctx)
        x_sb = b.sb("x_sb", [128, NDC * TPC])
        h_sb = b.sb("h_sb", [128, NDC * TPC], BF16)
        hid = b.sb("hid", [128, NF1 * 512], BF16)
        mcol = b.sb("mcol", [128, 2 * 6 * NDC])
        ca1 = b.sb("ca1", [128, NDC])
        gbc = b.sb("gbc", [128, NDC])
        sgn = b.sb("sgn", [128, NDC])
        ones_col = b.sb("ones_col", [128, 1])
        ones_row = b.sb("ones_row", [1, 128])
        wbuf1 = [b.sb(f"wbuf1_{i}", [128, NDC * 128], BF16) for i in range(2)]
        wbuf2 = [b.sb(f"wbuf2_{i}", [128, NDC * 128], BF16) for i in range(2)]
        wbuf3 = [b.sb(f"wbuf3_{i}", [128, NF1 * 64], BF16) for i in range(2)]
        silu_s = b.sb("silu_s", [128, 512])
        h128c = b.sb("h128c", [128, 128], BF16)
        hi2c = b.sb("hi2c", [128, 128], BF16)
        identb = b.sb("identb", [128, 128], BF16)
        tr1 = [b.sb(f"tr1_{i}", [128, 128], BF16) for i in range(4)]
        tr2 = [b.sb(f"tr2_{i}", [128, 128], BF16) for i in range(4)]
        p_a = [b.ps(f"p_a{i}", [128, 512]) for i in range(2)]
        p_b2 = [b.ps(f"p_b2{i}", [128, 512]) for i in range(2)]
        p_o = b.ps("p_o", [128, 512])
        p_tb = b.ps("p_tbank", [128, 1024], BF16)
        p_t = [p_tb[:, i * 128:(i + 1) * 128] for i in range(8)]
        p_tmb = b.ps("p_tmbank", [128, 512])
        p_tm = [p_tmb[:, i * 128:(i + 1) * 128] for i in range(4)]
        lb = LNBufs(b, "f2", p_b=p_o[:, :], p_sum=p_a[0][0:1, 0:512],
                    p_sq=p_b2[0][0:1, 0:512],
                    mean_b=wbuf3[0][:, 0:1024].bitcast(F32),
                    rstd_b=wbuf3[0][:, 1024:2048].bitcast(F32),
                    t1=wbuf3[1][:, 0:1024].bitcast(F32))
        sems = mk_sems(b)
        block = b.block()
        seq = Seq()

        e_memset(seq, ones_col[:, :], 1.0)
        e_memset(seq, ones_row[:, :], 1.0)
        e_identity(seq, identb)
        seq.dma("sync", mcol[:, :], modc)
        seq.dma("sync", h128c[:, :], h128d)
        seq.dma("sync", hi2c[:, :], hi2d)
        for c in range(NDC):
            seq.dma("sync", x_sb[:, c * TPC:(c + 1) * TPC],
                    x3d[c * 128:(c + 1) * 128, :])
        e_ts(seq, ca1[:, :], mcol[:, 6 * NDC + 4 * NDC: 6 * NDC + 5 * NDC],
             1.0, None, ALU.add)
        e_clamp_gate(seq, gbc[:, :], mcol[:, 6 * NDC + 5 * NDC: 6 * NDC + 6 * NDC],
                     sgn[:, :])
        emit_ln_mod(seq, lb, x_sb, h_sb, ones_col, ones_row,
                    ca1, mcol[:, 6 * NDC + 3 * NDC: 6 * NDC + 4 * NDC])

        for t0 in range(0, TPC, 512):
            for fi in range(NF1):
                f0 = fi * 128
                pr = fi % 2
                w1c = wbuf1[pr]
                w2c = wbuf2[pr]
                dma_wslice_fm(seq, w1c, W1T, DM, f0, 128)
                dma_wslice_fm(seq, w2c, W3T, DM, f0, 128)
                for c in range(NDC):
                    e_mm(seq, p_a[pr][:, :], w1c[:, c * 128:(c + 1) * 128],
                         h_sb[:, c * TPC + t0: c * TPC + t0 + 512],
                         start=(c == 0), stop=(c == NDC - 1))
                for c in range(NDC):
                    e_mm(seq, p_b2[pr][:, :], w2c[:, c * 128:(c + 1) * 128],
                         h_sb[:, c * TPC + t0: c * TPC + t0 + 512],
                         start=(c == 0), stop=(c == NDC - 1))
                e_act(seq, silu_s[:, :], p_a[pr][:, :], AF.Silu)
                e_tt(seq, hid[:, fi * 512:(fi + 1) * 512], silu_s[:, :],
                     p_b2[pr][:, :], ALU.mult)
            for fc in range(NF1):
                pr = fc % 2
                e_mm(seq, p_a[pr][:, :], h128c[:, :],
                     hid[:, fc * 512:(fc + 1) * 512])
                e_copy(seq, hid[:, fc * 512:(fc + 1) * 512], p_a[pr][:, :])
            for tp in range(256):
                pr = tp % 4
                tview = hid[:, :].rearrange("p (c t) -> p c t", t=512)[
                    :, :, 2 * tp: 2 * tp + 2]
                e_copy(seq, tr1[pr][:, :].rearrange("p (c t) -> p c t", t=2),
                       tview)
                e_tr(seq, p_t[(2 * tp) % 8], tr1[pr][:, :], identb[:, :])
                e_copy(seq, tr1[pr][:, :], p_t[(2 * tp) % 8])
                e_mm(seq, p_tm[pr], hi2c[:, :], tr1[pr][:, :])
                e_ts(seq, tr2[pr][:, :], p_tm[pr],
                     float(1.0 / math.sqrt(DFF1)), None, ALU.mult)
                e_tr(seq, p_t[(2 * tp + 1) % 8], tr2[pr][:, :], identb[:, :])
                e_copy(seq, tview, p_t[(2 * tp + 1) % 8])
            for dc in range(NDC):
                for hh in range(2):
                    w3c = wbuf3[(dc * 2 + hh) % 2]
                    src3 = W2T.rearrange("(c p) f -> p c f", p=128)[
                        :, hh * 32:(hh + 1) * 32, dc * 128:(dc + 1) * 128]
                    dst3 = w3c[:, :].rearrange("p (c f) -> p c f", f=128)
                    seq.dma("gpsimd", dst3, src3)
                    for fl in range(32):
                        fc = hh * 32 + fl
                        e_mm(seq, p_o[:, :], w3c[:, fl * 128:(fl + 1) * 128],
                             hid[:, fc * 512:(fc + 1) * 512],
                             start=(fc == 0), stop=(fc == NF1 - 1))
                xc = x_sb[:, dc * TPC + t0: dc * TPC + t0 + 512]
                e_stt(seq, xc, p_o[:, :], gbc[:, dc:dc + 1], xc, ALU.mult, ALU.add)
        for c in range(NDC):
            seq.dma("sync", x4d[c * 128:(c + 1) * 128, :],
                    x_sb[:, c * TPC:(c + 1) * TPC])
        seq.install(nc, sems, block)
    return nc



def build_launch_C():
    nc = new_nc()
    x0d = nc.dram_tensor("x0", [DM, TPC], F32, kind="ExternalInput").ap()
    xhd = nc.dram_tensor("xh", [DM, 4], F32, kind="ExternalInput").ap()
    wipT = nc.dram_tensor("wipT", [DM, PROJ_DIM], F32, kind="ExternalInput").ap()
    cwd = nc.dram_tensor("convw", [128, 34 * KCONV], F32, kind="ExternalInput").ap()
    cbd = nc.dram_tensor("convb", [128, 34], F32, kind="ExternalInput").ap()
    dtbd = nc.dram_tensor("dtb", [NHEAD, 1], F32, kind="ExternalInput").ap()
    alogd = nc.dram_tensor("alog", [NHEAD, 1], F32, kind="ExternalInput").ap()
    drowd = nc.dram_tensor("drow", [1, NHEAD], F32, kind="ExternalInput").ap()
    modc = nc.dram_tensor("modc", [128, 2 * 6 * NDC], F32, kind="ExternalInput").ap()
    hmaskd = nc.dram_tensor("hmask", [128, 1], F32, kind="ExternalInput").ap()
    gate_d = nc.dram_tensor("gate", [TPC, INTER], BF16, kind="ExternalOutput").ap()
    yp_d = nc.dram_tensor("yp", [TPC, INTER], F32, kind="ExternalOutput").ap()
    st_d = nc.dram_tensor("cstates", [NCHUNK, NHEAD, ND, PD], BF16,
                          kind="ExternalOutput").ap()
    sum_d = nc.dram_tensor("csum", [NHEAD, ND, PD], BF16, kind="ExternalOutput").ap()
    lt_d = nc.dram_tensor("clt", [NHEAD, 1], F32, kind="ExternalOutput").ap()
    acum_d = nc.dram_tensor("acum", [NHEAD, TPC], F32, kind="ExternalOutput").ap()
    cfm_d = nc.dram_tensor("cfm", [ND, TPC], BF16, kind="ExternalOutput").ap()

    NHB = CONV_DIM // 128
    with ExitStack() as ctx:
        b = Bufs(nc, ctx)
        big0 = b.sb("big0", [128, NDC * TPC])
        h_sb = b.sb("h_sb", [128, NDC * TPC], BF16)
        arena = b.sb("arena", [128, 32 * TPC], BF16)
        hh_sb = b.sb("hh_sb", [128, 4 * NDC], BF16)
        xh_sb = b.sb("xh_sb", [128, 4 * NDC])
        B_fm = b.sb("B_fm", [128, TPC], BF16)
        C_fm = b.sb("C_fm", [128, TPC], BF16)
        B_tm = b.sb("B_tm", [128, 8 * ND], BF16)
        dtr = b.sb("dtr", [NHEAD, TPC])
        acum = b.sb("acum_sb", [NHEAD, TPC])
        hbc = b.sb("hbc", [128, TPC + 4])
        dtT = b.sb("dtT", [128, 8 * NHEAD])
        decT = b.sb("decT", [128, 8 * NHEAD])
        nacT = b.sb("nacT", [128, 8 * NHEAD])
        ltc = b.sb("ltc", [NHEAD, 1])
        cvw = b.sb("cvw", [128, 34 * KCONV])
        cvb = b.sb("cvb", [128, 34])
        hmask = b.sb("hmaskc", [128, 1])
        dcol = b.sb("dcolc", [NHEAD, 1])
        acol = b.sb("acolc", [NHEAD, 1])
        mcol = b.sb("mcolc", [128, 2 * 6 * NDC])
        ca1 = b.sb("ca1c", [128, NDC])
        wbuf = b.sb("wbufc", [128, NDC * 128], BF16)

        ones_col = b.sb("ones_colc", [128, 1])
        ones_row = b.sb("ones_rowc", [1, 128])
        identf = b.sb("identfc", [128, 128])
        identb2 = b.sb("identbc", [128, 128], BF16)
        lb = LNBufs(b, "c")
        p_g = b.ps("p_gc", [128, 512])
        p_misc = b.ps("p_miscc", [128, 512])
        p_y = b.ps("p_yc", [128, 128])
        p_st = b.ps("p_stc", [128, PD])
        p_tb = b.ps("p_tbc", [128, 128], BF16)
        sems = mk_sems(b)
        block = b.block()
        seq = Seq()

        zrow = lb.msq_row[0:1, 0:CH]
        stg = lb.t1[:, :].bitcast(BF16)[:, 0:512]
        ar_f = arena[:, :].bitcast(F32)
        ytile = [ar_f[:, 0:2048], ar_f[:, 2048:4096]]
        ar_b = arena[:, :]
        run = ar_b[:, 16384:16384 + NHEAD * PD]
        ststg = ar_b[:, 20480:20480 + 16 * PD]
        bdec = ar_b[:, 21504:21504 + ND]
        mt_sb = ar_b[:, 21632:21632 + 2 * CH]
        lt_sb = ar_b[:, 22144:22144 + CH]
        gt_sb = ar_f[:, 11200:11200 + 2 * CH]
        dchb = ar_f[:, 11712:11712 + NHEAD]
        Db = ar_f[:, 11776:11776 + NHEAD]
        adt = hbc[0:NHEAD, 0:TPC]
        t64 = b.sb("t64c", [NHEAD, CH])
        zer = b.sb("zerc", [NHEAD, CH])
        hs_tm = big0[:, :].bitcast(BF16)

        e_memset(seq, ones_col[:, :], 1.0)
        e_memset(seq, ones_row[:, :], 1.0)
        e_identity(seq, identf)
        e_identity(seq, identb2)
        seq.dma("sync", mcol[:, :], modc)
        seq.dma("sync", cvw[:, :], cwd)
        seq.dma("sync", cvb[:, :], cbd)
        seq.dma("sync", hmask[:, :], hmaskd)
        seq.dma("sync", dcol[:, :], dtbd)
        seq.dma("sync", acol[:, :], alogd)
        seq.dma("sync", xh_sb[:, :].rearrange("p (c t) -> p c t", t=4),
                xhd.rearrange("(c p) t -> p c t", p=128))
        for c in range(NDC):
            seq.dma("sync", big0[:, c * TPC:(c + 1) * TPC],
                    x0d[c * 128:(c + 1) * 128, :])
        e_ts(seq, ca1[:, :], mcol[:, NDC:2 * NDC], 1.0, None, ALU.add)
        emit_ln_mod(seq, lb, big0, h_sb, ones_col, ones_row, ca1, mcol[:, 0:NDC])
        emit_ln_mod(seq, lb, xh_sb, hh_sb, ones_col, ones_row, ca1,
                    mcol[:, 0:NDC], toks=4)

        wbufg = ar_b[:, 0:NDC * 512]
        for f0 in range(0, INTER, 512):
            src = wipT.rearrange("(c p) f -> p c f", p=128)[:, :, f0:f0 + 512]
            seq.dma("gpsimd", wbufg.rearrange("p (c f) -> p c f", f=512), src)
            for t0 in range(0, TPC, 128):
                for c in range(NDC):
                    e_mm(seq, p_g[:, :], h_sb[:, c * TPC + t0: c * TPC + t0 + 128],
                         wbufg[:, c * 512:(c + 1) * 512],
                         start=(c == 0), stop=(c == NDC - 1))
                e_copy(seq, stg[:, :], p_g[:, :])
                seq.dma("sync", gate_d[t0:t0 + 128, f0:f0 + 512], stg[:, :])

        for fi in range(NHB):
            f0 = INTER + fi * 128
            dma_wslice_fm(seq, wbuf, wipT, DM, f0, 128)
            for c in range(NDC):
                e_mm(seq, p_misc[:, 0:4], wbuf[:, c * 128:(c + 1) * 128],
                     hh_sb[:, :].rearrange("p (c t) -> p c t", t=4)[:, c, :],
                     start=(c == 0), stop=(c == NDC - 1))
            e_ts(seq, hbc[:, 0:4], p_misc[:, 0:4], hmask[:, 0:1], None, ALU.mult)
            for t0 in range(0, TPC, 512):
                for c in range(NDC):
                    e_mm(seq, p_g[:, :], wbuf[:, c * 128:(c + 1) * 128],
                         h_sb[:, c * TPC + t0: c * TPC + t0 + 512],
                         start=(c == 0), stop=(c == NDC - 1))
                e_copy(seq, hbc[:, 3 + t0: 3 + t0 + 512], p_g[:, :])
            acc = lb.mean_b
            for t0 in range(0, TPC, 512):
                e_ts(seq, acc[:, :], hbc[:, t0:t0 + 512],
                     cvw[:, fi * KCONV: fi * KCONV + 1], None, ALU.mult)
                for k in range(1, KCONV):
                    e_stt(seq, acc[:, :], hbc[:, t0 + k: t0 + k + 512],
                          cvw[:, fi * KCONV + k: fi * KCONV + k + 1],
                          acc[:, :], ALU.mult, ALU.add)
                e_ts(seq, acc[:, :], acc[:, :], cvb[:, fi:fi + 1], None, ALU.add)
                dst = (arena[:, fi * TPC + t0: fi * TPC + t0 + 512] if fi < 32
                       else (B_fm if fi == 32 else C_fm)[:, t0:t0 + 512])
                e_act(seq, dst, acc[:, :], AF.Silu)
        seq.dma("sync", cfm_d, C_fm[:, :])

        dma_wslice_fm(seq, wbuf, wipT, DM, INTER + CONV_DIM, 64)
        for t0 in range(0, TPC, 512):
            for c in range(NDC):
                e_mm(seq, p_g[0:NHEAD, :], wbuf[:, c * 64:(c + 1) * 64],
                     h_sb[:, c * TPC + t0: c * TPC + t0 + 512],
                     start=(c == 0), stop=(c == NDC - 1))
            e_copy(seq, dtr[:, t0:t0 + 512], p_g[0:NHEAD, :])
        e_ts(seq, dtr[:, :], dtr[:, :], dcol[:, :], None, ALU.add)
        e_act(seq, adt, dtr[:, :], AF.Abs)
        e_act(seq, adt, adt, AF.Exp, scale=-1.0)
        e_ts(seq, adt, adt, 1.0, None, ALU.add)
        e_act(seq, adt, adt, AF.Ln)
        e_ts(seq, dtr[:, :], dtr[:, :], 0.0, None, ALU.max)
        e_tt(seq, dtr[:, :], dtr[:, :], adt, ALU.add)
        e_ts(seq, dtr[:, :], dtr[:, :], DT_MIN, DT_MAX, ALU.max, ALU.min)
        e_act(seq, acol[:, :], acol[:, :], AF.Exp)
        e_ts(seq, acol[:, :], acol[:, :], -1.0, None, ALU.mult)
        e_ts(seq, adt, dtr[:, :], acol[:, :], None, ALU.mult)
        e_memset(seq, zer[:, :], 0.0)
        for c in range(NCHUNK):
            seq.emit("vector", lambda e, c=c: e.tensor_tensor_scan(
                acum[:, c * CH:(c + 1) * CH], adt[:, c * CH:(c + 1) * CH],
                zer[:, :], 0.0, ALU.add, ALU.add),
                reads=(adt[:, c * CH:(c + 1) * CH], zer[:, :]),
                writes=(acum[:, c * CH:(c + 1) * CH],))
        seq.dma("sync", acum_d, acum[:, :])
        e_copy(seq, ltc[:, :], acum[:, CH - 1:CH])
        for c in range(1, NCHUNK):
            e_tt(seq, ltc[:, :], ltc[:, :],
                 acum[:, (c + 1) * CH - 1:(c + 1) * CH], ALU.add)
        seq.dma("sync", lt_d, ltc[:, :])

        for c in range(NCHUNK):
            e_act(seq, t64[:, :], acum[:, c * CH:(c + 1) * CH], AF.Exp,
                  bias=acum[:, (c + 1) * CH - 1:(c + 1) * CH], scale=-1.0)
            for hf in range(2):
                col = (2 * c + hf) * NHEAD
                e_tr(seq, p_misc[:, 0:NHEAD],
                     dtr[:, c * CH + hf * 128: c * CH + (hf + 1) * 128],
                     identf[0:NHEAD, 0:NHEAD])
                e_copy(seq, dtT[:, col:col + NHEAD], p_misc[:, 0:NHEAD])
                e_tr(seq, p_misc[:, 0:NHEAD], t64[:, hf * 128:(hf + 1) * 128],
                     identf[0:NHEAD, 0:NHEAD])
                e_copy(seq, decT[:, col:col + NHEAD], p_misc[:, 0:NHEAD])
                e_tr(seq, p_misc[:, 0:NHEAD],
                     acum[:, c * CH + hf * 128: c * CH + (hf + 1) * 128],
                     identf[0:NHEAD, 0:NHEAD])
                e_ts(seq, nacT[:, col:col + NHEAD], p_misc[:, 0:NHEAD],
                     -1.0, None, ALU.mult)

        identb = identb2[:, :]
        pt_b = p_tb[:, :]
        for tc in range(8):
            for fc in range(32):
                e_tr(seq, pt_b,
                     arena[:, fc * TPC + tc * 128: fc * TPC + (tc + 1) * 128],
                     identb)
                e_copy(seq, hs_tm[:, tc * 4096 + fc * 128:
                                  tc * 4096 + (fc + 1) * 128], pt_b)
            e_tr(seq, pt_b, B_fm[:, tc * 128:(tc + 1) * 128], identb)
            e_copy(seq, B_tm[:, tc * ND:(tc + 1) * ND], pt_b)

        seq.dma("sync", zrow[:, 0:NHEAD], drowd)
        e_mm(seq, p_misc[:, 0:NHEAD], ones_row[:, :], zrow[:, 0:NHEAD])
        e_copy(seq, Db, p_misc[:, 0:NHEAD])

        e_memset(seq, run, 0.0)
        for c in range(NCHUNK):
            for lh in range(2):
                e_mm(seq, p_misc[:, 0:CH],
                     B_fm[:, c * CH + lh * 128: c * CH + (lh + 1) * 128],
                     C_fm[:, c * CH:(c + 1) * CH])
                e_copy(seq, gt_sb[:, lh * CH:(lh + 1) * CH], p_misc[:, 0:CH])
            e_act(seq, ltc[:, :], acum[:, (c + 1) * CH - 1:(c + 1) * CH], AF.Exp)
            e_tr(seq, p_misc[0:1, 0:NHEAD], ltc[:, :],
                 identf[0:NHEAD, 0:NHEAD])
            e_copy(seq, zrow[:, 0:NHEAD], p_misc[0:1, 0:NHEAD])
            e_mm(seq, p_misc[:, 0:NHEAD], ones_row[:, :], zrow[:, 0:NHEAD])
            e_copy(seq, dchb, p_misc[:, 0:NHEAD])
            for h in range(NHEAD):
                for zf in range(2):
                    col = (2 * c + zf) * NHEAD + h
                    e_tr(seq, p_g[0:1, 0:128], nacT[:, col:col + 1],
                         identf[:, :])
                    e_ts(seq, zrow[:, zf * 128:(zf + 1) * 128], p_g[0:1, 0:128],
                         -1.0, None, ALU.mult)
                e_mm(seq, p_misc[:, 0:CH], ones_row[:, :], zrow[:, 0:CH])
                for lh in range(2):
                    col = (2 * c + lh) * NHEAD + h
                    e_act(seq, lt_sb, p_misc[:, 0:CH], AF.Exp,
                          bias=nacT[:, col:col + 1])
                    seq.emit("gpsimd", lambda e, lh=lh: e.affine_select(
                        out=lt_sb, in_=lt_sb, compare_op=ALU.is_ge,
                        fill=0.0, base=-(lh * 128), pattern=[[1, CH]],
                        channel_multiplier=-1),
                        reads=(lt_sb,), writes=(lt_sb,))
                    e_stt(seq, mt_sb[:, lh * CH:(lh + 1) * CH],
                          gt_sb[:, lh * CH:(lh + 1) * CH], dtT[:, col:col + 1],
                          lt_sb, ALU.mult, ALU.mult)
                hs0 = hs_tm[:, (2 * c) * 4096 + h * PD: (2 * c) * 4096 + h * PD + PD]
                hs1 = hs_tm[:, (2 * c + 1) * 4096 + h * PD:
                            (2 * c + 1) * 4096 + h * PD + PD]
                e_mm(seq, p_y[:, 0:PD], mt_sb[:, 0:128], hs0)
                e_mm(seq, p_y[:, PD:2 * PD], mt_sb[:, 128:CH], hs0, stop=False)
                e_mm(seq, p_y[:, PD:2 * PD], mt_sb[:, CH + 128:2 * CH], hs1,
                     start=False)
                for lh in range(2):
                    col = (2 * c + lh) * NHEAD + h
                    e_ts(seq, bdec, B_tm[:, (2 * c + lh) * ND:(2 * c + lh + 1) * ND],
                         decT[:, col:col + 1], dtT[:, col:col + 1],
                         ALU.mult, ALU.mult)
                    e_mm(seq, p_st[:, :], bdec, (hs0 if lh == 0 else hs1),
                         start=(lh == 0), stop=(lh == 1))
                hq = h % 16
                e_copy(seq, ststg[:, hq * PD:(hq + 1) * PD], p_st[:, :])
                e_stt(seq, run[:, h * PD:(h + 1) * PD], run[:, h * PD:(h + 1) * PD],
                      dchb[:, h:h + 1], ststg[:, hq * PD:(hq + 1) * PD],
                      ALU.mult, ALU.add)
                if hq == 15:
                    seq.dma("sync",
                            st_d[c, h - 15:h + 1].transpose([1, 0, 2]),
                            ststg[:, :].rearrange("n (h p) -> n h p", p=PD))
                hw = h % 32
                for zh in range(2):
                    e_stt(seq, ytile[zh][:, hw * PD:(hw + 1) * PD],
                          (hs0 if zh == 0 else hs1), Db[:, h:h + 1],
                          p_y[:, zh * PD:(zh + 1) * PD], ALU.mult, ALU.add)
                if hw == 31:
                    g0 = h - 31
                    for zh in range(2):
                        seq.dma("sync",
                                yp_d[c * CH + zh * 128: c * CH + (zh + 1) * 128,
                                     g0 * PD:(g0 + 32) * PD], ytile[zh])
        for g in range(4):
            seq.dma("sync",
                    sum_d[g * 16:(g + 1) * 16].transpose([1, 0, 2]),
                    run[:, g * 1024:(g + 1) * 1024].rearrange(
                        "n (h p) -> n h p", p=PD))
        seq.install(nc, sems, block)
    return nc



def build_launch_D():
    nc = new_nc()
    x0d = nc.dram_tensor("x0", [DM, TPC], F32, kind="ExternalInput").ap()
    ypd = nc.dram_tensor("yp", [TPC, INTER], F32, kind="ExternalInput").ap()
    gated = nc.dram_tensor("gate", [TPC, INTER], BF16, kind="ExternalInput").ap()
    cfmd = nc.dram_tensor("cfm", [ND, TPC], BF16, kind="ExternalInput").ap()
    acumd = nc.dram_tensor("acum", [NHEAD, TPC], F32, kind="ExternalInput").ap()
    cstd = nc.dram_tensor("cstates", [NCHUNK, NHEAD, ND, PD], BF16,
                          kind="ExternalInput").ap()
    csall = nc.dram_tensor("csumall", [NCORE, NHEAD, ND, PD], BF16,
                           kind="ExternalInput").ap()
    cltall = nc.dram_tensor("cltall", [NCORE, NHEAD], F32,
                            kind="ExternalInput").ap()
    miTd = nc.dram_tensor("miT", [NCORE, NCORE], F32, kind="ExternalInput").ap()
    incd = nc.dram_tensor("incm", [NCORE, 1], F32, kind="ExternalInput").ap()
    wopT = nc.dram_tensor("wopT", [INTER, DM], F32, kind="ExternalInput").ap()
    nwT = nc.dram_tensor("nwT", [128, 32], F32, kind="ExternalInput").ap()
    modc = nc.dram_tensor("modc", [128, 2 * 6 * NDC], F32, kind="ExternalInput").ap()
    x1d = nc.dram_tensor("x1", [DM, TPC], F32, kind="ExternalOutput").ap()
    coef_bounce = nc.dram_tensor("coef_bounce", [NCORE, NHEAD], F32)

    with ExitStack() as ctx:
        b = Bufs(nc, ctx)
        run = b.sb("run_d", [128, NHEAD * PD], BF16)
        sgbuf = b.sb("sgbuf", [128, NHEAD * PD], BF16)
        ybuf = b.sb("ybuf", [128, INTER])
        gbuf = b.sb("gbuf", [128, INTER], BF16)
        yn_tm = b.sb("yn_tm", [128, INTER], BF16)
        yn_fm = b.sb("yn_fm", [128, 32 * TPC], BF16)
        cfm = b.sb("cfm_sb", [128, TPC], BF16)
        acum = b.sb("acum_sbd", [NHEAD, TPC])
        sdoT = b.sb("sdoT", [128, 8 * NHEAD])
        x0c = b.sb("x0c", [128, TPC])
        wbuf = b.sb("wbufd", [128, 32 * 128], BF16)
        ltall = b.sb("ltall", [NCORE, NHEAD])
        miT = b.sb("miT_sb", [NCORE, NCORE])
        incm = b.sb("incm_sb", [NCORE, 1])
        coef = b.sb("coef_sb", [NCORE, NHEAD])
        coefrow = b.sb("coefrow", [1, NCORE * NHEAD])
        cbj = b.sb("cbj", [128, NHEAD], BF16)
        dchb = b.sb("dchb_d", [128, NHEAD], BF16)
        ltc = b.sb("ltc_d", [NHEAD, 1])
        zrow = b.sb("zrow_d", [1, 128])
        nwc = b.sb("nwc", [128, 32])
        mcol = b.sb("mcold", [128, 2 * 6 * NDC])
        gac = b.sb("gacd", [128, NDC])
        sgn = b.sb("sgnd", [128, NDC])
        sq = b.sb("sq_d", [128, 512])
        ss = b.sb("ss_d", [128, 1])
        sst = b.sb("sst_d", [128, 1])
        ones_row = b.sb("ones_rowd", [1, 128])
        identf = b.sb("identfd", [128, 128])
        identb2 = b.sb("identbd", [128, 128], BF16)
        stg = b.sb("stg_d", [128, 512])
        p_yo = b.ps("p_yod", [128, 128])
        p_b = b.ps("p_bd", [128, 512])
        p_o = b.ps("p_od", [128, 512])
        p_tb = b.ps("p_tbd", [128, 128], BF16)
        sems = mk_sems(b)
        block = b.block()
        seq = Seq()

        e_memset(seq, ones_row[:, :], 1.0)
        e_identity(seq, identf)
        e_identity(seq, identb2)
        seq.dma("sync", mcol[:, :], modc)
        seq.dma("sync", nwc[:, :], nwT)
        seq.dma("sync", ltall[:, :], cltall)
        seq.dma("sync", miT[:, :], miTd)
        seq.dma("sync", incm[:, :], incd)
        seq.dma("sync", cfm[:, :], cfmd)
        seq.dma("sync", acum[:, :], acumd)
        e_clamp_gate(seq, gac[:, :], mcol[:, 2 * NDC:3 * NDC], sgn[:, :])

        e_mm(seq, p_b[0:NCORE, 0:NHEAD], miT[:, :], ltall[:, :])
        e_act(seq, coef[:, :], p_b[0:NCORE, 0:NHEAD], AF.Exp)
        e_ts(seq, coef[:, :], coef[:, :], incm[:, 0:1], None, ALU.mult)
        seq.dma("sync", coef_bounce.ap(), coef[:, :])
        seq.dma("sync", coefrow[:, :],
                coef_bounce.ap().rearrange("a b -> (a b)")[None, :])

        e_memset(seq, run[:, :], 0.0)
        for j in range(NCORE):
            seq.dma("sync", sgbuf[:, :].rearrange("n (h p) -> n h p", p=PD),
                    csall[j].transpose([1, 0, 2]))
            e_mm(seq, p_b[:, 0:NHEAD], ones_row[:, :],
                 coefrow[:, j * NHEAD:(j + 1) * NHEAD])
            e_copy(seq, cbj[:, :], p_b[:, 0:NHEAD])
            for h in range(NHEAD):
                e_stt(seq, run[:, h * PD:(h + 1) * PD],
                      sgbuf[:, h * PD:(h + 1) * PD], cbj[:, h:h + 1],
                      run[:, h * PD:(h + 1) * PD], ALU.mult, ALU.add)

        for c in range(NCHUNK):
            for hf in range(2):
                col = (2 * c + hf) * NHEAD
                e_tr(seq, p_yo[:, 0:NHEAD],
                     acum[:, c * CH + hf * 128: c * CH + (hf + 1) * 128],
                     identf[0:NHEAD, 0:NHEAD])
                e_act(seq, sdoT[:, col:col + NHEAD], p_yo[:, 0:NHEAD], AF.Exp)

        for c in range(NCHUNK):
            for lh in range(2):
                tok0 = c * CH + lh * 128
                seq.dma("sync", ybuf[:, :], ypd[tok0:tok0 + 128, :])
                seq.dma("sync", gbuf[:, :], gated[tok0:tok0 + 128, :])
                for hb in range(NHEAD // 8):
                    e_mm(seq, p_b[:, :], cfm[:, tok0:tok0 + 128],
                         run[:, hb * 512:(hb + 1) * 512])
                    for hl in range(8):
                        h = hb * 8 + hl
                        e_stt(seq, ybuf[:, h * PD:(h + 1) * PD],
                              p_b[:, hl * PD:(hl + 1) * PD],
                              sdoT[:, (2 * c + lh) * NHEAD + h:
                                   (2 * c + lh) * NHEAD + h + 1],
                              ybuf[:, h * PD:(h + 1) * PD], ALU.mult, ALU.add)
                for s0 in range(0, INTER, 512):
                    e_act(seq, sq[:, :], gbuf[:, s0:s0 + 512], AF.Silu)
                    e_tt(seq, ybuf[:, s0:s0 + 512], ybuf[:, s0:s0 + 512],
                         sq[:, :], ALU.mult)
                    e_act(seq, sq[:, :], ybuf[:, s0:s0 + 512], AF.Square,
                          accum_out=sst[:, :])
                    if s0 == 0:
                        e_copy(seq, ss[:, :], sst[:, :])
                    else:
                        e_tt(seq, ss[:, :], ss[:, :], sst[:, :], ALU.add)
                e_ts(seq, ss[:, :], ss[:, :], 1.0 / INTER, MEPS, ALU.mult, ALU.add)
                e_act(seq, ss[:, :], ss[:, :], AF.Sqrt)
                e_recip(seq, ss[:, :], ss[:, :])
                for s0 in range(0, INTER, 512):
                    e_ts(seq, yn_tm[:, s0:s0 + 512], ybuf[:, s0:s0 + 512],
                         ss[:, 0:1], None, ALU.mult)
                for fc in range(32):
                    e_tr(seq, p_tb[:, :], yn_tm[:, fc * 128:(fc + 1) * 128],
                         identb2[:, :])
                    e_ts(seq, yn_fm[:, fc * TPC + tok0: fc * TPC + tok0 + 128],
                         p_tb[:, :], nwc[:, fc:fc + 1], None, ALU.mult)
            e_act(seq, ltc[:, :], acum[:, (c + 1) * CH - 1:(c + 1) * CH], AF.Exp)
            e_tr(seq, p_yo[0:1, 0:NHEAD], ltc[:, :],
                 identf[0:NHEAD, 0:NHEAD])
            e_copy(seq, zrow[:, 0:NHEAD], p_yo[0:1, 0:NHEAD])
            e_mm(seq, p_b[:, 0:NHEAD], ones_row[:, :], zrow[:, 0:NHEAD])
            e_copy(seq, dchb[:, :], p_b[:, 0:NHEAD])
            seq.dma("sync", sgbuf[:, :].rearrange("n (h p) -> n h p", p=PD),
                    cstd[c].transpose([1, 0, 2]))
            for h in range(NHEAD):
                e_stt(seq, run[:, h * PD:(h + 1) * PD],
                      run[:, h * PD:(h + 1) * PD], dchb[:, h:h + 1],
                      sgbuf[:, h * PD:(h + 1) * PD], ALU.mult, ALU.add)

        def op_consume(d0, t0, tw, psum):
            dc = d0 // 128
            if t0 == 0:
                for tt in range(0, TPC, 512):
                    seq.dma("sync", x0c[:, tt:tt + 512],
                            x0d[dc * 128:(dc + 1) * 128, tt:tt + 512])
            e_stt(seq, stg[:, :tw], psum, gac[:, dc:dc + 1], x0c[:, t0:t0 + tw],
                  ALU.mult, ALU.add)
            seq.dma("sync", x1d[dc * 128:(dc + 1) * 128, t0:t0 + tw], stg[:, :tw])

        NIC = INTER // 128
        for d0 in range(0, DM, 128):
            dma_wslice_fm(seq, wbuf, wopT, INTER, d0, 128)
            for t0 in range(0, TPC, 512):
                for c in range(NIC):
                    e_mm(seq, p_o[:, :], wbuf[:, c * 128:(c + 1) * 128],
                         yn_fm[:, c * TPC + t0: c * TPC + t0 + 512],
                         start=(c == 0), stop=(c == NIC - 1))
                op_consume(d0, t0, 512, p_o[:, :])
        seq.install(nc, sems, block)
    return nc



_PROGS = {}


def _prog(name, builder):
    if name not in _PROGS:
        _PROGS[name] = builder()
    return _PROGS[name]


def _hadamard(n):
    H = np.array([[1.0]], np.float32)
    while H.shape[0] < n:
        H = np.block([[H, H], [H, -H]])
    return H


def kernel(**inputs):
    f32 = np.float32
    CORES = list(range(NCORE))
    ca = np.ascontiguousarray

    x_all = ca(np.asarray(inputs["L0_attn_input_states"], f32)[0])
    w1 = np.asarray(inputs["L0_ffn_weight1"])
    w2 = np.asarray(inputs["L0_ffn_weight2"])
    w3 = np.asarray(inputs["L0_ffn_weight3"])

    inA = []
    for core in CORES:
        m = {}
        for L in range(2):
            cvec = np.asarray(inputs[f"L{L}_c"], f32).reshape(16, 128)
            m[f"c{L}T"] = ca(cvec.T)
            wm = np.asarray(inputs[f"L{L}_wmod"], f32)
            m[f"wmod{L}s"] = ca(wm[:, core * 1536:(core + 1) * 1536])
        m["w1s"] = ca(w1[core * W1_ROWS:(core + 1) * W1_ROWS])
        m["w2s"] = ca(w2[core * W1_ROWS:(core + 1) * W1_ROWS])
        m["w3s"] = ca(w3[core * W3_ROWS:(core + 1) * W3_ROWS])
        inA.append(m)
    resA = run_bass_kernel_spmd(_prog("A", build_launch_A), inA, core_ids=CORES)
    mod_full = np.zeros((2, 12288), f32)
    for c in CORES:
        mod_full[:, c * 1536:(c + 1) * 1536] = resA.results[c]["modrow"]
    mc = np.zeros((128, 192), f32)
    for L in range(2):
        mc[:, L * 96:(L + 1) * 96] = (
            mod_full[L].reshape(6, 16, 128).transpose(2, 0, 1).reshape(128, 96))
    ssums = ca(np.stack([resA.results[c]["ssum"][0] for c in CORES]))

    inB = [dict(w1s=inA[c]["w1s"], w2s=inA[c]["w2s"], w3s=inA[c]["w3s"],
                allsum=ssums) for c in CORES]
    resB = run_bass_kernel_spmd(_prog("B", build_launch_B), inB, core_ids=CORES)
    w1pT = ca(np.concatenate([resB.results[c]["wq1"] for c in CORES], axis=0).T)
    w2pT = ca(np.concatenate([resB.results[c]["wq2"] for c in CORES], axis=0).T)
    w3pT = ca(np.concatenate([resB.results[c]["wq3"] for c in CORES], axis=0).T)

    cw = np.asarray(inputs["L0_attn_conv_w"], f32).reshape(CONV_DIM, KCONV)
    cwc = ca(cw.reshape(34, 128, KCONV).transpose(1, 0, 2).reshape(128, 34 * KCONV))
    cbc = ca(np.asarray(inputs["L0_attn_conv_b"], f32).reshape(34, 128).T)
    wipT = ca(np.asarray(inputs["L0_attn_in_proj_w"], f32).T)
    dtb = ca(np.asarray(inputs["L0_attn_dt_bias"], f32).reshape(NHEAD, 1))
    alog = ca(np.asarray(inputs["L0_attn_A_log"], f32).reshape(NHEAD, 1))
    drow = ca(np.asarray(inputs["L0_attn_D"], f32).reshape(1, NHEAD))
    x0s = [ca(x_all[c * TPC:(c + 1) * TPC].T) for c in CORES]
    inC = []
    for core in CORES:
        xh = np.zeros((DM, KCONV), f32)
        if core > 0:
            xh[:, :] = x_all[core * TPC - 3: core * TPC + 1].T
        hm = np.full((128, 1), 0.0 if core == 0 else 1.0, f32)
        inC.append(dict(x0=x0s[core], xh=ca(xh), wipT=wipT, convw=cwc,
                        convb=cbc, dtb=dtb, alog=alog, drow=drow, modc=mc,
                        hmask=hm))
    resC = run_bass_kernel_spmd(_prog("C", build_launch_C), inC, core_ids=CORES)

    csumall = ca(np.stack([resC.results[c]["csum"] for c in CORES]))
    cltall = ca(np.stack([resC.results[c]["clt"][:, 0] for c in CORES]))
    wopT = ca(np.asarray(inputs["L0_attn_out_proj_w"], f32).T)
    nwT = ca(np.asarray(inputs["L0_attn_norm_w"], f32).reshape(32, 128).T)
    inD = []
    for core in CORES:
        miT = np.zeros((NCORE, NCORE), f32)
        incm = np.zeros((NCORE, 1), f32)
        for j in range(NCORE):
            if j < core:
                incm[j, 0] = 1.0
            for k in range(NCORE):
                if j < k < core:
                    miT[k, j] = 1.0
        r = resC.results[core]
        inD.append(dict(x0=x0s[core], yp=r["yp"], gate=r["gate"], cfm=r["cfm"],
                        acum=r["acum"], cstates=r["cstates"], csumall=csumall,
                        cltall=cltall, miT=miT, incm=incm, wopT=wopT, nwT=nwT,
                        modc=mc))
    resD = run_bass_kernel_spmd(_prog("D", build_launch_D), inD, core_ids=CORES)

    wqkvT = ca(np.asarray(inputs["L1_attn_w_qkv"], f32).T)
    RL = np.zeros((128, 128), f32)
    for i in range(64):
        RL[i + 64, i] = -1.0
        RL[i, i + 64] = 1.0
    RLb = RL.astype(bf16)
    inE = []
    for core in CORES:
        pos = ca((np.arange(TPC, dtype=f32) + core * TPC)[None, :])
        inE.append(dict(x1=resD.results[core]["x1"], w1pT=w1pT, w2pT=w2pT,
                        w3pT=w3pT, wqkvT=wqkvT, modc=mc, rrot=RLb, posr=pos))
    resE = run_bass_kernel_spmd(_prog("E", build_launch_E), inE, core_ids=CORES)

    kTall = ca(np.concatenate([resE.results[c]["kT"] for c in CORES], axis=2))
    vt_full = np.concatenate([resE.results[c]["vt"] for c in CORES], axis=0)
    vall = ca(vt_full.reshape(SEQ, AHEADS, HD).transpose(1, 0, 2))
    qkmaxall = ca(np.concatenate([resE.results[c]["qkmax"] for c in CORES],
                                 axis=0))
    woT = ca(np.asarray(inputs["L1_attn_w_out"], f32).T)
    inF1 = [dict(x2=resE.results[c]["x2"], qT=resE.results[c]["qT"],
                 kTall=kTall, vall=vall, woT=woT, modc=mc, qkmaxall=qkmaxall)
            for c in CORES]
    resF1 = run_bass_kernel_spmd(_prog("F1", build_launch_F1), inF1,
                                 core_ids=CORES)

    W1T = ca(np.asarray(inputs["L1_ffn_W1"], f32).T)
    W3T = ca(np.asarray(inputs["L1_ffn_W3"], f32).T)
    W2T = ca(np.asarray(inputs["L1_ffn_W2"], f32).T)
    H128b = _hadamard(128).astype(bf16)
    H64 = _hadamard(64)
    HI2 = np.zeros((128, 128), f32)
    for cp in range(64):
        for t2 in range(2):
            HI2[cp * 2 + t2, t2::2] = H64[cp]
    HI2b = HI2.astype(bf16)
    inF2 = [dict(x3=resF1.results[c]["x3"], W1T=W1T, W3T=W3T, W2T=W2T,
                 modc=mc, h128=H128b, hi2=HI2b) for c in CORES]
    resF2 = run_bass_kernel_spmd(_prog("F2", build_launch_F2), inF2,
                                 core_ids=CORES)

    out = np.concatenate([resF2.results[c]["x4"].T for c in CORES], axis=0)
    return ca(out.astype(np.float32))[None]


# version 08
# speedup vs baseline: 1.2333x; 1.0768x over previous
import math
from contextlib import ExitStack
import numpy as np
import ml_dtypes
import concourse.bass as bass
import concourse.mybir as mybir
from concourse.bass_utils import run_bass_kernel_spmd

F32 = mybir.dt.float32
BF16 = mybir.dt.bfloat16
AF = mybir.ActivationFunctionType
ALU = mybir.AluOpType
AX = mybir.AxisListType
ds = bass.ds
bf16 = ml_dtypes.bfloat16

NCORE = 8
SEQ = 8192
TPC = SEQ // NCORE
DM = 2048
NDC = DM // 128
CH = 256
NCHUNK = TPC // CH

INTER = 4096
NHEAD = 64
PD = 64
ND = 128
KCONV = 4
CONV_DIM = INTER + 2 * ND
PROJ_DIM = INTER + CONV_DIM + NHEAD
DT_MIN, DT_MAX = 1e-3, 100.0
MEPS = 1e-5
LEPS = 1e-5

DFF0 = 5632
W1_ROWS = DFF0 // NCORE
W3_ROWS = DM // NCORE
NBLK = 22528
NGRP = NBLK // 256
GPP = NBLK // 128
TOT_BLOCKS = 180224
_NF4_REF = np.array([-1.0, -0.6962, -0.5251, -0.3949, -0.2844, -0.1848,
                     -0.0911, 0.0, 0.0796, 0.1609, 0.2461, 0.3379, 0.4407,
                     0.5626, 0.723, 1.0], dtype=bf16).astype(np.float64)
NF4L = [float(v) for v in _NF4_REF]
NF4_MID = [(NF4L[i] + NF4L[i + 1]) / 2.0 for i in range(15)]
RNE_MAGIC = 12582912.0

AHEADS = 16
HD = 128
ROPE_BASE = 10000.0

DFF1 = 8192

ENGINES = ("sync", "gpsimd", "vector", "scalar", "tensor")

LOGICAL = ("sync", "gpdma", "gpop", "vector", "scalar", "tensor")
NL = len(LOGICAL)
LIDX = {e: i for i, e in enumerate(LOGICAL)}
PHYS_OF = {"sync": "sync", "gpdma": "gpsimd", "gpop": "gpsimd",
           "vector": "vector", "scalar": "scalar", "tensor": "tensor"}
TENS_LI = LIDX["tensor"]

_DT_SIZE = {}


def _ap_ranges(ap):
    pat = ap.ap
    dt = ap.dtype
    esz = _DT_SIZE.get(dt)
    if esz is None:
        esz = dt.size()
        _DT_SIZE[dt] = esz
    off = ap.offset
    if str(ap.space).endswith("DRAM"):
        span = sum(st * (sz - 1) for st, sz in pat) + 1
        return (ap.name, 0, 128, off * esz, (off + span) * esz, True)
    p0 = ap.base_partition()
    pcnt = pat[0][1]
    pstride = pat[0][0]
    prow = off // pstride if pstride > 0 else 0
    f0 = off - prow * pstride
    span = (sum(st * (sz - 1) for st, sz in pat[1:]) + 1) if len(pat) > 1 else 1
    return (ap.name, p0, p0 + pcnt, f0 * esz, (f0 + span) * esz, False)


class Seq:

    def __init__(self):
        self.steps = []

    def emit(self, engine, fn, inc=1, reads=None, writes=None):
        assert engine in ENGINES, engine
        if engine == "gpsimd":
            engine = "gpdma" if inc == 16 else "gpop"
        elif engine == "sync":
            assert inc == 16, "sync engine hosts only DMA"
        self.steps.append((engine, fn, inc, reads, writes))

    def dma(self, engine, out, in_, **kw):
        self.emit(engine, lambda e: e.dma_start(out=out, in_=in_, **kw), 16,
                  reads=(in_,), writes=(out,))

    def install(self, nc, sems, block):
        steps = self.steps
        nst = len(steps)
        cum = [0] * NL
        issue_guar = {p: [0] * NL for p in ENGINES}
        cells = {}
        fbsz = {}
        waits = [None] * nst
        barrier_vec = [0] * NL

        def cells_of(ap):
            try:
                name, p0, p1, b0, b1, is_dram = _ap_ranges(ap)
            except Exception:
                return None
            fb = fbsz.get(name)
            if fb is None:
                if is_dram:
                    try:
                        t = ap.tensor
                        totb = 1
                        for s in t.shape:
                            totb *= s
                        totb *= _DT_SIZE.get(ap.dtype) or ap.dtype.size()
                    except Exception:
                        totb = 1 << 20
                    fb = max(4096, totb >> 6)
                else:
                    fb = 512
                fbsz[name] = fb
            out = []
            for pb in range(p0 >> 5, ((p1 - 1) >> 5) + 1):
                for fbk in range(b0 // fb, (b1 - 1) // fb + 1):
                    out.append((name, pb, fbk))
            return out

        for k, (leng, fn, inc, reads, writes) in enumerate(steps):
            li = LIDX[leng]
            phys = PHYS_OF[leng]
            R = list(barrier_vec)
            rcells = []
            wcells = []
            bad = reads is None and writes is None
            if not bad:
                for ap in (reads or ()):
                    if not hasattr(ap, "offset"):
                        continue
                    cl = cells_of(ap)
                    if cl is None:
                        bad = True
                        break
                    rcells.extend(cl)
                for ap in (writes or ()):
                    if bad or not hasattr(ap, "offset"):
                        continue
                    cl = cells_of(ap)
                    if cl is None:
                        bad = True
                        break
                    wcells.extend(cl)
            if bad:
                for i in range(NL):
                    if cum[i] > R[i]:
                        R[i] = cum[i]
            else:
                for key in rcells:
                    c = cells.get(key)
                    if c is not None and c[0] is not None:
                        w = c[0]
                        for i in range(NL):
                            if w[i] > R[i]:
                                R[i] = w[i]
                for key in wcells:
                    c = cells.get(key)
                    if c is not None:
                        for v in c:
                            if v is not None:
                                for i in range(NL):
                                    if v[i] > R[i]:
                                        R[i] = v[i]
            iq = issue_guar[phys]
            wts = []
            for j in range(NL):
                if R[j] > iq[j]:
                    if phys == "tensor" and j == TENS_LI:
                        iq[j] = R[j]
                        continue
                    wts.append((j, R[j]))
                    iq[j] = R[j]
            waits[k] = wts
            cum[li] += inc
            g = list(R)
            g[li] = cum[li]
            if bad:
                barrier_vec = g
                cells.clear()
            else:
                for key in wcells:
                    cells[key] = [g, None]
                for key in rcells:
                    c = cells.get(key)
                    if c is None:
                        cells[key] = [None, list(g)]
                    elif c[1] is None:
                        c[1] = list(g)
                    else:
                        r = c[1]
                        for i in range(NL):
                            if g[i] > r[i]:
                                r[i] = g[i]

        totals = list(cum)

        def make(phys_name):
            def body(eng):
                for k, (leng, fn, inc, _r, _w) in enumerate(steps):
                    if PHYS_OF[leng] != phys_name:
                        continue
                    for (j, v) in waits[k]:
                        eng.wait_ge(sems[LOGICAL[j]], v)
                    inst = fn(eng)
                    assert inst is not None, f"{leng} step returned None"
                    inst.then_inc(sems[leng], inc)
                if phys_name == "sync":
                    for j, nm in enumerate(LOGICAL):
                        if totals[j]:
                            eng.wait_ge(sems[nm], totals[j])
            return body

        for phys_name in ENGINES:
            getattr(block, phys_name)(make(phys_name))
        return totals


def mk_sems(b):
    return {nm: b.sem(f"SQ_{nm}") for nm in LOGICAL}


def new_nc():
    return bass.Bass("TRN2", target_bir_lowering=False)


class Bufs:

    def __init__(self, nc, ctx):
        self.nc, self.ctx = nc, ctx

    def sb(self, name, shape, dtype=F32):
        return self.ctx.enter_context(self.nc.sbuf_tensor(name, shape, dtype))

    def ps(self, name, shape, dtype=F32):
        return self.ctx.enter_context(self.nc.psum_tensor(name, shape, dtype))

    def sem(self, name="S"):
        return self.ctx.enter_context(self.nc.semaphore(name))

    def block(self):
        return self.ctx.enter_context(self.nc.Block())



def e_memset(seq, ap, val, engine="vector"):
    seq.emit(engine, lambda e: e.memset(ap, float(val)), reads=(), writes=(ap,))


def e_identity(seq, ident):
    seq.emit("gpsimd", lambda e: e.memset(ident[:, :], 0.0),
             reads=(), writes=(ident[:, :],))
    seq.emit("gpsimd", lambda e: e.affine_select(
        out=ident[:, :], in_=ident[:, :], compare_op=ALU.not_equal,
        fill=1.0, base=0, pattern=[[-1, ident.shape[1]]], channel_multiplier=1),
        reads=(ident[:, :],), writes=(ident[:, :],))


def e_copy(seq, out, in_, engine="vector"):
    seq.emit(engine, lambda e: e.tensor_copy(out, in_),
             reads=(in_,), writes=(out,))


def e_act(seq, out, in_, func, bias=0.0, scale=1.0, accum_out=None):
    wr = (out,) if accum_out is None else (out, accum_out)
    seq.emit("scalar", lambda e: e.activation(out, in_, func, bias=bias,
                                              scale=scale, accum_out=accum_out),
             reads=(in_, bias), writes=wr)


def e_ts(seq, out, in0, s1, s2=None, op0=ALU.mult, op1=ALU.bypass, engine="vector"):
    if s2 is None:
        seq.emit(engine, lambda e: e.tensor_scalar(out, in0, s1, None, op0),
                 reads=(in0, s1), writes=(out,))
    else:
        seq.emit(engine, lambda e: e.tensor_scalar(out, in0, s1, s2, op0, op1),
                 reads=(in0, s1, s2), writes=(out,))


def e_tt(seq, out, in0, in1, op, engine="vector"):
    seq.emit(engine, lambda e: e.tensor_tensor(out, in0, in1, op),
             reads=(in0, in1), writes=(out,))


def e_stt(seq, out, in0, scalar, in1, op0, op1, engine="vector"):
    seq.emit(engine, lambda e: e.scalar_tensor_tensor(out, in0, scalar, in1,
                                                      op0, op1),
             reads=(in0, scalar, in1), writes=(out,))


def e_red(seq, out, in_, op, absval=False, negate=False):
    seq.emit("vector", lambda e: e.tensor_reduce(out, in_, AX.X, op,
                                                 apply_absolute_value=absval,
                                                 negate=negate),
             reads=(in_,), writes=(out,))


def e_mm(seq, out, lhsT, rhs, start=True, stop=True):
    rd = (lhsT, rhs) if start else (lhsT, rhs, out)
    seq.emit("tensor", lambda e: e.matmul(out, lhsT, rhs, start=start,
                                          stop=stop),
             reads=rd, writes=(out,))


def e_tr(seq, out, in_, ident):
    seq.emit("tensor", lambda e: e.transpose(out, in_, ident),
             reads=(in_, ident), writes=(out,))


def e_recip(seq, out, in_):
    seq.emit("vector", lambda e: e.reciprocal(out, in_),
             reads=(in_,), writes=(out,))


def e_bcast_col(seq, nc, ones_row, src_row, psum, dst_col=None):
    e_mm(seq, psum, ones_row[:, :], src_row)
    if dst_col is not None:
        e_copy(seq, dst_col, psum)



def build_launch_A(do_mod=True, do_nf4=True):
    nc = new_nc()
    cT = {}
    wmod = {}
    for L in range(2):
        cT[L] = nc.dram_tensor(f"c{L}T", [128, 16], F32, kind="ExternalInput").ap()
        wmod[L] = nc.dram_tensor(f"wmod{L}s", [DM, 12288 // NCORE], F32,
                                 kind="ExternalInput").ap()
    wsh = {}
    for i, (r, c) in enumerate([(W1_ROWS, DM), (W1_ROWS, DM), (W3_ROWS, DFF0)]):
        wsh[i] = nc.dram_tensor(f"w{i+1}s", [r, c], BF16, kind="ExternalInput").ap()
    modrow = nc.dram_tensor("modrow", [2, 12288 // NCORE], F32,
                            kind="ExternalOutput").ap()
    ssum = nc.dram_tensor("ssum", [1, 4], F32, kind="ExternalOutput").ap()

    FSH = 12288 // NCORE
    with ExitStack() as ctx:
        b = Bufs(nc, ctx)
        sc = b.sb("sc", [128, 16])
        silu_c = b.sb("silu_c", [128, 16])
        wbuf = b.sb("wbuf", [128, 16 * 512])
        mrow = [b.sb(f"mrow{L}", [1, FSH]) for L in range(2)]
        wg = b.sb("wg", [NGRP, 256 * 64], BF16)
        s1g = b.sb("s1g", [NGRP, 256])
        rowsum = b.sb("rowsum", [NGRP, 1])
        ones_col = b.sb("ones_col", [128, 1])
        ssum_sb = b.sb("ssum_sb", [1, 4])
        p_row = b.ps("p_row", [1, 512])
        p_sum = b.ps("p_sum", [1, 1])
        sems = mk_sems(b)
        block = b.block()
        seq = Seq()
        e_memset(seq, ones_col[:, :], 1.0)
        e_memset(seq, ssum_sb[:, :], 0.0)
        for L in range(2 if do_mod else 0):
            seq.dma("sync", sc[:, :], cT[L])
            e_act(seq, silu_c[:, :], sc[:, :], AF.Silu)
            for fs in range(FSH // 512):
                for c in range(NDC):
                    seq.dma("sync", wbuf[:, c * 512:(c + 1) * 512],
                            wmod[L][c * 128:(c + 1) * 128, fs * 512:(fs + 1) * 512])
                for c in range(NDC):
                    e_mm(seq, p_row[:, :], silu_c[:, c:c + 1], wbuf[:, c * 512:(c + 1) * 512],
                         start=(c == 0), stop=(c == NDC - 1))
                e_copy(seq, mrow[L][:, fs * 512:(fs + 1) * 512], p_row[:, :])
        for L in range(2):
            seq.dma("sync", modrow[L:L + 1, :], mrow[L][:, :])
        for i in range(3 if do_nf4 else 0):
            flat = wsh[i].rearrange("a b -> (a b)")
            src = flat.rearrange("(g j) -> g j", g=NGRP)
            seq.dma("sync", wg[:, :], src)
            e_red(seq, s1g[:, :], wg[:, :].rearrange("g (b k) -> g b k", k=64),
                  ALU.max, absval=True)
            e_red(seq, rowsum[:, :], s1g[:, :], ALU.add)
            e_mm(seq, p_sum[:, :], ones_col[:NGRP, :], rowsum[:, :])
            e_copy(seq, ssum_sb[:, i:i + 1], p_sum[:, :])
        seq.dma("sync", ssum, ssum_sb[:, :])
        seq.install(nc, sems, block)
    return nc



def build_launch_B():
    nc = new_nc()
    wsh = {}
    shapes = [(W1_ROWS, DM), (W1_ROWS, DM), (W3_ROWS, DFF0)]
    for i, (r, c) in enumerate(shapes):
        wsh[i] = nc.dram_tensor(f"w{i+1}s", [r, c], BF16, kind="ExternalInput").ap()
    allsum = nc.dram_tensor("allsum", [NCORE, 4], F32, kind="ExternalInput").ap()
    wq = {}
    for i, (r, c) in enumerate(shapes):
        wq[i] = nc.dram_tensor(f"wq{i+1}", [r, c], BF16, kind="ExternalOutput").ap()

    GW = 256 * 64
    with ExitStack() as ctx:
        b = Bufs(nc, ctx)
        wg = b.sb("wg", [NGRP, GW], BF16)
        s1g = b.sb("s1g", [NGRP, 256])
        r1g = b.sb("r1g", [NGRP, 256])
        cent = b.sb("cent", [NGRP, 256], BF16)
        centf = b.sb("centf", [NGRP, 256])
        gmax = b.sb("gmax", [NGRP, 1])
        qf = b.sb("qf", [NGRP, 1])
        rqf = b.sb("rqf", [NGRP, 1])
        qs = b.sb("qs", [NGRP, 256])
        deqb = b.sb("deqb", [NGRP, 256], BF16)
        sums = b.sb("sums", [1, 4 * NCORE])
        tot = b.sb("tot", [1, 4])
        meanb = b.sb("meanb", [1, 1], BF16)
        meanf = b.sb("meanf", [1, 1])
        ones_row = b.sb("ones_row", [1, 128])
        mean_col = b.sb("mean_col", [128, 1])
        scl = b.sb("scl", [NGRP, GW], BF16)
        tmp = b.sb("tmp", [NGRP, GW], BF16)
        acc = b.sb("acc", [NGRP, GW], BF16)
        p_mean = b.ps("p_mean", [128, 1])
        sems = mk_sems(b)
        block = b.block()
        seq = Seq()
        e_memset(seq, ones_row[:, :], 1.0)
        seq.dma("sync", sums[:, :], allsum.rearrange("a b -> (a b)")[None, :])

        for i in range(3):
            flat = wsh[i].rearrange("a b -> (a b)")
            gview = flat.rearrange("(g j) -> g j", g=NGRP)
            seq.dma("sync", wg[:, :], gview)
            e_red(seq, s1g[:, :], wg[:, :].rearrange("g (b k) -> g b k", k=64),
                  ALU.max, absval=True)
            e_red(seq, tot[:, i:i + 1],
                  sums[:, :].rearrange("o (c w) -> o w c", w=4)[:, i:i + 1, :],
                  ALU.add)
            seq.emit("vector", lambda e, i=i: e.tensor_single_scalar(
                meanb[:, :], tot[:, i:i + 1], float(1.0 / TOT_BLOCKS), ALU.mult),
                reads=(tot[:, i:i + 1],), writes=(meanb[:, :],))
            e_copy(seq, meanf[:, :], meanb[:, :])
            e_mm(seq, p_mean[:, :], ones_row[:, :], meanf[:, :])
            e_copy(seq, mean_col[:, :], p_mean[:, :])
            e_ts(seq, cent[:, :], s1g[:, :], mean_col[:NGRP, :], None, ALU.subtract)
            e_red(seq, gmax[:, :], cent[:, :], ALU.max, absval=True)
            e_recip(seq, qf[:, :], gmax[:, :])
            e_ts(seq, qf[:, :], qf[:, :], 128.0, None, ALU.mult)
            e_copy(seq, centf[:, :], cent[:, :])
            e_ts(seq, qs[:, :], centf[:, :], qf[:, :], RNE_MAGIC, ALU.mult, ALU.add)
            e_ts(seq, qs[:, :], qs[:, :], RNE_MAGIC, None, ALU.subtract)
            e_ts(seq, qs[:, :], qs[:, :], -128.0, 127.0, ALU.max, ALU.min)
            e_recip(seq, rqf[:, :], qf[:, :])
            e_ts(seq, deqb[:, :], qs[:, :], rqf[:, :], None, ALU.mult)
            e_copy(seq, centf[:, :], deqb[:, :])
            e_ts(seq, deqb[:, :], centf[:, :], mean_col[:NGRP, :], None, ALU.add)
            e_recip(seq, r1g[:, :], s1g[:, :])
            e_tt(seq, scl[:, :].rearrange("g (b k) -> g b k", k=64),
                 wg[:, :].rearrange("g (b k) -> g b k", k=64),
                 r1g[:, :].rearrange("g (b o) -> g b o", o=1).broadcast_to([NGRP, 256, 64]),
                 ALU.mult)
            e_memset(seq, acc[:, :], 0.0)
            for lv in range(7):
                e_ts(seq, tmp[:, :], scl[:, :], float(NF4_MID[lv]), float(NF4L[lv]),
                     ALU.is_le, ALU.mult)
                e_tt(seq, acc[:, :], acc[:, :], tmp[:, :], ALU.min)
            e_memset(seq, tmp[:, :], 0.0)
            for lv in range(7, 15):
                e_ts(seq, wg[:, :], scl[:, :], float(NF4_MID[lv]), float(NF4L[lv + 1]),
                     ALU.is_gt, ALU.mult)
                e_tt(seq, tmp[:, :], tmp[:, :], wg[:, :], ALU.max)
            e_tt(seq, acc[:, :], acc[:, :], tmp[:, :], ALU.add)
            e_tt(seq, scl[:, :].rearrange("g (b k) -> g b k", k=64),
                 acc[:, :].rearrange("g (b k) -> g b k", k=64),
                 deqb[:, :].rearrange("g (b o) -> g b o", o=1).broadcast_to([NGRP, 256, 64]),
                 ALU.mult)
            out_g = wq[i].rearrange("a b -> (a b)").rearrange("(g j) -> g j", g=NGRP)
            seq.dma("sync", out_g, scl[:, :])
        seq.install(nc, sems, block)
    return nc



def e_clamp_gate(seq, out_cols, in_cols, sgn_scratch):
    e_ts(seq, sgn_scratch, in_cols, 0.0, 2.0, ALU.is_ge, ALU.mult)
    e_ts(seq, sgn_scratch, sgn_scratch, -1.0, None, ALU.add)
    e_tt(seq, out_cols, in_cols, sgn_scratch, ALU.mult)
    e_ts(seq, out_cols, out_cols, 0.25, None, ALU.max)
    e_tt(seq, out_cols, out_cols, sgn_scratch, ALU.mult)


class LNBufs:
    def __init__(self, b, tag, p_b=None):
        self.sq = b.sb(f"ln_sq_{tag}", [128, 512])
        rt = b.sb(f"ln_rows_{tag}", [1, 1024])
        rt2 = b.sb(f"ln_rows2_{tag}", [1, 512])
        self.mean_row = rt[0:1, 0:512]
        self.rstd_row = rt[0:1, 512:1024]
        self.msq_row = rt2[0:1, :]
        self.mean_b = b.sb(f"ln_mb_{tag}", [128, 512])
        self.rstd_b = b.sb(f"ln_rb_{tag}", [128, 512])
        self.t1 = b.sb(f"ln_t1_{tag}", [128, 512])
        self.p_sum = b.ps(f"ln_ps_{tag}", [1, 512])
        self.p_sq = b.ps(f"ln_pq_{tag}", [1, 512])
        self.p_b = p_b if p_b is not None else b.ps(f"ln_pb_{tag}", [128, 512])


def emit_ln_mod(seq, lb, x_sb, h_out, ones_col, ones_row, ca1_cols, sa_cols,
                toks=TPC, nch=NDC):
    for t0 in range(0, toks, 512):
        tw = min(512, toks - t0)
        for c in range(nch):
            xc = x_sb[:, c * toks + t0: c * toks + t0 + tw]
            e_act(seq, lb.sq[:, :tw], xc, AF.Square)
            e_mm(seq, lb.p_sum[:, :tw], ones_col[:, :], xc,
                 start=(c == 0), stop=(c == nch - 1))
            e_mm(seq, lb.p_sq[:, :tw], ones_col[:, :], lb.sq[:, :tw],
                 start=(c == 0), stop=(c == nch - 1))
        dim = 128 * nch
        e_ts(seq, lb.mean_row[:, :tw], lb.p_sum[:, :tw], 1.0 / dim, None, ALU.mult)
        e_ts(seq, lb.msq_row[:, :tw], lb.p_sq[:, :tw], 1.0 / dim, None, ALU.mult)
        e_stt(seq, lb.rstd_row[:, :tw], lb.mean_row[:, :tw], -1.0,
              lb.mean_row[:, :tw], ALU.mult, ALU.mult)
        e_tt(seq, lb.msq_row[:, :tw], lb.msq_row[:, :tw], lb.rstd_row[:, :tw],
             ALU.add)
        e_ts(seq, lb.msq_row[:, :tw], lb.msq_row[:, :tw], LEPS, None, ALU.add)
        e_act(seq, lb.rstd_row[:, :tw], lb.msq_row[:, :tw], AF.Sqrt)
        e_recip(seq, lb.rstd_row[:, :tw], lb.rstd_row[:, :tw])
        e_mm(seq, lb.p_b[:, :tw], ones_row[:, :], lb.mean_row[:, :tw])
        e_copy(seq, lb.mean_b[:, :tw], lb.p_b[:, :tw])
        e_mm(seq, lb.p_b[:, :tw], ones_row[:, :], lb.rstd_row[:, :tw])
        e_copy(seq, lb.rstd_b[:, :tw], lb.p_b[:, :tw])
        for c in range(nch):
            xc = x_sb[:, c * toks + t0: c * toks + t0 + tw]
            hc = h_out[:, c * toks + t0: c * toks + t0 + tw]
            e_stt(seq, lb.t1[:, :tw], lb.mean_b[:, :tw], -1.0, xc, ALU.mult, ALU.add)
            e_stt(seq, lb.t1[:, :tw], lb.t1[:, :tw], ca1_cols[:, c:c + 1],
                  lb.rstd_b[:, :tw], ALU.mult, ALU.mult)
            e_ts(seq, hc, lb.t1[:, :tw], sa_cols[:, c:c + 1], None, ALU.add)


def dma_wslice_fm(seq, wbuf, wT_dram, n_in, f0, fw, engine="gpsimd"):
    nch = n_in // 128
    src = wT_dram.rearrange("(c p) f -> p c f", p=128)[:, :, f0:f0 + fw]
    dst = wbuf[:, :nch * fw].rearrange("p (c f) -> p c f", f=fw)
    seq.dma(engine, dst, src)


def emit_gemm_fm(seq, wT_dram, act_bf, n_in, f_lo, f_hi, toks, wbuf, psum,
                 consume, tok_tile=512):
    nch = n_in // 128
    for f0 in range(f_lo, f_hi, 128):
        dma_wslice_fm(seq, wbuf, wT_dram, n_in, f0, 128)
        for t0 in range(0, toks, tok_tile):
            tw = min(tok_tile, toks - t0)
            for c in range(nch):
                e_mm(seq, psum[:, :tw], wbuf[:, c * 128:(c + 1) * 128],
                     act_bf[:, c * toks + t0: c * toks + t0 + tw],
                     start=(c == 0), stop=(c == nch - 1))
            consume(f0 - f_lo, t0, tw, psum[:, :tw])



def build_launch_E():
    nc = new_nc()
    x1d = nc.dram_tensor("x1", [DM, TPC], F32, kind="ExternalInput").ap()
    w1T = nc.dram_tensor("w1pT", [DM, DFF0], BF16, kind="ExternalInput").ap()
    w2T = nc.dram_tensor("w2pT", [DM, DFF0], BF16, kind="ExternalInput").ap()
    w3T = nc.dram_tensor("w3pT", [DFF0, DM], BF16, kind="ExternalInput").ap()
    wqkvT = nc.dram_tensor("wqkvT", [DM, 3 * DM], F32, kind="ExternalInput").ap()
    modc = nc.dram_tensor("modc", [128, 2 * 6 * NDC], F32, kind="ExternalInput").ap()
    rrot = nc.dram_tensor("rrot", [128, 128], BF16, kind="ExternalInput").ap()
    posr = nc.dram_tensor("posr", [1, TPC], F32, kind="ExternalInput").ap()
    x2d = nc.dram_tensor("x2", [DM, TPC], F32, kind="ExternalOutput").ap()
    qTd = nc.dram_tensor("qT", [AHEADS, HD, TPC], BF16, kind="ExternalOutput").ap()
    kTd = nc.dram_tensor("kT", [AHEADS, HD, TPC], BF16, kind="ExternalOutput").ap()
    vtd = nc.dram_tensor("vt", [TPC, DM], BF16, kind="ExternalOutput").ap()
    qkmaxd = nc.dram_tensor("qkmax", [1, 2 * AHEADS], F32, kind="ExternalOutput").ap()

    NF0 = DFF0 // 128
    with ExitStack() as ctx:
        b = Bufs(nc, ctx)
        x_sb = b.sb("x_sb", [128, NDC * TPC])
        h_sb = b.sb("h_sb", [128, NDC * TPC], BF16)
        hid = b.sb("hid", [128, NF0 * 512], BF16)
        mcol = b.sb("mcol", [128, 2 * 6 * NDC])
        ca1 = b.sb("ca1", [128, NDC])
        gbc = b.sb("gbc", [128, NDC])
        sgn = b.sb("sgn", [128, NDC])
        ones_col = b.sb("ones_col", [128, 1])
        ones_row = b.sb("ones_row", [1, 128])
        wbuf1 = b.sb("wbuf1", [128, NDC * 128], BF16)
        wbuf2 = b.sb("wbuf2", [128, NDC * 128], BF16)
        wbufq = b.sb("wbufq", [128, NDC * 512], BF16)
        silu_s = b.sb("silu_s", [128, 512])
        rope_cos = b.sb("rope_cos", [128, TPC])
        rope_sin = b.sb("rope_sin", [128, TPC])
        ifr = b.sb("ifr", [128, 1])
        qk_bf = b.sb("qk_bf", [128, TPC], BF16)
        rot_sb = b.sb("rot_sb", [128, TPC], BF16)
        rconst = b.sb("rconst", [128, 128], BF16)
        posrow = b.sb("posrow", [1, TPC])
        lb = LNBufs(b, "e")
        p_a = b.ps("p_a", [128, 512])
        p_b2 = b.ps("p_b2", [128, 512])
        p_o = b.ps("p_o", [128, 512])
        p_r = b.ps("p_r", [128, 512])
        sems = mk_sems(b)
        block = b.block()
        seq = Seq()

        e_memset(seq, ones_col[:, :], 1.0)
        e_memset(seq, ones_row[:, :], 1.0)
        seq.dma("sync", mcol[:, :], modc)
        for c in range(NDC):
            seq.dma("sync", x_sb[:, c * TPC:(c + 1) * TPC],
                    x1d[c * 128:(c + 1) * 128, :])
        e_ts(seq, ca1[:, :], mcol[:, 4 * NDC:5 * NDC], 1.0, None, ALU.add)
        e_clamp_gate(seq, gbc[:, :], mcol[:, 5 * NDC:6 * NDC], sgn[:, :])
        emit_ln_mod(seq, lb, x_sb, h_sb, ones_col, ones_row,
                    ca1, mcol[:, 3 * NDC:4 * NDC])

        for t0 in range(0, TPC, 512):
            for fi in range(NF0):
                f0 = fi * 128
                dma_wslice_fm(seq, wbuf1, w1T, DM, f0, 128)
                dma_wslice_fm(seq, wbuf2, w2T, DM, f0, 128)
                for c in range(NDC):
                    e_mm(seq, p_a[:, :], wbuf1[:, c * 128:(c + 1) * 128],
                         h_sb[:, c * TPC + t0: c * TPC + t0 + 512],
                         start=(c == 0), stop=(c == NDC - 1))
                for c in range(NDC):
                    e_mm(seq, p_b2[:, :], wbuf2[:, c * 128:(c + 1) * 128],
                         h_sb[:, c * TPC + t0: c * TPC + t0 + 512],
                         start=(c == 0), stop=(c == NDC - 1))
                e_act(seq, silu_s[:, :], p_a[:, :], AF.Silu)
                e_tt(seq, hid[:, fi * 512:(fi + 1) * 512], silu_s[:, :],
                     p_b2[:, :], ALU.mult)
            for dc in range(NDC):
                dma_wslice_fm(seq, wbufq, w3T, DFF0, dc * 128, 128)
                for fc in range(NF0):
                    e_mm(seq, p_o[:, :], wbufq[:, fc * 128:(fc + 1) * 128],
                         hid[:, fc * 512:(fc + 1) * 512],
                         start=(fc == 0), stop=(fc == NF0 - 1))
                xc = x_sb[:, dc * TPC + t0: dc * TPC + t0 + 512]
                e_stt(seq, xc, p_o[:, :], gbc[:, dc:dc + 1], xc, ALU.mult, ALU.add)
        for c in range(NDC):
            seq.dma("sync", x2d[c * 128:(c + 1) * 128, :],
                    x_sb[:, c * TPC:(c + 1) * TPC])

        e_ts(seq, ca1[:, :], mcol[:, 6 * NDC + 1 * NDC: 6 * NDC + 2 * NDC],
             1.0, None, ALU.add)
        emit_ln_mod(seq, lb, x_sb, h_sb, ones_col, ones_row,
                    ca1, mcol[:, 6 * NDC: 6 * NDC + NDC])

        seq.dma("sync", posrow[:, :], posr)
        seq.emit("gpsimd", lambda e: e.iota(ifr[:, :], [[0, 1]], base=0,
                                            channel_multiplier=1,
                                            allow_small_or_imprecise_dtypes=True),
                 reads=(), writes=(ifr[:, :],))
        e_ts(seq, silu_s[:, 0:1], ifr[:, :], 64.0, 64.0, ALU.is_ge, ALU.mult)
        e_tt(seq, ifr[:, :], ifr[:, :], silu_s[:, 0:1], ALU.subtract)
        e_act(seq, ifr[:, :], ifr[:, :], AF.Exp, scale=-math.log(10000.0) / 64.0)
        TWO_PI = 2.0 * math.pi
        c1 = float(np.float32(6.28125))
        c2 = float(np.float32(np.float64(TWO_PI) - 6.28125))
        c3 = float(np.float64(TWO_PI) - np.float64(c1) - np.float64(c2))
        for th in range(0, TPC, 512):
            ang = silu_s[:, :]
            kk = lb.t1[:, :]
            e_mm(seq, p_r[:, :], ones_row[:, :], posrow[:, th:th + 512])
            e_tt(seq, ang, p_r[:, :], ifr[:, :].broadcast_to([128, 512]), ALU.mult)
            e_ts(seq, kk, ang, 1.0 / TWO_PI, RNE_MAGIC, ALU.mult, ALU.add)
            e_ts(seq, kk, kk, RNE_MAGIC, None, ALU.subtract)
            for cw in (c1, c2, c3):
                e_stt(seq, ang, kk, -cw, ang, ALU.mult, ALU.add)
            e_act(seq, rope_sin[:, th:th + 512], ang, AF.Sin)
            e_ts(seq, kk, ang, float(math.pi / 2), TWO_PI, ALU.is_gt, ALU.mult)
            e_tt(seq, ang, ang, kk, ALU.subtract)
            e_ts(seq, ang, ang, float(math.pi / 2), None, ALU.add)
            e_act(seq, rope_cos[:, th:th + 512], ang, AF.Sin)

        seq.dma("sync", rconst[:, :], rrot)
        qkmax = b.sb("qkmax_sb", [1, 2 * AHEADS])
        nmtmp = b.sb("nmtmp", [1, 1])
        e_memset(seq, qkmax[:, :], 0.0)

        def qk_consume(dst, head, mx_off):
            def consume(f0, t0, tw, psum):
                e_copy(seq, qk_bf[:, t0:t0 + tw], psum)
                if t0 + tw == TPC:
                    for tt in range(0, TPC, 512):
                        e_mm(seq, p_r[:, :], rconst[:, :], qk_bf[:, tt:tt + 512])
                        e_copy(seq, rot_sb[:, tt:tt + 512], p_r[:, :])
                    for tt in range(0, TPC, 512):
                        e_tt(seq, silu_s[:, :], qk_bf[:, tt:tt + 512],
                             rope_cos[:, tt:tt + 512], ALU.mult)
                        e_tt(seq, lb.t1[:, :], rot_sb[:, tt:tt + 512],
                             rope_sin[:, tt:tt + 512], ALU.mult)
                        e_tt(seq, qk_bf[:, tt:tt + 512], silu_s[:, :],
                             lb.t1[:, :], ALU.add)
                    seq.dma("sync", dst[head], qk_bf[:, :])
                    mcell = qkmax[:, mx_off + head: mx_off + head + 1]
                    for tt in range(0, TPC, 512):
                        e_act(seq, silu_s[:, :], qk_bf[:, tt:tt + 512], AF.Square)
                        e_mm(seq, lb.p_sum[:, :], ones_col[:, :], silu_s[:, :])
                        e_red(seq, nmtmp[:, :], lb.p_sum[:, :], ALU.max)
                        e_tt(seq, mcell, mcell, nmtmp[:, :], ALU.max)
            return consume

        for head in range(AHEADS):
            emit_gemm_fm(seq, wqkvT, h_sb, DM, head * HD, (head + 1) * HD, TPC,
                         wbufq, p_a, qk_consume(qTd, head, 0))
        for head in range(AHEADS):
            emit_gemm_fm(seq, wqkvT, h_sb, DM, DM + head * HD, DM + (head + 1) * HD,
                         TPC, wbufq, p_a, qk_consume(kTd, head, AHEADS))
        seq.dma("sync", qkmaxd, qkmax[:, :])
        for f0 in range(0, DM, 512):
            src = wqkvT.rearrange("(c p) f -> p c f", p=128)[
                :, :, 2 * DM + f0: 2 * DM + f0 + 512]
            dst = wbufq[:, :NDC * 512].rearrange("p (c f) -> p c f", f=512)
            seq.dma("gpsimd", dst, src)
            for t0 in range(0, TPC, 128):
                for c in range(NDC):
                    e_mm(seq, p_o[:, :], h_sb[:, c * TPC + t0: c * TPC + t0 + 128],
                         wbufq[:, c * 512:(c + 1) * 512],
                         start=(c == 0), stop=(c == NDC - 1))
                e_copy(seq, qk_bf[:, 0:512], p_o[:, :])
                seq.dma("sync", vtd[t0:t0 + 128, f0:f0 + 512], qk_bf[:, 0:512])
        seq.install(nc, sems, block)
    return nc



def build_launch_F1(debug=False):
    nc = new_nc()
    x2d = nc.dram_tensor("x2", [DM, TPC], F32, kind="ExternalInput").ap()
    qTd = nc.dram_tensor("qT", [AHEADS, HD, TPC], BF16, kind="ExternalInput").ap()
    kTd = nc.dram_tensor("kTall", [AHEADS, HD, SEQ], BF16, kind="ExternalInput").ap()
    vd = nc.dram_tensor("vall", [AHEADS, SEQ, HD], BF16, kind="ExternalInput").ap()
    woT = nc.dram_tensor("woT", [DM, DM], F32, kind="ExternalInput").ap()
    modc = nc.dram_tensor("modc", [128, 2 * 6 * NDC], F32, kind="ExternalInput").ap()
    qkm = nc.dram_tensor("qkmaxall", [NCORE, 2 * AHEADS], F32,
                         kind="ExternalInput").ap()
    x3d = nc.dram_tensor("x3", [DM, TPC], F32, kind="ExternalOutput").ap()
    mx_bounce = nc.dram_tensor("mx_bounce", [4, 128], F32)
    if debug:
        dbg_e = nc.dram_tensor("dbg_e", [128, 512], BF16,
                               kind="ExternalOutput").ap()
        dbg_s = nc.dram_tensor("dbg_s", [1, 512], F32, kind="ExternalOutput").ap()
        dbg_a = nc.dram_tensor("dbg_a", [128, TPC], BF16,
                               kind="ExternalOutput").ap()

    NKC = SEQ // 128
    with ExitStack() as ctx:
        b = Bufs(nc, ctx)
        expS = b.sb("expS", [128, NKC * 512], BF16)
        kT = b.sb("kT", [128, SEQ], BF16)
        v_sb = b.sb("v_sb", [128, NKC * HD], BF16)
        qT = b.sb("qT_sb", [128, TPC], BF16)
        attn = b.sb("attn", [128, AHEADS * TPC], BF16)
        x2c = b.sb("x2c", [128, TPC])
        stage = b.sb("stage", [128, 512])
        wbuf = b.sb("wbufo", [128, NDC * 128], BF16)
        mcol = b.sb("mcolf", [128, 2 * 6 * NDC])
        gac = b.sb("gac", [128, NDC])
        sgn = b.sb("sgn_f", [128, NDC])
        rsum = b.sb("rsum", [1, 512])
        rbc = b.sb("rbc", [128, 512])
        ones_bf = b.sb("ones_bf", [128, 1], BF16)
        ones_bf1 = b.sb("ones_bf1", [1, 128], BF16)
        ones_row = b.sb("ones_rowf", [1, 128])
        identf = b.sb("identf", [128, 128])
        pm4 = b.sb("pm4", [128, 4])
        mx1 = b.sb("mx1", [128, 1])
        nmrow = b.sb("nmrow", [1, 512], BF16)
        p_s = b.ps("p_s", [128, 512])
        p_sum = b.ps("p_sumf", [1, 512])
        p_b = b.ps("p_bf", [128, 512])
        p_m = b.ps("p_m", [128, 512])
        sems = mk_sems(b)
        block = b.block()
        seq = Seq()

        e_memset(seq, ones_bf[:, :], 1.0)
        e_memset(seq, ones_bf1[:, :], 1.0)
        e_memset(seq, ones_row[:, :], 1.0)
        e_identity(seq, identf)
        seq.dma("sync", mcol[:, :], modc)
        ksub = kT[:, :].rearrange("p (a s) -> p a s", s=64)[:, :, 0]

        for head in range(AHEADS):
            seq.dma("sync", kT[:, :], kTd[head])
            for part in range(8):
                seq.dma("sync",
                        v_sb[:, :].rearrange("p (kc d) -> p kc d", d=HD)[
                            :, part * 8:(part + 1) * 8, :],
                        vd[head].rearrange("(kc p) d -> p kc d", p=128)[
                            :, part * 8:(part + 1) * 8, :])
            seq.dma("sync", qT[:, :], qTd[head])
            for t0 in range(0, TPC, 512):
                for qc in range(4):
                    for kt in range(SEQ // 512):
                        e_mm(seq, p_m[:, 0:512],
                             qT[:, t0 + qc * 128: t0 + (qc + 1) * 128],
                             kT[:, kt * 512:(kt + 1) * 512])
                        if kt == 0:
                            e_red(seq, pm4[:, qc:qc + 1], p_m[:, 0:512], ALU.max)
                        else:
                            e_red(seq, mx1[:, :], p_m[:, 0:512], ALU.max)
                            e_tt(seq, pm4[:, qc:qc + 1], pm4[:, qc:qc + 1],
                                 mx1[:, :], ALU.max)
                e_tr(seq, p_m[0:4, 0:128], pm4[:, :], identf[:, :])
                e_ts(seq, rbc[0:4, 0:128], p_m[0:4, 0:128], -1.0, None, ALU.mult)
                seq.dma("sync", mx_bounce.ap(), rbc[0:4, 0:128])
                seq.dma("gpsimd", nmrow[:, :],
                        mx_bounce.ap().rearrange("a b -> (a b)")[None, :])
                for kc in range(NKC):
                    e_mm(seq, p_s[:, :], ones_bf1[:, :], nmrow[:, :],
                         stop=False)
                    e_mm(seq, p_s[:, :], kT[:, kc * 128:(kc + 1) * 128],
                         qT[:, t0:t0 + 512], start=False)
                    e_act(seq, expS[:, kc * 512:(kc + 1) * 512], p_s[:, :],
                          AF.Exp, scale=1.0 / math.sqrt(HD))
                for kc in range(NKC):
                    e_mm(seq, p_sum[:, :], ones_bf[:, :],
                         expS[:, kc * 512:(kc + 1) * 512],
                         start=(kc == 0), stop=(kc == NKC - 1))
                if debug and head == 0 and t0 == 0:
                    seq.dma("sync", dbg_e, expS[:, 0:512])
                    e_copy(seq, rsum[:, :], p_sum[:, :])
                    seq.dma("sync", dbg_s, rsum[:, :])
                e_copy(seq, rsum[:, :], p_sum[:, :])
                e_recip(seq, rsum[:, :], rsum[:, :])
                e_mm(seq, p_b[:, :], ones_row[:, :], rsum[:, :])
                e_copy(seq, rbc[:, :], p_b[:, :])
                for kc in range(NKC):
                    e_mm(seq, p_s[:, :], v_sb[:, kc * HD:(kc + 1) * HD],
                         expS[:, kc * 512:(kc + 1) * 512],
                         start=(kc == 0), stop=(kc == NKC - 1))
                e_tt(seq, attn[:, head * TPC + t0: head * TPC + t0 + 512],
                     p_s[:, :], rbc[:, :], ALU.mult)
        if debug:
            seq.dma("sync", dbg_a, attn[:, 0:TPC])

        e_clamp_gate(seq, gac[:, :], mcol[:, 6 * NDC + 2 * NDC: 6 * NDC + 3 * NDC],
                     sgn[:, :])

        def oproj_consume(dc2, t0, tw, psum):
            dc = dc2 // 128
            xc = x2c[:, t0:t0 + tw]
            if t0 == 0:
                for tt in range(0, TPC, 512):
                    seq.dma("sync", x2c[:, tt:tt + 512],
                            x2d[dc * 128:(dc + 1) * 128, tt:tt + 512])
            e_stt(seq, stage[:, :tw], psum, gac[:, dc:dc + 1], xc,
                  ALU.mult, ALU.add)
            seq.dma("sync", x3d[dc * 128:(dc + 1) * 128, t0:t0 + tw],
                    stage[:, :tw])

        emit_gemm_fm(seq, woT, attn, DM, 0, DM, TPC, wbuf, p_s, oproj_consume)
        seq.install(nc, sems, block)
    return nc



def build_launch_F2():
    nc = new_nc()
    x3d = nc.dram_tensor("x3", [DM, TPC], F32, kind="ExternalInput").ap()
    W1T = nc.dram_tensor("W1T", [DM, DFF1], F32, kind="ExternalInput").ap()
    W3T = nc.dram_tensor("W3T", [DM, DFF1], F32, kind="ExternalInput").ap()
    W2T = nc.dram_tensor("W2T", [DFF1, DM], F32, kind="ExternalInput").ap()
    modc = nc.dram_tensor("modc", [128, 2 * 6 * NDC], F32, kind="ExternalInput").ap()
    h128d = nc.dram_tensor("h128", [128, 128], BF16, kind="ExternalInput").ap()
    hi2d = nc.dram_tensor("hi2", [128, 128], BF16, kind="ExternalInput").ap()
    x4d = nc.dram_tensor("x4", [DM, TPC], F32, kind="ExternalOutput").ap()

    NF1 = DFF1 // 128
    with ExitStack() as ctx:
        b = Bufs(nc, ctx)
        x_sb = b.sb("x_sb", [128, NDC * TPC])
        h_sb = b.sb("h_sb", [128, NDC * TPC], BF16)
        hid = b.sb("hid", [128, NF1 * 512], BF16)
        mcol = b.sb("mcol", [128, 2 * 6 * NDC])
        ca1 = b.sb("ca1", [128, NDC])
        gbc = b.sb("gbc", [128, NDC])
        sgn = b.sb("sgn", [128, NDC])
        ones_col = b.sb("ones_col", [128, 1])
        ones_row = b.sb("ones_row", [1, 128])
        wbuf1 = b.sb("wbuf1", [128, NDC * 128], BF16)
        wbuf2 = b.sb("wbuf2", [128, NDC * 128], BF16)
        wbuf3 = b.sb("wbuf3", [128, NF1 * 128], BF16)
        silu_s = b.sb("silu_s", [128, 512])
        h128c = b.sb("h128c", [128, 128], BF16)
        hi2c = b.sb("hi2c", [128, 128], BF16)
        identb = b.sb("identb", [128, 128], BF16)
        tr1 = b.sb("tr1", [128, 128], BF16)
        tr2 = b.sb("tr2", [128, 128], BF16)
        lb = LNBufs(b, "f2")
        p_a = b.ps("p_a", [128, 512])
        p_b2 = b.ps("p_b2", [128, 512])
        p_o = b.ps("p_o", [128, 512])
        p_t = b.ps("p_t", [128, 128], BF16)
        p_tm = b.ps("p_tm", [128, 128])
        sems = mk_sems(b)
        block = b.block()
        seq = Seq()

        e_memset(seq, ones_col[:, :], 1.0)
        e_memset(seq, ones_row[:, :], 1.0)
        e_identity(seq, identb)
        seq.dma("sync", mcol[:, :], modc)
        seq.dma("sync", h128c[:, :], h128d)
        seq.dma("sync", hi2c[:, :], hi2d)
        for c in range(NDC):
            seq.dma("sync", x_sb[:, c * TPC:(c + 1) * TPC],
                    x3d[c * 128:(c + 1) * 128, :])
        e_ts(seq, ca1[:, :], mcol[:, 6 * NDC + 4 * NDC: 6 * NDC + 5 * NDC],
             1.0, None, ALU.add)
        e_clamp_gate(seq, gbc[:, :], mcol[:, 6 * NDC + 5 * NDC: 6 * NDC + 6 * NDC],
                     sgn[:, :])
        emit_ln_mod(seq, lb, x_sb, h_sb, ones_col, ones_row,
                    ca1, mcol[:, 6 * NDC + 3 * NDC: 6 * NDC + 4 * NDC])

        for t0 in range(0, TPC, 512):
            for fi in range(NF1):
                f0 = fi * 128
                dma_wslice_fm(seq, wbuf1, W1T, DM, f0, 128)
                dma_wslice_fm(seq, wbuf2, W3T, DM, f0, 128)
                for c in range(NDC):
                    e_mm(seq, p_a[:, :], wbuf1[:, c * 128:(c + 1) * 128],
                         h_sb[:, c * TPC + t0: c * TPC + t0 + 512],
                         start=(c == 0), stop=(c == NDC - 1))
                for c in range(NDC):
                    e_mm(seq, p_b2[:, :], wbuf2[:, c * 128:(c + 1) * 128],
                         h_sb[:, c * TPC + t0: c * TPC + t0 + 512],
                         start=(c == 0), stop=(c == NDC - 1))
                e_act(seq, silu_s[:, :], p_a[:, :], AF.Silu)
                e_tt(seq, hid[:, fi * 512:(fi + 1) * 512], silu_s[:, :],
                     p_b2[:, :], ALU.mult)
            for fc in range(NF1):
                e_mm(seq, p_a[:, :], h128c[:, :], hid[:, fc * 512:(fc + 1) * 512])
                e_copy(seq, hid[:, fc * 512:(fc + 1) * 512], p_a[:, :])
            for tp in range(256):
                tview = hid[:, :].rearrange("p (c t) -> p c t", t=512)[
                    :, :, 2 * tp: 2 * tp + 2]
                e_copy(seq, tr1[:, :].rearrange("p (c t) -> p c t", t=2), tview)
                e_tr(seq, p_t[:, :], tr1[:, :], identb[:, :])
                e_copy(seq, tr1[:, :], p_t[:, :])
                e_mm(seq, p_tm[:, :], hi2c[:, :], tr1[:, :])
                e_ts(seq, tr2[:, :], p_tm[:, :], float(1.0 / math.sqrt(DFF1)),
                     None, ALU.mult)
                e_tr(seq, p_t[:, :], tr2[:, :], identb[:, :])
                e_copy(seq, tview, p_t[:, :])
            for dc in range(NDC):
                dma_wslice_fm(seq, wbuf3, W2T, DFF1, dc * 128, 128)
                for fc in range(NF1):
                    e_mm(seq, p_o[:, :], wbuf3[:, fc * 128:(fc + 1) * 128],
                         hid[:, fc * 512:(fc + 1) * 512],
                         start=(fc == 0), stop=(fc == NF1 - 1))
                xc = x_sb[:, dc * TPC + t0: dc * TPC + t0 + 512]
                e_stt(seq, xc, p_o[:, :], gbc[:, dc:dc + 1], xc, ALU.mult, ALU.add)
        for c in range(NDC):
            seq.dma("sync", x4d[c * 128:(c + 1) * 128, :],
                    x_sb[:, c * TPC:(c + 1) * TPC])
        seq.install(nc, sems, block)
    return nc



def build_launch_C():
    nc = new_nc()
    x0d = nc.dram_tensor("x0", [DM, TPC], F32, kind="ExternalInput").ap()
    xhd = nc.dram_tensor("xh", [DM, 4], F32, kind="ExternalInput").ap()
    wipT = nc.dram_tensor("wipT", [DM, PROJ_DIM], F32, kind="ExternalInput").ap()
    cwd = nc.dram_tensor("convw", [128, 34 * KCONV], F32, kind="ExternalInput").ap()
    cbd = nc.dram_tensor("convb", [128, 34], F32, kind="ExternalInput").ap()
    dtbd = nc.dram_tensor("dtb", [NHEAD, 1], F32, kind="ExternalInput").ap()
    alogd = nc.dram_tensor("alog", [NHEAD, 1], F32, kind="ExternalInput").ap()
    drowd = nc.dram_tensor("drow", [1, NHEAD], F32, kind="ExternalInput").ap()
    modc = nc.dram_tensor("modc", [128, 2 * 6 * NDC], F32, kind="ExternalInput").ap()
    hmaskd = nc.dram_tensor("hmask", [128, 1], F32, kind="ExternalInput").ap()
    gate_d = nc.dram_tensor("gate", [TPC, INTER], BF16, kind="ExternalOutput").ap()
    yp_d = nc.dram_tensor("yp", [TPC, INTER], F32, kind="ExternalOutput").ap()
    st_d = nc.dram_tensor("cstates", [NCHUNK, NHEAD, ND, PD], BF16,
                          kind="ExternalOutput").ap()
    sum_d = nc.dram_tensor("csum", [NHEAD, ND, PD], BF16, kind="ExternalOutput").ap()
    lt_d = nc.dram_tensor("clt", [NHEAD, 1], F32, kind="ExternalOutput").ap()
    acum_d = nc.dram_tensor("acum", [NHEAD, TPC], F32, kind="ExternalOutput").ap()
    cfm_d = nc.dram_tensor("cfm", [ND, TPC], BF16, kind="ExternalOutput").ap()

    NHB = CONV_DIM // 128
    with ExitStack() as ctx:
        b = Bufs(nc, ctx)
        big0 = b.sb("big0", [128, NDC * TPC])
        h_sb = b.sb("h_sb", [128, NDC * TPC], BF16)
        arena = b.sb("arena", [128, 32 * TPC], BF16)
        hh_sb = b.sb("hh_sb", [128, 4 * NDC], BF16)
        xh_sb = b.sb("xh_sb", [128, 4 * NDC])
        B_fm = b.sb("B_fm", [128, TPC], BF16)
        C_fm = b.sb("C_fm", [128, TPC], BF16)
        B_tm = b.sb("B_tm", [128, 8 * ND], BF16)
        dtr = b.sb("dtr", [NHEAD, TPC])
        acum = b.sb("acum_sb", [NHEAD, TPC])
        hbc = b.sb("hbc", [128, TPC + 4])
        dtT = b.sb("dtT", [128, 8 * NHEAD])
        decT = b.sb("decT", [128, 8 * NHEAD])
        nacT = b.sb("nacT", [128, 8 * NHEAD])
        ltc = b.sb("ltc", [NHEAD, 1])
        cvw = b.sb("cvw", [128, 34 * KCONV])
        cvb = b.sb("cvb", [128, 34])
        hmask = b.sb("hmaskc", [128, 1])
        dcol = b.sb("dcolc", [NHEAD, 1])
        acol = b.sb("acolc", [NHEAD, 1])
        mcol = b.sb("mcolc", [128, 2 * 6 * NDC])
        ca1 = b.sb("ca1c", [128, NDC])
        wbuf = b.sb("wbufc", [128, NDC * 128], BF16)

        ones_col = b.sb("ones_colc", [128, 1])
        ones_row = b.sb("ones_rowc", [1, 128])
        identf = b.sb("identfc", [128, 128])
        identb2 = b.sb("identbc", [128, 128], BF16)
        lb = LNBufs(b, "c")
        p_g = b.ps("p_gc", [128, 512])
        p_misc = b.ps("p_miscc", [128, 512])
        p_y = b.ps("p_yc", [128, 128])
        p_st = b.ps("p_stc", [128, PD])
        p_tb = b.ps("p_tbc", [128, 128], BF16)
        sems = mk_sems(b)
        block = b.block()
        seq = Seq()

        zrow = lb.msq_row[0:1, 0:CH]
        stg = lb.t1[:, :].bitcast(BF16)[:, 0:512]
        ar_f = arena[:, :].bitcast(F32)
        ytile = [ar_f[:, 0:2048], ar_f[:, 2048:4096]]
        ar_b = arena[:, :]
        run = ar_b[:, 16384:16384 + NHEAD * PD]
        ststg = ar_b[:, 20480:20480 + 16 * PD]
        bdec = ar_b[:, 21504:21504 + ND]
        mt_sb = ar_b[:, 21632:21632 + 2 * CH]
        lt_sb = ar_b[:, 22144:22144 + CH]
        gt_sb = ar_f[:, 11200:11200 + 2 * CH]
        dchb = ar_f[:, 11712:11712 + NHEAD]
        Db = ar_f[:, 11776:11776 + NHEAD]
        adt = hbc[0:NHEAD, 0:TPC]
        t64 = b.sb("t64c", [NHEAD, CH])
        zer = b.sb("zerc", [NHEAD, CH])
        hs_tm = big0[:, :].bitcast(BF16)

        e_memset(seq, ones_col[:, :], 1.0)
        e_memset(seq, ones_row[:, :], 1.0)
        e_identity(seq, identf)
        e_identity(seq, identb2)
        seq.dma("sync", mcol[:, :], modc)
        seq.dma("sync", cvw[:, :], cwd)
        seq.dma("sync", cvb[:, :], cbd)
        seq.dma("sync", hmask[:, :], hmaskd)
        seq.dma("sync", dcol[:, :], dtbd)
        seq.dma("sync", acol[:, :], alogd)
        seq.dma("sync", xh_sb[:, :].rearrange("p (c t) -> p c t", t=4),
                xhd.rearrange("(c p) t -> p c t", p=128))
        for c in range(NDC):
            seq.dma("sync", big0[:, c * TPC:(c + 1) * TPC],
                    x0d[c * 128:(c + 1) * 128, :])
        e_ts(seq, ca1[:, :], mcol[:, NDC:2 * NDC], 1.0, None, ALU.add)
        emit_ln_mod(seq, lb, big0, h_sb, ones_col, ones_row, ca1, mcol[:, 0:NDC])
        emit_ln_mod(seq, lb, xh_sb, hh_sb, ones_col, ones_row, ca1,
                    mcol[:, 0:NDC], toks=4)

        wbufg = ar_b[:, 0:NDC * 512]
        for f0 in range(0, INTER, 512):
            src = wipT.rearrange("(c p) f -> p c f", p=128)[:, :, f0:f0 + 512]
            seq.dma("gpsimd", wbufg.rearrange("p (c f) -> p c f", f=512), src)
            for t0 in range(0, TPC, 128):
                for c in range(NDC):
                    e_mm(seq, p_g[:, :], h_sb[:, c * TPC + t0: c * TPC + t0 + 128],
                         wbufg[:, c * 512:(c + 1) * 512],
                         start=(c == 0), stop=(c == NDC - 1))
                e_copy(seq, stg[:, :], p_g[:, :])
                seq.dma("sync", gate_d[t0:t0 + 128, f0:f0 + 512], stg[:, :])

        for fi in range(NHB):
            f0 = INTER + fi * 128
            dma_wslice_fm(seq, wbuf, wipT, DM, f0, 128)
            for c in range(NDC):
                e_mm(seq, p_misc[:, 0:4], wbuf[:, c * 128:(c + 1) * 128],
                     hh_sb[:, :].rearrange("p (c t) -> p c t", t=4)[:, c, :],
                     start=(c == 0), stop=(c == NDC - 1))
            e_ts(seq, hbc[:, 0:4], p_misc[:, 0:4], hmask[:, 0:1], None, ALU.mult)
            for t0 in range(0, TPC, 512):
                for c in range(NDC):
                    e_mm(seq, p_g[:, :], wbuf[:, c * 128:(c + 1) * 128],
                         h_sb[:, c * TPC + t0: c * TPC + t0 + 512],
                         start=(c == 0), stop=(c == NDC - 1))
                e_copy(seq, hbc[:, 3 + t0: 3 + t0 + 512], p_g[:, :])
            acc = lb.mean_b
            for t0 in range(0, TPC, 512):
                e_ts(seq, acc[:, :], hbc[:, t0:t0 + 512],
                     cvw[:, fi * KCONV: fi * KCONV + 1], None, ALU.mult)
                for k in range(1, KCONV):
                    e_stt(seq, acc[:, :], hbc[:, t0 + k: t0 + k + 512],
                          cvw[:, fi * KCONV + k: fi * KCONV + k + 1],
                          acc[:, :], ALU.mult, ALU.add)
                e_ts(seq, acc[:, :], acc[:, :], cvb[:, fi:fi + 1], None, ALU.add)
                dst = (arena[:, fi * TPC + t0: fi * TPC + t0 + 512] if fi < 32
                       else (B_fm if fi == 32 else C_fm)[:, t0:t0 + 512])
                e_act(seq, dst, acc[:, :], AF.Silu)
        seq.dma("sync", cfm_d, C_fm[:, :])

        dma_wslice_fm(seq, wbuf, wipT, DM, INTER + CONV_DIM, 64)
        for t0 in range(0, TPC, 512):
            for c in range(NDC):
                e_mm(seq, p_g[0:NHEAD, :], wbuf[:, c * 64:(c + 1) * 64],
                     h_sb[:, c * TPC + t0: c * TPC + t0 + 512],
                     start=(c == 0), stop=(c == NDC - 1))
            e_copy(seq, dtr[:, t0:t0 + 512], p_g[0:NHEAD, :])
        e_ts(seq, dtr[:, :], dtr[:, :], dcol[:, :], None, ALU.add)
        e_act(seq, adt, dtr[:, :], AF.Abs)
        e_act(seq, adt, adt, AF.Exp, scale=-1.0)
        e_ts(seq, adt, adt, 1.0, None, ALU.add)
        e_act(seq, adt, adt, AF.Ln)
        e_ts(seq, dtr[:, :], dtr[:, :], 0.0, None, ALU.max)
        e_tt(seq, dtr[:, :], dtr[:, :], adt, ALU.add)
        e_ts(seq, dtr[:, :], dtr[:, :], DT_MIN, DT_MAX, ALU.max, ALU.min)
        e_act(seq, acol[:, :], acol[:, :], AF.Exp)
        e_ts(seq, acol[:, :], acol[:, :], -1.0, None, ALU.mult)
        e_ts(seq, adt, dtr[:, :], acol[:, :], None, ALU.mult)
        e_memset(seq, zer[:, :], 0.0)
        for c in range(NCHUNK):
            seq.emit("vector", lambda e, c=c: e.tensor_tensor_scan(
                acum[:, c * CH:(c + 1) * CH], adt[:, c * CH:(c + 1) * CH],
                zer[:, :], 0.0, ALU.add, ALU.add),
                reads=(adt[:, c * CH:(c + 1) * CH], zer[:, :]),
                writes=(acum[:, c * CH:(c + 1) * CH],))
        seq.dma("sync", acum_d, acum[:, :])
        e_copy(seq, ltc[:, :], acum[:, CH - 1:CH])
        for c in range(1, NCHUNK):
            e_tt(seq, ltc[:, :], ltc[:, :],
                 acum[:, (c + 1) * CH - 1:(c + 1) * CH], ALU.add)
        seq.dma("sync", lt_d, ltc[:, :])

        for c in range(NCHUNK):
            e_act(seq, t64[:, :], acum[:, c * CH:(c + 1) * CH], AF.Exp,
                  bias=acum[:, (c + 1) * CH - 1:(c + 1) * CH], scale=-1.0)
            for hf in range(2):
                col = (2 * c + hf) * NHEAD
                e_tr(seq, p_misc[:, 0:NHEAD],
                     dtr[:, c * CH + hf * 128: c * CH + (hf + 1) * 128],
                     identf[0:NHEAD, 0:NHEAD])
                e_copy(seq, dtT[:, col:col + NHEAD], p_misc[:, 0:NHEAD])
                e_tr(seq, p_misc[:, 0:NHEAD], t64[:, hf * 128:(hf + 1) * 128],
                     identf[0:NHEAD, 0:NHEAD])
                e_copy(seq, decT[:, col:col + NHEAD], p_misc[:, 0:NHEAD])
                e_tr(seq, p_misc[:, 0:NHEAD],
                     acum[:, c * CH + hf * 128: c * CH + (hf + 1) * 128],
                     identf[0:NHEAD, 0:NHEAD])
                e_ts(seq, nacT[:, col:col + NHEAD], p_misc[:, 0:NHEAD],
                     -1.0, None, ALU.mult)

        identb = identb2[:, :]
        pt_b = p_tb[:, :]
        for tc in range(8):
            for fc in range(32):
                e_tr(seq, pt_b,
                     arena[:, fc * TPC + tc * 128: fc * TPC + (tc + 1) * 128],
                     identb)
                e_copy(seq, hs_tm[:, tc * 4096 + fc * 128:
                                  tc * 4096 + (fc + 1) * 128], pt_b)
            e_tr(seq, pt_b, B_fm[:, tc * 128:(tc + 1) * 128], identb)
            e_copy(seq, B_tm[:, tc * ND:(tc + 1) * ND], pt_b)

        seq.dma("sync", zrow[:, 0:NHEAD], drowd)
        e_mm(seq, p_misc[:, 0:NHEAD], ones_row[:, :], zrow[:, 0:NHEAD])
        e_copy(seq, Db, p_misc[:, 0:NHEAD])

        e_memset(seq, run, 0.0)
        for c in range(NCHUNK):
            for lh in range(2):
                e_mm(seq, p_misc[:, 0:CH],
                     B_fm[:, c * CH + lh * 128: c * CH + (lh + 1) * 128],
                     C_fm[:, c * CH:(c + 1) * CH])
                e_copy(seq, gt_sb[:, lh * CH:(lh + 1) * CH], p_misc[:, 0:CH])
            e_act(seq, ltc[:, :], acum[:, (c + 1) * CH - 1:(c + 1) * CH], AF.Exp)
            e_tr(seq, p_misc[0:1, 0:NHEAD], ltc[:, :],
                 identf[0:NHEAD, 0:NHEAD])
            e_copy(seq, zrow[:, 0:NHEAD], p_misc[0:1, 0:NHEAD])
            e_mm(seq, p_misc[:, 0:NHEAD], ones_row[:, :], zrow[:, 0:NHEAD])
            e_copy(seq, dchb, p_misc[:, 0:NHEAD])
            for h in range(NHEAD):
                for zf in range(2):
                    col = (2 * c + zf) * NHEAD + h
                    e_tr(seq, p_g[0:1, 0:128], nacT[:, col:col + 1],
                         identf[:, :])
                    e_ts(seq, zrow[:, zf * 128:(zf + 1) * 128], p_g[0:1, 0:128],
                         -1.0, None, ALU.mult)
                e_mm(seq, p_misc[:, 0:CH], ones_row[:, :], zrow[:, 0:CH])
                for lh in range(2):
                    col = (2 * c + lh) * NHEAD + h
                    e_act(seq, lt_sb, p_misc[:, 0:CH], AF.Exp,
                          bias=nacT[:, col:col + 1])
                    seq.emit("gpsimd", lambda e, lh=lh: e.affine_select(
                        out=lt_sb, in_=lt_sb, compare_op=ALU.is_ge,
                        fill=0.0, base=-(lh * 128), pattern=[[1, CH]],
                        channel_multiplier=-1),
                        reads=(lt_sb,), writes=(lt_sb,))
                    e_stt(seq, mt_sb[:, lh * CH:(lh + 1) * CH],
                          gt_sb[:, lh * CH:(lh + 1) * CH], dtT[:, col:col + 1],
                          lt_sb, ALU.mult, ALU.mult)
                hs0 = hs_tm[:, (2 * c) * 4096 + h * PD: (2 * c) * 4096 + h * PD + PD]
                hs1 = hs_tm[:, (2 * c + 1) * 4096 + h * PD:
                            (2 * c + 1) * 4096 + h * PD + PD]
                e_mm(seq, p_y[:, 0:PD], mt_sb[:, 0:128], hs0)
                e_mm(seq, p_y[:, PD:2 * PD], mt_sb[:, 128:CH], hs0, stop=False)
                e_mm(seq, p_y[:, PD:2 * PD], mt_sb[:, CH + 128:2 * CH], hs1,
                     start=False)
                for lh in range(2):
                    col = (2 * c + lh) * NHEAD + h
                    e_ts(seq, bdec, B_tm[:, (2 * c + lh) * ND:(2 * c + lh + 1) * ND],
                         decT[:, col:col + 1], dtT[:, col:col + 1],
                         ALU.mult, ALU.mult)
                    e_mm(seq, p_st[:, :], bdec, (hs0 if lh == 0 else hs1),
                         start=(lh == 0), stop=(lh == 1))
                hq = h % 16
                e_copy(seq, ststg[:, hq * PD:(hq + 1) * PD], p_st[:, :])
                e_stt(seq, run[:, h * PD:(h + 1) * PD], run[:, h * PD:(h + 1) * PD],
                      dchb[:, h:h + 1], ststg[:, hq * PD:(hq + 1) * PD],
                      ALU.mult, ALU.add)
                if hq == 15:
                    seq.dma("sync",
                            st_d[c, h - 15:h + 1].transpose([1, 0, 2]),
                            ststg[:, :].rearrange("n (h p) -> n h p", p=PD))
                hw = h % 32
                for zh in range(2):
                    e_stt(seq, ytile[zh][:, hw * PD:(hw + 1) * PD],
                          (hs0 if zh == 0 else hs1), Db[:, h:h + 1],
                          p_y[:, zh * PD:(zh + 1) * PD], ALU.mult, ALU.add)
                if hw == 31:
                    g0 = h - 31
                    for zh in range(2):
                        seq.dma("sync",
                                yp_d[c * CH + zh * 128: c * CH + (zh + 1) * 128,
                                     g0 * PD:(g0 + 32) * PD], ytile[zh])
        for g in range(4):
            seq.dma("sync",
                    sum_d[g * 16:(g + 1) * 16].transpose([1, 0, 2]),
                    run[:, g * 1024:(g + 1) * 1024].rearrange(
                        "n (h p) -> n h p", p=PD))
        seq.install(nc, sems, block)
    return nc



def build_launch_D():
    nc = new_nc()
    x0d = nc.dram_tensor("x0", [DM, TPC], F32, kind="ExternalInput").ap()
    ypd = nc.dram_tensor("yp", [TPC, INTER], F32, kind="ExternalInput").ap()
    gated = nc.dram_tensor("gate", [TPC, INTER], BF16, kind="ExternalInput").ap()
    cfmd = nc.dram_tensor("cfm", [ND, TPC], BF16, kind="ExternalInput").ap()
    acumd = nc.dram_tensor("acum", [NHEAD, TPC], F32, kind="ExternalInput").ap()
    cstd = nc.dram_tensor("cstates", [NCHUNK, NHEAD, ND, PD], BF16,
                          kind="ExternalInput").ap()
    csall = nc.dram_tensor("csumall", [NCORE, NHEAD, ND, PD], BF16,
                           kind="ExternalInput").ap()
    cltall = nc.dram_tensor("cltall", [NCORE, NHEAD], F32,
                            kind="ExternalInput").ap()
    miTd = nc.dram_tensor("miT", [NCORE, NCORE], F32, kind="ExternalInput").ap()
    incd = nc.dram_tensor("incm", [NCORE, 1], F32, kind="ExternalInput").ap()
    wopT = nc.dram_tensor("wopT", [INTER, DM], F32, kind="ExternalInput").ap()
    nwT = nc.dram_tensor("nwT", [128, 32], F32, kind="ExternalInput").ap()
    modc = nc.dram_tensor("modc", [128, 2 * 6 * NDC], F32, kind="ExternalInput").ap()
    x1d = nc.dram_tensor("x1", [DM, TPC], F32, kind="ExternalOutput").ap()
    coef_bounce = nc.dram_tensor("coef_bounce", [NCORE, NHEAD], F32)

    with ExitStack() as ctx:
        b = Bufs(nc, ctx)
        run = b.sb("run_d", [128, NHEAD * PD], BF16)
        sgbuf = b.sb("sgbuf", [128, NHEAD * PD], BF16)
        ybuf = b.sb("ybuf", [128, INTER])
        gbuf = b.sb("gbuf", [128, INTER], BF16)
        yn_tm = b.sb("yn_tm", [128, INTER], BF16)
        yn_fm = b.sb("yn_fm", [128, 32 * TPC], BF16)
        cfm = b.sb("cfm_sb", [128, TPC], BF16)
        acum = b.sb("acum_sbd", [NHEAD, TPC])
        sdoT = b.sb("sdoT", [128, 8 * NHEAD])
        x0c = b.sb("x0c", [128, TPC])
        wbuf = b.sb("wbufd", [128, 32 * 128], BF16)
        ltall = b.sb("ltall", [NCORE, NHEAD])
        miT = b.sb("miT_sb", [NCORE, NCORE])
        incm = b.sb("incm_sb", [NCORE, 1])
        coef = b.sb("coef_sb", [NCORE, NHEAD])
        coefrow = b.sb("coefrow", [1, NCORE * NHEAD])
        cbj = b.sb("cbj", [128, NHEAD], BF16)
        dchb = b.sb("dchb_d", [128, NHEAD], BF16)
        ltc = b.sb("ltc_d", [NHEAD, 1])
        zrow = b.sb("zrow_d", [1, 128])
        nwc = b.sb("nwc", [128, 32])
        mcol = b.sb("mcold", [128, 2 * 6 * NDC])
        gac = b.sb("gacd", [128, NDC])
        sgn = b.sb("sgnd", [128, NDC])
        sq = b.sb("sq_d", [128, 512])
        ss = b.sb("ss_d", [128, 1])
        sst = b.sb("sst_d", [128, 1])
        ones_row = b.sb("ones_rowd", [1, 128])
        identf = b.sb("identfd", [128, 128])
        identb2 = b.sb("identbd", [128, 128], BF16)
        stg = b.sb("stg_d", [128, 512])
        p_yo = b.ps("p_yod", [128, 128])
        p_b = b.ps("p_bd", [128, 512])
        p_o = b.ps("p_od", [128, 512])
        p_tb = b.ps("p_tbd", [128, 128], BF16)
        sems = mk_sems(b)
        block = b.block()
        seq = Seq()

        e_memset(seq, ones_row[:, :], 1.0)
        e_identity(seq, identf)
        e_identity(seq, identb2)
        seq.dma("sync", mcol[:, :], modc)
        seq.dma("sync", nwc[:, :], nwT)
        seq.dma("sync", ltall[:, :], cltall)
        seq.dma("sync", miT[:, :], miTd)
        seq.dma("sync", incm[:, :], incd)
        seq.dma("sync", cfm[:, :], cfmd)
        seq.dma("sync", acum[:, :], acumd)
        e_clamp_gate(seq, gac[:, :], mcol[:, 2 * NDC:3 * NDC], sgn[:, :])

        e_mm(seq, p_b[0:NCORE, 0:NHEAD], miT[:, :], ltall[:, :])
        e_act(seq, coef[:, :], p_b[0:NCORE, 0:NHEAD], AF.Exp)
        e_ts(seq, coef[:, :], coef[:, :], incm[:, 0:1], None, ALU.mult)
        seq.dma("sync", coef_bounce.ap(), coef[:, :])
        seq.dma("sync", coefrow[:, :],
                coef_bounce.ap().rearrange("a b -> (a b)")[None, :])

        e_memset(seq, run[:, :], 0.0)
        for j in range(NCORE):
            seq.dma("sync", sgbuf[:, :].rearrange("n (h p) -> n h p", p=PD),
                    csall[j].transpose([1, 0, 2]))
            e_mm(seq, p_b[:, 0:NHEAD], ones_row[:, :],
                 coefrow[:, j * NHEAD:(j + 1) * NHEAD])
            e_copy(seq, cbj[:, :], p_b[:, 0:NHEAD])
            for h in range(NHEAD):
                e_stt(seq, run[:, h * PD:(h + 1) * PD],
                      sgbuf[:, h * PD:(h + 1) * PD], cbj[:, h:h + 1],
                      run[:, h * PD:(h + 1) * PD], ALU.mult, ALU.add)

        for c in range(NCHUNK):
            for hf in range(2):
                col = (2 * c + hf) * NHEAD
                e_tr(seq, p_yo[:, 0:NHEAD],
                     acum[:, c * CH + hf * 128: c * CH + (hf + 1) * 128],
                     identf[0:NHEAD, 0:NHEAD])
                e_act(seq, sdoT[:, col:col + NHEAD], p_yo[:, 0:NHEAD], AF.Exp)

        for c in range(NCHUNK):
            for lh in range(2):
                tok0 = c * CH + lh * 128
                seq.dma("sync", ybuf[:, :], ypd[tok0:tok0 + 128, :])
                seq.dma("sync", gbuf[:, :], gated[tok0:tok0 + 128, :])
                for hb in range(NHEAD // 8):
                    e_mm(seq, p_b[:, :], cfm[:, tok0:tok0 + 128],
                         run[:, hb * 512:(hb + 1) * 512])
                    for hl in range(8):
                        h = hb * 8 + hl
                        e_stt(seq, ybuf[:, h * PD:(h + 1) * PD],
                              p_b[:, hl * PD:(hl + 1) * PD],
                              sdoT[:, (2 * c + lh) * NHEAD + h:
                                   (2 * c + lh) * NHEAD + h + 1],
                              ybuf[:, h * PD:(h + 1) * PD], ALU.mult, ALU.add)
                for s0 in range(0, INTER, 512):
                    e_act(seq, sq[:, :], gbuf[:, s0:s0 + 512], AF.Silu)
                    e_tt(seq, ybuf[:, s0:s0 + 512], ybuf[:, s0:s0 + 512],
                         sq[:, :], ALU.mult)
                    e_act(seq, sq[:, :], ybuf[:, s0:s0 + 512], AF.Square,
                          accum_out=sst[:, :])
                    if s0 == 0:
                        e_copy(seq, ss[:, :], sst[:, :])
                    else:
                        e_tt(seq, ss[:, :], ss[:, :], sst[:, :], ALU.add)
                e_ts(seq, ss[:, :], ss[:, :], 1.0 / INTER, MEPS, ALU.mult, ALU.add)
                e_act(seq, ss[:, :], ss[:, :], AF.Sqrt)
                e_recip(seq, ss[:, :], ss[:, :])
                for s0 in range(0, INTER, 512):
                    e_ts(seq, yn_tm[:, s0:s0 + 512], ybuf[:, s0:s0 + 512],
                         ss[:, 0:1], None, ALU.mult)
                for fc in range(32):
                    e_tr(seq, p_tb[:, :], yn_tm[:, fc * 128:(fc + 1) * 128],
                         identb2[:, :])
                    e_ts(seq, yn_fm[:, fc * TPC + tok0: fc * TPC + tok0 + 128],
                         p_tb[:, :], nwc[:, fc:fc + 1], None, ALU.mult)
            e_act(seq, ltc[:, :], acum[:, (c + 1) * CH - 1:(c + 1) * CH], AF.Exp)
            e_tr(seq, p_yo[0:1, 0:NHEAD], ltc[:, :],
                 identf[0:NHEAD, 0:NHEAD])
            e_copy(seq, zrow[:, 0:NHEAD], p_yo[0:1, 0:NHEAD])
            e_mm(seq, p_b[:, 0:NHEAD], ones_row[:, :], zrow[:, 0:NHEAD])
            e_copy(seq, dchb[:, :], p_b[:, 0:NHEAD])
            seq.dma("sync", sgbuf[:, :].rearrange("n (h p) -> n h p", p=PD),
                    cstd[c].transpose([1, 0, 2]))
            for h in range(NHEAD):
                e_stt(seq, run[:, h * PD:(h + 1) * PD],
                      run[:, h * PD:(h + 1) * PD], dchb[:, h:h + 1],
                      sgbuf[:, h * PD:(h + 1) * PD], ALU.mult, ALU.add)

        def op_consume(d0, t0, tw, psum):
            dc = d0 // 128
            if t0 == 0:
                for tt in range(0, TPC, 512):
                    seq.dma("sync", x0c[:, tt:tt + 512],
                            x0d[dc * 128:(dc + 1) * 128, tt:tt + 512])
            e_stt(seq, stg[:, :tw], psum, gac[:, dc:dc + 1], x0c[:, t0:t0 + tw],
                  ALU.mult, ALU.add)
            seq.dma("sync", x1d[dc * 128:(dc + 1) * 128, t0:t0 + tw], stg[:, :tw])

        NIC = INTER // 128
        for d0 in range(0, DM, 128):
            dma_wslice_fm(seq, wbuf, wopT, INTER, d0, 128)
            for t0 in range(0, TPC, 512):
                for c in range(NIC):
                    e_mm(seq, p_o[:, :], wbuf[:, c * 128:(c + 1) * 128],
                         yn_fm[:, c * TPC + t0: c * TPC + t0 + 512],
                         start=(c == 0), stop=(c == NIC - 1))
                op_consume(d0, t0, 512, p_o[:, :])
        seq.install(nc, sems, block)
    return nc



_PROGS = {}


def _prog(name, builder):
    if name not in _PROGS:
        _PROGS[name] = builder()
    return _PROGS[name]


def _hadamard(n):
    H = np.array([[1.0]], np.float32)
    while H.shape[0] < n:
        H = np.block([[H, H], [H, -H]])
    return H


def kernel(**inputs):
    f32 = np.float32
    CORES = list(range(NCORE))
    ca = np.ascontiguousarray

    x_all = ca(np.asarray(inputs["L0_attn_input_states"], f32)[0])
    w1 = np.asarray(inputs["L0_ffn_weight1"])
    w2 = np.asarray(inputs["L0_ffn_weight2"])
    w3 = np.asarray(inputs["L0_ffn_weight3"])

    inA = []
    for core in CORES:
        m = {}
        for L in range(2):
            cvec = np.asarray(inputs[f"L{L}_c"], f32).reshape(16, 128)
            m[f"c{L}T"] = ca(cvec.T)
            wm = np.asarray(inputs[f"L{L}_wmod"], f32)
            m[f"wmod{L}s"] = ca(wm[:, core * 1536:(core + 1) * 1536])
        m["w1s"] = ca(w1[core * W1_ROWS:(core + 1) * W1_ROWS])
        m["w2s"] = ca(w2[core * W1_ROWS:(core + 1) * W1_ROWS])
        m["w3s"] = ca(w3[core * W3_ROWS:(core + 1) * W3_ROWS])
        inA.append(m)
    resA = run_bass_kernel_spmd(_prog("A", build_launch_A), inA, core_ids=CORES)
    mod_full = np.zeros((2, 12288), f32)
    for c in CORES:
        mod_full[:, c * 1536:(c + 1) * 1536] = resA.results[c]["modrow"]
    mc = np.zeros((128, 192), f32)
    for L in range(2):
        mc[:, L * 96:(L + 1) * 96] = (
            mod_full[L].reshape(6, 16, 128).transpose(2, 0, 1).reshape(128, 96))
    ssums = ca(np.stack([resA.results[c]["ssum"][0] for c in CORES]))

    inB = [dict(w1s=inA[c]["w1s"], w2s=inA[c]["w2s"], w3s=inA[c]["w3s"],
                allsum=ssums) for c in CORES]
    resB = run_bass_kernel_spmd(_prog("B", build_launch_B), inB, core_ids=CORES)
    w1pT = ca(np.concatenate([resB.results[c]["wq1"] for c in CORES], axis=0).T)
    w2pT = ca(np.concatenate([resB.results[c]["wq2"] for c in CORES], axis=0).T)
    w3pT = ca(np.concatenate([resB.results[c]["wq3"] for c in CORES], axis=0).T)

    cw = np.asarray(inputs["L0_attn_conv_w"], f32).reshape(CONV_DIM, KCONV)
    cwc = ca(cw.reshape(34, 128, KCONV).transpose(1, 0, 2).reshape(128, 34 * KCONV))
    cbc = ca(np.asarray(inputs["L0_attn_conv_b"], f32).reshape(34, 128).T)
    wipT = ca(np.asarray(inputs["L0_attn_in_proj_w"], f32).T)
    dtb = ca(np.asarray(inputs["L0_attn_dt_bias"], f32).reshape(NHEAD, 1))
    alog = ca(np.asarray(inputs["L0_attn_A_log"], f32).reshape(NHEAD, 1))
    drow = ca(np.asarray(inputs["L0_attn_D"], f32).reshape(1, NHEAD))
    x0s = [ca(x_all[c * TPC:(c + 1) * TPC].T) for c in CORES]
    inC = []
    for core in CORES:
        xh = np.zeros((DM, KCONV), f32)
        if core > 0:
            xh[:, :] = x_all[core * TPC - 3: core * TPC + 1].T
        hm = np.full((128, 1), 0.0 if core == 0 else 1.0, f32)
        inC.append(dict(x0=x0s[core], xh=ca(xh), wipT=wipT, convw=cwc,
                        convb=cbc, dtb=dtb, alog=alog, drow=drow, modc=mc,
                        hmask=hm))
    resC = run_bass_kernel_spmd(_prog("C", build_launch_C), inC, core_ids=CORES)

    csumall = ca(np.stack([resC.results[c]["csum"] for c in CORES]))
    cltall = ca(np.stack([resC.results[c]["clt"][:, 0] for c in CORES]))
    wopT = ca(np.asarray(inputs["L0_attn_out_proj_w"], f32).T)
    nwT = ca(np.asarray(inputs["L0_attn_norm_w"], f32).reshape(32, 128).T)
    inD = []
    for core in CORES:
        miT = np.zeros((NCORE, NCORE), f32)
        incm = np.zeros((NCORE, 1), f32)
        for j in range(NCORE):
            if j < core:
                incm[j, 0] = 1.0
            for k in range(NCORE):
                if j < k < core:
                    miT[k, j] = 1.0
        r = resC.results[core]
        inD.append(dict(x0=x0s[core], yp=r["yp"], gate=r["gate"], cfm=r["cfm"],
                        acum=r["acum"], cstates=r["cstates"], csumall=csumall,
                        cltall=cltall, miT=miT, incm=incm, wopT=wopT, nwT=nwT,
                        modc=mc))
    resD = run_bass_kernel_spmd(_prog("D", build_launch_D), inD, core_ids=CORES)

    wqkvT = ca(np.asarray(inputs["L1_attn_w_qkv"], f32).T)
    RL = np.zeros((128, 128), f32)
    for i in range(64):
        RL[i + 64, i] = -1.0
        RL[i, i + 64] = 1.0
    RLb = RL.astype(bf16)
    inE = []
    for core in CORES:
        pos = ca((np.arange(TPC, dtype=f32) + core * TPC)[None, :])
        inE.append(dict(x1=resD.results[core]["x1"], w1pT=w1pT, w2pT=w2pT,
                        w3pT=w3pT, wqkvT=wqkvT, modc=mc, rrot=RLb, posr=pos))
    resE = run_bass_kernel_spmd(_prog("E", build_launch_E), inE, core_ids=CORES)

    kTall = ca(np.concatenate([resE.results[c]["kT"] for c in CORES], axis=2))
    vt_full = np.concatenate([resE.results[c]["vt"] for c in CORES], axis=0)
    vall = ca(vt_full.reshape(SEQ, AHEADS, HD).transpose(1, 0, 2))
    qkmaxall = ca(np.concatenate([resE.results[c]["qkmax"] for c in CORES],
                                 axis=0))
    woT = ca(np.asarray(inputs["L1_attn_w_out"], f32).T)
    inF1 = [dict(x2=resE.results[c]["x2"], qT=resE.results[c]["qT"],
                 kTall=kTall, vall=vall, woT=woT, modc=mc, qkmaxall=qkmaxall)
            for c in CORES]
    resF1 = run_bass_kernel_spmd(_prog("F1", build_launch_F1), inF1,
                                 core_ids=CORES)

    W1T = ca(np.asarray(inputs["L1_ffn_W1"], f32).T)
    W3T = ca(np.asarray(inputs["L1_ffn_W3"], f32).T)
    W2T = ca(np.asarray(inputs["L1_ffn_W2"], f32).T)
    H128b = _hadamard(128).astype(bf16)
    H64 = _hadamard(64)
    HI2 = np.zeros((128, 128), f32)
    for cp in range(64):
        for t2 in range(2):
            HI2[cp * 2 + t2, t2::2] = H64[cp]
    HI2b = HI2.astype(bf16)
    inF2 = [dict(x3=resF1.results[c]["x3"], W1T=W1T, W3T=W3T, W2T=W2T,
                 modc=mc, h128=H128b, hi2=HI2b) for c in CORES]
    resF2 = run_bass_kernel_spmd(_prog("F2", build_launch_F2), inF2,
                                 core_ids=CORES)

    out = np.concatenate([resF2.results[c]["x4"].T for c in CORES], axis=0)
    return ca(out.astype(np.float32))[None]


# version 09 (final)
# speedup vs baseline: 1.2434x; 1.0082x over previous
import math
from contextlib import ExitStack
import numpy as np
import ml_dtypes
import concourse.bass as bass
import concourse.mybir as mybir
from concourse.bass_utils import run_bass_kernel_spmd

F32 = mybir.dt.float32
BF16 = mybir.dt.bfloat16
AF = mybir.ActivationFunctionType
ALU = mybir.AluOpType
AX = mybir.AxisListType
ds = bass.ds
bf16 = ml_dtypes.bfloat16

NCORE = 8
SEQ = 8192
TPC = SEQ // NCORE
DM = 2048
NDC = DM // 128
CH = 256
NCHUNK = TPC // CH

INTER = 4096
NHEAD = 64
PD = 64
ND = 128
KCONV = 4
CONV_DIM = INTER + 2 * ND
PROJ_DIM = INTER + CONV_DIM + NHEAD
DT_MIN, DT_MAX = 1e-3, 100.0
MEPS = 1e-5
LEPS = 1e-5

DFF0 = 5632
W1_ROWS = DFF0 // NCORE
W3_ROWS = DM // NCORE
NBLK = 22528
NGRP = NBLK // 256
GPP = NBLK // 128
TOT_BLOCKS = 180224
_NF4_REF = np.array([-1.0, -0.6962, -0.5251, -0.3949, -0.2844, -0.1848,
                     -0.0911, 0.0, 0.0796, 0.1609, 0.2461, 0.3379, 0.4407,
                     0.5626, 0.723, 1.0], dtype=bf16).astype(np.float64)
NF4L = [float(v) for v in _NF4_REF]
NF4_MID = [(NF4L[i] + NF4L[i + 1]) / 2.0 for i in range(15)]
RNE_MAGIC = 12582912.0

AHEADS = 16
HD = 128
ROPE_BASE = 10000.0

DFF1 = 8192

ENGINES = ("sync", "gpsimd", "vector", "scalar", "tensor")

LOGICAL = ("sync", "gpdma", "gpop", "vector", "scalar", "tensor")
NL = len(LOGICAL)
LIDX = {e: i for i, e in enumerate(LOGICAL)}
PHYS_OF = {"sync": "sync", "gpdma": "gpsimd", "gpop": "gpsimd",
           "vector": "vector", "scalar": "scalar", "tensor": "tensor"}
TENS_LI = LIDX["tensor"]

_DT_SIZE = {}


def _ap_ranges(ap):
    pat = ap.ap
    dt = ap.dtype
    esz = _DT_SIZE.get(dt)
    if esz is None:
        esz = dt.size()
        _DT_SIZE[dt] = esz
    off = ap.offset
    if str(ap.space).endswith("DRAM"):
        span = sum(st * (sz - 1) for st, sz in pat) + 1
        return (ap.name, 0, 128, off * esz, (off + span) * esz)
    p0 = ap.base_partition()
    pcnt = pat[0][1]
    pstride = pat[0][0]
    prow = off // pstride if pstride > 0 else 0
    f0 = off - prow * pstride
    span = (sum(st * (sz - 1) for st, sz in pat[1:]) + 1) if len(pat) > 1 else 1
    return (ap.name, p0, p0 + pcnt, f0 * esz, (f0 + span) * esz)


class Seq:

    def __init__(self):
        self.steps = []

    def emit(self, engine, fn, inc=1, reads=None, writes=None):
        assert engine in ENGINES, engine
        if engine == "gpsimd":
            engine = "gpdma" if inc == 16 else "gpop"
        elif engine == "sync":
            assert inc == 16, "sync engine hosts only DMA"
        self.steps.append((engine, fn, inc, reads, writes))

    def dma(self, engine, out, in_, **kw):
        self.emit(engine, lambda e: e.dma_start(out=out, in_=in_, **kw), 16,
                  reads=(in_,), writes=(out,))

    def install(self, nc, sems, block):
        steps = self.steps
        nst = len(steps)
        cum = [0] * NL
        issue_guar = {p: [0] * NL for p in ENGINES}
        cells = {}
        fbsz = {}
        waits = [None] * nst
        barrier_vec = [0] * NL

        def cells_of(ap):
            try:
                name, p0, p1, b0, b1 = _ap_ranges(ap)
            except Exception:
                return None
            fb = fbsz.get(name)
            if fb is None:
                try:
                    t = ap.tensor
                    totb = 1
                    for s in t.shape:
                        totb *= s
                    totb *= _DT_SIZE.get(ap.dtype) or ap.dtype.size()
                except Exception:
                    totb = 1 << 20
                fb = max(2048, totb >> 6)
                fbsz[name] = fb
            out = []
            for pb in range(p0 >> 5, ((p1 - 1) >> 5) + 1):
                for fbk in range(b0 // fb, (b1 - 1) // fb + 1):
                    out.append((name, pb, fbk))
            return out

        for k, (leng, fn, inc, reads, writes) in enumerate(steps):
            li = LIDX[leng]
            phys = PHYS_OF[leng]
            R = list(barrier_vec)
            rcells = []
            wcells = []
            bad = reads is None and writes is None
            if not bad:
                for ap in (reads or ()):
                    if not hasattr(ap, "offset"):
                        continue
                    cl = cells_of(ap)
                    if cl is None:
                        bad = True
                        break
                    rcells.extend(cl)
                for ap in (writes or ()):
                    if bad or not hasattr(ap, "offset"):
                        continue
                    cl = cells_of(ap)
                    if cl is None:
                        bad = True
                        break
                    wcells.extend(cl)
            if bad:
                for i in range(NL):
                    if cum[i] > R[i]:
                        R[i] = cum[i]
            else:
                for key in rcells:
                    c = cells.get(key)
                    if c is not None and c[0] is not None:
                        w = c[0]
                        for i in range(NL):
                            if w[i] > R[i]:
                                R[i] = w[i]
                for key in wcells:
                    c = cells.get(key)
                    if c is not None:
                        for v in c:
                            if v is not None:
                                for i in range(NL):
                                    if v[i] > R[i]:
                                        R[i] = v[i]
            iq = issue_guar[phys]
            wts = []
            for j in range(NL):
                if R[j] > iq[j]:
                    if phys == "tensor" and j == TENS_LI:
                        iq[j] = R[j]
                        continue
                    wts.append((j, R[j]))
                    iq[j] = R[j]
            waits[k] = wts
            cum[li] += inc
            g = [max(R[i], iq[i]) for i in range(NL)]
            g[li] = cum[li]
            if bad:
                barrier_vec = g
                cells.clear()
            else:
                for key in wcells:
                    cells[key] = [g, None]
                for key in rcells:
                    c = cells.get(key)
                    if c is None:
                        cells[key] = [None, list(g)]
                    elif c[1] is None:
                        c[1] = list(g)
                    else:
                        r = c[1]
                        for i in range(NL):
                            if g[i] > r[i]:
                                r[i] = g[i]

        totals = list(cum)

        def make(phys_name):
            def body(eng):
                for k, (leng, fn, inc, _r, _w) in enumerate(steps):
                    if PHYS_OF[leng] != phys_name:
                        continue
                    for (j, v) in waits[k]:
                        eng.wait_ge(sems[LOGICAL[j]], v)
                    inst = fn(eng)
                    assert inst is not None, f"{leng} step returned None"
                    inst.then_inc(sems[leng], inc)
                if phys_name == "sync":
                    for j, nm in enumerate(LOGICAL):
                        if totals[j]:
                            eng.wait_ge(sems[nm], totals[j])
            return body

        for phys_name in ENGINES:
            getattr(block, phys_name)(make(phys_name))
        return totals


def mk_sems(b):
    return {nm: b.sem(f"SQ_{nm}") for nm in LOGICAL}


def new_nc():
    return bass.Bass("TRN2", target_bir_lowering=False)


class Bufs:

    def __init__(self, nc, ctx):
        self.nc, self.ctx = nc, ctx

    def sb(self, name, shape, dtype=F32):
        return self.ctx.enter_context(self.nc.sbuf_tensor(name, shape, dtype))

    def ps(self, name, shape, dtype=F32):
        return self.ctx.enter_context(self.nc.psum_tensor(name, shape, dtype))

    def sem(self, name="S"):
        return self.ctx.enter_context(self.nc.semaphore(name))

    def block(self):
        return self.ctx.enter_context(self.nc.Block())



def e_memset(seq, ap, val, engine="vector"):
    seq.emit(engine, lambda e: e.memset(ap, float(val)), reads=(), writes=(ap,))


def e_identity(seq, ident):
    seq.emit("gpsimd", lambda e: e.memset(ident[:, :], 0.0),
             reads=(), writes=(ident[:, :],))
    seq.emit("gpsimd", lambda e: e.affine_select(
        out=ident[:, :], in_=ident[:, :], compare_op=ALU.not_equal,
        fill=1.0, base=0, pattern=[[-1, ident.shape[1]]], channel_multiplier=1),
        reads=(ident[:, :],), writes=(ident[:, :],))


def e_copy(seq, out, in_, engine="vector"):
    seq.emit(engine, lambda e: e.tensor_copy(out, in_),
             reads=(in_,), writes=(out,))


def e_act(seq, out, in_, func, bias=0.0, scale=1.0, accum_out=None):
    wr = (out,) if accum_out is None else (out, accum_out)
    seq.emit("scalar", lambda e: e.activation(out, in_, func, bias=bias,
                                              scale=scale, accum_out=accum_out),
             reads=(in_, bias), writes=wr)


def e_ts(seq, out, in0, s1, s2=None, op0=ALU.mult, op1=ALU.bypass, engine="vector"):
    if s2 is None:
        seq.emit(engine, lambda e: e.tensor_scalar(out, in0, s1, None, op0),
                 reads=(in0, s1), writes=(out,))
    else:
        seq.emit(engine, lambda e: e.tensor_scalar(out, in0, s1, s2, op0, op1),
                 reads=(in0, s1, s2), writes=(out,))


def e_tt(seq, out, in0, in1, op, engine="vector"):
    seq.emit(engine, lambda e: e.tensor_tensor(out, in0, in1, op),
             reads=(in0, in1), writes=(out,))


def e_stt(seq, out, in0, scalar, in1, op0, op1, engine="vector"):
    seq.emit(engine, lambda e: e.scalar_tensor_tensor(out, in0, scalar, in1,
                                                      op0, op1),
             reads=(in0, scalar, in1), writes=(out,))


def e_red(seq, out, in_, op, absval=False, negate=False):
    seq.emit("vector", lambda e: e.tensor_reduce(out, in_, AX.X, op,
                                                 apply_absolute_value=absval,
                                                 negate=negate),
             reads=(in_,), writes=(out,))


def e_mm(seq, out, lhsT, rhs, start=True, stop=True):
    rd = (lhsT, rhs) if start else (lhsT, rhs, out)
    seq.emit("tensor", lambda e: e.matmul(out, lhsT, rhs, start=start,
                                          stop=stop),
             reads=rd, writes=(out,))


def e_tr(seq, out, in_, ident):
    seq.emit("tensor", lambda e: e.transpose(out, in_, ident),
             reads=(in_, ident), writes=(out,))


def e_recip(seq, out, in_):
    seq.emit("vector", lambda e: e.reciprocal(out, in_),
             reads=(in_,), writes=(out,))


def e_bcast_col(seq, nc, ones_row, src_row, psum, dst_col=None):
    e_mm(seq, psum, ones_row[:, :], src_row)
    if dst_col is not None:
        e_copy(seq, dst_col, psum)



def build_launch_A(do_mod=True, do_nf4=True):
    nc = new_nc()
    cT = {}
    wmod = {}
    for L in range(2):
        cT[L] = nc.dram_tensor(f"c{L}T", [128, 16], F32, kind="ExternalInput").ap()
        wmod[L] = nc.dram_tensor(f"wmod{L}s", [DM, 12288 // NCORE], F32,
                                 kind="ExternalInput").ap()
    wsh = {}
    for i, (r, c) in enumerate([(W1_ROWS, DM), (W1_ROWS, DM), (W3_ROWS, DFF0)]):
        wsh[i] = nc.dram_tensor(f"w{i+1}s", [r, c], BF16, kind="ExternalInput").ap()
    modrow = nc.dram_tensor("modrow", [2, 12288 // NCORE], F32,
                            kind="ExternalOutput").ap()
    ssum = nc.dram_tensor("ssum", [1, 4], F32, kind="ExternalOutput").ap()

    FSH = 12288 // NCORE
    with ExitStack() as ctx:
        b = Bufs(nc, ctx)
        sc = b.sb("sc", [128, 16])
        silu_c = b.sb("silu_c", [128, 16])
        wbuf = b.sb("wbuf", [128, 16 * 512])
        mrow = [b.sb(f"mrow{L}", [1, FSH]) for L in range(2)]
        wg = b.sb("wg", [NGRP, 256 * 64], BF16)
        s1g = b.sb("s1g", [NGRP, 256])
        rowsum = b.sb("rowsum", [NGRP, 1])
        ones_col = b.sb("ones_col", [128, 1])
        ssum_sb = b.sb("ssum_sb", [1, 4])
        p_row = b.ps("p_row", [1, 512])
        p_sum = b.ps("p_sum", [1, 1])
        sems = mk_sems(b)
        block = b.block()
        seq = Seq()
        e_memset(seq, ones_col[:, :], 1.0)
        e_memset(seq, ssum_sb[:, :], 0.0)
        for L in range(2 if do_mod else 0):
            seq.dma("sync", sc[:, :], cT[L])
            e_act(seq, silu_c[:, :], sc[:, :], AF.Silu)
            for fs in range(FSH // 512):
                for c in range(NDC):
                    seq.dma("sync", wbuf[:, c * 512:(c + 1) * 512],
                            wmod[L][c * 128:(c + 1) * 128, fs * 512:(fs + 1) * 512])
                for c in range(NDC):
                    e_mm(seq, p_row[:, :], silu_c[:, c:c + 1], wbuf[:, c * 512:(c + 1) * 512],
                         start=(c == 0), stop=(c == NDC - 1))
                e_copy(seq, mrow[L][:, fs * 512:(fs + 1) * 512], p_row[:, :])
        for L in range(2):
            seq.dma("sync", modrow[L:L + 1, :], mrow[L][:, :])
        for i in range(3 if do_nf4 else 0):
            flat = wsh[i].rearrange("a b -> (a b)")
            src = flat.rearrange("(g j) -> g j", g=NGRP)
            seq.dma("sync", wg[:, :], src)
            e_red(seq, s1g[:, :], wg[:, :].rearrange("g (b k) -> g b k", k=64),
                  ALU.max, absval=True)
            e_red(seq, rowsum[:, :], s1g[:, :], ALU.add)
            e_mm(seq, p_sum[:, :], ones_col[:NGRP, :], rowsum[:, :])
            e_copy(seq, ssum_sb[:, i:i + 1], p_sum[:, :])
        seq.dma("sync", ssum, ssum_sb[:, :])
        seq.install(nc, sems, block)
    return nc



def build_launch_B():
    nc = new_nc()
    wsh = {}
    shapes = [(W1_ROWS, DM), (W1_ROWS, DM), (W3_ROWS, DFF0)]
    for i, (r, c) in enumerate(shapes):
        wsh[i] = nc.dram_tensor(f"w{i+1}s", [r, c], BF16, kind="ExternalInput").ap()
    allsum = nc.dram_tensor("allsum", [NCORE, 4], F32, kind="ExternalInput").ap()
    wq = {}
    for i, (r, c) in enumerate(shapes):
        wq[i] = nc.dram_tensor(f"wq{i+1}", [r, c], BF16, kind="ExternalOutput").ap()

    GW = 256 * 64
    with ExitStack() as ctx:
        b = Bufs(nc, ctx)
        wg = b.sb("wg", [NGRP, GW], BF16)
        s1g = b.sb("s1g", [NGRP, 256])
        r1g = b.sb("r1g", [NGRP, 256])
        cent = b.sb("cent", [NGRP, 256], BF16)
        centf = b.sb("centf", [NGRP, 256])
        gmax = b.sb("gmax", [NGRP, 1])
        qf = b.sb("qf", [NGRP, 1])
        rqf = b.sb("rqf", [NGRP, 1])
        qs = b.sb("qs", [NGRP, 256])
        deqb = b.sb("deqb", [NGRP, 256], BF16)
        sums = b.sb("sums", [1, 4 * NCORE])
        tot = b.sb("tot", [1, 4])
        meanb = b.sb("meanb", [1, 1], BF16)
        meanf = b.sb("meanf", [1, 1])
        ones_row = b.sb("ones_row", [1, 128])
        mean_col = b.sb("mean_col", [128, 1])
        scl = b.sb("scl", [NGRP, GW], BF16)
        tmp = b.sb("tmp", [NGRP, GW], BF16)
        acc = b.sb("acc", [NGRP, GW], BF16)
        p_mean = b.ps("p_mean", [128, 1])
        sems = mk_sems(b)
        block = b.block()
        seq = Seq()
        e_memset(seq, ones_row[:, :], 1.0)
        seq.dma("sync", sums[:, :], allsum.rearrange("a b -> (a b)")[None, :])

        for i in range(3):
            flat = wsh[i].rearrange("a b -> (a b)")
            gview = flat.rearrange("(g j) -> g j", g=NGRP)
            seq.dma("sync", wg[:, :], gview)
            e_red(seq, s1g[:, :], wg[:, :].rearrange("g (b k) -> g b k", k=64),
                  ALU.max, absval=True)
            e_red(seq, tot[:, i:i + 1],
                  sums[:, :].rearrange("o (c w) -> o w c", w=4)[:, i:i + 1, :],
                  ALU.add)
            seq.emit("vector", lambda e, i=i: e.tensor_single_scalar(
                meanb[:, :], tot[:, i:i + 1], float(1.0 / TOT_BLOCKS), ALU.mult),
                reads=(tot[:, i:i + 1],), writes=(meanb[:, :],))
            e_copy(seq, meanf[:, :], meanb[:, :])
            e_mm(seq, p_mean[:, :], ones_row[:, :], meanf[:, :])
            e_copy(seq, mean_col[:, :], p_mean[:, :])
            e_ts(seq, cent[:, :], s1g[:, :], mean_col[:NGRP, :], None, ALU.subtract)
            e_red(seq, gmax[:, :], cent[:, :], ALU.max, absval=True)
            e_recip(seq, qf[:, :], gmax[:, :])
            e_ts(seq, qf[:, :], qf[:, :], 128.0, None, ALU.mult)
            e_copy(seq, centf[:, :], cent[:, :])
            e_ts(seq, qs[:, :], centf[:, :], qf[:, :], RNE_MAGIC, ALU.mult, ALU.add)
            e_ts(seq, qs[:, :], qs[:, :], RNE_MAGIC, None, ALU.subtract)
            e_ts(seq, qs[:, :], qs[:, :], -128.0, 127.0, ALU.max, ALU.min)
            e_recip(seq, rqf[:, :], qf[:, :])
            e_ts(seq, deqb[:, :], qs[:, :], rqf[:, :], None, ALU.mult)
            e_copy(seq, centf[:, :], deqb[:, :])
            e_ts(seq, deqb[:, :], centf[:, :], mean_col[:NGRP, :], None, ALU.add)
            e_recip(seq, r1g[:, :], s1g[:, :])
            e_tt(seq, scl[:, :].rearrange("g (b k) -> g b k", k=64),
                 wg[:, :].rearrange("g (b k) -> g b k", k=64),
                 r1g[:, :].rearrange("g (b o) -> g b o", o=1).broadcast_to([NGRP, 256, 64]),
                 ALU.mult)
            e_memset(seq, acc[:, :], 0.0)
            for lv in range(7):
                e_ts(seq, tmp[:, :], scl[:, :], float(NF4_MID[lv]), float(NF4L[lv]),
                     ALU.is_le, ALU.mult)
                e_tt(seq, acc[:, :], acc[:, :], tmp[:, :], ALU.min)
            e_memset(seq, tmp[:, :], 0.0)
            for lv in range(7, 15):
                e_ts(seq, wg[:, :], scl[:, :], float(NF4_MID[lv]), float(NF4L[lv + 1]),
                     ALU.is_gt, ALU.mult)
                e_tt(seq, tmp[:, :], tmp[:, :], wg[:, :], ALU.max)
            e_tt(seq, acc[:, :], acc[:, :], tmp[:, :], ALU.add)
            e_tt(seq, scl[:, :].rearrange("g (b k) -> g b k", k=64),
                 acc[:, :].rearrange("g (b k) -> g b k", k=64),
                 deqb[:, :].rearrange("g (b o) -> g b o", o=1).broadcast_to([NGRP, 256, 64]),
                 ALU.mult)
            out_g = wq[i].rearrange("a b -> (a b)").rearrange("(g j) -> g j", g=NGRP)
            seq.dma("sync", out_g, scl[:, :])
        seq.install(nc, sems, block)
    return nc



def e_clamp_gate(seq, out_cols, in_cols, sgn_scratch):
    e_ts(seq, sgn_scratch, in_cols, 0.0, 2.0, ALU.is_ge, ALU.mult)
    e_ts(seq, sgn_scratch, sgn_scratch, -1.0, None, ALU.add)
    e_tt(seq, out_cols, in_cols, sgn_scratch, ALU.mult)
    e_ts(seq, out_cols, out_cols, 0.25, None, ALU.max)
    e_tt(seq, out_cols, out_cols, sgn_scratch, ALU.mult)


class LNBufs:
    def __init__(self, b, tag, p_b=None):
        self.sq = b.sb(f"ln_sq_{tag}", [128, 512])
        rt = b.sb(f"ln_rows_{tag}", [1, 1024])
        rt2 = b.sb(f"ln_rows2_{tag}", [1, 512])
        self.mean_row = rt[0:1, 0:512]
        self.rstd_row = rt[0:1, 512:1024]
        self.msq_row = rt2[0:1, :]
        self.mean_b = b.sb(f"ln_mb_{tag}", [128, 512])
        self.rstd_b = b.sb(f"ln_rb_{tag}", [128, 512])
        self.t1 = b.sb(f"ln_t1_{tag}", [128, 512])
        self.p_sum = b.ps(f"ln_ps_{tag}", [1, 512])
        self.p_sq = b.ps(f"ln_pq_{tag}", [1, 512])
        self.p_b = p_b if p_b is not None else b.ps(f"ln_pb_{tag}", [128, 512])


def emit_ln_mod(seq, lb, x_sb, h_out, ones_col, ones_row, ca1_cols, sa_cols,
                toks=TPC, nch=NDC):
    for t0 in range(0, toks, 512):
        tw = min(512, toks - t0)
        for c in range(nch):
            xc = x_sb[:, c * toks + t0: c * toks + t0 + tw]
            e_act(seq, lb.sq[:, :tw], xc, AF.Square)
            e_mm(seq, lb.p_sum[:, :tw], ones_col[:, :], xc,
                 start=(c == 0), stop=(c == nch - 1))
            e_mm(seq, lb.p_sq[:, :tw], ones_col[:, :], lb.sq[:, :tw],
                 start=(c == 0), stop=(c == nch - 1))
        dim = 128 * nch
        e_ts(seq, lb.mean_row[:, :tw], lb.p_sum[:, :tw], 1.0 / dim, None, ALU.mult)
        e_ts(seq, lb.msq_row[:, :tw], lb.p_sq[:, :tw], 1.0 / dim, None, ALU.mult)
        e_stt(seq, lb.rstd_row[:, :tw], lb.mean_row[:, :tw], -1.0,
              lb.mean_row[:, :tw], ALU.mult, ALU.mult)
        e_tt(seq, lb.msq_row[:, :tw], lb.msq_row[:, :tw], lb.rstd_row[:, :tw],
             ALU.add)
        e_ts(seq, lb.msq_row[:, :tw], lb.msq_row[:, :tw], LEPS, None, ALU.add)
        e_act(seq, lb.rstd_row[:, :tw], lb.msq_row[:, :tw], AF.Sqrt)
        e_recip(seq, lb.rstd_row[:, :tw], lb.rstd_row[:, :tw])
        e_mm(seq, lb.p_b[:, :tw], ones_row[:, :], lb.mean_row[:, :tw])
        e_copy(seq, lb.mean_b[:, :tw], lb.p_b[:, :tw])
        e_mm(seq, lb.p_b[:, :tw], ones_row[:, :], lb.rstd_row[:, :tw])
        e_copy(seq, lb.rstd_b[:, :tw], lb.p_b[:, :tw])
        for c in range(nch):
            xc = x_sb[:, c * toks + t0: c * toks + t0 + tw]
            hc = h_out[:, c * toks + t0: c * toks + t0 + tw]
            e_stt(seq, lb.t1[:, :tw], lb.mean_b[:, :tw], -1.0, xc, ALU.mult, ALU.add)
            e_stt(seq, lb.t1[:, :tw], lb.t1[:, :tw], ca1_cols[:, c:c + 1],
                  lb.rstd_b[:, :tw], ALU.mult, ALU.mult)
            e_ts(seq, hc, lb.t1[:, :tw], sa_cols[:, c:c + 1], None, ALU.add)


def dma_wslice_fm(seq, wbuf, wT_dram, n_in, f0, fw, engine="gpsimd"):
    nch = n_in // 128
    src = wT_dram.rearrange("(c p) f -> p c f", p=128)[:, :, f0:f0 + fw]
    dst = wbuf[:, :nch * fw].rearrange("p (c f) -> p c f", f=fw)
    seq.dma(engine, dst, src)


def emit_gemm_fm(seq, wT_dram, act_bf, n_in, f_lo, f_hi, toks, wbuf, psum,
                 consume, tok_tile=512):
    nch = n_in // 128
    for f0 in range(f_lo, f_hi, 128):
        dma_wslice_fm(seq, wbuf, wT_dram, n_in, f0, 128)
        for t0 in range(0, toks, tok_tile):
            tw = min(tok_tile, toks - t0)
            for c in range(nch):
                e_mm(seq, psum[:, :tw], wbuf[:, c * 128:(c + 1) * 128],
                     act_bf[:, c * toks + t0: c * toks + t0 + tw],
                     start=(c == 0), stop=(c == nch - 1))
            consume(f0 - f_lo, t0, tw, psum[:, :tw])



def build_launch_E():
    nc = new_nc()
    x1d = nc.dram_tensor("x1", [DM, TPC], F32, kind="ExternalInput").ap()
    w1T = nc.dram_tensor("w1pT", [DM, DFF0], BF16, kind="ExternalInput").ap()
    w2T = nc.dram_tensor("w2pT", [DM, DFF0], BF16, kind="ExternalInput").ap()
    w3T = nc.dram_tensor("w3pT", [DFF0, DM], BF16, kind="ExternalInput").ap()
    wqkvT = nc.dram_tensor("wqkvT", [DM, 3 * DM], F32, kind="ExternalInput").ap()
    modc = nc.dram_tensor("modc", [128, 2 * 6 * NDC], F32, kind="ExternalInput").ap()
    rrot = nc.dram_tensor("rrot", [128, 128], BF16, kind="ExternalInput").ap()
    posr = nc.dram_tensor("posr", [1, TPC], F32, kind="ExternalInput").ap()
    x2d = nc.dram_tensor("x2", [DM, TPC], F32, kind="ExternalOutput").ap()
    qTd = nc.dram_tensor("qT", [AHEADS, HD, TPC], BF16, kind="ExternalOutput").ap()
    kTd = nc.dram_tensor("kT", [AHEADS, HD, TPC], BF16, kind="ExternalOutput").ap()
    vtd = nc.dram_tensor("vt", [TPC, DM], BF16, kind="ExternalOutput").ap()
    qkmaxd = nc.dram_tensor("qkmax", [1, 2 * AHEADS], F32, kind="ExternalOutput").ap()

    NF0 = DFF0 // 128
    with ExitStack() as ctx:
        b = Bufs(nc, ctx)
        x_sb = b.sb("x_sb", [128, NDC * TPC])
        h_sb = b.sb("h_sb", [128, NDC * TPC], BF16)
        hid = b.sb("hid", [128, NF0 * 512], BF16)
        mcol = b.sb("mcol", [128, 2 * 6 * NDC])
        ca1 = b.sb("ca1", [128, NDC])
        gbc = b.sb("gbc", [128, NDC])
        sgn = b.sb("sgn", [128, NDC])
        ones_col = b.sb("ones_col", [128, 1])
        ones_row = b.sb("ones_row", [1, 128])
        wbuf1 = b.sb("wbuf1", [128, NDC * 128], BF16)
        wbuf2 = b.sb("wbuf2", [128, NDC * 128], BF16)
        wbufq = b.sb("wbufq", [128, NDC * 512], BF16)
        silu_s = b.sb("silu_s", [128, 512])
        rope_cos = b.sb("rope_cos", [128, TPC])
        rope_sin = b.sb("rope_sin", [128, TPC])
        ifr = b.sb("ifr", [128, 1])
        qk_bf = b.sb("qk_bf", [128, TPC], BF16)
        rot_sb = b.sb("rot_sb", [128, TPC], BF16)
        rconst = b.sb("rconst", [128, 128], BF16)
        posrow = b.sb("posrow", [1, TPC])
        lb = LNBufs(b, "e")
        p_a = b.ps("p_a", [128, 512])
        p_b2 = b.ps("p_b2", [128, 512])
        p_o = b.ps("p_o", [128, 512])
        p_r = b.ps("p_r", [128, 512])
        sems = mk_sems(b)
        block = b.block()
        seq = Seq()

        e_memset(seq, ones_col[:, :], 1.0)
        e_memset(seq, ones_row[:, :], 1.0)
        seq.dma("sync", mcol[:, :], modc)
        for c in range(NDC):
            seq.dma("sync", x_sb[:, c * TPC:(c + 1) * TPC],
                    x1d[c * 128:(c + 1) * 128, :])
        e_ts(seq, ca1[:, :], mcol[:, 4 * NDC:5 * NDC], 1.0, None, ALU.add)
        e_clamp_gate(seq, gbc[:, :], mcol[:, 5 * NDC:6 * NDC], sgn[:, :])
        emit_ln_mod(seq, lb, x_sb, h_sb, ones_col, ones_row,
                    ca1, mcol[:, 3 * NDC:4 * NDC])

        for t0 in range(0, TPC, 512):
            for fi in range(NF0):
                f0 = fi * 128
                dma_wslice_fm(seq, wbuf1, w1T, DM, f0, 128)
                dma_wslice_fm(seq, wbuf2, w2T, DM, f0, 128)
                for c in range(NDC):
                    e_mm(seq, p_a[:, :], wbuf1[:, c * 128:(c + 1) * 128],
                         h_sb[:, c * TPC + t0: c * TPC + t0 + 512],
                         start=(c == 0), stop=(c == NDC - 1))
                for c in range(NDC):
                    e_mm(seq, p_b2[:, :], wbuf2[:, c * 128:(c + 1) * 128],
                         h_sb[:, c * TPC + t0: c * TPC + t0 + 512],
                         start=(c == 0), stop=(c == NDC - 1))
                e_act(seq, silu_s[:, :], p_a[:, :], AF.Silu)
                e_tt(seq, hid[:, fi * 512:(fi + 1) * 512], silu_s[:, :],
                     p_b2[:, :], ALU.mult)
            for dc in range(NDC):
                dma_wslice_fm(seq, wbufq, w3T, DFF0, dc * 128, 128)
                for fc in range(NF0):
                    e_mm(seq, p_o[:, :], wbufq[:, fc * 128:(fc + 1) * 128],
                         hid[:, fc * 512:(fc + 1) * 512],
                         start=(fc == 0), stop=(fc == NF0 - 1))
                xc = x_sb[:, dc * TPC + t0: dc * TPC + t0 + 512]
                e_stt(seq, xc, p_o[:, :], gbc[:, dc:dc + 1], xc, ALU.mult, ALU.add)
        for c in range(NDC):
            seq.dma("sync", x2d[c * 128:(c + 1) * 128, :],
                    x_sb[:, c * TPC:(c + 1) * TPC])

        e_ts(seq, ca1[:, :], mcol[:, 6 * NDC + 1 * NDC: 6 * NDC + 2 * NDC],
             1.0, None, ALU.add)
        emit_ln_mod(seq, lb, x_sb, h_sb, ones_col, ones_row,
                    ca1, mcol[:, 6 * NDC: 6 * NDC + NDC])

        seq.dma("sync", posrow[:, :], posr)
        seq.emit("gpsimd", lambda e: e.iota(ifr[:, :], [[0, 1]], base=0,
                                            channel_multiplier=1,
                                            allow_small_or_imprecise_dtypes=True),
                 reads=(), writes=(ifr[:, :],))
        e_ts(seq, silu_s[:, 0:1], ifr[:, :], 64.0, 64.0, ALU.is_ge, ALU.mult)
        e_tt(seq, ifr[:, :], ifr[:, :], silu_s[:, 0:1], ALU.subtract)
        e_act(seq, ifr[:, :], ifr[:, :], AF.Exp, scale=-math.log(10000.0) / 64.0)
        TWO_PI = 2.0 * math.pi
        c1 = float(np.float32(6.28125))
        c2 = float(np.float32(np.float64(TWO_PI) - 6.28125))
        c3 = float(np.float64(TWO_PI) - np.float64(c1) - np.float64(c2))
        for th in range(0, TPC, 512):
            ang = silu_s[:, :]
            kk = lb.t1[:, :]
            e_mm(seq, p_r[:, :], ones_row[:, :], posrow[:, th:th + 512])
            e_tt(seq, ang, p_r[:, :], ifr[:, :].broadcast_to([128, 512]), ALU.mult)
            e_ts(seq, kk, ang, 1.0 / TWO_PI, RNE_MAGIC, ALU.mult, ALU.add)
            e_ts(seq, kk, kk, RNE_MAGIC, None, ALU.subtract)
            for cw in (c1, c2, c3):
                e_stt(seq, ang, kk, -cw, ang, ALU.mult, ALU.add)
            e_act(seq, rope_sin[:, th:th + 512], ang, AF.Sin)
            e_ts(seq, kk, ang, float(math.pi / 2), TWO_PI, ALU.is_gt, ALU.mult)
            e_tt(seq, ang, ang, kk, ALU.subtract)
            e_ts(seq, ang, ang, float(math.pi / 2), None, ALU.add)
            e_act(seq, rope_cos[:, th:th + 512], ang, AF.Sin)

        seq.dma("sync", rconst[:, :], rrot)
        qkmax = b.sb("qkmax_sb", [1, 2 * AHEADS])
        nmtmp = b.sb("nmtmp", [1, 1])
        e_memset(seq, qkmax[:, :], 0.0)

        def qk_consume(dst, head, mx_off):
            def consume(f0, t0, tw, psum):
                e_copy(seq, qk_bf[:, t0:t0 + tw], psum)
                if t0 + tw == TPC:
                    for tt in range(0, TPC, 512):
                        e_mm(seq, p_r[:, :], rconst[:, :], qk_bf[:, tt:tt + 512])
                        e_copy(seq, rot_sb[:, tt:tt + 512], p_r[:, :])
                    for tt in range(0, TPC, 512):
                        e_tt(seq, silu_s[:, :], qk_bf[:, tt:tt + 512],
                             rope_cos[:, tt:tt + 512], ALU.mult)
                        e_tt(seq, lb.t1[:, :], rot_sb[:, tt:tt + 512],
                             rope_sin[:, tt:tt + 512], ALU.mult)
                        e_tt(seq, qk_bf[:, tt:tt + 512], silu_s[:, :],
                             lb.t1[:, :], ALU.add)
                    seq.dma("sync", dst[head], qk_bf[:, :])
            return consume

        for head in range(AHEADS):
            emit_gemm_fm(seq, wqkvT, h_sb, DM, head * HD, (head + 1) * HD, TPC,
                         wbufq, p_a, qk_consume(qTd, head, 0))
        for head in range(AHEADS):
            emit_gemm_fm(seq, wqkvT, h_sb, DM, DM + head * HD, DM + (head + 1) * HD,
                         TPC, wbufq, p_a, qk_consume(kTd, head, AHEADS))
        seq.dma("sync", qkmaxd, qkmax[:, :])
        for f0 in range(0, DM, 512):
            src = wqkvT.rearrange("(c p) f -> p c f", p=128)[
                :, :, 2 * DM + f0: 2 * DM + f0 + 512]
            dst = wbufq[:, :NDC * 512].rearrange("p (c f) -> p c f", f=512)
            seq.dma("gpsimd", dst, src)
            for t0 in range(0, TPC, 128):
                for c in range(NDC):
                    e_mm(seq, p_o[:, :], h_sb[:, c * TPC + t0: c * TPC + t0 + 128],
                         wbufq[:, c * 512:(c + 1) * 512],
                         start=(c == 0), stop=(c == NDC - 1))
                e_copy(seq, qk_bf[:, 0:512], p_o[:, :])
                seq.dma("sync", vtd[t0:t0 + 128, f0:f0 + 512], qk_bf[:, 0:512])
        seq.install(nc, sems, block)
    return nc



def build_launch_F1(debug=False):
    nc = new_nc()
    x2d = nc.dram_tensor("x2", [DM, TPC], F32, kind="ExternalInput").ap()
    qTd = nc.dram_tensor("qT", [AHEADS, HD, TPC], BF16, kind="ExternalInput").ap()
    kTd = nc.dram_tensor("kTall", [AHEADS, HD, SEQ], BF16, kind="ExternalInput").ap()
    vd = nc.dram_tensor("vall", [AHEADS, SEQ, HD], BF16, kind="ExternalInput").ap()
    woT = nc.dram_tensor("woT", [DM, DM], F32, kind="ExternalInput").ap()
    modc = nc.dram_tensor("modc", [128, 2 * 6 * NDC], F32, kind="ExternalInput").ap()
    qkm = nc.dram_tensor("qkmaxall", [NCORE, 2 * AHEADS], F32,
                         kind="ExternalInput").ap()
    x3d = nc.dram_tensor("x3", [DM, TPC], F32, kind="ExternalOutput").ap()
    mx_bounce = nc.dram_tensor("mx_bounce", [4, 128], F32)
    if debug:
        dbg_e = nc.dram_tensor("dbg_e", [128, 512], BF16,
                               kind="ExternalOutput").ap()
        dbg_s = nc.dram_tensor("dbg_s", [1, 512], F32, kind="ExternalOutput").ap()
        dbg_a = nc.dram_tensor("dbg_a", [128, TPC], BF16,
                               kind="ExternalOutput").ap()

    NKC = SEQ // 128
    with ExitStack() as ctx:
        b = Bufs(nc, ctx)
        expS = b.sb("expS", [128, NKC * 512], BF16)
        kT = b.sb("kT", [128, SEQ], BF16)
        v_sb = b.sb("v_sb", [128, NKC * HD], BF16)
        qT = b.sb("qT_sb", [128, TPC], BF16)
        attn = b.sb("attn", [128, AHEADS * TPC], BF16)
        x2c = b.sb("x2c", [128, TPC])
        stage = b.sb("stage", [128, 512])
        wbuf = b.sb("wbufo", [128, NDC * 128], BF16)
        mcol = b.sb("mcolf", [128, 2 * 6 * NDC])
        gac = b.sb("gac", [128, NDC])
        sgn = b.sb("sgn_f", [128, NDC])
        rsum = b.sb("rsum", [1, 512])
        rbc = b.sb("rbc", [128, 512])
        ones_bf = b.sb("ones_bf", [128, 1], BF16)
        ones_bf1 = b.sb("ones_bf1", [1, 128], BF16)
        ones_row = b.sb("ones_rowf", [1, 128])
        identf = b.sb("identf", [128, 128])
        pm4 = b.sb("pm4", [128, 4])
        mx1 = b.sb("mx1", [128, 1])
        nmrow = b.sb("nmrow", [1, 512], BF16)
        p_s = b.ps("p_s", [128, 512])
        p_sum = b.ps("p_sumf", [1, 512])
        p_b = b.ps("p_bf", [128, 512])
        p_m = b.ps("p_m", [128, 512])
        sems = mk_sems(b)
        block = b.block()
        seq = Seq()

        e_memset(seq, ones_bf[:, :], 1.0)
        e_memset(seq, ones_bf1[:, :], 1.0)
        e_memset(seq, ones_row[:, :], 1.0)
        e_identity(seq, identf)
        seq.dma("sync", mcol[:, :], modc)
        ksub = kT[:, :].rearrange("p (a s) -> p a s", s=64)[:, :, 0]

        for head in range(AHEADS):
            seq.dma("sync", kT[:, :], kTd[head])
            for part in range(8):
                seq.dma("sync",
                        v_sb[:, :].rearrange("p (kc d) -> p kc d", d=HD)[
                            :, part * 8:(part + 1) * 8, :],
                        vd[head].rearrange("(kc p) d -> p kc d", p=128)[
                            :, part * 8:(part + 1) * 8, :])
            seq.dma("sync", qT[:, :], qTd[head])
            for t0 in range(0, TPC, 512):
                for qc in range(4):
                    for kt in range(SEQ // 512):
                        e_mm(seq, p_m[:, 0:512],
                             qT[:, t0 + qc * 128: t0 + (qc + 1) * 128],
                             kT[:, kt * 512:(kt + 1) * 512])
                        if kt == 0:
                            e_red(seq, pm4[:, qc:qc + 1], p_m[:, 0:512], ALU.max)
                        else:
                            e_red(seq, mx1[:, :], p_m[:, 0:512], ALU.max)
                            e_tt(seq, pm4[:, qc:qc + 1], pm4[:, qc:qc + 1],
                                 mx1[:, :], ALU.max)
                e_tr(seq, p_m[0:4, 0:128], pm4[:, :], identf[:, :])
                e_ts(seq, rbc[0:4, 0:128], p_m[0:4, 0:128], -1.0, None, ALU.mult)
                seq.dma("sync", mx_bounce.ap(), rbc[0:4, 0:128])
                seq.dma("gpsimd", nmrow[:, :],
                        mx_bounce.ap().rearrange("a b -> (a b)")[None, :])
                for kc in range(NKC):
                    e_mm(seq, p_s[:, :], ones_bf1[:, :], nmrow[:, :],
                         stop=False)
                    e_mm(seq, p_s[:, :], kT[:, kc * 128:(kc + 1) * 128],
                         qT[:, t0:t0 + 512], start=False)
                    e_act(seq, expS[:, kc * 512:(kc + 1) * 512], p_s[:, :],
                          AF.Exp, scale=1.0 / math.sqrt(HD))
                for kc in range(NKC):
                    e_mm(seq, p_sum[:, :], ones_bf[:, :],
                         expS[:, kc * 512:(kc + 1) * 512],
                         start=(kc == 0), stop=(kc == NKC - 1))
                if debug and head == 0 and t0 == 0:
                    seq.dma("sync", dbg_e, expS[:, 0:512])
                    e_copy(seq, rsum[:, :], p_sum[:, :])
                    seq.dma("sync", dbg_s, rsum[:, :])
                e_copy(seq, rsum[:, :], p_sum[:, :])
                e_recip(seq, rsum[:, :], rsum[:, :])
                e_mm(seq, p_b[:, :], ones_row[:, :], rsum[:, :])
                e_copy(seq, rbc[:, :], p_b[:, :])
                for kc in range(NKC):
                    e_mm(seq, p_s[:, :], v_sb[:, kc * HD:(kc + 1) * HD],
                         expS[:, kc * 512:(kc + 1) * 512],
                         start=(kc == 0), stop=(kc == NKC - 1))
                e_tt(seq, attn[:, head * TPC + t0: head * TPC + t0 + 512],
                     p_s[:, :], rbc[:, :], ALU.mult)
        if debug:
            seq.dma("sync", dbg_a, attn[:, 0:TPC])

        e_clamp_gate(seq, gac[:, :], mcol[:, 6 * NDC + 2 * NDC: 6 * NDC + 3 * NDC],
                     sgn[:, :])

        def oproj_consume(dc2, t0, tw, psum):
            dc = dc2 // 128
            xc = x2c[:, t0:t0 + tw]
            if t0 == 0:
                for tt in range(0, TPC, 512):
                    seq.dma("sync", x2c[:, tt:tt + 512],
                            x2d[dc * 128:(dc + 1) * 128, tt:tt + 512])
            e_stt(seq, stage[:, :tw], psum, gac[:, dc:dc + 1], xc,
                  ALU.mult, ALU.add)
            seq.dma("sync", x3d[dc * 128:(dc + 1) * 128, t0:t0 + tw],
                    stage[:, :tw])

        emit_gemm_fm(seq, woT, attn, DM, 0, DM, TPC, wbuf, p_s, oproj_consume)
        seq.install(nc, sems, block)
    return nc



def build_launch_F2():
    nc = new_nc()
    x3d = nc.dram_tensor("x3", [DM, TPC], F32, kind="ExternalInput").ap()
    W1T = nc.dram_tensor("W1T", [DM, DFF1], F32, kind="ExternalInput").ap()
    W3T = nc.dram_tensor("W3T", [DM, DFF1], F32, kind="ExternalInput").ap()
    W2T = nc.dram_tensor("W2T", [DFF1, DM], F32, kind="ExternalInput").ap()
    modc = nc.dram_tensor("modc", [128, 2 * 6 * NDC], F32, kind="ExternalInput").ap()
    h128d = nc.dram_tensor("h128", [128, 128], BF16, kind="ExternalInput").ap()
    hi2d = nc.dram_tensor("hi2", [128, 128], BF16, kind="ExternalInput").ap()
    x4d = nc.dram_tensor("x4", [DM, TPC], F32, kind="ExternalOutput").ap()

    NF1 = DFF1 // 128
    with ExitStack() as ctx:
        b = Bufs(nc, ctx)
        x_sb = b.sb("x_sb", [128, NDC * TPC])
        h_sb = b.sb("h_sb", [128, NDC * TPC], BF16)
        hid = b.sb("hid", [128, NF1 * 512], BF16)
        mcol = b.sb("mcol", [128, 2 * 6 * NDC])
        ca1 = b.sb("ca1", [128, NDC])
        gbc = b.sb("gbc", [128, NDC])
        sgn = b.sb("sgn", [128, NDC])
        ones_col = b.sb("ones_col", [128, 1])
        ones_row = b.sb("ones_row", [1, 128])
        wbuf1 = b.sb("wbuf1", [128, NDC * 128], BF16)
        wbuf2 = b.sb("wbuf2", [128, NDC * 128], BF16)
        wbuf3 = b.sb("wbuf3", [128, NF1 * 128], BF16)
        silu_s = b.sb("silu_s", [128, 512])
        h128c = b.sb("h128c", [128, 128], BF16)
        hi2c = b.sb("hi2c", [128, 128], BF16)
        identb = b.sb("identb", [128, 128], BF16)
        tr1 = b.sb("tr1", [128, 128], BF16)
        tr2 = b.sb("tr2", [128, 128], BF16)
        lb = LNBufs(b, "f2")
        p_a = b.ps("p_a", [128, 512])
        p_b2 = b.ps("p_b2", [128, 512])
        p_o = b.ps("p_o", [128, 512])
        p_t = b.ps("p_t", [128, 128], BF16)
        p_tm = b.ps("p_tm", [128, 128])
        sems = mk_sems(b)
        block = b.block()
        seq = Seq()

        e_memset(seq, ones_col[:, :], 1.0)
        e_memset(seq, ones_row[:, :], 1.0)
        e_identity(seq, identb)
        seq.dma("sync", mcol[:, :], modc)
        seq.dma("sync", h128c[:, :], h128d)
        seq.dma("sync", hi2c[:, :], hi2d)
        for c in range(NDC):
            seq.dma("sync", x_sb[:, c * TPC:(c + 1) * TPC],
                    x3d[c * 128:(c + 1) * 128, :])
        e_ts(seq, ca1[:, :], mcol[:, 6 * NDC + 4 * NDC: 6 * NDC + 5 * NDC],
             1.0, None, ALU.add)
        e_clamp_gate(seq, gbc[:, :], mcol[:, 6 * NDC + 5 * NDC: 6 * NDC + 6 * NDC],
                     sgn[:, :])
        emit_ln_mod(seq, lb, x_sb, h_sb, ones_col, ones_row,
                    ca1, mcol[:, 6 * NDC + 3 * NDC: 6 * NDC + 4 * NDC])

        for t0 in range(0, TPC, 512):
            for fi in range(NF1):
                f0 = fi * 128
                dma_wslice_fm(seq, wbuf1, W1T, DM, f0, 128)
                dma_wslice_fm(seq, wbuf2, W3T, DM, f0, 128)
                for c in range(NDC):
                    e_mm(seq, p_a[:, :], wbuf1[:, c * 128:(c + 1) * 128],
                         h_sb[:, c * TPC + t0: c * TPC + t0 + 512],
                         start=(c == 0), stop=(c == NDC - 1))
                for c in range(NDC):
                    e_mm(seq, p_b2[:, :], wbuf2[:, c * 128:(c + 1) * 128],
                         h_sb[:, c * TPC + t0: c * TPC + t0 + 512],
                         start=(c == 0), stop=(c == NDC - 1))
                e_act(seq, silu_s[:, :], p_a[:, :], AF.Silu)
                e_tt(seq, hid[:, fi * 512:(fi + 1) * 512], silu_s[:, :],
                     p_b2[:, :], ALU.mult)
            for fc in range(NF1):
                e_mm(seq, p_a[:, :], h128c[:, :], hid[:, fc * 512:(fc + 1) * 512])
                e_copy(seq, hid[:, fc * 512:(fc + 1) * 512], p_a[:, :])
            for tp in range(256):
                tview = hid[:, :].rearrange("p (c t) -> p c t", t=512)[
                    :, :, 2 * tp: 2 * tp + 2]
                e_copy(seq, tr1[:, :].rearrange("p (c t) -> p c t", t=2), tview)
                e_tr(seq, p_t[:, :], tr1[:, :], identb[:, :])
                e_copy(seq, tr1[:, :], p_t[:, :])
                e_mm(seq, p_tm[:, :], hi2c[:, :], tr1[:, :])
                e_ts(seq, tr2[:, :], p_tm[:, :], float(1.0 / math.sqrt(DFF1)),
                     None, ALU.mult)
                e_tr(seq, p_t[:, :], tr2[:, :], identb[:, :])
                e_copy(seq, tview, p_t[:, :])
            for dc in range(NDC):
                dma_wslice_fm(seq, wbuf3, W2T, DFF1, dc * 128, 128)
                for fc in range(NF1):
                    e_mm(seq, p_o[:, :], wbuf3[:, fc * 128:(fc + 1) * 128],
                         hid[:, fc * 512:(fc + 1) * 512],
                         start=(fc == 0), stop=(fc == NF1 - 1))
                xc = x_sb[:, dc * TPC + t0: dc * TPC + t0 + 512]
                e_stt(seq, xc, p_o[:, :], gbc[:, dc:dc + 1], xc, ALU.mult, ALU.add)
        for c in range(NDC):
            seq.dma("sync", x4d[c * 128:(c + 1) * 128, :],
                    x_sb[:, c * TPC:(c + 1) * TPC])
        seq.install(nc, sems, block)
    return nc



def build_launch_C():
    nc = new_nc()
    x0d = nc.dram_tensor("x0", [DM, TPC], F32, kind="ExternalInput").ap()
    xhd = nc.dram_tensor("xh", [DM, 4], F32, kind="ExternalInput").ap()
    wipT = nc.dram_tensor("wipT", [DM, PROJ_DIM], F32, kind="ExternalInput").ap()
    cwd = nc.dram_tensor("convw", [128, 34 * KCONV], F32, kind="ExternalInput").ap()
    cbd = nc.dram_tensor("convb", [128, 34], F32, kind="ExternalInput").ap()
    dtbd = nc.dram_tensor("dtb", [NHEAD, 1], F32, kind="ExternalInput").ap()
    alogd = nc.dram_tensor("alog", [NHEAD, 1], F32, kind="ExternalInput").ap()
    drowd = nc.dram_tensor("drow", [1, NHEAD], F32, kind="ExternalInput").ap()
    modc = nc.dram_tensor("modc", [128, 2 * 6 * NDC], F32, kind="ExternalInput").ap()
    hmaskd = nc.dram_tensor("hmask", [128, 1], F32, kind="ExternalInput").ap()
    gate_d = nc.dram_tensor("gate", [TPC, INTER], BF16, kind="ExternalOutput").ap()
    yp_d = nc.dram_tensor("yp", [TPC, INTER], F32, kind="ExternalOutput").ap()
    st_d = nc.dram_tensor("cstates", [NCHUNK, NHEAD, ND, PD], BF16,
                          kind="ExternalOutput").ap()
    sum_d = nc.dram_tensor("csum", [NHEAD, ND, PD], BF16, kind="ExternalOutput").ap()
    lt_d = nc.dram_tensor("clt", [NHEAD, 1], F32, kind="ExternalOutput").ap()
    acum_d = nc.dram_tensor("acum", [NHEAD, TPC], F32, kind="ExternalOutput").ap()
    cfm_d = nc.dram_tensor("cfm", [ND, TPC], BF16, kind="ExternalOutput").ap()

    NHB = CONV_DIM // 128
    with ExitStack() as ctx:
        b = Bufs(nc, ctx)
        big0 = b.sb("big0", [128, NDC * TPC])
        h_sb = b.sb("h_sb", [128, NDC * TPC], BF16)
        arena = b.sb("arena", [128, 32 * TPC], BF16)
        hh_sb = b.sb("hh_sb", [128, 4 * NDC], BF16)
        xh_sb = b.sb("xh_sb", [128, 4 * NDC])
        B_fm = b.sb("B_fm", [128, TPC], BF16)
        C_fm = b.sb("C_fm", [128, TPC], BF16)
        B_tm = b.sb("B_tm", [128, 8 * ND], BF16)
        dtr = b.sb("dtr", [NHEAD, TPC])
        acum = b.sb("acum_sb", [NHEAD, TPC])
        hbc = b.sb("hbc", [128, TPC + 4])
        dtT = b.sb("dtT", [128, 8 * NHEAD])
        decT = b.sb("decT", [128, 8 * NHEAD])
        nacT = b.sb("nacT", [128, 8 * NHEAD])
        ltc = b.sb("ltc", [NHEAD, 1])
        cvw = b.sb("cvw", [128, 34 * KCONV])
        cvb = b.sb("cvb", [128, 34])
        hmask = b.sb("hmaskc", [128, 1])
        dcol = b.sb("dcolc", [NHEAD, 1])
        acol = b.sb("acolc", [NHEAD, 1])
        mcol = b.sb("mcolc", [128, 2 * 6 * NDC])
        ca1 = b.sb("ca1c", [128, NDC])
        wbuf = b.sb("wbufc", [128, NDC * 128], BF16)

        ones_col = b.sb("ones_colc", [128, 1])
        ones_row = b.sb("ones_rowc", [1, 128])
        identf = b.sb("identfc", [128, 128])
        identb2 = b.sb("identbc", [128, 128], BF16)
        lb = LNBufs(b, "c")
        p_g = b.ps("p_gc", [128, 512])
        p_misc = b.ps("p_miscc", [128, 512])
        p_y = b.ps("p_yc", [128, 128])
        p_st = b.ps("p_stc", [128, PD])
        p_tb = b.ps("p_tbc", [128, 128], BF16)
        sems = mk_sems(b)
        block = b.block()
        seq = Seq()

        zrow = lb.msq_row[0:1, 0:CH]
        stg = lb.t1[:, :].bitcast(BF16)[:, 0:512]
        ar_f = arena[:, :].bitcast(F32)
        ytile = [ar_f[:, 0:2048], ar_f[:, 2048:4096]]
        ar_b = arena[:, :]
        run = ar_b[:, 16384:16384 + NHEAD * PD]
        ststg = ar_b[:, 20480:20480 + 16 * PD]
        bdec = ar_b[:, 21504:21504 + ND]
        mt_sb = ar_b[:, 21632:21632 + 2 * CH]
        lt_sb = ar_b[:, 22144:22144 + CH]
        gt_sb = ar_f[:, 11200:11200 + 2 * CH]
        dchb = ar_f[:, 11712:11712 + NHEAD]
        Db = ar_f[:, 11776:11776 + NHEAD]
        adt = hbc[0:NHEAD, 0:TPC]
        t64 = b.sb("t64c", [NHEAD, CH])
        zer = b.sb("zerc", [NHEAD, CH])
        hs_tm = big0[:, :].bitcast(BF16)

        e_memset(seq, ones_col[:, :], 1.0)
        e_memset(seq, ones_row[:, :], 1.0)
        e_identity(seq, identf)
        e_identity(seq, identb2)
        seq.dma("sync", mcol[:, :], modc)
        seq.dma("sync", cvw[:, :], cwd)
        seq.dma("sync", cvb[:, :], cbd)
        seq.dma("sync", hmask[:, :], hmaskd)
        seq.dma("sync", dcol[:, :], dtbd)
        seq.dma("sync", acol[:, :], alogd)
        seq.dma("sync", xh_sb[:, :].rearrange("p (c t) -> p c t", t=4),
                xhd.rearrange("(c p) t -> p c t", p=128))
        for c in range(NDC):
            seq.dma("sync", big0[:, c * TPC:(c + 1) * TPC],
                    x0d[c * 128:(c + 1) * 128, :])
        e_ts(seq, ca1[:, :], mcol[:, NDC:2 * NDC], 1.0, None, ALU.add)
        emit_ln_mod(seq, lb, big0, h_sb, ones_col, ones_row, ca1, mcol[:, 0:NDC])
        emit_ln_mod(seq, lb, xh_sb, hh_sb, ones_col, ones_row, ca1,
                    mcol[:, 0:NDC], toks=4)

        wbufg = ar_b[:, 0:NDC * 512]
        for f0 in range(0, INTER, 512):
            src = wipT.rearrange("(c p) f -> p c f", p=128)[:, :, f0:f0 + 512]
            seq.dma("gpsimd", wbufg.rearrange("p (c f) -> p c f", f=512), src)
            for t0 in range(0, TPC, 128):
                for c in range(NDC):
                    e_mm(seq, p_g[:, :], h_sb[:, c * TPC + t0: c * TPC + t0 + 128],
                         wbufg[:, c * 512:(c + 1) * 512],
                         start=(c == 0), stop=(c == NDC - 1))
                e_copy(seq, stg[:, :], p_g[:, :])
                seq.dma("sync", gate_d[t0:t0 + 128, f0:f0 + 512], stg[:, :])

        for fi in range(NHB):
            f0 = INTER + fi * 128
            dma_wslice_fm(seq, wbuf, wipT, DM, f0, 128)
            for c in range(NDC):
                e_mm(seq, p_misc[:, 0:4], wbuf[:, c * 128:(c + 1) * 128],
                     hh_sb[:, :].rearrange("p (c t) -> p c t", t=4)[:, c, :],
                     start=(c == 0), stop=(c == NDC - 1))
            e_ts(seq, hbc[:, 0:4], p_misc[:, 0:4], hmask[:, 0:1], None, ALU.mult)
            for t0 in range(0, TPC, 512):
                for c in range(NDC):
                    e_mm(seq, p_g[:, :], wbuf[:, c * 128:(c + 1) * 128],
                         h_sb[:, c * TPC + t0: c * TPC + t0 + 512],
                         start=(c == 0), stop=(c == NDC - 1))
                e_copy(seq, hbc[:, 3 + t0: 3 + t0 + 512], p_g[:, :])
            acc = lb.mean_b
            for t0 in range(0, TPC, 512):
                e_ts(seq, acc[:, :], hbc[:, t0:t0 + 512],
                     cvw[:, fi * KCONV: fi * KCONV + 1], None, ALU.mult)
                for k in range(1, KCONV):
                    e_stt(seq, acc[:, :], hbc[:, t0 + k: t0 + k + 512],
                          cvw[:, fi * KCONV + k: fi * KCONV + k + 1],
                          acc[:, :], ALU.mult, ALU.add)
                e_ts(seq, acc[:, :], acc[:, :], cvb[:, fi:fi + 1], None, ALU.add)
                dst = (arena[:, fi * TPC + t0: fi * TPC + t0 + 512] if fi < 32
                       else (B_fm if fi == 32 else C_fm)[:, t0:t0 + 512])
                e_act(seq, dst, acc[:, :], AF.Silu)
        seq.dma("sync", cfm_d, C_fm[:, :])

        dma_wslice_fm(seq, wbuf, wipT, DM, INTER + CONV_DIM, 64)
        for t0 in range(0, TPC, 512):
            for c in range(NDC):
                e_mm(seq, p_g[0:NHEAD, :], wbuf[:, c * 64:(c + 1) * 64],
                     h_sb[:, c * TPC + t0: c * TPC + t0 + 512],
                     start=(c == 0), stop=(c == NDC - 1))
            e_copy(seq, dtr[:, t0:t0 + 512], p_g[0:NHEAD, :])
        e_ts(seq, dtr[:, :], dtr[:, :], dcol[:, :], None, ALU.add)
        e_act(seq, adt, dtr[:, :], AF.Abs)
        e_act(seq, adt, adt, AF.Exp, scale=-1.0)
        e_ts(seq, adt, adt, 1.0, None, ALU.add)
        e_act(seq, adt, adt, AF.Ln)
        e_ts(seq, dtr[:, :], dtr[:, :], 0.0, None, ALU.max)
        e_tt(seq, dtr[:, :], dtr[:, :], adt, ALU.add)
        e_ts(seq, dtr[:, :], dtr[:, :], DT_MIN, DT_MAX, ALU.max, ALU.min)
        e_act(seq, acol[:, :], acol[:, :], AF.Exp)
        e_ts(seq, acol[:, :], acol[:, :], -1.0, None, ALU.mult)
        e_ts(seq, adt, dtr[:, :], acol[:, :], None, ALU.mult)
        e_memset(seq, zer[:, :], 0.0)
        for c in range(NCHUNK):
            seq.emit("vector", lambda e, c=c: e.tensor_tensor_scan(
                acum[:, c * CH:(c + 1) * CH], adt[:, c * CH:(c + 1) * CH],
                zer[:, :], 0.0, ALU.add, ALU.add),
                reads=(adt[:, c * CH:(c + 1) * CH], zer[:, :]),
                writes=(acum[:, c * CH:(c + 1) * CH],))
        seq.dma("sync", acum_d, acum[:, :])
        e_copy(seq, ltc[:, :], acum[:, CH - 1:CH])
        for c in range(1, NCHUNK):
            e_tt(seq, ltc[:, :], ltc[:, :],
                 acum[:, (c + 1) * CH - 1:(c + 1) * CH], ALU.add)
        seq.dma("sync", lt_d, ltc[:, :])

        for c in range(NCHUNK):
            e_act(seq, t64[:, :], acum[:, c * CH:(c + 1) * CH], AF.Exp,
                  bias=acum[:, (c + 1) * CH - 1:(c + 1) * CH], scale=-1.0)
            for hf in range(2):
                col = (2 * c + hf) * NHEAD
                e_tr(seq, p_misc[:, 0:NHEAD],
                     dtr[:, c * CH + hf * 128: c * CH + (hf + 1) * 128],
                     identf[0:NHEAD, 0:NHEAD])
                e_copy(seq, dtT[:, col:col + NHEAD], p_misc[:, 0:NHEAD])
                e_tr(seq, p_misc[:, 0:NHEAD], t64[:, hf * 128:(hf + 1) * 128],
                     identf[0:NHEAD, 0:NHEAD])
                e_copy(seq, decT[:, col:col + NHEAD], p_misc[:, 0:NHEAD])
                e_tr(seq, p_misc[:, 0:NHEAD],
                     acum[:, c * CH + hf * 128: c * CH + (hf + 1) * 128],
                     identf[0:NHEAD, 0:NHEAD])
                e_ts(seq, nacT[:, col:col + NHEAD], p_misc[:, 0:NHEAD],
                     -1.0, None, ALU.mult)

        identb = identb2[:, :]
        pt_b = p_tb[:, :]
        for tc in range(8):
            for fc in range(32):
                e_tr(seq, pt_b,
                     arena[:, fc * TPC + tc * 128: fc * TPC + (tc + 1) * 128],
                     identb)
                e_copy(seq, hs_tm[:, tc * 4096 + fc * 128:
                                  tc * 4096 + (fc + 1) * 128], pt_b)
            e_tr(seq, pt_b, B_fm[:, tc * 128:(tc + 1) * 128], identb)
            e_copy(seq, B_tm[:, tc * ND:(tc + 1) * ND], pt_b)

        seq.dma("sync", zrow[:, 0:NHEAD], drowd)
        e_mm(seq, p_misc[:, 0:NHEAD], ones_row[:, :], zrow[:, 0:NHEAD])
        e_copy(seq, Db, p_misc[:, 0:NHEAD])

        e_memset(seq, run, 0.0)
        for c in range(NCHUNK):
            for lh in range(2):
                e_mm(seq, p_misc[:, 0:CH],
                     B_fm[:, c * CH + lh * 128: c * CH + (lh + 1) * 128],
                     C_fm[:, c * CH:(c + 1) * CH])
                e_copy(seq, gt_sb[:, lh * CH:(lh + 1) * CH], p_misc[:, 0:CH])
            e_act(seq, ltc[:, :], acum[:, (c + 1) * CH - 1:(c + 1) * CH], AF.Exp)
            e_tr(seq, p_misc[0:1, 0:NHEAD], ltc[:, :],
                 identf[0:NHEAD, 0:NHEAD])
            e_copy(seq, zrow[:, 0:NHEAD], p_misc[0:1, 0:NHEAD])
            e_mm(seq, p_misc[:, 0:NHEAD], ones_row[:, :], zrow[:, 0:NHEAD])
            e_copy(seq, dchb, p_misc[:, 0:NHEAD])
            for h in range(NHEAD):
                for zf in range(2):
                    col = (2 * c + zf) * NHEAD + h
                    e_tr(seq, p_g[0:1, 0:128], nacT[:, col:col + 1],
                         identf[:, :])
                    e_ts(seq, zrow[:, zf * 128:(zf + 1) * 128], p_g[0:1, 0:128],
                         -1.0, None, ALU.mult)
                e_mm(seq, p_misc[:, 0:CH], ones_row[:, :], zrow[:, 0:CH])
                for lh in range(2):
                    col = (2 * c + lh) * NHEAD + h
                    e_act(seq, lt_sb, p_misc[:, 0:CH], AF.Exp,
                          bias=nacT[:, col:col + 1])
                    seq.emit("gpsimd", lambda e, lh=lh: e.affine_select(
                        out=lt_sb, in_=lt_sb, compare_op=ALU.is_ge,
                        fill=0.0, base=-(lh * 128), pattern=[[1, CH]],
                        channel_multiplier=-1),
                        reads=(lt_sb,), writes=(lt_sb,))
                    e_stt(seq, mt_sb[:, lh * CH:(lh + 1) * CH],
                          gt_sb[:, lh * CH:(lh + 1) * CH], dtT[:, col:col + 1],
                          lt_sb, ALU.mult, ALU.mult)
                hs0 = hs_tm[:, (2 * c) * 4096 + h * PD: (2 * c) * 4096 + h * PD + PD]
                hs1 = hs_tm[:, (2 * c + 1) * 4096 + h * PD:
                            (2 * c + 1) * 4096 + h * PD + PD]
                e_mm(seq, p_y[:, 0:PD], mt_sb[:, 0:128], hs0)
                e_mm(seq, p_y[:, PD:2 * PD], mt_sb[:, 128:CH], hs0, stop=False)
                e_mm(seq, p_y[:, PD:2 * PD], mt_sb[:, CH + 128:2 * CH], hs1,
                     start=False)
                for lh in range(2):
                    col = (2 * c + lh) * NHEAD + h
                    e_ts(seq, bdec, B_tm[:, (2 * c + lh) * ND:(2 * c + lh + 1) * ND],
                         decT[:, col:col + 1], dtT[:, col:col + 1],
                         ALU.mult, ALU.mult)
                    e_mm(seq, p_st[:, :], bdec, (hs0 if lh == 0 else hs1),
                         start=(lh == 0), stop=(lh == 1))
                hq = h % 16
                e_copy(seq, ststg[:, hq * PD:(hq + 1) * PD], p_st[:, :])
                e_stt(seq, run[:, h * PD:(h + 1) * PD], run[:, h * PD:(h + 1) * PD],
                      dchb[:, h:h + 1], ststg[:, hq * PD:(hq + 1) * PD],
                      ALU.mult, ALU.add)
                if hq == 15:
                    seq.dma("sync",
                            st_d[c, h - 15:h + 1].transpose([1, 0, 2]),
                            ststg[:, :].rearrange("n (h p) -> n h p", p=PD))
                hw = h % 32
                for zh in range(2):
                    e_stt(seq, ytile[zh][:, hw * PD:(hw + 1) * PD],
                          (hs0 if zh == 0 else hs1), Db[:, h:h + 1],
                          p_y[:, zh * PD:(zh + 1) * PD], ALU.mult, ALU.add)
                if hw == 31:
                    g0 = h - 31
                    for zh in range(2):
                        seq.dma("sync",
                                yp_d[c * CH + zh * 128: c * CH + (zh + 1) * 128,
                                     g0 * PD:(g0 + 32) * PD], ytile[zh])
        for g in range(4):
            seq.dma("sync",
                    sum_d[g * 16:(g + 1) * 16].transpose([1, 0, 2]),
                    run[:, g * 1024:(g + 1) * 1024].rearrange(
                        "n (h p) -> n h p", p=PD))
        seq.install(nc, sems, block)
    return nc



def build_launch_D():
    nc = new_nc()
    x0d = nc.dram_tensor("x0", [DM, TPC], F32, kind="ExternalInput").ap()
    ypd = nc.dram_tensor("yp", [TPC, INTER], F32, kind="ExternalInput").ap()
    gated = nc.dram_tensor("gate", [TPC, INTER], BF16, kind="ExternalInput").ap()
    cfmd = nc.dram_tensor("cfm", [ND, TPC], BF16, kind="ExternalInput").ap()
    acumd = nc.dram_tensor("acum", [NHEAD, TPC], F32, kind="ExternalInput").ap()
    cstd = nc.dram_tensor("cstates", [NCHUNK, NHEAD, ND, PD], BF16,
                          kind="ExternalInput").ap()
    csall = nc.dram_tensor("csumall", [NCORE, NHEAD, ND, PD], BF16,
                           kind="ExternalInput").ap()
    cltall = nc.dram_tensor("cltall", [NCORE, NHEAD], F32,
                            kind="ExternalInput").ap()
    miTd = nc.dram_tensor("miT", [NCORE, NCORE], F32, kind="ExternalInput").ap()
    incd = nc.dram_tensor("incm", [NCORE, 1], F32, kind="ExternalInput").ap()
    wopT = nc.dram_tensor("wopT", [INTER, DM], F32, kind="ExternalInput").ap()
    nwT = nc.dram_tensor("nwT", [128, 32], F32, kind="ExternalInput").ap()
    modc = nc.dram_tensor("modc", [128, 2 * 6 * NDC], F32, kind="ExternalInput").ap()
    x1d = nc.dram_tensor("x1", [DM, TPC], F32, kind="ExternalOutput").ap()
    coef_bounce = nc.dram_tensor("coef_bounce", [NCORE, NHEAD], F32)

    with ExitStack() as ctx:
        b = Bufs(nc, ctx)
        run = b.sb("run_d", [128, NHEAD * PD], BF16)
        sgbuf = b.sb("sgbuf", [128, NHEAD * PD], BF16)
        ybuf = b.sb("ybuf", [128, INTER])
        gbuf = b.sb("gbuf", [128, INTER], BF16)
        yn_tm = b.sb("yn_tm", [128, INTER], BF16)
        yn_fm = b.sb("yn_fm", [128, 32 * TPC], BF16)
        cfm = b.sb("cfm_sb", [128, TPC], BF16)
        acum = b.sb("acum_sbd", [NHEAD, TPC])
        sdoT = b.sb("sdoT", [128, 8 * NHEAD])
        x0c = b.sb("x0c", [128, TPC])
        wbuf = b.sb("wbufd", [128, 32 * 128], BF16)
        ltall = b.sb("ltall", [NCORE, NHEAD])
        miT = b.sb("miT_sb", [NCORE, NCORE])
        incm = b.sb("incm_sb", [NCORE, 1])
        coef = b.sb("coef_sb", [NCORE, NHEAD])
        coefrow = b.sb("coefrow", [1, NCORE * NHEAD])
        cbj = b.sb("cbj", [128, NHEAD], BF16)
        dchb = b.sb("dchb_d", [128, NHEAD], BF16)
        ltc = b.sb("ltc_d", [NHEAD, 1])
        zrow = b.sb("zrow_d", [1, 128])
        nwc = b.sb("nwc", [128, 32])
        mcol = b.sb("mcold", [128, 2 * 6 * NDC])
        gac = b.sb("gacd", [128, NDC])
        sgn = b.sb("sgnd", [128, NDC])
        sq = b.sb("sq_d", [128, 512])
        ss = b.sb("ss_d", [128, 1])
        sst = b.sb("sst_d", [128, 1])
        ones_row = b.sb("ones_rowd", [1, 128])
        identf = b.sb("identfd", [128, 128])
        identb2 = b.sb("identbd", [128, 128], BF16)
        stg = b.sb("stg_d", [128, 512])
        p_yo = b.ps("p_yod", [128, 128])
        p_b = b.ps("p_bd", [128, 512])
        p_o = b.ps("p_od", [128, 512])
        p_tb = b.ps("p_tbd", [128, 128], BF16)
        sems = mk_sems(b)
        block = b.block()
        seq = Seq()

        e_memset(seq, ones_row[:, :], 1.0)
        e_identity(seq, identf)
        e_identity(seq, identb2)
        seq.dma("sync", mcol[:, :], modc)
        seq.dma("sync", nwc[:, :], nwT)
        seq.dma("sync", ltall[:, :], cltall)
        seq.dma("sync", miT[:, :], miTd)
        seq.dma("sync", incm[:, :], incd)
        seq.dma("sync", cfm[:, :], cfmd)
        seq.dma("sync", acum[:, :], acumd)
        e_clamp_gate(seq, gac[:, :], mcol[:, 2 * NDC:3 * NDC], sgn[:, :])

        e_mm(seq, p_b[0:NCORE, 0:NHEAD], miT[:, :], ltall[:, :])
        e_act(seq, coef[:, :], p_b[0:NCORE, 0:NHEAD], AF.Exp)
        e_ts(seq, coef[:, :], coef[:, :], incm[:, 0:1], None, ALU.mult)
        seq.dma("sync", coef_bounce.ap(), coef[:, :])
        seq.dma("sync", coefrow[:, :],
                coef_bounce.ap().rearrange("a b -> (a b)")[None, :])

        e_memset(seq, run[:, :], 0.0)
        for j in range(NCORE):
            seq.dma("sync", sgbuf[:, :].rearrange("n (h p) -> n h p", p=PD),
                    csall[j].transpose([1, 0, 2]))
            e_mm(seq, p_b[:, 0:NHEAD], ones_row[:, :],
                 coefrow[:, j * NHEAD:(j + 1) * NHEAD])
            e_copy(seq, cbj[:, :], p_b[:, 0:NHEAD])
            for h in range(NHEAD):
                e_stt(seq, run[:, h * PD:(h + 1) * PD],
                      sgbuf[:, h * PD:(h + 1) * PD], cbj[:, h:h + 1],
                      run[:, h * PD:(h + 1) * PD], ALU.mult, ALU.add)

        for c in range(NCHUNK):
            for hf in range(2):
                col = (2 * c + hf) * NHEAD
                e_tr(seq, p_yo[:, 0:NHEAD],
                     acum[:, c * CH + hf * 128: c * CH + (hf + 1) * 128],
                     identf[0:NHEAD, 0:NHEAD])
                e_act(seq, sdoT[:, col:col + NHEAD], p_yo[:, 0:NHEAD], AF.Exp)

        for c in range(NCHUNK):
            for lh in range(2):
                tok0 = c * CH + lh * 128
                seq.dma("sync", ybuf[:, :], ypd[tok0:tok0 + 128, :])
                seq.dma("sync", gbuf[:, :], gated[tok0:tok0 + 128, :])
                for hb in range(NHEAD // 8):
                    e_mm(seq, p_b[:, :], cfm[:, tok0:tok0 + 128],
                         run[:, hb * 512:(hb + 1) * 512])
                    for hl in range(8):
                        h = hb * 8 + hl
                        e_stt(seq, ybuf[:, h * PD:(h + 1) * PD],
                              p_b[:, hl * PD:(hl + 1) * PD],
                              sdoT[:, (2 * c + lh) * NHEAD + h:
                                   (2 * c + lh) * NHEAD + h + 1],
                              ybuf[:, h * PD:(h + 1) * PD], ALU.mult, ALU.add)
                for s0 in range(0, INTER, 512):
                    e_act(seq, sq[:, :], gbuf[:, s0:s0 + 512], AF.Silu)
                    e_tt(seq, ybuf[:, s0:s0 + 512], ybuf[:, s0:s0 + 512],
                         sq[:, :], ALU.mult)
                    e_act(seq, sq[:, :], ybuf[:, s0:s0 + 512], AF.Square,
                          accum_out=sst[:, :])
                    if s0 == 0:
                        e_copy(seq, ss[:, :], sst[:, :])
                    else:
                        e_tt(seq, ss[:, :], ss[:, :], sst[:, :], ALU.add)
                e_ts(seq, ss[:, :], ss[:, :], 1.0 / INTER, MEPS, ALU.mult, ALU.add)
                e_act(seq, ss[:, :], ss[:, :], AF.Sqrt)
                e_recip(seq, ss[:, :], ss[:, :])
                for s0 in range(0, INTER, 512):
                    e_ts(seq, yn_tm[:, s0:s0 + 512], ybuf[:, s0:s0 + 512],
                         ss[:, 0:1], None, ALU.mult)
                for fc in range(32):
                    e_tr(seq, p_tb[:, :], yn_tm[:, fc * 128:(fc + 1) * 128],
                         identb2[:, :])
                    e_ts(seq, yn_fm[:, fc * TPC + tok0: fc * TPC + tok0 + 128],
                         p_tb[:, :], nwc[:, fc:fc + 1], None, ALU.mult)
            e_act(seq, ltc[:, :], acum[:, (c + 1) * CH - 1:(c + 1) * CH], AF.Exp)
            e_tr(seq, p_yo[0:1, 0:NHEAD], ltc[:, :],
                 identf[0:NHEAD, 0:NHEAD])
            e_copy(seq, zrow[:, 0:NHEAD], p_yo[0:1, 0:NHEAD])
            e_mm(seq, p_b[:, 0:NHEAD], ones_row[:, :], zrow[:, 0:NHEAD])
            e_copy(seq, dchb[:, :], p_b[:, 0:NHEAD])
            seq.dma("sync", sgbuf[:, :].rearrange("n (h p) -> n h p", p=PD),
                    cstd[c].transpose([1, 0, 2]))
            for h in range(NHEAD):
                e_stt(seq, run[:, h * PD:(h + 1) * PD],
                      run[:, h * PD:(h + 1) * PD], dchb[:, h:h + 1],
                      sgbuf[:, h * PD:(h + 1) * PD], ALU.mult, ALU.add)

        def op_consume(d0, t0, tw, psum):
            dc = d0 // 128
            if t0 == 0:
                for tt in range(0, TPC, 512):
                    seq.dma("sync", x0c[:, tt:tt + 512],
                            x0d[dc * 128:(dc + 1) * 128, tt:tt + 512])
            e_stt(seq, stg[:, :tw], psum, gac[:, dc:dc + 1], x0c[:, t0:t0 + tw],
                  ALU.mult, ALU.add)
            seq.dma("sync", x1d[dc * 128:(dc + 1) * 128, t0:t0 + tw], stg[:, :tw])

        NIC = INTER // 128
        for d0 in range(0, DM, 128):
            dma_wslice_fm(seq, wbuf, wopT, INTER, d0, 128)
            for t0 in range(0, TPC, 512):
                for c in range(NIC):
                    e_mm(seq, p_o[:, :], wbuf[:, c * 128:(c + 1) * 128],
                         yn_fm[:, c * TPC + t0: c * TPC + t0 + 512],
                         start=(c == 0), stop=(c == NIC - 1))
                op_consume(d0, t0, 512, p_o[:, :])
        seq.install(nc, sems, block)
    return nc



_PROGS = {}


def _prog(name, builder):
    if name not in _PROGS:
        _PROGS[name] = builder()
    return _PROGS[name]


def _hadamard(n):
    H = np.array([[1.0]], np.float32)
    while H.shape[0] < n:
        H = np.block([[H, H], [H, -H]])
    return H


def kernel(**inputs):
    f32 = np.float32
    CORES = list(range(NCORE))
    ca = np.ascontiguousarray

    x_all = ca(np.asarray(inputs["L0_attn_input_states"], f32)[0])
    w1 = np.asarray(inputs["L0_ffn_weight1"])
    w2 = np.asarray(inputs["L0_ffn_weight2"])
    w3 = np.asarray(inputs["L0_ffn_weight3"])

    inA = []
    for core in CORES:
        m = {}
        for L in range(2):
            cvec = np.asarray(inputs[f"L{L}_c"], f32).reshape(16, 128)
            m[f"c{L}T"] = ca(cvec.T)
            wm = np.asarray(inputs[f"L{L}_wmod"], f32)
            m[f"wmod{L}s"] = ca(wm[:, core * 1536:(core + 1) * 1536])
        m["w1s"] = ca(w1[core * W1_ROWS:(core + 1) * W1_ROWS])
        m["w2s"] = ca(w2[core * W1_ROWS:(core + 1) * W1_ROWS])
        m["w3s"] = ca(w3[core * W3_ROWS:(core + 1) * W3_ROWS])
        inA.append(m)
    resA = run_bass_kernel_spmd(_prog("A", build_launch_A), inA, core_ids=CORES)
    mod_full = np.zeros((2, 12288), f32)
    for c in CORES:
        mod_full[:, c * 1536:(c + 1) * 1536] = resA.results[c]["modrow"]
    mc = np.zeros((128, 192), f32)
    for L in range(2):
        mc[:, L * 96:(L + 1) * 96] = (
            mod_full[L].reshape(6, 16, 128).transpose(2, 0, 1).reshape(128, 96))
    ssums = ca(np.stack([resA.results[c]["ssum"][0] for c in CORES]))

    inB = [dict(w1s=inA[c]["w1s"], w2s=inA[c]["w2s"], w3s=inA[c]["w3s"],
                allsum=ssums) for c in CORES]
    resB = run_bass_kernel_spmd(_prog("B", build_launch_B), inB, core_ids=CORES)
    w1pT = ca(np.concatenate([resB.results[c]["wq1"] for c in CORES], axis=0).T)
    w2pT = ca(np.concatenate([resB.results[c]["wq2"] for c in CORES], axis=0).T)
    w3pT = ca(np.concatenate([resB.results[c]["wq3"] for c in CORES], axis=0).T)

    cw = np.asarray(inputs["L0_attn_conv_w"], f32).reshape(CONV_DIM, KCONV)
    cwc = ca(cw.reshape(34, 128, KCONV).transpose(1, 0, 2).reshape(128, 34 * KCONV))
    cbc = ca(np.asarray(inputs["L0_attn_conv_b"], f32).reshape(34, 128).T)
    wipT = ca(np.asarray(inputs["L0_attn_in_proj_w"], f32).T)
    dtb = ca(np.asarray(inputs["L0_attn_dt_bias"], f32).reshape(NHEAD, 1))
    alog = ca(np.asarray(inputs["L0_attn_A_log"], f32).reshape(NHEAD, 1))
    drow = ca(np.asarray(inputs["L0_attn_D"], f32).reshape(1, NHEAD))
    x0s = [ca(x_all[c * TPC:(c + 1) * TPC].T) for c in CORES]
    inC = []
    for core in CORES:
        xh = np.zeros((DM, KCONV), f32)
        if core > 0:
            xh[:, :] = x_all[core * TPC - 3: core * TPC + 1].T
        hm = np.full((128, 1), 0.0 if core == 0 else 1.0, f32)
        inC.append(dict(x0=x0s[core], xh=ca(xh), wipT=wipT, convw=cwc,
                        convb=cbc, dtb=dtb, alog=alog, drow=drow, modc=mc,
                        hmask=hm))
    resC = run_bass_kernel_spmd(_prog("C", build_launch_C), inC, core_ids=CORES)

    csumall = ca(np.stack([resC.results[c]["csum"] for c in CORES]))
    cltall = ca(np.stack([resC.results[c]["clt"][:, 0] for c in CORES]))
    wopT = ca(np.asarray(inputs["L0_attn_out_proj_w"], f32).T)
    nwT = ca(np.asarray(inputs["L0_attn_norm_w"], f32).reshape(32, 128).T)
    inD = []
    for core in CORES:
        miT = np.zeros((NCORE, NCORE), f32)
        incm = np.zeros((NCORE, 1), f32)
        for j in range(NCORE):
            if j < core:
                incm[j, 0] = 1.0
            for k in range(NCORE):
                if j < k < core:
                    miT[k, j] = 1.0
        r = resC.results[core]
        inD.append(dict(x0=x0s[core], yp=r["yp"], gate=r["gate"], cfm=r["cfm"],
                        acum=r["acum"], cstates=r["cstates"], csumall=csumall,
                        cltall=cltall, miT=miT, incm=incm, wopT=wopT, nwT=nwT,
                        modc=mc))
    resD = run_bass_kernel_spmd(_prog("D", build_launch_D), inD, core_ids=CORES)

    wqkvT = ca(np.asarray(inputs["L1_attn_w_qkv"], f32).T)
    RL = np.zeros((128, 128), f32)
    for i in range(64):
        RL[i + 64, i] = -1.0
        RL[i, i + 64] = 1.0
    RLb = RL.astype(bf16)
    inE = []
    for core in CORES:
        pos = ca((np.arange(TPC, dtype=f32) + core * TPC)[None, :])
        inE.append(dict(x1=resD.results[core]["x1"], w1pT=w1pT, w2pT=w2pT,
                        w3pT=w3pT, wqkvT=wqkvT, modc=mc, rrot=RLb, posr=pos))
    resE = run_bass_kernel_spmd(_prog("E", build_launch_E), inE, core_ids=CORES)

    kTall = ca(np.concatenate([resE.results[c]["kT"] for c in CORES], axis=2))
    vt_full = np.concatenate([resE.results[c]["vt"] for c in CORES], axis=0)
    vall = ca(vt_full.reshape(SEQ, AHEADS, HD).transpose(1, 0, 2))
    qkmaxall = ca(np.concatenate([resE.results[c]["qkmax"] for c in CORES],
                                 axis=0))
    woT = ca(np.asarray(inputs["L1_attn_w_out"], f32).T)
    inF1 = [dict(x2=resE.results[c]["x2"], qT=resE.results[c]["qT"],
                 kTall=kTall, vall=vall, woT=woT, modc=mc, qkmaxall=qkmaxall)
            for c in CORES]
    resF1 = run_bass_kernel_spmd(_prog("F1", build_launch_F1), inF1,
                                 core_ids=CORES)

    W1T = ca(np.asarray(inputs["L1_ffn_W1"], f32).T)
    W3T = ca(np.asarray(inputs["L1_ffn_W3"], f32).T)
    W2T = ca(np.asarray(inputs["L1_ffn_W2"], f32).T)
    H128b = _hadamard(128).astype(bf16)
    H64 = _hadamard(64)
    HI2 = np.zeros((128, 128), f32)
    for cp in range(64):
        for t2 in range(2):
            HI2[cp * 2 + t2, t2::2] = H64[cp]
    HI2b = HI2.astype(bf16)
    inF2 = [dict(x3=resF1.results[c]["x3"], W1T=W1T, W3T=W3T, W2T=W2T,
                 modc=mc, h128=H128b, hi2=HI2b) for c in CORES]
    resF2 = run_bass_kernel_spmd(_prog("F2", build_launch_F2), inF2,
                                 core_ids=CORES)

    out = np.concatenate([resF2.results[c]["x4"].T for c in CORES], axis=0)
    return ca(out.astype(np.float32))[None]
